# version 07
# baseline (speedup 1.0000x reference)
import numpy as np
from contextlib import ExitStack
import concourse.bass as bass
import concourse.mybir as mybir
from concourse.bass_utils import run_bass_kernel_spmd

F32 = mybir.dt.float32
BF16 = mybir.dt.bfloat16
U8 = mybir.dt.uint8
AF = mybir.ActivationFunctionType
OP = mybir.AluOpType
AX = mybir.AxisListType.X

B, T, D = 8, 2048, 1024
NT = T // 128
ND = D // 128
MAGIC = 1.5 * 2.0 ** 23
MAGIC2 = 1.5 * 2.0 ** 22
LDEC = 0.01 * 0.01


class Phys:
    def __init__(self, sem):
        self.sem = sem
        self.total = 0


class Ctr:

    def __init__(self, phys):
        self.phys = phys
        self.base = phys.total
        self.v = 0

    @property
    def sem(self):
        return self.phys.sem

    def inc(self, n):
        self.v += n
        self.phys.total = self.base + self.v
        return self.v


class Prog:
    ENGINES = ("sync", "vector", "scalar", "tensor", "gpsimd")

    def __init__(self, nc, ctx, npool=72):
        self.nc = nc
        self.ctx = ctx
        self.steps = {e: [] for e in self.ENGINES}
        self.pool = [Phys(ctx.enter_context(nc.semaphore("sp%d" % i)))
                     for i in range(npool)]
        self.pool_i = 0
        self._ctrs = {}
        self._stage_phys = set()
        self._dma_ctrs = set()
        self._marks = {e: 0 for e in self.ENGINES}

    def begin_stage(self):
        self._ctrs = {}
        self._stage_phys = set()
        self._dma_ctrs = set()
        self._marks = {e: len(self.steps[e]) for e in self.ENGINES}

    def ctr(self, name):
        if name not in self._ctrs:
            for _ in range(len(self.pool)):
                ph = self.pool[self.pool_i]
                self.pool_i = (self.pool_i + 1) % len(self.pool)
                if id(ph) not in self._stage_phys:
                    break
            else:
                raise RuntimeError("semaphore pool exhausted")
            self._stage_phys.add(id(ph))
            self._ctrs[name] = Ctr(ph)
        return self._ctrs[name]

    def on(self, eng, fn, waits=(), inc=None, n=1):
        c = self.ctr(inc) if isinstance(inc, str) else inc
        tv = c.inc(n) if c is not None else None
        self.steps[eng].append((fn, [w for w in waits if w is not None], c, n, tv))
        if c is not None:
            return (c, tv)
        return None

    def drain(self, eng):
        self.steps[eng].append((lambda e: e.drain(), [], None, 0, None))

    def dma(self, out, in_, waits=(), inc="dma", eng="sync"):
        c = self.ctr(inc) if isinstance(inc, str) else inc
        self._dma_ctrs.add(c)
        return self.on(eng, lambda e: e.dma_start(out=out, in_=in_), waits, c, 16)

    def end_stage(self):
        best = {}
        for e in self.ENGINES:
            for fn, waits, c, n, tv in reversed(self.steps[e][self._marks[e]:]):
                if c is not None:
                    if best.get(id(c), (None, -1))[1] < tv:
                        best[id(c)] = (c, tv)
                    break
        for c in self._dma_ctrs:
            if best.get(id(c), (None, -1))[1] < c.v:
                best[id(c)] = (c, c.v)
        return list(best.values())

    def emit(self):
        blk = self.ctx.enter_context(self.nc.Block())

        def mk(eng_name):
            def body(e):
                for fn, waits, c, n, tv in self.steps[eng_name]:
                    for (wc, wv) in waits:
                        e.wait_ge(wc.sem, wc.base + wv)
                    ins = fn(e)
                    if c is not None:
                        assert ins is not None
                        ins.then_inc(c.sem, n)
            return body

        for e in self.ENGINES:
            if self.steps[e]:
                getattr(blk, e)(mk(e))


def _nc():
    return bass.Bass("TRN2", target_bir_lowering=False)


def emit_transpose_seq(p, pairs, idt, psA, psB, wait_src, name, first_extra=None):
    mt = p.ctr(name + "_mt")
    ev = p.ctr(name + "_ev")
    toks = []
    for i, (s, d) in enumerate(pairs):
        bank = psA if i % 2 == 0 else psB
        waits = []
        if i == 0 and first_extra is not None:
            waits.append(first_extra)
        w = wait_src(i)
        if w is not None:
            waits.append(w)
        if i >= 2:
            waits.append((ev, i - 1))
        toks.append(p.on("tensor", lambda e, b=bank, s=s: e.transpose(b[:, 0:128], s, idt[:]),
                         waits, mt))
    evtoks = []
    for i, (s, d) in enumerate(pairs):
        bank = psA if i % 2 == 0 else psB
        evtoks.append(p.on("vector", lambda e, b=bank, d=d: e.tensor_copy(d, b[:, 0:128]),
                           [toks[i]], ev))
    return ev


def emit_rowstats(p, tiles_ap, n_dt, onc, psS, rowout, wait_fn, name, evac_eng="vector"):
    mm = p.ctr(name + "_mm")
    evc = p.ctr(name + "_ev")
    tok = None
    for ch in range(T // 512):
        mts = []
        for dt in range(n_dt):
            waits = [wait_fn(dt)]
            if ch > 0 and dt == 0:
                waits.append((evc, ch))
            mts.append(p.on("tensor", lambda e, ch=ch, dt=dt, st=(dt == 0), sp=(dt == n_dt - 1):
                            e.matmul(psS[0:1, :], onc[:],
                                     tiles_ap(dt)[:, ch * 512:(ch + 1) * 512],
                                     start=st, stop=sp), waits, mm))
        tok = p.on(evac_eng, lambda e, ch=ch: e.tensor_copy(rowout[0:1, ch * 512:(ch + 1) * 512],
                                                            psS[0:1, :]),
                   [mts[-1]], evc)
    return tok



def emit_ln0(nc, p, pfx, B, x, ident, xnT):
    with ExitStack() as ctx:
        xt = ctx.enter_context(nc.sbuf_tensor(pfx + "xt", [128, NT * D], F32))
        xo = ctx.enter_context(nc.sbuf_tensor(pfx + "xo", [128, ND * T], F32))
        sq = ctx.enter_context(nc.sbuf_tensor(pfx + "sq", [128, 2 * D], F32))
        idt = ctx.enter_context(nc.sbuf_tensor(pfx + "idt", [128, 128], F32))
        stats = ctx.enter_context(nc.sbuf_tensor(pfx + "stats", [128, 6 * NT], F32))
        mcol = ctx.enter_context(nc.sbuf_tensor(pfx + "mcol", [128, NT], F32))
        icol = ctx.enter_context(nc.sbuf_tensor(pfx + "icol", [128, NT], F32))
        psA = ctx.enter_context(nc.psum_tensor(pfx + "psA", [128, 512], F32))
        psB = ctx.enter_context(nc.psum_tensor(pfx + "psB", [128, 512], F32))
        p.dma(xt[:].rearrange("p (k d) -> p k d", k=NT),
              x.rearrange("(k p) d -> p k d", p=128), waits=B)
        p.dma(idt[:], ident, inc="dma", waits=B)
        tok_ld = (p.ctr("dma"), p.ctr("dma").v)

        nrm = p.ctr("nrm")
        for k in range(NT):
            xk = xt[:, k * D:(k + 1) * D]
            sqk = sq[:, (k % 2) * D:(k % 2) * D + D]
            c = lambda j, k=k: stats[:, j * NT + k:j * NT + k + 1]
            sq_w = [tok_ld] if k < 2 else [tok_ld, (nrm, k - 1)]
            t_sq = p.on("scalar", lambda e, xk=xk, sqk=sqk: e.activation(sqk, xk, AF.Square),
                        sq_w, "sqd")
            t_s1 = p.on("vector", lambda e, xk=xk, o=c(0): e.reduce_sum(o, xk, axis=AX),
                        [tok_ld], "vst")
            t_s2 = p.on("vector", lambda e, sqk=sqk, o=c(1): e.reduce_sum(o, sqk, axis=AX),
                        [t_sq], "vst")
            t_m = p.on("vector", lambda e, i=c(0), o=mcol[:, k:k + 1]:
                       e.tensor_scalar(o, i, 1.0 / D, None, OP.mult), [t_s1], "vst")
            t_m2 = p.on("vector", lambda e, i=mcol[:, k:k + 1], o=c(5):
                        e.tensor_scalar(o, i, i, None, OP.mult), [t_m], "vst")
            t_e2 = p.on("vector", lambda e, i=c(1), o=c(2):
                        e.tensor_scalar(o, i, 1.0 / D, None, OP.mult), [t_s2], "vst")
            t_v = p.on("vector", lambda e, a=c(5), b=c(2), o=c(3):
                       e.scalar_tensor_tensor(o, a, -1.0, b, OP.mult, OP.add),
                       [t_m2, t_e2], "vst")
            p.drain("vector")
            t_ve = p.on("vector", lambda e, io=c(3), o=c(4):
                        e.tensor_scalar(o, io, 1e-5, None, OP.add), [t_v], "vst")
            t_ln = p.on("scalar", lambda e, i=c(4), o=c(3):
                        e.activation(o, i, AF.Ln), [t_ve], "sqd")
            p.drain("scalar")
            t_iv = p.on("scalar", lambda e, i=c(3), o=icol[:, k:k + 1]:
                        e.activation(o, i, AF.Exp, scale=-0.5), [t_ln], "sqd")
            p.on("vector", lambda e, xk=xk, m=mcol[:, k:k + 1], iv=icol[:, k:k + 1]:
                 e.tensor_scalar(xk, xk, m, iv, OP.subtract, OP.mult),
                 [t_iv], nrm)

        pairs = []
        for k in range(NT):
            for dt in range(ND):
                pairs.append((xt[:, k * D + dt * 128: k * D + (dt + 1) * 128],
                              xo[:, dt * T + k * 128: dt * T + (k + 1) * 128]))
        ev = emit_transpose_seq(p, pairs, idt, psA, psB,
                                lambda i: (nrm, i // ND + 1) if i % ND == 0 else None, "tr",
                                first_extra=tok_ld)
        sto = p.ctr("sto")
        for dt in range(ND):
            p.dma(xnT.rearrange("(dt p) t -> dt p t", p=128)[dt],
                  xo[:, dt * T:(dt + 1) * T],
                  waits=[(ev, (NT - 1) * ND + dt + 1)], inc=sto)



def emit_res_ln(nc, p, pfx, B, x, a, f, ident, onescol, ones1x, x1T, xn1T):
    with ExitStack() as ctx:
        xt = ctx.enter_context(nc.sbuf_tensor(pfx + "xt", [128, NT * D], F32))
        x1 = ctx.enter_context(nc.sbuf_tensor(pfx + "x1", [128, ND * T], F32))
        ar = xt[:, 0:2 * T]
        fb = xt[:, 2 * T:4 * T]
        sqb = xt[:, 4 * T:6 * T]
        idt = ctx.enter_context(nc.sbuf_tensor(pfx + "idt", [128, 128], F32))
        onc = ctx.enter_context(nc.sbuf_tensor(pfx + "onc", [128, 1], F32))
        on1 = ctx.enter_context(nc.sbuf_tensor(pfx + "on1", [1, 128], F32))
        row = ctx.enter_context(nc.sbuf_tensor(pfx + "row", [1, 6 * T], F32))
        psA = ctx.enter_context(nc.psum_tensor(pfx + "psA", [128, 512], F32))
        psB = ctx.enter_context(nc.psum_tensor(pfx + "psB", [128, 512], F32))
        psS = ctx.enter_context(nc.psum_tensor(pfx + "psS", [128, 512], F32))
        psI = ctx.enter_context(nc.psum_tensor(pfx + "psI", [128, 512], F32))
        psM = ctx.enter_context(nc.psum_tensor(pfx + "psM", [128, 512], F32))

        p.dma(xt[:].rearrange("p (k d) -> p k d", k=NT),
              x.rearrange("(k p) d -> p k d", p=128), waits=B)
        p.dma(idt[:], ident, waits=B)
        p.dma(onc[:], onescol, waits=B)
        p.dma(on1[:], ones1x, waits=B)
        alldma = (p.ctr("dma"), p.ctr("dma").v)

        pairs = []
        for k in range(NT):
            for dt in range(ND):
                pairs.append((xt[:, k * D + dt * 128: k * D + (dt + 1) * 128],
                              x1[:, dt * T + k * 128: dt * T + (k + 1) * 128]))
        ev = emit_transpose_seq(p, pairs, idt, psA, psB,
                                lambda i: None, "tr", first_extra=alldma)
        evall = (ev, NT * ND)

        add1 = p.ctr("add1")
        for dt in range(ND):
            wa = [evall] if dt < 2 else [(add1, dt - 1)]
            la = p.dma(ar[:, (dt % 2) * T:(dt % 2) * T + T],
                       a.rearrange("(dt p) t -> dt p t", p=128)[dt],
                       waits=wa, inc="dma%d" % (dt % 2))
            lf = p.dma(fb[:, (dt % 2) * T:(dt % 2) * T + T],
                       f.rearrange("(dt p) t -> dt p t", p=128)[dt],
                       waits=wa, inc="dmf%d" % (dt % 2))
            t1 = p.on("vector", lambda e, dt=dt: e.tensor_tensor(
                x1[:, dt * T:(dt + 1) * T], x1[:, dt * T:(dt + 1) * T],
                ar[:, (dt % 2) * T:(dt % 2) * T + T], OP.add), [la, lf], "vadd")
            p.on("vector", lambda e, dt=dt: e.tensor_tensor(
                x1[:, dt * T:(dt + 1) * T], x1[:, dt * T:(dt + 1) * T],
                fb[:, (dt % 2) * T:(dt % 2) * T + T], OP.add), [t1], add1)
        sto = p.ctr("sto")
        for dt in range(ND):
            p.dma(x1T.rearrange("(dt p) t -> dt p t", p=128)[dt],
                  x1[:, dt * T:(dt + 1) * T], waits=[(add1, dt + 1)], inc=sto)

        tok_s1 = emit_rowstats(p, lambda dt: x1[:, dt * T:(dt + 1) * T], ND,
                               onc, psS, row, lambda dt: (add1, ND), "s1")
        sqd = p.ctr("sqd")
        s2mm = p.ctr("s2mm")
        s2ev = p.ctr("s2ev")
        for ch in range(4):
            for dt in range(ND):
                i = ch * ND + dt
                sq_w = [(add1, ND)] if i < 2 else [(s2mm, i - 1)]
                t_q = p.on("scalar", lambda e, ch=ch, dt=dt: e.activation(
                    sqb[:, (dt % 2) * T: (dt % 2) * T + 512],
                    x1[:, dt * T + ch * 512: dt * T + ch * 512 + 512], AF.Square),
                    sq_w, sqd)
                waits = [t_q]
                if dt == 0 and ch > 0:
                    waits.append((s2ev, ch))
                p.on("tensor", lambda e, ch=ch, dt=dt, st=(dt == 0), sp=(dt == ND - 1):
                     e.matmul(psI[0:1, :], onc[:],
                              sqb[:, (dt % 2) * T: (dt % 2) * T + 512],
                              start=st, stop=sp), waits, s2mm)
            p.on("vector", lambda e, ch=ch: e.tensor_copy(
                row[0:1, T + ch * 512: T + ch * 512 + 512], psI[0:1, :]),
                [(s2mm, (ch + 1) * ND)], s2ev)

        r = lambda j: row[0:1, j * T:(j + 1) * T]
        t1 = p.on("vector", lambda e: e.tensor_scalar(r(2), r(0), 1.0 / D, None, OP.mult),
                  [tok_s1], "rs")
        p.drain("vector")
        t2 = p.on("vector", lambda e: e.tensor_tensor(r(3), r(2), r(2), OP.mult),
                  [t1], "rs")
        t3 = p.on("vector", lambda e: e.tensor_scalar(r(5), r(1), 1.0 / D, None, OP.mult),
                  [(s2ev, 4)], "rs")
        t4 = p.on("vector", lambda e: e.scalar_tensor_tensor(r(3), r(3), -1.0, r(5),
                                                             OP.mult, OP.add),
                  [t2, t3], "rs")
        p.drain("vector")
        t5 = p.on("vector", lambda e: e.tensor_scalar(r(3), r(3), 1e-5, None, OP.add),
                  [t4], "rs")
        t6 = p.on("scalar", lambda e: e.activation(r(1), r(3), AF.Ln), [t5], "ras")
        p.drain("scalar")
        t7 = p.on("scalar", lambda e: e.activation(r(4), r(1), AF.Exp, scale=-0.5), [t6], "ras")
        t8 = p.on("vector", lambda e: e.tensor_tensor(r(5), r(2), r(4), OP.mult), [t7], "rs")

        nsto = p.ctr("nsto")
        for ch in range(4):
            wb = [t8, (sto, 8 * 16)]
            if ch > 0:
                wb.append((nsto, ch * ND))
            bi = p.on("tensor", lambda e, ch=ch: e.matmul(
                psI[:, :], on1[:], r(4)[:, ch * 512:(ch + 1) * 512], start=True, stop=True),
                wb, "bmm")
            bm = p.on("tensor", lambda e, ch=ch: e.matmul(
                psM[:, :], on1[:], r(5)[:, ch * 512:(ch + 1) * 512], start=True, stop=True),
                [t8], "bmm")
            for dt in range(ND):
                tn = p.on("vector", lambda e, ch=ch, dt=dt: e.tensor_tensor(
                    x1[:, dt * T + ch * 512: dt * T + ch * 512 + 512],
                    x1[:, dt * T + ch * 512: dt * T + ch * 512 + 512], psI[:, :], OP.mult),
                    [bi, bm], "vn")
                p.on("vector", lambda e, ch=ch, dt=dt: e.tensor_tensor(
                    x1[:, dt * T + ch * 512: dt * T + ch * 512 + 512],
                    x1[:, dt * T + ch * 512: dt * T + ch * 512 + 512], psM[:, :], OP.subtract),
                    [tn], nsto)
        for dt in range(ND):
            p.dma(xn1T.rearrange("(dt p) t -> dt p t", p=128)[dt],
                  x1[:, dt * T:(dt + 1) * T], waits=[(nsto, 4 * ND)], inc=sto)



def emit_final(nc, p, pfx, B, x1, a, f, ident, out):
    with ExitStack() as ctx:
        xb = ctx.enter_context(nc.sbuf_tensor(pfx + "xb", [128, ND * T], F32))
        ab = ctx.enter_context(nc.sbuf_tensor(pfx + "ab", [128, ND * T], F32))
        ob = ctx.enter_context(nc.sbuf_tensor(pfx + "ob", [128, NT * D], F32))
        idt = ctx.enter_context(nc.sbuf_tensor(pfx + "idt", [128, 128], F32))
        psA = ctx.enter_context(nc.psum_tensor(pfx + "psA", [128, 512], F32))
        psB = ctx.enter_context(nc.psum_tensor(pfx + "psB", [128, 512], F32))
        t_x = p.dma(xb[:].rearrange("p (dt t) -> p dt t", dt=ND),
                    x1.rearrange("(dt p) t -> p dt t", p=128), waits=B)
        t_a = p.dma(ab[:].rearrange("p (dt t) -> p dt t", dt=ND),
                    a.rearrange("(dt p) t -> p dt t", p=128), waits=B)
        p.dma(idt[:], ident, waits=B)
        alldma = (p.ctr("dma"), p.ctr("dma").v)
        add1 = p.ctr("add1")
        fl = []
        for dt in range(ND):
            w = [alldma] if dt < 2 else [(add1, dt - 1)]
            fl.append(p.dma(ob[:, (dt % 2) * T:(dt % 2) * T + T],
                            f.rearrange("(dt p) t -> dt p t", p=128)[dt],
                            waits=w, inc="dmf%d" % (dt % 2)))
        for dt in range(ND):
            t1 = p.on("vector", lambda e, dt=dt: e.tensor_tensor(
                xb[:, dt * T:(dt + 1) * T], xb[:, dt * T:(dt + 1) * T],
                ab[:, dt * T:(dt + 1) * T], OP.add), [alldma, fl[dt]], "vadd")
            p.on("vector", lambda e, dt=dt: e.tensor_tensor(
                xb[:, dt * T:(dt + 1) * T], xb[:, dt * T:(dt + 1) * T],
                ob[:, (dt % 2) * T:(dt % 2) * T + T], OP.add), [t1], add1)
        pairs = []
        for k in range(NT):
            for dt in range(ND):
                pairs.append((xb[:, dt * T + k * 128: dt * T + (k + 1) * 128],
                              ob[:, k * D + dt * 128: k * D + (dt + 1) * 128]))
        ev = emit_transpose_seq(p, pairs, idt, psA, psB,
                                lambda i: (add1, ND), "tr")
        sto = p.ctr("sto")
        for k in range(NT):
            p.dma(out.rearrange("(k p) d -> k p d", p=128)[k],
                  ob[:, k * D:(k + 1) * D], waits=[(ev, (k + 1) * ND)], inc=sto)





def emit_ffn1(nc, p, pfx, B, xnT, wupT, wdnT, onescol, ones1x, fT, wup_c, wdn_c):
    DF = 4096
    NF = DF // 128
    with ExitStack() as ctx:
        xb = ctx.enter_context(nc.sbuf_tensor(pfx + "xb", [128, ND * T], F32))
        xnb = ctx.enter_context(nc.sbuf_tensor(pfx + "xnb", [128, ND * T], BF16))
        wring = ctx.enter_context(nc.sbuf_tensor(pfx + "wring", [128, 2 * ND * 128], BF16))
        wdb = ctx.enter_context(nc.sbuf_tensor(pfx + "wdb", [128, 2 * NF * 128], BF16))
        i2sb = ctx.enter_context(nc.sbuf_tensor(pfx + "i2sb", [128, 1024], F32))
        fsl = ctx.enter_context(nc.sbuf_tensor(pfx + "fsl", [128, 2 * 1024], F32))
        row = ctx.enter_context(nc.sbuf_tensor(pfx + "row", [1, 4 * T], F32))
        acc = ctx.enter_context(nc.sbuf_tensor(pfx + "acc", [128, 4], F32))
        iac = ctx.enter_context(nc.sbuf_tensor(pfx + "iac", [128, 2], F32))
        onc = ctx.enter_context(nc.sbuf_tensor(pfx + "onc", [128, 1], F32))
        on1 = ctx.enter_context(nc.sbuf_tensor(pfx + "on1", [1, 128], F32))
        psS = ctx.enter_context(nc.psum_tensor(pfx + "psS", [128, 512], F32))
        psI = ctx.enter_context(nc.psum_tensor(pfx + "psI", [128, 512], F32))
        psU0 = ctx.enter_context(nc.psum_tensor(pfx + "psU0", [128, 512], F32))
        psU1 = ctx.enter_context(nc.psum_tensor(pfx + "psU1", [128, 512], F32))
        hb = xb[:].bitcast(BF16)
        stg = xb

        p.dma(xb[:].rearrange("p (dt t) -> p dt t", dt=ND),
              xnT.rearrange("(dt p) t -> p dt t", p=128), waits=B)
        p.dma(onc[:], onescol, waits=B)
        p.dma(on1[:], ones1x, waits=B)
        alldma = (p.ctr("dma"), p.ctr("dma").v)

        sqd = p.ctr("sqd")
        s1mm = p.ctr("s1mm")
        s1ev = p.ctr("s1ev")
        for ch in range(4):
            for dt in range(ND):
                i = ch * ND + dt
                sq_w = [alldma] if i < 2 else [(s1mm, i - 1)]
                t_q = p.on("scalar", lambda e, ch=ch, dt=dt: e.activation(
                    fsl[:, (dt % 2) * 1024:(dt % 2) * 1024 + 512],
                    xb[:, dt * T + ch * 512: dt * T + ch * 512 + 512], AF.Square),
                    sq_w, sqd)
                waits = [t_q]
                if dt == 0 and ch > 0:
                    waits.append((s1ev, ch))
                p.on("tensor", lambda e, dt=dt, st=(dt == 0), sp=(dt == ND - 1):
                     e.matmul(psS[0:1, :], onc[:],
                              fsl[:, (dt % 2) * 1024:(dt % 2) * 1024 + 512],
                              start=st, stop=sp), waits, s1mm)
            p.on("vector", lambda e, ch=ch: e.tensor_copy(row[0:1, ch * 512:(ch + 1) * 512],
                                                          psS[0:1, :]),
                 [(s1mm, (ch + 1) * ND)], s1ev)
        r = lambda j: row[0:1, j * T:(j + 1) * T]
        t1 = p.on("scalar", lambda e: e.activation(r(1), r(0), AF.Ln), [(s1ev, 4)], "ras")
        p.drain("scalar")
        t2 = p.on("scalar", lambda e: e.activation(r(2), r(1), AF.Exp, scale=0.5), [t1], "ras")
        t3 = p.on("vector", lambda e: e.tensor_scalar(r(1), r(2), 1.0 / 32.0, 1e-8,
                                                      OP.mult, OP.add), [t2], "rs")
        t4 = p.on("scalar", lambda e: e.activation(r(2), r(1), AF.Ln), [t3], "ras")
        p.drain("scalar")
        t5 = p.on("scalar", lambda e: e.activation(r(3), r(2), AF.Exp, scale=-1.0), [t4], "ras")
        xnbd = p.ctr("xnbd")
        for ch in range(4):
            bi = p.on("tensor", lambda e, ch=ch: e.matmul(
                psI[:, :], on1[:], r(3)[:, ch * 512:(ch + 1) * 512], start=True, stop=True),
                [t5] + ([(xnbd, ch * ND)] if ch else []), "bmm")
            for dt in range(ND):
                p.on("vector", lambda e, ch=ch, dt=dt: e.tensor_tensor(
                    xnb[:, dt * T + ch * 512: dt * T + ch * 512 + 512],
                    xb[:, dt * T + ch * 512: dt * T + ch * 512 + 512], psI[:, :], OP.mult),
                    [bi], xnbd)
        tok_xnb = (xnbd, 4 * ND)

        def quant_weight(tag, wT, n_slab, slab_cols, count, cache_store, res_tok):
            al = p.ctr(tag + "_al")
            toks = []
            res_list = res_tok if isinstance(res_tok, list) else [res_tok]
            for s in range(n_slab):
                w = list(res_list) if s < 2 else [(al, s - 1)]
                toks.append(p.dma(stg[:, (s % 2) * slab_cols:(s % 2 + 1) * slab_cols],
                                  wT.rearrange("(s p) c -> s p c", p=128)[s],
                                  waits=w, inc=tag + "_ld%d" % (s % 2)))
            for s in range(n_slab):
                t_r = p.on("vector", lambda e, s=s: e.reduce_sum(
                    acc[:, (s % 2) + 2:(s % 2) + 3],
                    stg[:, (s % 2) * slab_cols:(s % 2 + 1) * slab_cols],
                    axis=AX, apply_absolute_value=True), [toks[s]], tag + "_r")
                p.drain("vector")
                if s == 0:
                    p.on("vector", lambda e: e.tensor_copy(acc[:, 0:1], acc[:, 2:3]),
                         [t_r], al)
                else:
                    p.on("vector", lambda e, s=s: e.tensor_tensor(
                        acc[:, 0:1], acc[:, 0:1], acc[:, (s % 2) + 2:(s % 2) + 3], OP.add),
                        [t_r], al)
                p.drain("vector")
            tm = p.on("tensor", lambda e: e.matmul(psS[0:1, 0:1], acc[:, 0:1], onc[:],
                                                   start=True, stop=True),
                      [(al, n_slab)], tag + "_tm")
            te = p.on("vector", lambda e: e.tensor_copy(iac[0:1, 0:1], psS[0:1, 0:1]),
                      [tm], tag + "_te")
            p.drain("vector")
            tr = p.on("vector", lambda e: e.reciprocal(iac[0:1, 1:2], iac[0:1, 0:1]),
                      [te], tag + "_te")
            p.drain("vector")
            ts_ = p.on("vector", lambda e: e.tensor_scalar(iac[0:1, 0:1], iac[0:1, 1:2],
                                                           float(count), None, OP.mult),
                       [tr], tag + "_te")
            tb = p.on("tensor", lambda e: e.matmul(psS[:, 0:1], on1[:], iac[0:1, 0:1],
                                                   start=True, stop=True), [ts_], tag + "_tm")
            tc = p.on("vector", lambda e: e.tensor_copy(iac[:, 1:2], psS[:, 0:1]), [tb], tag + "_te")
            qd = p.ctr(tag + "_qd")
            dns = [p.ctr(tag + "_dn0"), p.ctr(tag + "_dn1")]
            for s in range(n_slab):
                done = dns[s % 2]
                w = [tc] if s < 2 else [(done, ((s - 2) // 2 + 1) * 16)]
                tok2 = p.dma(stg[:, (s % 2) * slab_cols:(s % 2 + 1) * slab_cols],
                             wT.rearrange("(s p) c -> s p c", p=128)[s],
                             waits=w, inc=tag + "_qld%d" % (s % 2))
                toks2 = {s: tok2}
                sl = stg[:, (s % 2) * slab_cols:(s % 2 + 1) * slab_cols]
                gb = stg[:, (2 + s % 2) * slab_cols:(3 + s % 2) * slab_cols]
                gbb = gb.bitcast(BF16)[:, 0:slab_cols]
                ta = p.on("vector", lambda e, sl=sl: e.tensor_scalar(
                    sl, sl, iac[:, 1:2], None, OP.mult), [toks2[s]], qd)
                p.drain("vector")
                tg = p.on("vector", lambda e, sl=sl, gb=gb: e.tensor_scalar(
                    gb, sl, 0.5, None, OP.is_gt), [ta], qd)
                tl = p.on("vector", lambda e, sl=sl: e.tensor_scalar(
                    sl, sl, -0.5, None, OP.is_lt), [tg], qd)
                p.drain("vector")
                tb2 = p.on("vector", lambda e, sl=sl, gb=gb, gbb=gbb: e.tensor_tensor(
                    gbb, gb, sl, OP.subtract), [tl], qd)
                cache_store(s, gbb, [tb2], done)
            return [(d, d.v) for d in dns]

        def store_up(s, srcb, waits, done):
            return p.dma(wup_c[:, s, :, :].rearrange("a p c -> p a c"),
                         srcb.rearrange("p (a c) -> p a c", a=NF),
                         waits=waits, inc=done)

        def store_dn(s, srcb, waits, done):
            return p.dma(wdn_c[:, s, :, :].rearrange("a p c -> p a c"),
                         srcb.rearrange("p (a c) -> p a c", a=ND),
                         waits=waits, inc=done)

        tok_wup = quant_weight("wu", wupT, ND, DF, D * DF, store_up, tok_xnb)
        tok_wdn = quant_weight("wd", wdnT, NF, D, DF * D, store_dn, tok_wup)

        upmm = p.ctr("upmm")
        gel = p.ctr("gel")
        s2mm = p.ctr("s2mm")
        s2ev = p.ctr("s2ev")
        i2d = p.ctr("i2d")
        dnmm = p.ctr("dnmm")
        fev = p.ctr("fev")
        fstos = [p.ctr("fsto0"), p.ctr("fsto1")]
        gtok = {}
        dn_ev = {}
        for hv in range(2):
            hoff = hv * 1024
            for ft in range(NF):
                g = hv * NF + ft
                wl_w = list(tok_wup)
                if g >= 2:
                    wl_w.append((gel, (g - 2 + 1) * 2))
                wl = p.dma(wring[:, (ft % 2) * ND * 128:(ft % 2 + 1) * ND * 128],
                           wup_c[ft].rearrange("b p c -> p b c"),
                           waits=wl_w, inc="uld%d" % (ft % 2))
                for ch in range(2):
                    ps = psU0 if ch == 0 else psU1
                    for dt in range(ND):
                        w = []
                        if dt == 0:
                            w = [wl, tok_xnb]
                            if g >= 1:
                                w.append((gel, (g - 1) * 2 + ch + 1))
                            if hv == 1 and ft == 0:
                                w.append((fev, 2 * ND))
                        p.on("tensor", lambda e, ft=ft, ch=ch, dt=dt, ps=ps, hoff=hoff,
                             st=(dt == 0), sp=(dt == ND - 1):
                             e.matmul(ps[:, :],
                                      wring[:, (ft % 2) * ND * 128 + dt * 128:(ft % 2) * ND * 128 + (dt + 1) * 128],
                                      xnb[:, dt * T + hoff + ch * 512: dt * T + hoff + ch * 512 + 512],
                                      start=st, stop=sp), w, upmm)
                    gw = [(upmm, (g * 2 + ch + 1) * ND)]
                    if hv == 0 and ft == 0 and ch == 0:
                        gw.extend(tok_wdn)
                    if hv == 1 and ft == 0:
                        gw.append((fev, hv * 2 * ND))
                    gtok[(hv, ft, ch)] = p.on("scalar", lambda e, ft=ft, ch=ch, ps=ps: e.activation(
                        hb[:, ft * 1024 + ch * 512: ft * 1024 + ch * 512 + 512],
                        ps[:, :], AF.Gelu), gw, gel)
            for ch in range(2):
                for ft in range(NF):
                    i = hv * 2 * NF + ch * NF + ft
                    sq_w = [gtok[(hv, ft, ch)]]
                    if ch * NF + ft >= 2:
                        sq_w.append((s2mm, i - 1))
                    elif hv == 1:
                        sq_w.append((s2mm, hv * 2 * NF))
                        sq_w.append((fstos[0], 64))
                        sq_w.append((fstos[1], 64))
                    t_q = p.on("scalar", lambda e, ft=ft, ch=ch: e.activation(
                        fsl[:, (ft % 2) * 1024:(ft % 2) * 1024 + 512],
                        hb[:, ft * 1024 + ch * 512: ft * 1024 + ch * 512 + 512], AF.Square),
                        sq_w, sqd)
                    waits = [t_q]
                    if ft == 0 and (ch > 0 or hv > 0):
                        waits.append((s2ev, hv * 2 + ch))
                    p.on("tensor", lambda e, ft=ft, st=(ft == 0), sp=(ft == NF - 1):
                         e.matmul(psS[0:1, :], onc[:],
                                  fsl[:, (ft % 2) * 1024:(ft % 2) * 1024 + 512],
                                  start=st, stop=sp), waits, s2mm)
                p.on("vector", lambda e, ch=ch: e.tensor_copy(
                    row[0:1, ch * 512: ch * 512 + 512], psS[0:1, :]),
                     [(s2mm, (hv * 2 + ch + 1) * NF)], s2ev)
            rr = lambda j: row[0:1, j * T: j * T + 1024]
            u1 = p.on("scalar", lambda e, rr=rr: e.activation(rr(1), rr(0), AF.Ln),
                      [(s2ev, hv * 2 + 2)], "ras")
            p.drain("scalar")
            u2 = p.on("scalar", lambda e, rr=rr: e.activation(rr(2), rr(1), AF.Exp, scale=0.5),
                      [u1], "ras")
            u3 = p.on("vector", lambda e, rr=rr: e.tensor_scalar(rr(1), rr(2), 1.0 / 64.0, 1e-8,
                                                                 OP.mult, OP.add), [u2], "rs")
            u4 = p.on("scalar", lambda e, rr=rr: e.activation(rr(2), rr(1), AF.Ln), [u3], "ras")
            p.drain("scalar")
            u5 = p.on("scalar", lambda e, rr=rr: e.activation(rr(1), rr(2), AF.Exp, scale=-1.0),
                      [u4], "ras")
            for ch in range(2):
                bw = [u5]
                if hv > 0 or ch > 0:
                    bw.append((i2d, hv * 2 + ch))
                bi = p.on("tensor", lambda e, ch=ch, rr=rr: e.matmul(
                    psI[:, :], on1[:], rr(1)[:, ch * 512:(ch + 1) * 512],
                    start=True, stop=True), bw, "bmm")
                p.on("vector", lambda e, ch=ch: e.tensor_copy(
                    i2sb[:, ch * 512:(ch + 1) * 512], psI[:, :]), [bi], i2d)
            for do in range(ND):
                gdo = hv * ND + do
                wl_w = list(tok_wdn)
                if gdo >= 2:
                    wl_w.append(dn_ev[gdo - 2])
                wl = p.dma(wdb[:, (do % 2) * NF * 128:(do % 2 + 1) * NF * 128],
                           wdn_c[do].rearrange("b p c -> p b c"),
                           waits=wl_w, inc="dld%d" % (do % 2))
                for ch in range(2):
                    ps = psU0 if ch == 0 else psU1
                    for ft in range(NF):
                        w = []
                        if ft == 0:
                            w = [wl, gtok[(hv, NF - 1, 1)], (gel, (hv * NF + NF) * 2)]
                            if gdo >= 1:
                                w.append((fev, gdo * 2))
                        p.on("tensor", lambda e, do=do, ft=ft, ch=ch, ps=ps, hoff=hoff,
                             st=(ft == 0), sp=(ft == NF - 1):
                             e.matmul(ps[:, :],
                                      wdb[:, (do % 2) * NF * 128 + ft * 128:(do % 2) * NF * 128 + (ft + 1) * 128],
                                      hb[:, ft * 1024 + ch * 512: ft * 1024 + ch * 512 + 512],
                                      start=st, stop=sp), w, dnmm)
                    ew = [(dnmm, (gdo * 2 + ch + 1) * NF), (i2d, hv * 2 + 2)]
                    if gdo >= 2:
                        ew.append((fstos[do % 2], ((gdo - 2) // 2 + 1) * 16))
                    dn_ev[gdo] = p.on("vector", lambda e, do=do, ch=ch, ps=ps: e.tensor_tensor(
                        fsl[:, (do % 2) * 1024 + ch * 512:(do % 2) * 1024 + ch * 512 + 512],
                        ps[:, :], i2sb[:, ch * 512:(ch + 1) * 512], OP.mult), ew, fev)
                p.dma(fT.rearrange("(do p) t -> do p t", p=128)[do][:, hoff:hoff + 1024],
                      fsl[:, (do % 2) * 1024:(do % 2) * 1024 + 1024],
                      waits=[(fev, (gdo + 1) * 2)], inc=fstos[do % 2])



def emit_fp4_quant(p, src_ap, dst_bf_ap, cols, scr_a, scr_b, scr_c, scl, rcp,
                   wait_tok, tag):
    nb = cols // 16
    v = lambda fn, w, i="qv": p.on("vector", fn, w, i)
    s = lambda fn, w, i="qsc": p.on("scalar", fn, w, i)
    t1 = v(lambda e: e.tensor_reduce(scl, src_ap.rearrange("p (b j) -> p b j", j=16),
                                     axis=AX, op=OP.max, apply_absolute_value=True),
           [wait_tok])
    p.drain("vector")
    t2 = v(lambda e: e.tensor_scalar(scl, scl, 1.0 / 6.0, 1e-12, OP.mult, OP.max), [t1])
    p.drain("vector")
    t3 = v(lambda e: e.reciprocal(rcp, scl), [t2])
    p.drain("vector")
    t3b = v(lambda e: e.tensor_scalar(scl, scl, 0.5, None, OP.mult), [t3])
    t4 = s(lambda e: e.activation(scr_a, src_ap, AF.Abs), [wait_tok])
    rcp_rep = rcp.unsqueeze(2).to_broadcast([128, nb, 16])
    t5 = v(lambda e: e.tensor_tensor(scr_a.rearrange("p (b j) -> p b j", j=16),
                                     scr_a.rearrange("p (b j) -> p b j", j=16),
                                     rcp_rep, OP.mult), [t3, t4])
    p.drain("vector")
    steps = [
        lambda e: e.tensor_scalar(scr_b, scr_a, 2.5, None, OP.is_gt),
        lambda e: e.scalar_tensor_tensor(scr_c, scr_a, 3.5, scr_b, OP.is_gt, OP.add),
        lambda e: e.scalar_tensor_tensor(scr_b, scr_a, 5.0, scr_c, OP.is_gt, OP.add),
        lambda e: e.scalar_tensor_tensor(scr_c, scr_a, 5.0, scr_b, OP.is_gt, OP.add),
        lambda e: e.tensor_scalar(scr_b, scr_c, 2.0, None, OP.mult),
        lambda e: e.scalar_tensor_tensor(scr_c, scr_a, 0.25, scr_b, OP.is_gt, OP.add),
        lambda e: e.scalar_tensor_tensor(scr_b, scr_a, 0.75, scr_c, OP.is_gt, OP.add),
        lambda e: e.scalar_tensor_tensor(scr_c, scr_a, 1.25, scr_b, OP.is_gt, OP.add),
        lambda e: e.scalar_tensor_tensor(scr_b, scr_a, 1.75, scr_c, OP.is_gt, OP.add),
    ]
    tprev = t5
    for fn in steps:
        tprev = v(fn, [tprev])
        p.drain("vector")
    t13 = s(lambda e: e.activation(scr_c, src_ap, AF.Sign), [tprev])
    t14 = v(lambda e: e.tensor_tensor(dst_bf_ap, scr_b, scr_c, OP.mult), [t13, tprev])
    p.drain("vector")
    scl_rep = scl.unsqueeze(2).to_broadcast([128, nb, 16])
    t15 = v(lambda e: e.tensor_tensor(dst_bf_ap.rearrange("p (b j) -> p b j", j=16),
                                      dst_bf_ap.rearrange("p (b j) -> p b j", j=16),
                                      scl_rep, OP.mult), [t14, t3b])
    return t15


def emit_ffn0(nc, p, pfx, B, xnT, wg, wu, wd, ident, fT, xq_c):
    DN = 2816
    NH = DN // 128
    CHUNKS = [512, 512, 512, 512, 512, 256]
    debug = False
    wg_dbg = None
    with ExitStack() as ctx:
        big = ctx.enter_context(nc.sbuf_tensor(pfx + "big", [128, 13056], F32))
        wgT = ctx.enter_context(nc.sbuf_tensor(pfx + "wgT", [128, ND * DN], BF16))
        wuT = ctx.enter_context(nc.sbuf_tensor(pfx + "wuT", [128, ND * DN], BF16))
        wdT = ctx.enter_context(nc.sbuf_tensor(pfx + "wdT", [128, NH * D], BF16))
        hq = ctx.enter_context(nc.sbuf_tensor(pfx + "hq", [128, DN], BF16))
        hqT = ctx.enter_context(nc.sbuf_tensor(pfx + "hqT", [128, NH * 128], BF16))
        gs = ctx.enter_context(nc.sbuf_tensor(pfx + "gs", [128, DN], BF16))
        xqr = ctx.enter_context(nc.sbuf_tensor(pfx + "xqr", [128, 2 * D], BF16))
        fst = ctx.enter_context(nc.sbuf_tensor(pfx + "fst", [128, 2 * 128], F32))
        idt = ctx.enter_context(nc.sbuf_tensor(pfx + "idt", [128, 128], F32))
        idb = ctx.enter_context(nc.sbuf_tensor(pfx + "idb", [128, 128], BF16))
        psA = ctx.enter_context(nc.psum_tensor(pfx + "psA", [128, 1024], BF16))
        psB = ctx.enter_context(nc.psum_tensor(pfx + "psB", [128, 1024], BF16))
        psG = ctx.enter_context(nc.psum_tensor(pfx + "psG", [128, 1536], F32))
        psU = ctx.enter_context(nc.psum_tensor(pfx + "psU", [128, 1536], F32))
        psAb = psA[:, 0:128]
        psBb = psB[:, 0:128]
        h_f = big[:, 0:DN]
        h_a = big[:, DN:2 * DN]
        h_b = big[:, 2 * DN:3 * DN]
        h_i = big[:, 3 * DN:4 * DN]
        h_scl = big[:, 4 * DN:4 * DN + DN // 16]
        h_rcp = big[:, 4 * DN + DN // 16:4 * DN + 2 * (DN // 16)]

        p.dma(idt[:], ident, waits=B)
        alldma = (p.ctr("dma"), p.ctr("dma").v)
        tok_idb = p.on("vector", lambda e: e.tensor_copy(idb[:], idt[:]), [alldma], "misc")

        xa_mt = p.ctr("xa_mt")
        xa_ev = p.ctr("xa_ev")
        xq_st = [p.ctr("xq_st0"), p.ctr("xq_st1")]
        qprev = alldma
        qhist = {}
        for k in range(NT):
            qprev2 = qhist.get(k - 2)
            xtok = big[:, (k % 2) * D:(k % 2) * D + D]
            xqk = big[:, 2 * D + (k % 2) * D:2 * D + (k % 2) * D + D]
            xqkb = xqk.bitcast(BF16)[:, 0:D]
            sa = big[:, 4 * D:5 * D]
            sb = big[:, 5 * D:6 * D]
            si = big[:, 6 * D:7 * D]
            sc = big[:, 7 * D:7 * D + 64]
            rc = big[:, 7 * D + 64:7 * D + 128]
            ldt = []
            for dt in range(ND):
                i = k * ND + dt
                w = [(xa_mt, i - 3)] if i >= 4 else list(B)
                ldt.append(p.dma(big[:, 8 * D + ((i % 4) * 128):8 * D + ((i % 4) * 128) + 128],
                                 xnT.rearrange("(dt p) t -> dt p t", p=128)[dt][:, k * 128:(k + 1) * 128],
                                 waits=w, inc="xl%d" % (i % 4)))
            for dt in range(ND):
                i = k * ND + dt
                bank = psG if i % 2 == 0 else psU
                w2 = [ldt[dt]]
                if i >= 2:
                    w2.append((xa_ev, i - 1))
                tk = p.on("tensor", lambda e, b=bank, i=i:
                          e.transpose(b[:, 0:128], big[:, 8 * D + ((i % 4) * 128):8 * D + ((i % 4) * 128) + 128], idt[:]),
                          w2, xa_mt)
                ew = [tk]
                if dt == 0 and k >= 2:
                    ew.append(qprev2)
                p.on("vector", lambda e, b=bank, xtok=xtok, dt=dt: e.tensor_copy(
                    xtok[:, dt * 128:(dt + 1) * 128], b[:, 0:128]), ew, xa_ev)
            qt = emit_fp4_quant(p, xtok, xqkb, D, sa, sb, si, sc, rc,
                                (xa_ev, (k + 1) * ND), "x%d" % k)
            qprev = qt
            qhist[k] = qt
            for dt in range(ND):
                i = k * ND + dt
                bank = psAb if i % 2 == 0 else psBb
                w2 = [qt, tok_idb] if dt == 0 else []
                if k * ND + dt >= 2:
                    w2.append((p.ctr("xb_ev"), p.ctr("xb_ev").v - 1))
                tk = p.on("tensor", lambda e, b=bank, xqkb=xqkb, dt=dt:
                          e.transpose(b, xqkb[:, dt * 128:(dt + 1) * 128], idb[:]), w2, "xb_mt")
                p.on("vector", lambda e, b=bank, k=k, dt=dt: e.tensor_copy(
                    xqr[:, (k % 2) * D + dt * 128:(k % 2) * D + (dt + 1) * 128], b),
                    [tk] + ([(xq_st[k % 2], 16 * (k // 2))] if (dt == 0 and k >= 2) else []),
                    "xb_ev")
            p.dma(xq_c[k], xqr[:, (k % 2) * D:(k % 2) * D + D],
                  waits=[(p.ctr("xb_ev"), (k + 1) * ND)], inc=xq_st[k % 2])

        tok_xq_all = [(c, c.v) for c in xq_st]

        def quant_w(tag, wsrc, n_slab, slab_cols, dstfn, start_tok):
            dn = p.ctr(tag + "_dn")
            prev = start_tok
            for s_i in range(n_slab):
                sa = big[:, 0:slab_cols]
                sb = big[:, slab_cols:2 * slab_cols]
                si = big[:, 2 * slab_cols:3 * slab_cols]
                sf = big[:, 3 * slab_cols:4 * slab_cols]
                sc = big[:, 4 * slab_cols:4 * slab_cols + slab_cols // 16]
                rc = big[:, 4 * slab_cols + slab_cols // 16:4 * slab_cols + 2 * (slab_cols // 16)]
                qb = big[:, 4 * slab_cols + 256:4 * slab_cols + 256 + (slab_cols + 1) // 2]
                qbb = qb.bitcast(BF16)[:, 0:slab_cols]
                ldt = p.dma(sf, wsrc.rearrange("(s p) c -> s p c", p=128)[s_i],
                            waits=[prev], inc=tag + "_ld")
                qt = emit_fp4_quant(p, sf, qbb, slab_cols, sa, sb, si, sc, rc,
                                    ldt, tag + "%d" % s_i)
                ntile = slab_cols // 128
                for j in range(ntile):
                    bank = psAb if j % 2 == 0 else psBb
                    w2 = [qt] if j == 0 else []
                    if j >= 2:
                        w2.append((dn, dn.v - 1))
                    tk = p.on("tensor", lambda e, b=bank, s2=qbb[:, j * 128:(j + 1) * 128]:
                              e.transpose(b, s2, idb[:]), w2, tag + "_mt")
                    prev = p.on("vector", lambda e, b=bank, d2=dstfn(s_i, j):
                                e.tensor_copy(d2, b), [tk], dn)
            return prev

        tok_wg = quant_w("wg", wg, NH, D,
                         lambda s_i, j: wgT[:, j * DN + s_i * 128:j * DN + (s_i + 1) * 128],
                         qprev)
        tok_wu = quant_w("wu", wu, NH, D,
                         lambda s_i, j: wuT[:, j * DN + s_i * 128:j * DN + (s_i + 1) * 128],
                         tok_wg)
        tok_wd = quant_w("wd", wd, ND, DN,
                         lambda s_i, j: wdT[:, j * D + s_i * 128:j * D + (s_i + 1) * 128],
                         tok_wu)

        gmm = p.ctr("gmm")
        gev = p.ctr("gev")
        hmt = p.ctr("hmt")
        hev = p.ctr("hev")
        dmm = p.ctr("dmm")
        devq = p.ctr("devq")
        dtr = p.ctr("dtr")
        dev2 = p.ctr("dev2")
        fso = [p.ctr("fso0"), p.ctr("fso1")]
        xlr = [p.ctr("xlr0"), p.ctr("xlr1")]
        coff = [0, 512, 1024, 1536, 2048, 2560]
        dstage = gs[:].bitcast(F32)
        last_qh = tok_wd
        last_devac = None
        last_hevac = None
        xq_tok = {}
        for tt in range(NT):
            w = list(tok_xq_all) if tt < 2 else [xq_tok[tt - 2][1]]
            xq_tok[tt] = (None, None)
            lt = p.dma(xqr[:, (tt % 2) * D:(tt % 2) * D + D], xq_c[tt],
                       waits=w + ([last_qh] if tt < 2 else []), inc=xlr[tt % 2])
            ev_toks = []
            gmm_last = None
            for grp in range(2):
                chs = [0, 1, 2] if grp == 0 else [3, 4, 5]
                for dt in range(ND):
                    for ci, c in enumerate(chs):
                        w = []
                        if dt == 0 and ci == 0:
                            w = [lt]
                            if grp == 0:
                                w.append(tok_wd if tt == 0 else last_devac)
                            else:
                                w.extend(ev_toks)
                        p.on("tensor", lambda e, tt=tt, dt=dt, c=c,
                             st=(dt == 0), sp=(dt == ND - 1):
                             e.matmul(psG[:, (c % 3) * 512:(c % 3) * 512 + CHUNKS[c]],
                                      xqr[:, (tt % 2) * D + dt * 128:(tt % 2) * D + (dt + 1) * 128],
                                      wgT[:, dt * DN + coff[c]: dt * DN + coff[c] + CHUNKS[c]],
                                      start=st, stop=sp), w, gmm)
                        gmm_last = p.on("tensor", lambda e, tt=tt, dt=dt, c=c,
                                        st=(dt == 0), sp=(dt == ND - 1):
                                        e.matmul(psU[:, (c % 3) * 512:(c % 3) * 512 + CHUNKS[c]],
                                                 xqr[:, (tt % 2) * D + dt * 128:(tt % 2) * D + (dt + 1) * 128],
                                                 wuT[:, dt * DN + coff[c]: dt * DN + coff[c] + CHUNKS[c]],
                                                 start=st, stop=sp), [], gmm)
                base = gmm.v
                for c in chs:
                    off = (c % 3) * 512
                    sw = [(gmm, base)]
                    if c == chs[0] and tt > 0:
                        sw.append(last_qh)
                    tsil = p.on("scalar", lambda e, c=c, off=off: e.activation(
                        gs[:, coff[c]:coff[c] + CHUNKS[c]], psG[:, off:off + CHUNKS[c]],
                        AF.Silu), sw, "gsil")
                    ev_toks.append(p.on("vector", lambda e, c=c, off=off: e.tensor_tensor(
                        h_f[:, coff[c]:coff[c] + CHUNKS[c]], psU[:, off:off + CHUNKS[c]],
                        gs[:, coff[c]:coff[c] + CHUNKS[c]], OP.mult), [tsil], gev))
            qh = emit_fp4_quant(p, h_f, hq[:], DN, h_a, h_b, h_i, h_scl, h_rcp,
                                ev_toks[-1], "h%d" % tt)
            xq_tok[tt] = (None, qh)
            last_qh = qh
            htoks = []
            for j in range(NH):
                bank = psAb if j % 2 == 0 else psBb
                w2 = [qh] if j == 0 else []
                if j >= 2:
                    w2.append((hev, hev.v - 1))
                if tt > 0 and j < 2:
                    w2.append(last_hevac)
                tk = p.on("tensor", lambda e, b=bank, s2=hq[:, j * 128:(j + 1) * 128]:
                          e.transpose(b, s2, idb[:]), w2, hmt)
                htoks.append(p.on("vector", lambda e, b=bank, d2=hqT[:, j * 128:(j + 1) * 128]:
                                  e.tensor_copy(d2, b), [tk], hev))
            last_hevac = htoks[-1]
            psD = psG
            for j in range(NH):
                w = list(ev_toks) + [htoks[-1]] if j == 0 else []
                for half in range(2):
                    p.on("tensor", lambda e, j=j, half=half, st=(j == 0), sp=(j == NH - 1):
                         e.matmul(psD[:, half * 512: half * 512 + 512],
                                  hqT[:, j * 128:(j + 1) * 128],
                                  wdT[:, j * D + half * 512: j * D + half * 512 + 512],
                                  start=st, stop=sp), w if half == 0 else [], dmm)
            base_d = dmm.v
            de = p.on("vector", lambda e: e.tensor_copy(dstage[:, 0:1024], psD[:, 0:1024]),
                      [(dmm, base_d)] + list(ev_toks), devq)
            for do in range(ND):
                bank = psU[:, 0:128] if do % 2 == 0 else psU[:, 512:640]
                w2 = [de]
                if do >= 2:
                    w2.append((dev2, dev2.v - 1))
                tk = p.on("tensor", lambda e, b=bank, do=do:
                          e.transpose(b, dstage[:, do * 128:(do + 1) * 128], idt[:]),
                          w2, dtr)
                e2 = p.on("vector", lambda e, b=bank, do=do: e.tensor_copy(
                    fst[:, (do % 2) * 128:(do % 2) * 128 + 128], b),
                    [tk] + ([(fso[do % 2], (((tt * ND + do) - 2) // 2 + 1) * 16)]
                            if tt * ND + do >= 2 else []), dev2)
                p.dma(fT.rearrange("(do p) t -> do p t", p=128)[do][:, tt * 128:(tt + 1) * 128],
                      fst[:, (do % 2) * 128:(do % 2) * 128 + 128],
                      waits=[e2], inc=fso[do % 2])
                last_devac = e2



def emit_attn0(nc, p, pfx, B, xnT, Wq, Wk, Wv, Wo, Wdkv, Wuk, Wuv, decay,
               ones1x_d, aT):
    NHD = 16
    SSC = float(1.0 / (np.sqrt(np.float64(np.float32(1024.0))) + 1e-08))
    with ExitStack() as ctx:
        xnb = ctx.enter_context(nc.sbuf_tensor(pfx + "xnb", [128, ND * T], BF16))
        QT = ctx.enter_context(nc.sbuf_tensor(pfx + "QT", [128, ND * T], BF16))
        KrT = ctx.enter_context(nc.sbuf_tensor(pfx + "KrT", [128, ND * T], BF16))
        vtok = ctx.enter_context(nc.sbuf_tensor(pfx + "vtok", [128, NT * NHD * 65], BF16))
        latT = ctx.enter_context(nc.sbuf_tensor(pfx + "latT", [128, 2 * T], BF16))
        wbf = ctx.enter_context(nc.sbuf_tensor(pfx + "wbf", [128, ND * D], BF16))
        wst = ctx.enter_context(nc.sbuf_tensor(pfx + "wst", [128, 2 * D], F32))
        expt = ctx.enter_context(nc.sbuf_tensor(pfx + "expt", [128, NT * 512], BF16))
        decb = ctx.enter_context(nc.sbuf_tensor(pfx + "decb", [128, NT * 512], BF16))
        sdb = ctx.enter_context(nc.sbuf_tensor(pfx + "sdb", [128, 2 * 512], BF16))
        aout = ctx.enter_context(nc.sbuf_tensor(pfx + "aout", [128, 2 * 512], F32))
        onesr = ctx.enter_context(nc.sbuf_tensor(pfx + "onesr", [1, 128], F32))
        psS0 = ctx.enter_context(nc.psum_tensor(pfx + "psS0", [128, 512], F32))
        psS1 = ctx.enter_context(nc.psum_tensor(pfx + "psS1", [128, 512], F32))
        psAV = ctx.enter_context(nc.psum_tensor(pfx + "psAV", [128, 512], F32))
        psM = ctx.enter_context(nc.psum_tensor(pfx + "psM", [128, 512], F32))
        psP = ctx.enter_context(nc.psum_tensor(pfx + "psP", [128, 512], F32))
        psP2 = ctx.enter_context(nc.psum_tensor(pfx + "psP2", [128, 512], F32))
        aoT = xnb
        latf = latT[:].bitcast(F32)
        rden = latf[0:1, 0:1024]
        ones1x = None

        stage = QT[:].bitcast(F32)
        stage2 = KrT[:].bitcast(F32)
        p.dma(stage[:, 0:ND * T // 2].rearrange("p (dt t) -> p dt t", dt=ND // 2),
              xnT.rearrange("(dt p) t -> p dt t", p=128)[:, 0:ND // 2, :], waits=B)
        p.dma(stage2[:, 0:ND * T // 2].rearrange("p (dt t) -> p dt t", dt=ND // 2),
              xnT.rearrange("(dt p) t -> p dt t", p=128)[:, ND // 2:, :], waits=B)
        p.dma(onesr[:], ones1x_d, waits=B)
        alldma = (p.ctr("dma"), p.ctr("dma").v)
        cast_dn = p.ctr("cast")
        for dt in range(ND):
            st_ap = (stage if dt < 4 else stage2)[:, (dt % 4) * T:(dt % 4) * T + T]
            p.on("vector", lambda e, dt=dt, st_ap=st_ap: e.tensor_copy(
                xnb[:, dt * T:(dt + 1) * T], st_ap), [alldma], cast_dn)
        tok_xnb = (cast_dn, ND)

        wld = [p.ctr("wl0"), p.ctr("wl1")]
        wcast = p.ctr("wcast")

        def cast_w(wsrc, nslab, width, first_w):
            toks = []
            for s_i in range(nslab):
                w = [first_w] if s_i < 2 else [(wcast, wcast.v - 1)]
                lt = p.dma(wst[:, (s_i % 2) * D:(s_i % 2) * D + width],
                           wsrc.rearrange("(s p) c -> s p c", p=128)[s_i],
                           waits=w, inc=wld[s_i % 2])
                toks.append(p.on("vector", lambda e, s_i=s_i, width=width: e.tensor_copy(
                    wbf[:, s_i * width:(s_i + 1) * width],
                    wst[:, (s_i % 2) * D:(s_i % 2) * D + width]), [lt], wcast))
            return toks[-1]

        def cast_w_sum(wa, wb, nslab, first_w):
            toks = []
            for s_i in range(nslab):
                w = [first_w] if s_i == 0 else [toks[-1]]
                la = p.dma(wst[:, 0:D], wa.rearrange("(s p) c -> s p c", p=128)[s_i],
                           waits=w, inc=wld[0])
                lb = p.dma(wst[:, D:2 * D], wb.rearrange("(s p) c -> s p c", p=128)[s_i],
                           waits=w, inc=wld[1])
                toks.append(p.on("vector", lambda e, s_i=s_i: e.tensor_tensor(
                    wbf[:, s_i * D:(s_i + 1) * D], wst[:, 0:D], wst[:, D:2 * D], OP.add),
                    [la, lb], wcast))
            return toks[-1]

        def project(src_bf, dst_fn, wtok, tag, nk=ND, nmt=ND, width=D, store_fn=None,
                    egate_fn=None):
            mmc = p.ctr(tag + "m")
            evc = p.ctr(tag + "e")
            for mt in range(nmt):
                for ch in range(4):
                    i = mt * 4 + ch
                    ps = psP if i % 2 == 0 else psP2
                    for dt in range(nk):
                        w = [wtok] if (i == 0 and dt == 0) else []
                        if dt == 0 and i >= 2:
                            w.append((evc, i - 1))
                        p.on("tensor", lambda e, mt=mt, ch=ch, dt=dt, ps=ps,
                             st=(dt == 0), sp=(dt == nk - 1):
                             e.matmul(ps[:, :],
                                      wbf[:, dt * width + mt * 128:dt * width + (mt + 1) * 128],
                                      src_bf[:, dt * T + ch * 512: dt * T + ch * 512 + 512],
                                      start=st, stop=sp), w, mmc)
                    ew = [(mmc, (i + 1) * nk)]
                    if egate_fn is not None:
                        g = egate_fn(i)
                        if g is not None:
                            ew.append(g)
                    tok_e = p.on("scalar", lambda e, mt=mt, ch=ch, ps=ps: e.activation(
                        dst_fn(mt, ch), ps[:, :], AF.Copy),
                        ew, evc)
                    if store_fn is not None:
                        store_fn(mt, ch, tok_e)
            return (evc, nmt * 4)

        twq = cast_w(Wq, ND, D, tok_xnb)
        tok_q = project(xnb, lambda mt, ch: QT[:, mt * T + ch * 512: mt * T + ch * 512 + 512],
                        twq, "q")
        twkv = cast_w_sum(Wk, Wv, ND, tok_q)
        kvs = vtok[:].rearrange("p c -> p c")[:, 0:ND * T]
        tok_kv = project(xnb, lambda mt, ch: kvs[:, mt * T + ch * 512: mt * T + ch * 512 + 512],
                         twkv, "kv")
        twd = cast_w(Wdkv, ND, 256, tok_kv)
        tok_lat = project(kvs, lambda mt, ch: latT[:, mt * T + ch * 512: mt * T + ch * 512 + 512],
                          twd, "lat", nk=ND, nmt=2, width=256)
        twuk = cast_w(Wuk, 2, D, tok_lat)
        tok_kr = project(latT, lambda mt, ch: KrT[:, mt * T + ch * 512: mt * T + ch * 512 + 512],
                         twuk, "kr", nk=2, nmt=ND, width=D)
        twuv = cast_w(Wuv, 2, D, tok_kr)
        vview = vtok[:].rearrange("p (tt h c) -> p tt h c", tt=NT, h=NHD, c=65)
        vmm = p.ctr("vmm")
        vev = p.ctr("vev")
        for tt in range(NT):
            for ch in range(2):
                i = tt * 2 + ch
                ps = psP if i % 2 == 0 else psP2
                for ks in range(2):
                    w = [twuv, tok_kr] if (i == 0 and ks == 0) else []
                    if ks == 0 and i >= 2:
                        w.append((vev, i - 1))
                    p.on("tensor", lambda e, tt=tt, ch=ch, ks=ks, ps=ps,
                         st=(ks == 0), sp=(ks == 1):
                         e.matmul(ps[:, :],
                                  latT[:, ks * T + tt * 128:ks * T + (tt + 1) * 128],
                                  wbf[:, ks * D + ch * 512: ks * D + ch * 512 + 512],
                                  start=st, stop=sp), w, vmm)
                p.on("scalar", lambda e, tt=tt, ch=ch, ps=ps: e.activation(
                    vview[:, tt, ch * 8:(ch + 1) * 8, 0:64],
                    ps[:, :].rearrange("p (h c) -> p h c", c=64), AF.Copy),
                    [(vmm, (i + 1) * 2)], vev)
        tok_v = (vev, NT * 2)
        tok_ones = p.on("vector", lambda e: e.memset(
            vview[:, :, :, 64:65], 1.0), [tok_v], "vone")

        dld = p.ctr("dld")
        smm = p.ctr("smm")
        sdv = p.ctr("sdv")
        sev = p.ctr("sev")
        avm = p.ctr("avm")
        dcp = p.ctr("dcp")
        dln = p.ctr("dln")
        dbc = p.ctr("dbc")
        aev = p.ctr("aev")
        anm = p.ctr("anm")
        hq_done = {}
        for qc in range(4):
            wdec = [alldma] if qc == 0 else [hq_done[(qc - 1, NHD - 1)]]
            tdec = p.dma(decb[:].rearrange("p (kt c) -> p kt c", kt=NT),
                         decay[12 - 4 * qc:28 - 4 * qc], waits=wdec, inc=dld)
            for h in range(NHD):
                base = (h % 2) * 64
                prevhq = None
                if (qc, h) != (0, 0):
                    prevhq = hq_done[(qc, h - 1)] if h > 0 else hq_done[(qc - 1, NHD - 1)]
                se_toks = []
                for kt in range(NT):
                    ps = psS0 if kt % 2 == 0 else psS1
                    w = [tok_q, tok_kr, tdec]
                    if prevhq is not None:
                        w.append(prevhq)
                    if kt >= 2:
                        w = [(sev, sev.v - 1)]
                    tsm = p.on("tensor", lambda e, h=h, qc=qc, kt=kt, ps=ps, base=base: e.matmul(
                        ps[:, :],
                        KrT[:, (h // 2) * T + kt * 128:(h // 2) * T + (kt + 1) * 128][base:base + 64, :],
                        QT[:, (h // 2) * T + qc * 512:(h // 2) * T + qc * 512 + 512][base:base + 64, :],
                        start=True, stop=True, tile_position=(base, 0)), w, smm)
                    wv = [tsm]
                    if kt >= 2:
                        wv.append(se_toks[kt - 2])
                    tdv = p.on("vector", lambda e, kt=kt, ps=ps: e.tensor_tensor(
                        sdb[:, (kt % 2) * 512:(kt % 2) * 512 + 512], ps[:, :],
                        decb[:, kt * 512:(kt + 1) * 512], OP.mult),
                        wv, sdv)
                    p.drain("vector")
                    tse = p.on("scalar", lambda e, kt=kt: e.activation(
                        expt[:, kt * 512:(kt + 1) * 512],
                        sdb[:, (kt % 2) * 512:(kt % 2) * 512 + 512], AF.Exp,
                        scale=SSC), [tdv], sev)
                    se_toks.append(tse)
                for kt in range(NT):
                    w = [se_toks[kt], tok_ones]
                    if kt == 0 and prevhq is not None:
                        w.append((anm, anm.v))
                    p.on("tensor", lambda e, kt=kt, h=h, st=(kt == 0), sp=(kt == NT - 1):
                         e.matmul(psAV[0:65, :],
                                  vtok[:, kt * NHD * 65 + h * 65: kt * NHD * 65 + (h + 1) * 65],
                                  expt[:, kt * 512:(kt + 1) * 512],
                                  start=st, stop=sp), w, avm)
                t1 = p.on("vector", lambda e: e.tensor_copy(rden[0:1, 0:512], psAV[64:65, :]),
                          [(avm, avm.v), (tok_v if (qc, h) == (0, 0) else (avm, avm.v))], dcp)
                p.drain("vector")
                t2 = p.on("scalar", lambda e: e.activation(rden[0:1, 512:1024], rden[0:1, 0:512],
                                                           AF.Ln), [t1], dln)
                p.drain("scalar")
                t3 = p.on("scalar", lambda e: e.activation(rden[0:1, 0:512], rden[0:1, 512:1024],
                                                           AF.Exp, scale=-1.0), [t2], dln)
                t4 = p.on("tensor", lambda e: e.matmul(psM[0:64, :], onesr[0:1, 0:64],
                                                       rden[0:1, 0:512],
                                                       start=True, stop=True), [t3], dbc)
                t5 = p.on("scalar", lambda e: e.activation(aout[0:64, 0:512], psAV[0:64, :],
                                                           AF.Copy), [t3], aev)
                hq_done[(qc, h)] = p.on("vector", lambda e, h=h, qc=qc, base=base: e.tensor_tensor(
                    aoT[:, (h // 2) * T + qc * 512:(h // 2) * T + qc * 512 + 512][base:base + 64, :],
                    aout[0:64, 0:512], psM[0:64, :], OP.mult), [t4, t5], anm)

        twO = cast_w(Wo, ND, D, hq_done[(3, NHD - 1)])
        sto = [p.ctr("so0"), p.ctr("so1")]

        def st_fn(mt, ch, tok_e):
            i = mt * 4 + ch
            p.dma(aT.rearrange("(mt p) t -> mt p t", p=128)[mt][:, ch * 512:(ch + 1) * 512],
                  aout[:, (i % 2) * 512:(i % 2) * 512 + 512], waits=[tok_e], inc=sto[i % 2])

        def dstf(mt, ch):
            i = mt * 4 + ch
            return aout[:, (i % 2) * 512:(i % 2) * 512 + 512]

        tok_o = project(aoT, dstf, twO, "o", store_fn=st_fn,
                        egate_fn=lambda i: (sto[i % 2], ((i - 2) // 2 + 1) * 16) if i >= 2 else None)



def emit_attn1(nc, p, pfx, B, xnT, WinT, WrT, WkvT, WbT, bbt, WqT, WoT,
               blk4, sel4, ident, aT, knD, qD, r4D, uwD, smD):
    NS = 32
    CH = 32
    NCH = T // CH
    with ExitStack() as ctx:
        xnb = ctx.enter_context(nc.sbuf_tensor(pfx + "xnb", [128, ND * T], BF16))
        xpb = ctx.enter_context(nc.sbuf_tensor(pfx + "xpb", [128, ND * T], BF16))
        wbf = ctx.enter_context(nc.sbuf_tensor(pfx + "wbf", [128, ND * D], BF16))
        wst = ctx.enter_context(nc.sbuf_tensor(pfx + "wst", [128, 2 * D], F32))
        kvf = ctx.enter_context(nc.sbuf_tensor(pfx + "kvf", [128, 2 * T], F32))
        betab = ctx.enter_context(nc.sbuf_tensor(pfx + "betab", [128, T], F32))
        qf = ctx.enter_context(nc.sbuf_tensor(pfx + "qf", [32, T], F32))
        lgt = ctx.enter_context(nc.sbuf_tensor(pfx + "lgt", [4, T], F32))
        rn4 = ctx.enter_context(nc.sbuf_tensor(pfx + "rn4", [4, T], F32))
        rsc = ctx.enter_context(nc.sbuf_tensor(pfx + "rsc", [4, T], F32))
        u4 = ctx.enter_context(nc.sbuf_tensor(pfx + "u4", [4, T], F32))
        s4 = ctx.enter_context(nc.sbuf_tensor(pfx + "s4", [4, T], F32))
        rnb = ctx.enter_context(nc.sbuf_tensor(pfx + "rnb", [128, T], F32))
        sqs = ctx.enter_context(nc.sbuf_tensor(pfx + "sqs", [128, 512], F32))
        knr = ctx.enter_context(nc.sbuf_tensor(pfx + "knr", [128, 2 * CH * 33], F32))
        qr = ctx.enter_context(nc.sbuf_tensor(pfx + "qr", [128, 2 * CH * NS], F32))
        tok = ctx.enter_context(nc.sbuf_tensor(pfx + "tok", [128, 64], F32))
        stg = ctx.enter_context(nc.sbuf_tensor(pfx + "stg", [128, 320], F32))
        cellb = ctx.enter_context(nc.sbuf_tensor(pfx + "cellb", [32, T], BF16))
        aor = ctx.enter_context(nc.sbuf_tensor(pfx + "aor", [128, 2 * 512], F32))
        idt = ctx.enter_context(nc.sbuf_tensor(pfx + "idt", [128, 128], F32))
        blkb = ctx.enter_context(nc.sbuf_tensor(pfx + "blkb", [128, 4], F32))
        selb = ctx.enter_context(nc.sbuf_tensor(pfx + "selb", [128, 32], F32))
        bbs = ctx.enter_context(nc.sbuf_tensor(pfx + "bbs", [128, 1], F32))
        Sst = ctx.enter_context(nc.sbuf_tensor(pfx + "Sst", [128, 33], F32))
        scr = ctx.enter_context(nc.sbuf_tensor(pfx + "scr", [128, 33], F32))
        scr2 = ctx.enter_context(nc.sbuf_tensor(pfx + "scr2", [128, NS], F32))
        z0t = ctx.enter_context(nc.sbuf_tensor(pfx + "z0t", [128, NS], F32))
        spt = ctx.enter_context(nc.sbuf_tensor(pfx + "spt", [128, NS], F32))
        zbt = ctx.enter_context(nc.sbuf_tensor(pfx + "zbt", [128, NS], F32))
        Ut = ctx.enter_context(nc.sbuf_tensor(pfx + "Ut", [128, NS], F32))
        nsum = ctx.enter_context(nc.sbuf_tensor(pfx + "nsum", [128, 4], F32))
        psP = ctx.enter_context(nc.psum_tensor(pfx + "psP", [128, 512], F32))
        psP2 = ctx.enter_context(nc.psum_tensor(pfx + "psP2", [128, 512], F32))
        psZ = ctx.enter_context(nc.psum_tensor(pfx + "psZ", [128, 512], F32))
        psB = ctx.enter_context(nc.psum_tensor(pfx + "psB", [128, 512], F32))
        xnbf = xnb[:].bitcast(F32)
        uwb = lambda: xnbf[:, 0:T]
        smb = lambda: xnbf[:, T:2 * T]
        wmb = lambda: xnbf[:, 2 * T:3 * T]
        outsq = lambda: xnbf[:, 3 * T:4 * T]

        p.dma(idt[:], ident, waits=B)
        p.dma(blkb[:], blk4, waits=B)
        p.dma(selb[:], sel4, waits=B)
        p.dma(bbs[:], bbt, waits=B)
        alldma = (p.ctr("dma"), p.ctr("dma").v)
        cast_dn = p.ctr("cast")
        xst = [p.ctr("xst0"), p.ctr("xst1")]
        for g in range(4):
            w = list(B) if g == 0 else [(cast_dn, 2 * g)]
            lt = p.dma(kvf[:, 0:2 * T].rearrange("p (a t) -> p a t", a=2),
                       xnT.rearrange("(dt p) t -> p dt t", p=128)[:, 2 * g:2 * g + 2, :],
                       waits=w, inc=xst[g % 2])
            for j in range(2):
                p.on("vector", lambda e, g=g, j=j: e.tensor_copy(
                    xnb[:, (2 * g + j) * T:(2 * g + j + 1) * T],
                    kvf[:, j * T:(j + 1) * T]), [lt], cast_dn)
        tok_xnb = (cast_dn, ND)

        wld = [p.ctr("wl0"), p.ctr("wl1")]
        wcast = p.ctr("wcast")

        def cast_w(wsrc, nslab, width, first_w):
            toks = []
            for s_i in range(nslab):
                w = [first_w] if s_i < 2 else [(wcast, wcast.v - 1)]
                lt = p.dma(wst[:, (s_i % 2) * D:(s_i % 2) * D + width],
                           wsrc.rearrange("(s p) c -> s p c", p=128)[s_i],
                           waits=w, inc=wld[s_i % 2])
                toks.append(p.on("vector", lambda e, s_i=s_i, width=width: e.tensor_copy(
                    wbf[:, s_i * width:(s_i + 1) * width],
                    wst[:, (s_i % 2) * D:(s_i % 2) * D + width]), [lt], wcast))
            return toks[-1]

        def project(src_bf, dst_fn, wtok, tag, nk=ND, nmt=ND, width=D, act=None,
                    bias=None, mwidth=128):
            mmc = p.ctr(tag + "m")
            evc = p.ctr(tag + "e")
            for mt in range(nmt):
                for ch in range(4):
                    i = mt * 4 + ch
                    ps = psP if i % 2 == 0 else psP2
                    for dt in range(nk):
                        w = [wtok] if (i == 0 and dt == 0) else []
                        if dt == 0 and i >= 2:
                            w.append((evc, i - 1))
                        p.on("tensor", lambda e, mt=mt, ch=ch, dt=dt, ps=ps,
                             st=(dt == 0), sp=(dt == nk - 1):
                             e.matmul(ps[0:mwidth, :],
                                      wbf[:, dt * width + mt * mwidth:dt * width + (mt + 1) * mwidth],
                                      src_bf[:, dt * T + ch * 512: dt * T + ch * 512 + 512],
                                      start=st, stop=sp), w, mmc)
                    if bias is not None:
                        p.on("scalar", lambda e, mt=mt, ch=ch, ps=ps: e.activation(
                            dst_fn(mt, ch), ps[0:mwidth, :], act, bias=bias[:, 0:1]),
                            [(mmc, (i + 1) * nk), alldma], evc)
                    else:
                        p.on("scalar", lambda e, mt=mt, ch=ch, ps=ps: e.activation(
                            dst_fn(mt, ch), ps[0:mwidth, :],
                            act if act is not None else AF.Copy),
                            [(mmc, (i + 1) * nk)], evc)
            return (evc, nmt * 4)

        twin = cast_w(WinT, ND, D, tok_xnb)
        tok_xp = project(xnb, lambda mt, ch: xpb[:, mt * T + ch * 512: mt * T + ch * 512 + 512],
                         twin, "xp", act=AF.Silu)
        twb = cast_w(WbT, ND, 128, tok_xp)
        tok_b = project(xpb, lambda mt, ch: betab[:, ch * 512:(ch + 1) * 512],
                        twb, "bt", nmt=1, width=128, act=AF.Sigmoid, bias=bbs)
        negb = p.ctr("negb")
        for chn in range(4):
            p.on("vector", lambda e, chn=chn: e.tensor_scalar(
                betab[:, chn * 512:(chn + 1) * 512],
                betab[:, chn * 512:(chn + 1) * 512], -1.0, None, OP.mult),
                [tok_b], negb)
        tok_bn = (negb, 4)
        twkv = cast_w(WkvT, ND, 256, tok_b)
        tok_kv = project(xpb, lambda mt, ch: kvf[:, mt * T + ch * 512: mt * T + ch * 512 + 512],
                         twkv, "kv", nmt=2, width=256)
        twr = cast_w(WrT, ND, 4, tok_kv)
        tok_r = project(xpb, lambda mt, ch: lgt[0:4, ch * 512:(ch + 1) * 512],
                        twr, "rt", nmt=1, width=4, mwidth=4)
        twq = cast_w(WqT, ND, 32, tok_r)
        tok_qp = project(xpb, lambda mt, ch: qf[0:32, ch * 512:(ch + 1) * 512],
                         twq, "qp", nmt=1, width=32, mwidth=32)

        kq = p.ctr("kq")
        knm = p.ctr("knm")
        kne = p.ctr("kne")
        for ch in range(4):
            tsq = p.on("scalar", lambda e, ch=ch: e.activation(
                sqs[:, 0:512], kvf[:, ch * 512:(ch + 1) * 512], AF.Square),
                [tok_kv] if ch == 0 else [(knm, ch)], kq)
            tm = p.on("tensor", lambda e, ch=ch: e.matmul(
                psB[0:4, :], blkb[:], sqs[:, 0:512],
                start=True, stop=True), [tsq, alldma] + ([(kne, ch)] if ch else []), knm)
            p.on("vector", lambda e, ch=ch: e.tensor_copy(
                rn4[0:4, ch * 512:(ch + 1) * 512], psB[0:4, :]), [tm], kne)
        n1 = p.on("scalar", lambda e: e.activation(rsc[0:4, :], rn4[0:4, :], AF.Ln),
                  [(kne, 4)], "nrs")
        p.drain("scalar")
        n2 = p.on("scalar", lambda e: e.activation(rn4[0:4, :], rsc[0:4, :], AF.Exp, scale=0.5),
                  [n1], "nrs")
        n3 = p.on("vector", lambda e: e.tensor_scalar(rn4[0:4, :], rn4[0:4, :], 1e-6, None,
                                                      OP.add), [n2], "nrv")
        n4 = p.on("scalar", lambda e: e.activation(rsc[0:4, :], rn4[0:4, :], AF.Ln), [n3], "nrs")
        p.drain("scalar")
        n5 = p.on("scalar", lambda e: e.activation(rn4[0:4, :], rsc[0:4, :], AF.Exp, scale=-1.0),
                  [n4], "nrs")
        b1 = p.dma(r4D, rn4[0:4, :], waits=[n5], inc="b1a")
        b2toks = []
        for n in range(4):
            b2toks.append(p.dma(rnb[n * 32:(n + 1) * 32, :],
                                r4D[n:n + 1, :].to_broadcast([32, T]), waits=[b1], inc="b2r"))
        tok_rnb = (p.ctr("b2r"), p.ctr("b2r").v)
        knd = p.ctr("knd")
        for ch in range(4):
            p.on("vector", lambda e, ch=ch: e.tensor_tensor(
                kvf[:, ch * 512:(ch + 1) * 512], kvf[:, ch * 512:(ch + 1) * 512],
                rnb[:, ch * 512:(ch + 1) * 512], OP.mult), [tok_rnb, (kq, 4)], knd)
        tok_kn = (knd, 4)

        tmm = p.ctr("tmm")
        tev = p.ctr("tev")
        t2v = p.ctr("t2v")
        t2s = p.ctr("t2s")
        bmm = p.ctr("bmm")
        bev = p.ctr("bev")
        p.on("vector", lambda e: e.memset(tok[:, 8:16], -1e30), [alldma], "ms")
        for k in range(NT):
            w = [tok_r, alldma, (kne, 4)] if k == 0 else [(bev, 2 * k)]
            tk = p.on("tensor", lambda e, k=k: e.transpose(
                psB[0:128, 0:4], lgt[0:4, k * 128:(k + 1) * 128], idt[0:4, 0:4]), w, tmm)
            te = p.on("vector", lambda e: e.tensor_copy(tok[:, 0:4], psB[0:128, 0:4]),
                      [tk], tev)
            c1 = p.on("vector", lambda e: e.tensor_copy(tok[:, 8:12], tok[:, 0:4]), [te], t2v)
            p.drain("vector")
            c2 = p.on("vector", lambda e: e.max(out=tok[:, 16:24], in_=tok[:, 8:16]), [c1], t2v)
            p.drain("vector")
            c3 = p.on("vector", lambda e: e.tensor_scalar(tok[:, 24:28], tok[:, 0:4],
                                                          tok[:, 17:18], None, OP.is_ge),
                      [c2], t2v)
            c4 = p.on("scalar", lambda e: e.activation(tok[:, 28:32], tok[:, 0:4], AF.Exp),
                      [c2], t2s)
            c5 = p.on("vector", lambda e: e.reduce_sum(tok[:, 32:33], tok[:, 28:32], axis=AX),
                      [c4, c3], t2v)
            p.drain("vector")
            c6 = p.on("vector", lambda e: e.reciprocal(tok[:, 33:34], tok[:, 32:33]), [c5], t2v)
            p.drain("vector")
            c7 = p.on("vector", lambda e: e.tensor_scalar(tok[:, 34:38], tok[:, 28:32],
                                                          tok[:, 33:34], None, OP.mult),
                      [c6], t2v)
            p.drain("vector")
            c8 = p.on("vector", lambda e: e.tensor_tensor(tok[:, 38:42], tok[:, 24:28],
                                                          tok[:, 34:38], OP.mult), [c7], t2v)
            p.drain("vector")
            c9 = p.on("vector", lambda e: e.reduce_sum(tok[:, 42:43], tok[:, 38:42], axis=AX),
                      [c8], t2v)
            p.drain("vector")
            c10 = p.on("vector", lambda e: e.tensor_scalar(tok[:, 43:44], tok[:, 42:43], 1e-8,
                                                           None, OP.add), [c9], t2v)
            p.drain("vector")
            c11 = p.on("vector", lambda e: e.reciprocal(tok[:, 44:45], tok[:, 43:44]), [c10], t2v)
            p.drain("vector")
            c12 = p.on("vector", lambda e: e.tensor_scalar(tok[:, 45:49], tok[:, 38:42],
                                                           tok[:, 44:45], -1.0,
                                                           OP.mult, OP.mult),
                       [c11], t2v)
            p.drain("vector")
            tku = p.on("tensor", lambda e, k=k: e.transpose(psB[0:4, 128:256],
                                                             tok[:, 45:49], idt[:]),
                       [c12], bmm)
            p.on("vector", lambda e, k=k: e.tensor_copy(u4[0:4, k * 128:(k + 1) * 128],
                                                        psB[0:4, 128:256]),
                 [tku], bev)
            tks = p.on("tensor", lambda e, k=k: e.transpose(psB[0:4, 256:384],
                                                            tok[:, 34:38], idt[:]),
                       [c12], bmm)
            p.on("vector", lambda e, k=k: e.tensor_copy(s4[0:4, k * 128:(k + 1) * 128],
                                                        psB[0:4, 256:384]),
                 [tks], bev)
        bu = p.dma(uwD, u4[0:4, :], waits=[(bev, 2 * NT), tok_xp], inc="b1b")
        bs = p.dma(smD, s4[0:4, :], waits=[(bev, 2 * NT)], inc="b1c")
        for n in range(4):
            p.dma(uwb()[n * 32:(n + 1) * 32, :],
                  uwD[n:n + 1, :].to_broadcast([32, T]), waits=[bu], inc="b2u")
            p.dma(smb()[n * 32:(n + 1) * 32, :],
                  smD[n:n + 1, :].to_broadcast([32, T]), waits=[bs], inc="b2u")
        tok_uw = (p.ctr("b2u"), p.ctr("b2u").v)
        wmd = p.ctr("wmd")
        for ch in range(4):
            p.on("vector", lambda e, ch=ch: e.tensor_scalar(
                wmb()[:, ch * 512:(ch + 1) * 512], uwb()[:, ch * 512:(ch + 1) * 512],
                1.0, 1.0, OP.mult, OP.add), [tok_uw], wmd)
        tok_wm = (wmd, 4)

        kdm = p.ctr("kdm")
        kde = p.ctr("kde")
        kds = [p.ctr("kds0"), p.ctr("kds1")]
        for k in range(NT):
            w = [tok_kn, (bev, 2 * NT)] if k == 0 else []
            if k >= 2:
                w.append((kds[k % 2], (k // 2) * 16))
            tk = p.on("tensor", lambda e, k=k: e.transpose(
                psB[:, 0:128], kvf[:, k * 128:(k + 1) * 128], idt[:]),
                w + ([(kde, k)] if k >= 1 else []), kdm)
            te = p.on("vector", lambda e, k=k: e.tensor_scalar(
                stg[:, (k % 2) * 128:(k % 2) * 128 + 128], psB[:, 0:128],
                -1.0, None, OP.mult),
                [tk] + ([(kds[k % 2], (k // 2) * 16)] if k >= 2 else []), kde)
            p.dma(knD.rearrange("(k p) c -> k p c", p=128)[k],
                  stg[:, (k % 2) * 128:(k % 2) * 128 + 128],
                  waits=[te], inc=kds[k % 2])
        qdm = p.ctr("qdm")
        qde = p.ctr("qde")
        qds = [p.ctr("qds0"), p.ctr("qds1")]
        for k in range(NT):
            w = [tok_qp, (kde, NT)] if k == 0 else []
            tk = p.on("tensor", lambda e, k=k: e.transpose(
                psB[0:128, 128:160], qf[0:32, k * 128:(k + 1) * 128], idt[0:32, 0:32]),
                w + ([(qde, k)] if k >= 1 else []), qdm)
            te = p.on("vector", lambda e, k=k: e.tensor_copy(
                stg[:, 256 + (k % 2) * 32:256 + (k % 2) * 32 + 32], psB[:, 128:160]),
                [tk] + ([(qds[k % 2], (k // 2) * 16)] if k >= 2 else []), qde)
            p.dma(qD.rearrange("(k p) c -> k p c", p=128)[k],
                  stg[:, 256 + (k % 2) * 32:256 + (k % 2) * 32 + 32],
                  waits=[te], inc=qds[k % 2])
        tok_knD = (kds[0], kds[0].v)
        tok_knD1 = (kds[1], kds[1].v)
        tok_qD = (qds[0], qds[0].v)
        tok_qD1 = (qds[1], qds[1].v)

        knl = [p.ctr("knl0"), p.ctr("knl1")]
        ql = [p.ctr("ql0"), p.ctr("ql1")]
        vz = p.ctr("vz")
        va = p.ctr("va")
        th = p.ctr("th")
        svc = p.ctr("sv")
        iz0 = p.on("vector", lambda e: e.memset(Sst[:], 0.0),
                   [tok_wm, tok_uw, tok_rnb, tok_bn, (tev, NT)], "ms")
        iz1 = p.on("vector", lambda e: e.memset(z0t[:], 0.0), [], "ms")
        iz2 = p.on("vector", lambda e: e.memset(spt[:], 0.0), [], "ms")
        izk = p.on("vector", lambda e: e.memset(knr[:], 1.0), [], "ms")
        iz = p.on("vector", lambda e: e.tensor_copy(
            Sst[:, 32:33], kvf[:, T:T + 1]), [iz2, tok_kv], "ms")
        for c in range(NCH):
            wdk = [tok_knD, tok_knD1, tok_qD, tok_qD1, izk] if c < 2 else \
                [(va, (c - 1) * CH)]
            kt4 = []
            for n in range(4):
                kt4.append(p.dma(
                    knr[n * 32:(n + 1) * 32, (c % 2) * CH * 33:((c % 2) + 1) * CH * 33]
                    .rearrange("p (t j) -> p t j", j=33)[:, :, 0:NS],
                    knD.rearrange("t (n j) -> n t j", n=4)[n, c * CH:(c + 1) * CH, :]
                    .unsqueeze(0).to_broadcast([32, CH, NS]),
                    waits=wdk, inc=knl[c % 2]))
            qt1 = p.dma(qr[:, (c % 2) * CH * NS:((c % 2) + 1) * CH * NS]
                        .rearrange("p (t j) -> p t j", j=NS),
                        qD[c * CH:(c + 1) * CH, :].unsqueeze(0).to_broadcast([128, CH, NS]),
                        waits=wdk, inc=ql[c % 2])
            for s_ in range(CH):
                t = c * CH + s_
                knt = knr[:, (c % 2) * CH * 33 + s_ * 33:(c % 2) * CH * 33 + (s_ + 1) * 33]
                qt = qr[:, (c % 2) * CH * NS + s_ * NS:(c % 2) * CH * NS + (s_ + 1) * NS]
                w1 = [iz, iz1] if t == 0 else []
                if s_ == 0:
                    w1 += [kt4[-1], qt1]
                t_op1 = p.on("vector", lambda e, knt=knt: e.scalar_tensor_tensor(
                    scr[:], Sst[:], 1.0, knt, OP.mult, OP.mult, accum_out=nsum[:, 0:1]),
                    w1, svc)
                t_op5 = None
                if t > 0:
                    t_op5 = p.on("vector", lambda e, qtp=qt_prev, tp=t - 1:
                                 e.scalar_tensor_tensor(
                        scr2[:], Sst[:, 0:NS], 1.0, qtp, OP.mult, OP.mult,
                        accum_out=outsq()[:, tp:tp + 1]), [], va)
                if t + 1 < T:
                    p.on("vector", lambda e, t=t: e.tensor_copy(
                        Sst[:, 32:33], kvf[:, T + t + 1:T + t + 2]), [t_op1], "sv2")
                t_z = p.on("vector", lambda e, knt=knt: e.scalar_tensor_tensor(
                    zbt[:], knt[:, 0:NS], nsum[:, 0:1], z0t[:], OP.mult, OP.add),
                    [t_op1], vz)
                t_th = p.on("scalar", lambda e: e.activation(Ut[:], zbt[:], AF.Tanh),
                            [(vz, t + 1)], th)
                t_s4 = p.on("vector", lambda e, t=t: e.scalar_tensor_tensor(
                    Sst[:, 0:NS], Ut[:], uwb()[:, t:t + 1], spt[:], OP.mult, OP.add),
                    [(th, t + 1), t_op5], svc)
                if t + 1 < T:
                    p.on("vector", lambda e, t=t: e.tensor_scalar(
                        z0t[:], Sst[:, 0:NS], betab[:, t + 1:t + 2], None, OP.mult),
                        [t_s4], "sv2")
                    p.on("vector", lambda e, t=t: e.tensor_scalar(
                        spt[:], Sst[:, 0:NS], wmb()[:, t + 1:t + 2], None, OP.mult),
                        [], "sv2")
                qt_prev = qt
        p.on("vector", lambda e: e.scalar_tensor_tensor(
            scr2[:], Sst[:, 0:NS], 1.0, qt_prev, OP.mult, OP.mult,
            accum_out=outsq()[:, T - 1:T]), [t_s4], va)

        p.on("vector", lambda e: e.tensor_copy(nsum[:, 3:4], nsum[:, 0:1]), [(va, T)], va)
        pg = p.ctr("pg")
        sg = p.ctr("sg")
        for ch in range(4):
            sl = lambda b, ch=ch: b()[:, ch * 512:(ch + 1) * 512]
            g1 = p.on("scalar", lambda e, ch=ch: e.activation(
                betab[:, ch * 512:(ch + 1) * 512], outsq()[:, ch * 512:(ch + 1) * 512],
                AF.Sigmoid), [(va, T + 1)], sg)
            g2b = p.on("vector", lambda e, ch=ch: e.tensor_tensor(
                rnb[:, ch * 512:(ch + 1) * 512], outsq()[:, ch * 512:(ch + 1) * 512],
                outsq()[:, ch * 512:(ch + 1) * 512], OP.mult), [g1], pg)
            g3 = p.on("vector", lambda e, ch=ch: e.tensor_tensor(
                rnb[:, ch * 512:(ch + 1) * 512], rnb[:, ch * 512:(ch + 1) * 512],
                betab[:, ch * 512:(ch + 1) * 512], OP.mult), [g2b], pg)
            p.drain("vector")
            g4 = p.on("vector", lambda e, ch=ch: e.tensor_tensor(
                rnb[:, ch * 512:(ch + 1) * 512], rnb[:, ch * 512:(ch + 1) * 512],
                smb()[:, ch * 512:(ch + 1) * 512], OP.mult), [g3], pg)
        crm = p.ctr("crm")
        cre = p.ctr("cre")
        for ch in range(4):
            tm = p.on("tensor", lambda e, ch=ch: e.matmul(
                psB[0:32, 0:512], selb[:], rnb[:, ch * 512:(ch + 1) * 512],
                start=True, stop=True), [(pg, 12)] + ([(cre, ch)] if ch else []), crm)
            p.on("scalar", lambda e, ch=ch: e.activation(
                cellb[0:32, ch * 512:(ch + 1) * 512], psB[0:32, 0:512], AF.Copy),
                [tm], cre)
        lwo = p.dma(wst[0:32, 0:D], WoT, waits=[], inc="wl0")
        cwo = p.on("vector", lambda e: e.tensor_copy(wbf[0:32, 0:D], wst[0:32, 0:D]),
                   [lwo], wcast)
        fmm = p.ctr("fmm")
        fev = p.ctr("fev")
        fso = [p.ctr("fso0"), p.ctr("fso1")]
        for mt in range(ND):
            for ch in range(4):
                i = mt * 4 + ch
                ps = psP if i % 2 == 0 else psP2
                w = [cwo, (cre, 4)] if i == 0 else []
                if i >= 2:
                    w.append((fev, i - 1))
                p.on("tensor", lambda e, mt=mt, ch=ch, ps=ps: e.matmul(
                    ps[:, :], wbf[0:32, mt * 128:(mt + 1) * 128],
                    cellb[0:32, ch * 512:(ch + 1) * 512],
                    start=True, stop=True), w, fmm)
                ew = [(fmm, i + 1)]
                if i >= 2:
                    ew.append((fso[i % 2], ((i - 2) // 2 + 1) * 16))
                te = p.on("scalar", lambda e, i=i, ps=ps: e.activation(
                    aor[:, (i % 2) * 512:(i % 2) * 512 + 512], ps[:, :], AF.Copy), ew, fev)
                p.dma(aT.rearrange("(mt p) t -> mt p t", p=128)[mt][:, ch * 512:(ch + 1) * 512],
                      aor[:, (i % 2) * 512:(i % 2) * 512 + 512], waits=[te], inc=fso[i % 2])




def _consts():
    import ml_dtypes
    c = 1e-4
    dec = np.zeros((28, 128, 512), np.float64)
    pp = np.arange(128)[:, None]
    jj = np.arange(512)[None, :]
    for mi in range(28):
        m = mi - 12
        diff = (128 * m + pp) - jj
        dec[mi] = np.exp(-c * np.abs(diff))
    blk = np.zeros((128, 4), np.float32)
    for n in range(4):
        blk[n * 32:(n + 1) * 32, n] = 1.0
    sel = np.zeros((128, 32), np.float32)
    for n in range(4):
        for i in range(32):
            sel[n * 32 + i, i] = 1.0
    return {
        "ident": np.eye(128, dtype=np.float32),
        "onescol": np.ones((128, 1), np.float32),
        "ones1x": np.ones((1, 128), np.float32),
        "decay": dec.astype(ml_dtypes.bfloat16),
        "blk4": blk, "sel4": sel,
    }


def build_fused():
    nc = _nc()
    DI = lambda name, shape: nc.dram_tensor(name, shape, F32, kind="ExternalInput").ap()
    IN = lambda name, shape: nc.dram_tensor(name, shape, F32, kind="Internal").ap()
    x = DI("x", [T, D])
    ident = DI("ident", [128, 128])
    onescol = DI("onescol", [128, 1])
    ones1x = DI("ones1x", [1, 128])
    decay = nc.dram_tensor("decay", [28, 128, 512], BF16, kind="ExternalInput").ap()
    blk4 = DI("blk4", [128, 4])
    sel4 = DI("sel4", [128, 32])
    Wq = DI("Wq", [D, D]); Wk = DI("Wk", [D, D])
    Wv = DI("Wv", [D, D]); Wo = DI("Wo", [D, D])
    Wdkv = DI("Wdkv", [D, 256]); Wuk = DI("Wuk", [256, D]); Wuv = DI("Wuv", [256, D])
    wg = DI("wg", [2816, D]); wu = DI("wu", [2816, D]); wd = DI("wd", [D, 2816])
    WinT = DI("WinT", [D, D]); WrT = DI("WrT", [D, 4]); WkvT = DI("WkvT", [D, 256])
    WbT = DI("WbT", [D, 128]); bbt = DI("bbt", [128, 1])
    WqT = DI("WqT", [D, 32]); WoT = DI("WoT", [32, D])
    wupT = DI("wupT", [D, 4096]); wdnT = DI("wdnT", [4096, D])
    out = nc.dram_tensor("out", [T, D], F32, kind="ExternalOutput").ap()
    xnT = IN("xnT", [D, T]); a0T = IN("a0T", [D, T]); f0T = IN("f0T", [D, T])
    x1T = IN("x1T", [D, T]); xn1T = IN("xn1T", [D, T])
    a1T = IN("a1T", [D, T]); f1T = IN("f1T", [D, T])
    xq_c = nc.dram_tensor("xq_cache", [NT, 128, D], BF16, kind="Internal").ap()
    wup_c = nc.dram_tensor("wup_cache", [32, ND, 128, 128], BF16, kind="Internal").ap()
    wdn_c = nc.dram_tensor("wdn_cache", [ND, 32, 128, 128], BF16, kind="Internal").ap()
    knD = IN("knD", [T, 128]); qD = IN("qD", [T, 32])
    r4D = IN("r4D", [4, T]); uwD = IN("uwD", [4, T]); smD = IN("smD", [4, T])
    with ExitStack() as ctx:
        p = Prog(nc, ctx)

        def stage(fn, pfx, Bprev, *args):
            p.begin_stage()
            fn(nc, p, pfx, Bprev, *args)
            return p.end_stage()

        Bt = stage(emit_ln0, "a", [], x, ident, xnT)
        Bt = stage(emit_attn0, "b", Bt, xnT, Wq, Wk, Wv, Wo, Wdkv, Wuk, Wuv,
                   decay, ones1x, a0T)
        Bt = stage(emit_ffn0, "c", Bt, xnT, wg, wu, wd, ident, f0T, xq_c)
        Bt = stage(emit_res_ln, "d", Bt, x, a0T, f0T, ident, onescol, ones1x,
                   x1T, xn1T)
        Bt = stage(emit_attn1, "e", Bt, xn1T, WinT, WrT, WkvT, WbT, bbt, WqT,
                   WoT, blk4, sel4, ident, a1T, knD, qD, r4D, uwD, smD)
        Bt = stage(emit_ffn1, "f", Bt, xn1T, wupT, wdnT, onescol, ones1x, f1T,
                   wup_c, wdn_c)
        Bt = stage(emit_final, "g", Bt, x1T, a1T, f1T, ident, out)
        p.begin_stage()
        p.on("sync", lambda e: e.nop(), Bt, None)
        p.emit()
    return nc


def kernel(**inputs):
    C = _consts()
    x = np.ascontiguousarray(inputs["L0_attn_x"], np.float32)
    Wkv = inputs["L1_attn_W_kv"]
    perm = [n * 64 + j for n in range(4) for j in range(32)] + \
           [n * 64 + 32 + i for n in range(4) for i in range(32)]
    shared = {
        "ident": C["ident"], "onescol": C["onescol"], "ones1x": C["ones1x"],
        "decay": C["decay"], "blk4": C["blk4"], "sel4": C["sel4"],
        "Wq": inputs["L0_attn_Wq"], "Wk": inputs["L0_attn_Wk"],
        "Wv": inputs["L0_attn_Wv"], "Wo": inputs["L0_attn_Wo"],
        "Wdkv": inputs["L0_attn_W_down_kv"], "Wuk": inputs["L0_attn_W_up_k"],
        "Wuv": inputs["L0_attn_W_up_v"],
        "wg": inputs["L0_ffn_w_gate"], "wu": inputs["L0_ffn_w_up"],
        "wd": inputs["L0_ffn_w_down"],
        "WinT": np.ascontiguousarray(inputs["L1_attn_W_in"].T),
        "WrT": np.ascontiguousarray(inputs["L1_attn_W_router"].T),
        "WkvT": np.ascontiguousarray(np.asarray(Wkv)[perm].T),
        "WbT": np.ascontiguousarray(inputs["L1_attn_W_beta"].T),
        "bbt": np.ascontiguousarray(np.asarray(inputs["L1_attn_b_beta"],
                                               np.float32).reshape(128, 1)),
        "WqT": np.ascontiguousarray(inputs["L1_attn_W_q"].T),
        "WoT": np.ascontiguousarray(inputs["L1_attn_W_out"].T),
        "wupT": np.ascontiguousarray(inputs["L1_ffn_w_up"].T),
        "wdnT": np.ascontiguousarray(inputs["L1_ffn_w_down"].T),
    }
    nc = build_fused()
    in_maps = [dict(shared, x=x[b]) for b in range(B)]
    res = run_bass_kernel_spmd(nc, in_maps, core_ids=list(range(B)))
    return np.stack([np.asarray(res.results[b]["out"]) for b in range(B)], 0)



# version 08
# speedup vs baseline: 1.0244x; 1.0244x over previous
import numpy as np
from contextlib import ExitStack
import concourse.bass as bass
import concourse.mybir as mybir
from concourse.bass_utils import run_bass_kernel_spmd

F32 = mybir.dt.float32
BF16 = mybir.dt.bfloat16
U8 = mybir.dt.uint8
AF = mybir.ActivationFunctionType
OP = mybir.AluOpType
AX = mybir.AxisListType.X

B, T, D = 8, 2048, 1024
NT = T // 128
ND = D // 128
MAGIC = 1.5 * 2.0 ** 23
MAGIC2 = 1.5 * 2.0 ** 22
LDEC = 0.01 * 0.01


class Phys:
    def __init__(self, sem):
        self.sem = sem
        self.total = 0


class Ctr:

    def __init__(self, phys):
        self.phys = phys
        self.base = phys.total
        self.v = 0

    @property
    def sem(self):
        return self.phys.sem

    def inc(self, n):
        self.v += n
        self.phys.total = self.base + self.v
        return self.v


class Prog:
    ENGINES = ("sync", "vector", "scalar", "tensor", "gpsimd")

    def __init__(self, nc, ctx, npool=72):
        self.nc = nc
        self.ctx = ctx
        self.steps = {e: [] for e in self.ENGINES}
        self.pool = [Phys(ctx.enter_context(nc.semaphore("sp%d" % i)))
                     for i in range(npool)]
        self.pool_i = 0
        self._ctrs = {}
        self._stage_phys = set()
        self._dma_ctrs = set()
        self._marks = {e: 0 for e in self.ENGINES}

    def begin_stage(self):
        self._ctrs = {}
        self._stage_phys = set()
        self._dma_ctrs = set()
        self._marks = {e: len(self.steps[e]) for e in self.ENGINES}

    def ctr(self, name):
        if name not in self._ctrs:
            for _ in range(len(self.pool)):
                ph = self.pool[self.pool_i]
                self.pool_i = (self.pool_i + 1) % len(self.pool)
                if id(ph) not in self._stage_phys:
                    break
            else:
                raise RuntimeError("semaphore pool exhausted")
            self._stage_phys.add(id(ph))
            self._ctrs[name] = Ctr(ph)
        return self._ctrs[name]

    def on(self, eng, fn, waits=(), inc=None, n=1):
        c = self.ctr(inc) if isinstance(inc, str) else inc
        tv = c.inc(n) if c is not None else None
        self.steps[eng].append((fn, [w for w in waits if w is not None], c, n, tv))
        if c is not None:
            return (c, tv)
        return None

    def drain(self, eng):
        self.steps[eng].append((lambda e: e.drain(), [], None, 0, None))

    def dma(self, out, in_, waits=(), inc="dma", eng="sync"):
        c = self.ctr(inc) if isinstance(inc, str) else inc
        self._dma_ctrs.add(c)
        return self.on(eng, lambda e: e.dma_start(out=out, in_=in_), waits, c, 16)

    def end_stage(self):
        best = {}
        for e in self.ENGINES:
            for fn, waits, c, n, tv in reversed(self.steps[e][self._marks[e]:]):
                if c is not None:
                    if best.get(id(c), (None, -1))[1] < tv:
                        best[id(c)] = (c, tv)
                    break
        for c in self._dma_ctrs:
            if best.get(id(c), (None, -1))[1] < c.v:
                best[id(c)] = (c, c.v)
        return list(best.values())

    def emit(self):
        blk = self.ctx.enter_context(self.nc.Block())

        def mk(eng_name):
            def body(e):
                for fn, waits, c, n, tv in self.steps[eng_name]:
                    for (wc, wv) in waits:
                        e.wait_ge(wc.sem, wc.base + wv)
                    ins = fn(e)
                    if c is not None:
                        assert ins is not None
                        ins.then_inc(c.sem, n)
            return body

        for e in self.ENGINES:
            if self.steps[e]:
                getattr(blk, e)(mk(e))


def _nc():
    return bass.Bass("TRN2", target_bir_lowering=False)


def emit_transpose_seq(p, pairs, idt, psA, psB, wait_src, name, first_extra=None):
    mt = p.ctr(name + "_mt")
    ev = p.ctr(name + "_ev")
    toks = []
    for i, (s, d) in enumerate(pairs):
        bank = psA if i % 2 == 0 else psB
        waits = []
        if i == 0 and first_extra is not None:
            waits.append(first_extra)
        w = wait_src(i)
        if w is not None:
            waits.append(w)
        if i >= 2:
            waits.append((ev, i - 1))
        toks.append(p.on("tensor", lambda e, b=bank, s=s: e.transpose(b[:, 0:128], s, idt[:]),
                         waits, mt))
    evtoks = []
    for i, (s, d) in enumerate(pairs):
        bank = psA if i % 2 == 0 else psB
        evtoks.append(p.on("vector", lambda e, b=bank, d=d: e.tensor_copy(d, b[:, 0:128]),
                           [toks[i]], ev))
    return ev


def emit_rowstats(p, tiles_ap, n_dt, onc, psS, rowout, wait_fn, name, evac_eng="vector"):
    mm = p.ctr(name + "_mm")
    evc = p.ctr(name + "_ev")
    tok = None
    for ch in range(T // 512):
        mts = []
        for dt in range(n_dt):
            waits = [wait_fn(dt)]
            if ch > 0 and dt == 0:
                waits.append((evc, ch))
            mts.append(p.on("tensor", lambda e, ch=ch, dt=dt, st=(dt == 0), sp=(dt == n_dt - 1):
                            e.matmul(psS[0:1, :], onc[:],
                                     tiles_ap(dt)[:, ch * 512:(ch + 1) * 512],
                                     start=st, stop=sp), waits, mm))
        tok = p.on(evac_eng, lambda e, ch=ch: e.tensor_copy(rowout[0:1, ch * 512:(ch + 1) * 512],
                                                            psS[0:1, :]),
                   [mts[-1]], evc)
    return tok



def emit_ln0(nc, p, pfx, B, x, ident, xnT):
    with ExitStack() as ctx:
        xt = ctx.enter_context(nc.sbuf_tensor(pfx + "xt", [128, NT * D], F32))
        xo = ctx.enter_context(nc.sbuf_tensor(pfx + "xo", [128, ND * T], F32))
        sq = ctx.enter_context(nc.sbuf_tensor(pfx + "sq", [128, 2 * D], F32))
        idt = ctx.enter_context(nc.sbuf_tensor(pfx + "idt", [128, 128], F32))
        stats = ctx.enter_context(nc.sbuf_tensor(pfx + "stats", [128, 6 * NT], F32))
        mcol = ctx.enter_context(nc.sbuf_tensor(pfx + "mcol", [128, NT], F32))
        icol = ctx.enter_context(nc.sbuf_tensor(pfx + "icol", [128, NT], F32))
        psA = ctx.enter_context(nc.psum_tensor(pfx + "psA", [128, 512], F32))
        psB = ctx.enter_context(nc.psum_tensor(pfx + "psB", [128, 512], F32))
        p.dma(xt[:].rearrange("p (k d) -> p k d", k=NT),
              x.rearrange("(k p) d -> p k d", p=128), waits=B)
        p.dma(idt[:], ident, inc="dma", waits=B)
        tok_ld = (p.ctr("dma"), p.ctr("dma").v)

        nrm = p.ctr("nrm")
        for k in range(NT):
            xk = xt[:, k * D:(k + 1) * D]
            sqk = sq[:, (k % 2) * D:(k % 2) * D + D]
            c = lambda j, k=k: stats[:, j * NT + k:j * NT + k + 1]
            sq_w = [tok_ld] if k < 2 else [tok_ld, (nrm, k - 1)]
            t_sq = p.on("scalar", lambda e, xk=xk, sqk=sqk: e.activation(sqk, xk, AF.Square),
                        sq_w, "sqd")
            t_s1 = p.on("vector", lambda e, xk=xk, o=c(0): e.reduce_sum(o, xk, axis=AX),
                        [tok_ld], "vst")
            t_s2 = p.on("vector", lambda e, sqk=sqk, o=c(1): e.reduce_sum(o, sqk, axis=AX),
                        [t_sq], "vst")
            t_m = p.on("vector", lambda e, i=c(0), o=mcol[:, k:k + 1]:
                       e.tensor_scalar(o, i, 1.0 / D, None, OP.mult), [t_s1], "vst")
            t_m2 = p.on("vector", lambda e, i=mcol[:, k:k + 1], o=c(5):
                        e.tensor_scalar(o, i, i, None, OP.mult), [t_m], "vst")
            t_e2 = p.on("vector", lambda e, i=c(1), o=c(2):
                        e.tensor_scalar(o, i, 1.0 / D, None, OP.mult), [t_s2], "vst")
            t_v = p.on("vector", lambda e, a=c(5), b=c(2), o=c(3):
                       e.scalar_tensor_tensor(o, a, -1.0, b, OP.mult, OP.add),
                       [t_m2, t_e2], "vst")
            p.drain("vector")
            t_ve = p.on("vector", lambda e, io=c(3), o=c(4):
                        e.tensor_scalar(o, io, 1e-5, None, OP.add), [t_v], "vst")
            t_ln = p.on("scalar", lambda e, i=c(4), o=c(3):
                        e.activation(o, i, AF.Ln), [t_ve], "sqd")
            p.drain("scalar")
            t_iv = p.on("scalar", lambda e, i=c(3), o=icol[:, k:k + 1]:
                        e.activation(o, i, AF.Exp, scale=-0.5), [t_ln], "sqd")
            p.on("vector", lambda e, xk=xk, m=mcol[:, k:k + 1], iv=icol[:, k:k + 1]:
                 e.tensor_scalar(xk, xk, m, iv, OP.subtract, OP.mult),
                 [t_iv], nrm)

        pairs = []
        for k in range(NT):
            for dt in range(ND):
                pairs.append((xt[:, k * D + dt * 128: k * D + (dt + 1) * 128],
                              xo[:, dt * T + k * 128: dt * T + (k + 1) * 128]))
        ev = emit_transpose_seq(p, pairs, idt, psA, psB,
                                lambda i: (nrm, i // ND + 1) if i % ND == 0 else None, "tr",
                                first_extra=tok_ld)
        sto = p.ctr("sto")
        for dt in range(ND):
            p.dma(xnT.rearrange("(dt p) t -> dt p t", p=128)[dt],
                  xo[:, dt * T:(dt + 1) * T],
                  waits=[(ev, (NT - 1) * ND + dt + 1)], inc=sto)



def emit_res_ln(nc, p, pfx, B, x, a, f, ident, onescol, ones1x, x1T, xn1T):
    with ExitStack() as ctx:
        xt = ctx.enter_context(nc.sbuf_tensor(pfx + "xt", [128, NT * D], F32))
        x1 = ctx.enter_context(nc.sbuf_tensor(pfx + "x1", [128, ND * T], F32))
        ar = xt[:, 0:2 * T]
        fb = xt[:, 2 * T:4 * T]
        sqb = xt[:, 4 * T:6 * T]
        idt = ctx.enter_context(nc.sbuf_tensor(pfx + "idt", [128, 128], F32))
        onc = ctx.enter_context(nc.sbuf_tensor(pfx + "onc", [128, 1], F32))
        on1 = ctx.enter_context(nc.sbuf_tensor(pfx + "on1", [1, 128], F32))
        row = ctx.enter_context(nc.sbuf_tensor(pfx + "row", [1, 6 * T], F32))
        psA = ctx.enter_context(nc.psum_tensor(pfx + "psA", [128, 512], F32))
        psB = ctx.enter_context(nc.psum_tensor(pfx + "psB", [128, 512], F32))
        psS = ctx.enter_context(nc.psum_tensor(pfx + "psS", [128, 512], F32))
        psI = ctx.enter_context(nc.psum_tensor(pfx + "psI", [128, 512], F32))
        psM = ctx.enter_context(nc.psum_tensor(pfx + "psM", [128, 512], F32))

        p.dma(xt[:].rearrange("p (k d) -> p k d", k=NT),
              x.rearrange("(k p) d -> p k d", p=128), waits=B)
        p.dma(idt[:], ident, waits=B)
        p.dma(onc[:], onescol, waits=B)
        p.dma(on1[:], ones1x, waits=B)
        alldma = (p.ctr("dma"), p.ctr("dma").v)

        pairs = []
        for k in range(NT):
            for dt in range(ND):
                pairs.append((xt[:, k * D + dt * 128: k * D + (dt + 1) * 128],
                              x1[:, dt * T + k * 128: dt * T + (k + 1) * 128]))
        ev = emit_transpose_seq(p, pairs, idt, psA, psB,
                                lambda i: None, "tr", first_extra=alldma)
        evall = (ev, NT * ND)

        add1 = p.ctr("add1")
        for dt in range(ND):
            wa = [evall] if dt < 2 else [(add1, dt - 1)]
            la = p.dma(ar[:, (dt % 2) * T:(dt % 2) * T + T],
                       a.rearrange("(dt p) t -> dt p t", p=128)[dt],
                       waits=wa, inc="dma%d" % (dt % 2))
            lf = p.dma(fb[:, (dt % 2) * T:(dt % 2) * T + T],
                       f.rearrange("(dt p) t -> dt p t", p=128)[dt],
                       waits=wa, inc="dmf%d" % (dt % 2))
            t1 = p.on("vector", lambda e, dt=dt: e.tensor_tensor(
                x1[:, dt * T:(dt + 1) * T], x1[:, dt * T:(dt + 1) * T],
                ar[:, (dt % 2) * T:(dt % 2) * T + T], OP.add), [la, lf], "vadd")
            p.on("vector", lambda e, dt=dt: e.tensor_tensor(
                x1[:, dt * T:(dt + 1) * T], x1[:, dt * T:(dt + 1) * T],
                fb[:, (dt % 2) * T:(dt % 2) * T + T], OP.add), [t1], add1)
        sto = p.ctr("sto")
        for dt in range(ND):
            p.dma(x1T.rearrange("(dt p) t -> dt p t", p=128)[dt],
                  x1[:, dt * T:(dt + 1) * T], waits=[(add1, dt + 1)], inc=sto)

        tok_s1 = emit_rowstats(p, lambda dt: x1[:, dt * T:(dt + 1) * T], ND,
                               onc, psS, row, lambda dt: (add1, ND), "s1")
        sqd = p.ctr("sqd")
        s2mm = p.ctr("s2mm")
        s2ev = p.ctr("s2ev")
        for ch in range(4):
            for dt in range(ND):
                i = ch * ND + dt
                sq_w = [(add1, ND)] if i < 2 else [(s2mm, i - 1)]
                t_q = p.on("scalar", lambda e, ch=ch, dt=dt: e.activation(
                    sqb[:, (dt % 2) * T: (dt % 2) * T + 512],
                    x1[:, dt * T + ch * 512: dt * T + ch * 512 + 512], AF.Square),
                    sq_w, sqd)
                waits = [t_q]
                if dt == 0 and ch > 0:
                    waits.append((s2ev, ch))
                p.on("tensor", lambda e, ch=ch, dt=dt, st=(dt == 0), sp=(dt == ND - 1):
                     e.matmul(psI[0:1, :], onc[:],
                              sqb[:, (dt % 2) * T: (dt % 2) * T + 512],
                              start=st, stop=sp), waits, s2mm)
            p.on("vector", lambda e, ch=ch: e.tensor_copy(
                row[0:1, T + ch * 512: T + ch * 512 + 512], psI[0:1, :]),
                [(s2mm, (ch + 1) * ND)], s2ev)

        r = lambda j: row[0:1, j * T:(j + 1) * T]
        t1 = p.on("vector", lambda e: e.tensor_scalar(r(2), r(0), 1.0 / D, None, OP.mult),
                  [tok_s1], "rs")
        p.drain("vector")
        t2 = p.on("vector", lambda e: e.tensor_tensor(r(3), r(2), r(2), OP.mult),
                  [t1], "rs")
        t3 = p.on("vector", lambda e: e.tensor_scalar(r(5), r(1), 1.0 / D, None, OP.mult),
                  [(s2ev, 4)], "rs")
        t4 = p.on("vector", lambda e: e.scalar_tensor_tensor(r(3), r(3), -1.0, r(5),
                                                             OP.mult, OP.add),
                  [t2, t3], "rs")
        p.drain("vector")
        t5 = p.on("vector", lambda e: e.tensor_scalar(r(3), r(3), 1e-5, None, OP.add),
                  [t4], "rs")
        t6 = p.on("scalar", lambda e: e.activation(r(1), r(3), AF.Ln), [t5], "ras")
        p.drain("scalar")
        t7 = p.on("scalar", lambda e: e.activation(r(4), r(1), AF.Exp, scale=-0.5), [t6], "ras")
        t8 = p.on("vector", lambda e: e.tensor_tensor(r(5), r(2), r(4), OP.mult), [t7], "rs")

        nsto = p.ctr("nsto")
        for ch in range(4):
            wb = [t8, (sto, 8 * 16)]
            if ch > 0:
                wb.append((nsto, ch * ND))
            bi = p.on("tensor", lambda e, ch=ch: e.matmul(
                psI[:, :], on1[:], r(4)[:, ch * 512:(ch + 1) * 512], start=True, stop=True),
                wb, "bmm")
            bm = p.on("tensor", lambda e, ch=ch: e.matmul(
                psM[:, :], on1[:], r(5)[:, ch * 512:(ch + 1) * 512], start=True, stop=True),
                [t8], "bmm")
            for dt in range(ND):
                tn = p.on("vector", lambda e, ch=ch, dt=dt: e.tensor_tensor(
                    x1[:, dt * T + ch * 512: dt * T + ch * 512 + 512],
                    x1[:, dt * T + ch * 512: dt * T + ch * 512 + 512], psI[:, :], OP.mult),
                    [bi, bm], "vn")
                p.on("vector", lambda e, ch=ch, dt=dt: e.tensor_tensor(
                    x1[:, dt * T + ch * 512: dt * T + ch * 512 + 512],
                    x1[:, dt * T + ch * 512: dt * T + ch * 512 + 512], psM[:, :], OP.subtract),
                    [tn], nsto)
        for dt in range(ND):
            p.dma(xn1T.rearrange("(dt p) t -> dt p t", p=128)[dt],
                  x1[:, dt * T:(dt + 1) * T], waits=[(nsto, 4 * ND)], inc=sto)



def emit_final(nc, p, pfx, B, x1, a, f, ident, out):
    with ExitStack() as ctx:
        xb = ctx.enter_context(nc.sbuf_tensor(pfx + "xb", [128, ND * T], F32))
        ab = ctx.enter_context(nc.sbuf_tensor(pfx + "ab", [128, ND * T], F32))
        ob = ctx.enter_context(nc.sbuf_tensor(pfx + "ob", [128, NT * D], F32))
        idt = ctx.enter_context(nc.sbuf_tensor(pfx + "idt", [128, 128], F32))
        psA = ctx.enter_context(nc.psum_tensor(pfx + "psA", [128, 512], F32))
        psB = ctx.enter_context(nc.psum_tensor(pfx + "psB", [128, 512], F32))
        t_x = p.dma(xb[:].rearrange("p (dt t) -> p dt t", dt=ND),
                    x1.rearrange("(dt p) t -> p dt t", p=128), waits=B)
        t_a = p.dma(ab[:].rearrange("p (dt t) -> p dt t", dt=ND),
                    a.rearrange("(dt p) t -> p dt t", p=128), waits=B)
        p.dma(idt[:], ident, waits=B)
        alldma = (p.ctr("dma"), p.ctr("dma").v)
        add1 = p.ctr("add1")
        fl = []
        for dt in range(ND):
            w = [alldma] if dt < 2 else [(add1, dt - 1)]
            fl.append(p.dma(ob[:, (dt % 2) * T:(dt % 2) * T + T],
                            f.rearrange("(dt p) t -> dt p t", p=128)[dt],
                            waits=w, inc="dmf%d" % (dt % 2)))
        for dt in range(ND):
            t1 = p.on("vector", lambda e, dt=dt: e.tensor_tensor(
                xb[:, dt * T:(dt + 1) * T], xb[:, dt * T:(dt + 1) * T],
                ab[:, dt * T:(dt + 1) * T], OP.add), [alldma, fl[dt]], "vadd")
            p.on("vector", lambda e, dt=dt: e.tensor_tensor(
                xb[:, dt * T:(dt + 1) * T], xb[:, dt * T:(dt + 1) * T],
                ob[:, (dt % 2) * T:(dt % 2) * T + T], OP.add), [t1], add1)
        pairs = []
        for k in range(NT):
            for dt in range(ND):
                pairs.append((xb[:, dt * T + k * 128: dt * T + (k + 1) * 128],
                              ob[:, k * D + dt * 128: k * D + (dt + 1) * 128]))
        ev = emit_transpose_seq(p, pairs, idt, psA, psB,
                                lambda i: (add1, ND), "tr")
        sto = p.ctr("sto")
        for k in range(NT):
            p.dma(out.rearrange("(k p) d -> k p d", p=128)[k],
                  ob[:, k * D:(k + 1) * D], waits=[(ev, (k + 1) * ND)], inc=sto)





def emit_ffn1(nc, p, pfx, B, xnT, wupT, wdnT, onescol, ones1x, fT, wup_c, wdn_c):
    DF = 4096
    NF = DF // 128
    with ExitStack() as ctx:
        xb = ctx.enter_context(nc.sbuf_tensor(pfx + "xb", [128, ND * T], F32))
        xnb = ctx.enter_context(nc.sbuf_tensor(pfx + "xnb", [128, ND * T], BF16))
        wring = ctx.enter_context(nc.sbuf_tensor(pfx + "wring", [128, 2 * ND * 128], BF16))
        wdb = ctx.enter_context(nc.sbuf_tensor(pfx + "wdb", [128, 2 * NF * 128], BF16))
        i2sb = ctx.enter_context(nc.sbuf_tensor(pfx + "i2sb", [128, 1024], F32))
        fsl = ctx.enter_context(nc.sbuf_tensor(pfx + "fsl", [128, 2 * 1024], F32))
        row = ctx.enter_context(nc.sbuf_tensor(pfx + "row", [1, 4 * T], F32))
        acc = ctx.enter_context(nc.sbuf_tensor(pfx + "acc", [128, 4], F32))
        iac = ctx.enter_context(nc.sbuf_tensor(pfx + "iac", [128, 2], F32))
        onc = ctx.enter_context(nc.sbuf_tensor(pfx + "onc", [128, 1], F32))
        on1 = ctx.enter_context(nc.sbuf_tensor(pfx + "on1", [1, 128], F32))
        psS = ctx.enter_context(nc.psum_tensor(pfx + "psS", [128, 512], F32))
        psI = ctx.enter_context(nc.psum_tensor(pfx + "psI", [128, 512], F32))
        psU0 = ctx.enter_context(nc.psum_tensor(pfx + "psU0", [128, 512], F32))
        psU1 = ctx.enter_context(nc.psum_tensor(pfx + "psU1", [128, 512], F32))
        hb = xb[:].bitcast(BF16)
        stg = xb

        p.dma(xb[:].rearrange("p (dt t) -> p dt t", dt=ND),
              xnT.rearrange("(dt p) t -> p dt t", p=128), waits=B)
        p.dma(onc[:], onescol, waits=B)
        p.dma(on1[:], ones1x, waits=B)
        alldma = (p.ctr("dma"), p.ctr("dma").v)

        sqd = p.ctr("sqd")
        s1mm = p.ctr("s1mm")
        s1ev = p.ctr("s1ev")
        for ch in range(4):
            for dt in range(ND):
                i = ch * ND + dt
                sq_w = [alldma] if i < 2 else [(s1mm, i - 1)]
                t_q = p.on("scalar", lambda e, ch=ch, dt=dt: e.activation(
                    fsl[:, (dt % 2) * 1024:(dt % 2) * 1024 + 512],
                    xb[:, dt * T + ch * 512: dt * T + ch * 512 + 512], AF.Square),
                    sq_w, sqd)
                waits = [t_q]
                if dt == 0 and ch > 0:
                    waits.append((s1ev, ch))
                p.on("tensor", lambda e, dt=dt, st=(dt == 0), sp=(dt == ND - 1):
                     e.matmul(psS[0:1, :], onc[:],
                              fsl[:, (dt % 2) * 1024:(dt % 2) * 1024 + 512],
                              start=st, stop=sp), waits, s1mm)
            p.on("vector", lambda e, ch=ch: e.tensor_copy(row[0:1, ch * 512:(ch + 1) * 512],
                                                          psS[0:1, :]),
                 [(s1mm, (ch + 1) * ND)], s1ev)
        r = lambda j: row[0:1, j * T:(j + 1) * T]
        t1 = p.on("scalar", lambda e: e.activation(r(1), r(0), AF.Ln), [(s1ev, 4)], "ras")
        p.drain("scalar")
        t2 = p.on("scalar", lambda e: e.activation(r(2), r(1), AF.Exp, scale=0.5), [t1], "ras")
        t3 = p.on("vector", lambda e: e.tensor_scalar(r(1), r(2), 1.0 / 32.0, 1e-8,
                                                      OP.mult, OP.add), [t2], "rs")
        t4 = p.on("scalar", lambda e: e.activation(r(2), r(1), AF.Ln), [t3], "ras")
        p.drain("scalar")
        t5 = p.on("scalar", lambda e: e.activation(r(3), r(2), AF.Exp, scale=-1.0), [t4], "ras")
        xnbd = p.ctr("xnbd")
        for ch in range(4):
            bi = p.on("tensor", lambda e, ch=ch: e.matmul(
                psI[:, :], on1[:], r(3)[:, ch * 512:(ch + 1) * 512], start=True, stop=True),
                [t5] + ([(xnbd, ch * ND)] if ch else []), "bmm")
            for dt in range(ND):
                p.on("vector", lambda e, ch=ch, dt=dt: e.tensor_tensor(
                    xnb[:, dt * T + ch * 512: dt * T + ch * 512 + 512],
                    xb[:, dt * T + ch * 512: dt * T + ch * 512 + 512], psI[:, :], OP.mult),
                    [bi], xnbd)
        tok_xnb = (xnbd, 4 * ND)

        def quant_weight(tag, wT, n_slab, slab_cols, count, cache_store, res_tok):
            al = p.ctr(tag + "_al")
            toks = []
            res_list = res_tok if isinstance(res_tok, list) else [res_tok]
            for s in range(n_slab):
                w = list(res_list) if s < 2 else [(al, s - 1)]
                toks.append(p.dma(stg[:, (s % 2) * slab_cols:(s % 2 + 1) * slab_cols],
                                  wT.rearrange("(s p) c -> s p c", p=128)[s],
                                  waits=w, inc=tag + "_ld%d" % (s % 2)))
            for s in range(n_slab):
                t_r = p.on("vector", lambda e, s=s: e.reduce_sum(
                    acc[:, (s % 2) + 2:(s % 2) + 3],
                    stg[:, (s % 2) * slab_cols:(s % 2 + 1) * slab_cols],
                    axis=AX, apply_absolute_value=True), [toks[s]], tag + "_r")
                p.drain("vector")
                if s == 0:
                    p.on("vector", lambda e: e.tensor_copy(acc[:, 0:1], acc[:, 2:3]),
                         [t_r], al)
                else:
                    p.on("vector", lambda e, s=s: e.tensor_tensor(
                        acc[:, 0:1], acc[:, 0:1], acc[:, (s % 2) + 2:(s % 2) + 3], OP.add),
                        [t_r], al)
                p.drain("vector")
            tm = p.on("tensor", lambda e: e.matmul(psS[0:1, 0:1], acc[:, 0:1], onc[:],
                                                   start=True, stop=True),
                      [(al, n_slab)], tag + "_tm")
            te = p.on("vector", lambda e: e.tensor_copy(iac[0:1, 0:1], psS[0:1, 0:1]),
                      [tm], tag + "_te")
            p.drain("vector")
            tr = p.on("vector", lambda e: e.reciprocal(iac[0:1, 1:2], iac[0:1, 0:1]),
                      [te], tag + "_te")
            p.drain("vector")
            ts_ = p.on("vector", lambda e: e.tensor_scalar(iac[0:1, 0:1], iac[0:1, 1:2],
                                                           float(count), None, OP.mult),
                       [tr], tag + "_te")
            tb = p.on("tensor", lambda e: e.matmul(psS[:, 0:1], on1[:], iac[0:1, 0:1],
                                                   start=True, stop=True), [ts_], tag + "_tm")
            tc = p.on("vector", lambda e: e.tensor_copy(iac[:, 1:2], psS[:, 0:1]), [tb], tag + "_te")
            qd = p.ctr(tag + "_qd")
            dns = [p.ctr(tag + "_dn0"), p.ctr(tag + "_dn1")]
            for s in range(n_slab):
                done = dns[s % 2]
                w = [tc] if s < 2 else [(done, ((s - 2) // 2 + 1) * 16)]
                tok2 = p.dma(stg[:, (s % 2) * slab_cols:(s % 2 + 1) * slab_cols],
                             wT.rearrange("(s p) c -> s p c", p=128)[s],
                             waits=w, inc=tag + "_qld%d" % (s % 2))
                toks2 = {s: tok2}
                sl = stg[:, (s % 2) * slab_cols:(s % 2 + 1) * slab_cols]
                gb = stg[:, (2 + s % 2) * slab_cols:(3 + s % 2) * slab_cols]
                gbb = gb.bitcast(BF16)[:, 0:slab_cols]
                ta = p.on("vector", lambda e, sl=sl: e.tensor_scalar(
                    sl, sl, iac[:, 1:2], None, OP.mult), [toks2[s]], qd)
                p.drain("vector")
                tg = p.on("vector", lambda e, sl=sl, gb=gb: e.tensor_scalar(
                    gb, sl, 0.5, None, OP.is_gt), [ta], qd)
                tl = p.on("vector", lambda e, sl=sl: e.tensor_scalar(
                    sl, sl, -0.5, None, OP.is_lt), [tg], qd)
                p.drain("vector")
                tb2 = p.on("vector", lambda e, sl=sl, gb=gb, gbb=gbb: e.tensor_tensor(
                    gbb, gb, sl, OP.subtract), [tl], qd)
                cache_store(s, gbb, [tb2], done)
            return [(d, d.v) for d in dns]

        def store_up(s, srcb, waits, done):
            return p.dma(wup_c[:, s, :, :].rearrange("a p c -> p a c"),
                         srcb.rearrange("p (a c) -> p a c", a=NF),
                         waits=waits, inc=done)

        def store_dn(s, srcb, waits, done):
            return p.dma(wdn_c[:, s, :, :].rearrange("a p c -> p a c"),
                         srcb.rearrange("p (a c) -> p a c", a=ND),
                         waits=waits, inc=done)

        tok_wup = quant_weight("wu", wupT, ND, DF, D * DF, store_up, tok_xnb)
        tok_wdn = quant_weight("wd", wdnT, NF, D, DF * D, store_dn, tok_wup)

        upmm = p.ctr("upmm")
        gel = p.ctr("gel")
        s2mm = p.ctr("s2mm")
        s2ev = p.ctr("s2ev")
        i2d = p.ctr("i2d")
        dnmm = p.ctr("dnmm")
        fev = p.ctr("fev")
        fstos = [p.ctr("fsto0"), p.ctr("fsto1")]
        gtok = {}
        dn_ev = {}
        for hv in range(2):
            hoff = hv * 1024
            for ft in range(NF):
                g = hv * NF + ft
                wl_w = list(tok_wup)
                if g >= 2:
                    wl_w.append((gel, (g - 2 + 1) * 2))
                wl = p.dma(wring[:, (ft % 2) * ND * 128:(ft % 2 + 1) * ND * 128],
                           wup_c[ft].rearrange("b p c -> p b c"),
                           waits=wl_w, inc="uld%d" % (ft % 2))
                for ch in range(2):
                    ps = psU0 if ch == 0 else psU1
                    for dt in range(ND):
                        w = []
                        if dt == 0:
                            w = [wl, tok_xnb]
                            if g >= 1:
                                w.append((gel, (g - 1) * 2 + ch + 1))
                            if hv == 1 and ft == 0:
                                w.append((fev, 2 * ND))
                        p.on("tensor", lambda e, ft=ft, ch=ch, dt=dt, ps=ps, hoff=hoff,
                             st=(dt == 0), sp=(dt == ND - 1):
                             e.matmul(ps[:, :],
                                      wring[:, (ft % 2) * ND * 128 + dt * 128:(ft % 2) * ND * 128 + (dt + 1) * 128],
                                      xnb[:, dt * T + hoff + ch * 512: dt * T + hoff + ch * 512 + 512],
                                      start=st, stop=sp), w, upmm)
                    gw = [(upmm, (g * 2 + ch + 1) * ND)]
                    if hv == 0 and ft == 0 and ch == 0:
                        gw.extend(tok_wdn)
                    if hv == 1 and ft == 0:
                        gw.append((fev, hv * 2 * ND))
                    gtok[(hv, ft, ch)] = p.on("scalar", lambda e, ft=ft, ch=ch, ps=ps: e.activation(
                        hb[:, ft * 1024 + ch * 512: ft * 1024 + ch * 512 + 512],
                        ps[:, :], AF.Gelu), gw, gel)
            for ch in range(2):
                for ft in range(NF):
                    i = hv * 2 * NF + ch * NF + ft
                    sq_w = [gtok[(hv, ft, ch)]]
                    if ch * NF + ft >= 2:
                        sq_w.append((s2mm, i - 1))
                    elif hv == 1:
                        sq_w.append((s2mm, hv * 2 * NF))
                        sq_w.append((fstos[0], 64))
                        sq_w.append((fstos[1], 64))
                    t_q = p.on("scalar", lambda e, ft=ft, ch=ch: e.activation(
                        fsl[:, (ft % 2) * 1024:(ft % 2) * 1024 + 512],
                        hb[:, ft * 1024 + ch * 512: ft * 1024 + ch * 512 + 512], AF.Square),
                        sq_w, sqd)
                    waits = [t_q]
                    if ft == 0 and (ch > 0 or hv > 0):
                        waits.append((s2ev, hv * 2 + ch))
                    p.on("tensor", lambda e, ft=ft, st=(ft == 0), sp=(ft == NF - 1):
                         e.matmul(psS[0:1, :], onc[:],
                                  fsl[:, (ft % 2) * 1024:(ft % 2) * 1024 + 512],
                                  start=st, stop=sp), waits, s2mm)
                p.on("vector", lambda e, ch=ch: e.tensor_copy(
                    row[0:1, ch * 512: ch * 512 + 512], psS[0:1, :]),
                     [(s2mm, (hv * 2 + ch + 1) * NF)], s2ev)
            rr = lambda j: row[0:1, j * T: j * T + 1024]
            u1 = p.on("scalar", lambda e, rr=rr: e.activation(rr(1), rr(0), AF.Ln),
                      [(s2ev, hv * 2 + 2)], "ras")
            p.drain("scalar")
            u2 = p.on("scalar", lambda e, rr=rr: e.activation(rr(2), rr(1), AF.Exp, scale=0.5),
                      [u1], "ras")
            u3 = p.on("vector", lambda e, rr=rr: e.tensor_scalar(rr(1), rr(2), 1.0 / 64.0, 1e-8,
                                                                 OP.mult, OP.add), [u2], "rs")
            u4 = p.on("scalar", lambda e, rr=rr: e.activation(rr(2), rr(1), AF.Ln), [u3], "ras")
            p.drain("scalar")
            u5 = p.on("scalar", lambda e, rr=rr: e.activation(rr(1), rr(2), AF.Exp, scale=-1.0),
                      [u4], "ras")
            for ch in range(2):
                bw = [u5]
                if hv > 0 or ch > 0:
                    bw.append((i2d, hv * 2 + ch))
                bi = p.on("tensor", lambda e, ch=ch, rr=rr: e.matmul(
                    psI[:, :], on1[:], rr(1)[:, ch * 512:(ch + 1) * 512],
                    start=True, stop=True), bw, "bmm")
                p.on("vector", lambda e, ch=ch: e.tensor_copy(
                    i2sb[:, ch * 512:(ch + 1) * 512], psI[:, :]), [bi], i2d)
            for do in range(ND):
                gdo = hv * ND + do
                wl_w = list(tok_wdn)
                if gdo >= 2:
                    wl_w.append(dn_ev[gdo - 2])
                wl = p.dma(wdb[:, (do % 2) * NF * 128:(do % 2 + 1) * NF * 128],
                           wdn_c[do].rearrange("b p c -> p b c"),
                           waits=wl_w, inc="dld%d" % (do % 2))
                for ch in range(2):
                    ps = psU0 if ch == 0 else psU1
                    for ft in range(NF):
                        w = []
                        if ft == 0:
                            w = [wl, gtok[(hv, NF - 1, 1)], (gel, (hv * NF + NF) * 2)]
                            if gdo >= 1:
                                w.append((fev, gdo * 2))
                        p.on("tensor", lambda e, do=do, ft=ft, ch=ch, ps=ps, hoff=hoff,
                             st=(ft == 0), sp=(ft == NF - 1):
                             e.matmul(ps[:, :],
                                      wdb[:, (do % 2) * NF * 128 + ft * 128:(do % 2) * NF * 128 + (ft + 1) * 128],
                                      hb[:, ft * 1024 + ch * 512: ft * 1024 + ch * 512 + 512],
                                      start=st, stop=sp), w, dnmm)
                    ew = [(dnmm, (gdo * 2 + ch + 1) * NF), (i2d, hv * 2 + 2)]
                    if gdo >= 2:
                        ew.append((fstos[do % 2], ((gdo - 2) // 2 + 1) * 16))
                    dn_ev[gdo] = p.on("vector", lambda e, do=do, ch=ch, ps=ps: e.tensor_tensor(
                        fsl[:, (do % 2) * 1024 + ch * 512:(do % 2) * 1024 + ch * 512 + 512],
                        ps[:, :], i2sb[:, ch * 512:(ch + 1) * 512], OP.mult), ew, fev)
                p.dma(fT.rearrange("(do p) t -> do p t", p=128)[do][:, hoff:hoff + 1024],
                      fsl[:, (do % 2) * 1024:(do % 2) * 1024 + 1024],
                      waits=[(fev, (gdo + 1) * 2)], inc=fstos[do % 2])



def emit_fp4_quant(p, src_ap, dst_bf_ap, cols, scr_a, scr_b, scr_c, scl, rcp,
                   wait_tok, tag):
    nb = cols // 16
    v = lambda fn, w, i="qv": p.on("vector", fn, w, i)
    s = lambda fn, w, i="qsc": p.on("scalar", fn, w, i)
    t1 = v(lambda e: e.tensor_reduce(scl, src_ap.rearrange("p (b j) -> p b j", j=16),
                                     axis=AX, op=OP.max, apply_absolute_value=True),
           [wait_tok])
    p.drain("vector")
    t2 = v(lambda e: e.tensor_scalar(scl, scl, 1.0 / 6.0, 1e-12, OP.mult, OP.max), [t1])
    p.drain("vector")
    t3 = v(lambda e: e.reciprocal(rcp, scl), [t2])
    p.drain("vector")
    t3b = v(lambda e: e.tensor_scalar(scl, scl, 0.5, None, OP.mult), [t3])
    t4 = s(lambda e: e.activation(scr_a, src_ap, AF.Abs), [wait_tok])
    rcp_rep = rcp.unsqueeze(2).to_broadcast([128, nb, 16])
    t5 = v(lambda e: e.tensor_tensor(scr_a.rearrange("p (b j) -> p b j", j=16),
                                     scr_a.rearrange("p (b j) -> p b j", j=16),
                                     rcp_rep, OP.mult), [t3, t4])
    p.drain("vector")
    steps = [
        lambda e: e.tensor_scalar(scr_b, scr_a, 2.5, None, OP.is_gt),
        lambda e: e.scalar_tensor_tensor(scr_c, scr_a, 3.5, scr_b, OP.is_gt, OP.add),
        lambda e: e.scalar_tensor_tensor(scr_b, scr_a, 5.0, scr_c, OP.is_gt, OP.add),
        lambda e: e.scalar_tensor_tensor(scr_c, scr_a, 5.0, scr_b, OP.is_gt, OP.add),
        lambda e: e.tensor_scalar(scr_b, scr_c, 2.0, None, OP.mult),
        lambda e: e.scalar_tensor_tensor(scr_c, scr_a, 0.25, scr_b, OP.is_gt, OP.add),
        lambda e: e.scalar_tensor_tensor(scr_b, scr_a, 0.75, scr_c, OP.is_gt, OP.add),
        lambda e: e.scalar_tensor_tensor(scr_c, scr_a, 1.25, scr_b, OP.is_gt, OP.add),
        lambda e: e.scalar_tensor_tensor(scr_b, scr_a, 1.75, scr_c, OP.is_gt, OP.add),
    ]
    tprev = t5
    for fn in steps:
        tprev = v(fn, [tprev])
        p.drain("vector")
    t13 = s(lambda e: e.activation(scr_c, src_ap, AF.Sign), [tprev])
    t14 = v(lambda e: e.tensor_tensor(dst_bf_ap, scr_b, scr_c, OP.mult), [t13, tprev])
    p.drain("vector")
    scl_rep = scl.unsqueeze(2).to_broadcast([128, nb, 16])
    t15 = v(lambda e: e.tensor_tensor(dst_bf_ap.rearrange("p (b j) -> p b j", j=16),
                                      dst_bf_ap.rearrange("p (b j) -> p b j", j=16),
                                      scl_rep, OP.mult), [t14, t3b])
    return t15


def emit_ffn0(nc, p, pfx, B, xnT, wg, wu, wd, ident, fT, xq_c):
    DN = 2816
    NH = DN // 128
    CHUNKS = [512, 512, 512, 512, 512, 256]
    debug = False
    wg_dbg = None
    with ExitStack() as ctx:
        big = ctx.enter_context(nc.sbuf_tensor(pfx + "big", [128, 13056], F32))
        wgT = ctx.enter_context(nc.sbuf_tensor(pfx + "wgT", [128, ND * DN], BF16))
        wuT = ctx.enter_context(nc.sbuf_tensor(pfx + "wuT", [128, ND * DN], BF16))
        wdT = ctx.enter_context(nc.sbuf_tensor(pfx + "wdT", [128, NH * D], BF16))
        hq = ctx.enter_context(nc.sbuf_tensor(pfx + "hq", [128, DN], BF16))
        hqT = ctx.enter_context(nc.sbuf_tensor(pfx + "hqT", [128, NH * 128], BF16))
        gs = ctx.enter_context(nc.sbuf_tensor(pfx + "gs", [128, DN], BF16))
        xqr = ctx.enter_context(nc.sbuf_tensor(pfx + "xqr", [128, 2 * D], BF16))
        fst = ctx.enter_context(nc.sbuf_tensor(pfx + "fst", [128, 2 * 128], F32))
        idt = ctx.enter_context(nc.sbuf_tensor(pfx + "idt", [128, 128], F32))
        idb = ctx.enter_context(nc.sbuf_tensor(pfx + "idb", [128, 128], BF16))
        psA = ctx.enter_context(nc.psum_tensor(pfx + "psA", [128, 1024], BF16))
        psB = ctx.enter_context(nc.psum_tensor(pfx + "psB", [128, 1024], BF16))
        psG = ctx.enter_context(nc.psum_tensor(pfx + "psG", [128, 1536], F32))
        psU = ctx.enter_context(nc.psum_tensor(pfx + "psU", [128, 1536], F32))
        psAb = psA[:, 0:128]
        psBb = psB[:, 0:128]
        h_f = big[:, 0:DN]
        h_a = big[:, DN:2 * DN]
        h_b = big[:, 2 * DN:3 * DN]
        h_i = big[:, 3 * DN:4 * DN]
        h_scl = big[:, 4 * DN:4 * DN + DN // 16]
        h_rcp = big[:, 4 * DN + DN // 16:4 * DN + 2 * (DN // 16)]

        p.dma(idt[:], ident, waits=B)
        alldma = (p.ctr("dma"), p.ctr("dma").v)
        tok_idb = p.on("vector", lambda e: e.tensor_copy(idb[:], idt[:]), [alldma], "misc")

        xa_mt = p.ctr("xa_mt")
        xa_ev = p.ctr("xa_ev")
        xq_st = [p.ctr("xq_st0"), p.ctr("xq_st1")]
        qprev = alldma
        qhist = {}
        for k in range(NT):
            qprev2 = qhist.get(k - 2)
            xtok = big[:, (k % 2) * D:(k % 2) * D + D]
            xqk = big[:, 2 * D + (k % 2) * D:2 * D + (k % 2) * D + D]
            xqkb = xqk.bitcast(BF16)[:, 0:D]
            sa = big[:, 4 * D:5 * D]
            sb = big[:, 5 * D:6 * D]
            si = big[:, 6 * D:7 * D]
            sc = big[:, 7 * D:7 * D + 64]
            rc = big[:, 7 * D + 64:7 * D + 128]
            ldt = []
            for dt in range(ND):
                i = k * ND + dt
                w = [(xa_mt, i - 3)] if i >= 4 else list(B)
                ldt.append(p.dma(big[:, 8 * D + ((i % 4) * 128):8 * D + ((i % 4) * 128) + 128],
                                 xnT.rearrange("(dt p) t -> dt p t", p=128)[dt][:, k * 128:(k + 1) * 128],
                                 waits=w, inc="xl%d" % (i % 4)))
            for dt in range(ND):
                i = k * ND + dt
                bank = psG if i % 2 == 0 else psU
                w2 = [ldt[dt]]
                if i >= 2:
                    w2.append((xa_ev, i - 1))
                tk = p.on("tensor", lambda e, b=bank, i=i:
                          e.transpose(b[:, 0:128], big[:, 8 * D + ((i % 4) * 128):8 * D + ((i % 4) * 128) + 128], idt[:]),
                          w2, xa_mt)
                ew = [tk]
                if dt == 0 and k >= 2:
                    ew.append(qprev2)
                p.on("vector", lambda e, b=bank, xtok=xtok, dt=dt: e.tensor_copy(
                    xtok[:, dt * 128:(dt + 1) * 128], b[:, 0:128]), ew, xa_ev)
            qt = emit_fp4_quant(p, xtok, xqkb, D, sa, sb, si, sc, rc,
                                (xa_ev, (k + 1) * ND), "x%d" % k)
            qprev = qt
            qhist[k] = qt
            for dt in range(ND):
                i = k * ND + dt
                bank = psAb if i % 2 == 0 else psBb
                w2 = [qt, tok_idb] if dt == 0 else []
                if k * ND + dt >= 2:
                    w2.append((p.ctr("xb_ev"), p.ctr("xb_ev").v - 1))
                tk = p.on("tensor", lambda e, b=bank, xqkb=xqkb, dt=dt:
                          e.transpose(b, xqkb[:, dt * 128:(dt + 1) * 128], idb[:]), w2, "xb_mt")
                p.on("vector", lambda e, b=bank, k=k, dt=dt: e.tensor_copy(
                    xqr[:, (k % 2) * D + dt * 128:(k % 2) * D + (dt + 1) * 128], b),
                    [tk] + ([(xq_st[k % 2], 16 * (k // 2))] if (dt == 0 and k >= 2) else []),
                    "xb_ev")
            p.dma(xq_c[k], xqr[:, (k % 2) * D:(k % 2) * D + D],
                  waits=[(p.ctr("xb_ev"), (k + 1) * ND)], inc=xq_st[k % 2])

        tok_xq_all = [(c, c.v) for c in xq_st]

        def quant_w(tag, wsrc, n_slab, slab_cols, dstfn, start_tok):
            dn = p.ctr(tag + "_dn")
            prev = start_tok
            for s_i in range(n_slab):
                sa = big[:, 0:slab_cols]
                sb = big[:, slab_cols:2 * slab_cols]
                si = big[:, 2 * slab_cols:3 * slab_cols]
                sf = big[:, 3 * slab_cols:4 * slab_cols]
                sc = big[:, 4 * slab_cols:4 * slab_cols + slab_cols // 16]
                rc = big[:, 4 * slab_cols + slab_cols // 16:4 * slab_cols + 2 * (slab_cols // 16)]
                qb = big[:, 4 * slab_cols + 256:4 * slab_cols + 256 + (slab_cols + 1) // 2]
                qbb = qb.bitcast(BF16)[:, 0:slab_cols]
                ldt = p.dma(sf, wsrc.rearrange("(s p) c -> s p c", p=128)[s_i],
                            waits=[prev], inc=tag + "_ld")
                qt = emit_fp4_quant(p, sf, qbb, slab_cols, sa, sb, si, sc, rc,
                                    ldt, tag + "%d" % s_i)
                ntile = slab_cols // 128
                for j in range(ntile):
                    bank = psAb if j % 2 == 0 else psBb
                    w2 = [qt] if j == 0 else []
                    if j >= 2:
                        w2.append((dn, dn.v - 1))
                    tk = p.on("tensor", lambda e, b=bank, s2=qbb[:, j * 128:(j + 1) * 128]:
                              e.transpose(b, s2, idb[:]), w2, tag + "_mt")
                    prev = p.on("vector", lambda e, b=bank, d2=dstfn(s_i, j):
                                e.tensor_copy(d2, b), [tk], dn)
            return prev

        tok_wg = quant_w("wg", wg, NH, D,
                         lambda s_i, j: wgT[:, j * DN + s_i * 128:j * DN + (s_i + 1) * 128],
                         qprev)
        tok_wu = quant_w("wu", wu, NH, D,
                         lambda s_i, j: wuT[:, j * DN + s_i * 128:j * DN + (s_i + 1) * 128],
                         tok_wg)
        tok_wd = quant_w("wd", wd, ND, DN,
                         lambda s_i, j: wdT[:, j * D + s_i * 128:j * D + (s_i + 1) * 128],
                         tok_wu)

        gmm = p.ctr("gmm")
        gev = p.ctr("gev")
        hmt = p.ctr("hmt")
        hev = p.ctr("hev")
        dmm = p.ctr("dmm")
        devq = p.ctr("devq")
        dtr = p.ctr("dtr")
        dev2 = p.ctr("dev2")
        fso = [p.ctr("fso0"), p.ctr("fso1")]
        xlr = [p.ctr("xlr0"), p.ctr("xlr1")]
        coff = [0, 512, 1024, 1536, 2048, 2560]
        dstage = gs[:].bitcast(F32)
        last_qh = tok_wd
        last_devac = None
        last_hevac = None
        xq_tok = {}
        for tt in range(NT):
            w = list(tok_xq_all) if tt < 2 else [xq_tok[tt - 2][1]]
            xq_tok[tt] = (None, None)
            lt = p.dma(xqr[:, (tt % 2) * D:(tt % 2) * D + D], xq_c[tt],
                       waits=w + ([last_qh] if tt < 2 else []), inc=xlr[tt % 2])
            ev_toks = []
            gmm_last = None
            for grp in range(2):
                chs = [0, 1, 2] if grp == 0 else [3, 4, 5]
                for dt in range(ND):
                    for ci, c in enumerate(chs):
                        w = []
                        if dt == 0 and ci == 0:
                            w = [lt]
                            if grp == 0:
                                w.append(tok_wd if tt == 0 else last_devac)
                            else:
                                w.extend(ev_toks)
                        p.on("tensor", lambda e, tt=tt, dt=dt, c=c,
                             st=(dt == 0), sp=(dt == ND - 1):
                             e.matmul(psG[:, (c % 3) * 512:(c % 3) * 512 + CHUNKS[c]],
                                      xqr[:, (tt % 2) * D + dt * 128:(tt % 2) * D + (dt + 1) * 128],
                                      wgT[:, dt * DN + coff[c]: dt * DN + coff[c] + CHUNKS[c]],
                                      start=st, stop=sp), w, gmm)
                        gmm_last = p.on("tensor", lambda e, tt=tt, dt=dt, c=c,
                                        st=(dt == 0), sp=(dt == ND - 1):
                                        e.matmul(psU[:, (c % 3) * 512:(c % 3) * 512 + CHUNKS[c]],
                                                 xqr[:, (tt % 2) * D + dt * 128:(tt % 2) * D + (dt + 1) * 128],
                                                 wuT[:, dt * DN + coff[c]: dt * DN + coff[c] + CHUNKS[c]],
                                                 start=st, stop=sp), [], gmm)
                base = gmm.v
                for c in chs:
                    off = (c % 3) * 512
                    sw = [(gmm, base)]
                    if c == chs[0] and tt > 0:
                        sw.append(last_qh)
                    tsil = p.on("scalar", lambda e, c=c, off=off: e.activation(
                        gs[:, coff[c]:coff[c] + CHUNKS[c]], psG[:, off:off + CHUNKS[c]],
                        AF.Silu), sw, "gsil")
                    ev_toks.append(p.on("vector", lambda e, c=c, off=off: e.tensor_tensor(
                        h_f[:, coff[c]:coff[c] + CHUNKS[c]], psU[:, off:off + CHUNKS[c]],
                        gs[:, coff[c]:coff[c] + CHUNKS[c]], OP.mult), [tsil], gev))
            qh = emit_fp4_quant(p, h_f, hq[:], DN, h_a, h_b, h_i, h_scl, h_rcp,
                                ev_toks[-1], "h%d" % tt)
            xq_tok[tt] = (None, qh)
            last_qh = qh
            htoks = []
            for j in range(NH):
                bank = psAb if j % 2 == 0 else psBb
                w2 = [qh] if j == 0 else []
                if j >= 2:
                    w2.append((hev, hev.v - 1))
                if tt > 0 and j < 2:
                    w2.append(last_hevac)
                tk = p.on("tensor", lambda e, b=bank, s2=hq[:, j * 128:(j + 1) * 128]:
                          e.transpose(b, s2, idb[:]), w2, hmt)
                htoks.append(p.on("vector", lambda e, b=bank, d2=hqT[:, j * 128:(j + 1) * 128]:
                                  e.tensor_copy(d2, b), [tk], hev))
            last_hevac = htoks[-1]
            psD = psG
            for j in range(NH):
                w = list(ev_toks) + [htoks[-1]] if j == 0 else []
                for half in range(2):
                    p.on("tensor", lambda e, j=j, half=half, st=(j == 0), sp=(j == NH - 1):
                         e.matmul(psD[:, half * 512: half * 512 + 512],
                                  hqT[:, j * 128:(j + 1) * 128],
                                  wdT[:, j * D + half * 512: j * D + half * 512 + 512],
                                  start=st, stop=sp), w if half == 0 else [], dmm)
            base_d = dmm.v
            de = p.on("vector", lambda e: e.tensor_copy(dstage[:, 0:1024], psD[:, 0:1024]),
                      [(dmm, base_d)] + list(ev_toks), devq)
            for do in range(ND):
                bank = psU[:, 0:128] if do % 2 == 0 else psU[:, 512:640]
                w2 = [de]
                if do >= 2:
                    w2.append((dev2, dev2.v - 1))
                tk = p.on("tensor", lambda e, b=bank, do=do:
                          e.transpose(b, dstage[:, do * 128:(do + 1) * 128], idt[:]),
                          w2, dtr)
                e2 = p.on("vector", lambda e, b=bank, do=do: e.tensor_copy(
                    fst[:, (do % 2) * 128:(do % 2) * 128 + 128], b),
                    [tk] + ([(fso[do % 2], (((tt * ND + do) - 2) // 2 + 1) * 16)]
                            if tt * ND + do >= 2 else []), dev2)
                p.dma(fT.rearrange("(do p) t -> do p t", p=128)[do][:, tt * 128:(tt + 1) * 128],
                      fst[:, (do % 2) * 128:(do % 2) * 128 + 128],
                      waits=[e2], inc=fso[do % 2])
                last_devac = e2



def emit_attn0(nc, p, pfx, B, xnT, Wq, Wk, Wv, Wo, Wdkv, Wuk, Wuv, decay,
               ones1x_d, aT):
    NHD = 16
    SSC = float(1.0 / (np.sqrt(np.float64(np.float32(1024.0))) + 1e-08))
    with ExitStack() as ctx:
        xnb = ctx.enter_context(nc.sbuf_tensor(pfx + "xnb", [128, ND * T], BF16))
        QT = ctx.enter_context(nc.sbuf_tensor(pfx + "QT", [128, ND * T], BF16))
        KrT = ctx.enter_context(nc.sbuf_tensor(pfx + "KrT", [128, ND * T], BF16))
        vtok = ctx.enter_context(nc.sbuf_tensor(pfx + "vtok", [128, NT * NHD * 65], BF16))
        latT = ctx.enter_context(nc.sbuf_tensor(pfx + "latT", [128, 2 * T], BF16))
        wbf = ctx.enter_context(nc.sbuf_tensor(pfx + "wbf", [128, ND * D], BF16))
        wst = ctx.enter_context(nc.sbuf_tensor(pfx + "wst", [128, 2 * D], F32))
        expt = ctx.enter_context(nc.sbuf_tensor(pfx + "expt", [128, NT * 512], BF16))
        decb = ctx.enter_context(nc.sbuf_tensor(pfx + "decb", [128, NT * 512], BF16))
        sdb = ctx.enter_context(nc.sbuf_tensor(pfx + "sdb", [128, 2 * 512], BF16))
        aout = ctx.enter_context(nc.sbuf_tensor(pfx + "aout", [128, 2 * 512], F32))
        onesr = ctx.enter_context(nc.sbuf_tensor(pfx + "onesr", [1, 128], F32))
        psS0 = ctx.enter_context(nc.psum_tensor(pfx + "psS0", [128, 512], F32))
        psS1 = ctx.enter_context(nc.psum_tensor(pfx + "psS1", [128, 512], F32))
        psAV = ctx.enter_context(nc.psum_tensor(pfx + "psAV", [128, 512], F32))
        psM = ctx.enter_context(nc.psum_tensor(pfx + "psM", [128, 512], F32))
        psP = ctx.enter_context(nc.psum_tensor(pfx + "psP", [128, 512], F32))
        psP2 = ctx.enter_context(nc.psum_tensor(pfx + "psP2", [128, 512], F32))
        aoT = xnb
        latf = latT[:].bitcast(F32)
        rden = latf[0:1, 0:1024]
        ones1x = None

        stage = QT[:].bitcast(F32)
        stage2 = KrT[:].bitcast(F32)
        p.dma(stage[:, 0:ND * T // 2].rearrange("p (dt t) -> p dt t", dt=ND // 2),
              xnT.rearrange("(dt p) t -> p dt t", p=128)[:, 0:ND // 2, :], waits=B)
        p.dma(stage2[:, 0:ND * T // 2].rearrange("p (dt t) -> p dt t", dt=ND // 2),
              xnT.rearrange("(dt p) t -> p dt t", p=128)[:, ND // 2:, :], waits=B)
        p.dma(onesr[:], ones1x_d, waits=B)
        alldma = (p.ctr("dma"), p.ctr("dma").v)
        cast_dn = p.ctr("cast")
        for dt in range(ND):
            st_ap = (stage if dt < 4 else stage2)[:, (dt % 4) * T:(dt % 4) * T + T]
            p.on("vector", lambda e, dt=dt, st_ap=st_ap: e.tensor_copy(
                xnb[:, dt * T:(dt + 1) * T], st_ap), [alldma], cast_dn)
        tok_xnb = (cast_dn, ND)

        wld = [p.ctr("wl0"), p.ctr("wl1")]
        wcast = p.ctr("wcast")

        def cast_w(wsrc, nslab, width, first_w):
            toks = []
            for s_i in range(nslab):
                w = [first_w] if s_i < 2 else [(wcast, wcast.v - 1)]
                lt = p.dma(wst[:, (s_i % 2) * D:(s_i % 2) * D + width],
                           wsrc.rearrange("(s p) c -> s p c", p=128)[s_i],
                           waits=w, inc=wld[s_i % 2])
                toks.append(p.on("vector", lambda e, s_i=s_i, width=width: e.tensor_copy(
                    wbf[:, s_i * width:(s_i + 1) * width],
                    wst[:, (s_i % 2) * D:(s_i % 2) * D + width]), [lt], wcast))
            return toks[-1]

        def cast_w_sum(wa, wb, nslab, first_w):
            toks = []
            for s_i in range(nslab):
                w = [first_w] if s_i == 0 else [toks[-1]]
                la = p.dma(wst[:, 0:D], wa.rearrange("(s p) c -> s p c", p=128)[s_i],
                           waits=w, inc=wld[0])
                lb = p.dma(wst[:, D:2 * D], wb.rearrange("(s p) c -> s p c", p=128)[s_i],
                           waits=w, inc=wld[1])
                toks.append(p.on("vector", lambda e, s_i=s_i: e.tensor_tensor(
                    wbf[:, s_i * D:(s_i + 1) * D], wst[:, 0:D], wst[:, D:2 * D], OP.add),
                    [la, lb], wcast))
            return toks[-1]

        def project(src_bf, dst_fn, wtok, tag, nk=ND, nmt=ND, width=D, store_fn=None,
                    egate_fn=None):
            mmc = p.ctr(tag + "m")
            evc = p.ctr(tag + "e")
            for mt in range(nmt):
                for ch in range(4):
                    i = mt * 4 + ch
                    ps = psP if i % 2 == 0 else psP2
                    for dt in range(nk):
                        w = [wtok] if (i == 0 and dt == 0) else []
                        if dt == 0 and i >= 2:
                            w.append((evc, i - 1))
                        p.on("tensor", lambda e, mt=mt, ch=ch, dt=dt, ps=ps,
                             st=(dt == 0), sp=(dt == nk - 1):
                             e.matmul(ps[:, :],
                                      wbf[:, dt * width + mt * 128:dt * width + (mt + 1) * 128],
                                      src_bf[:, dt * T + ch * 512: dt * T + ch * 512 + 512],
                                      start=st, stop=sp), w, mmc)
                    ew = [(mmc, (i + 1) * nk)]
                    if egate_fn is not None:
                        g = egate_fn(i)
                        if g is not None:
                            ew.append(g)
                    tok_e = p.on("scalar", lambda e, mt=mt, ch=ch, ps=ps: e.activation(
                        dst_fn(mt, ch), ps[:, :], AF.Copy),
                        ew, evc)
                    if store_fn is not None:
                        store_fn(mt, ch, tok_e)
            return (evc, nmt * 4)

        twq = cast_w(Wq, ND, D, tok_xnb)
        tok_q = project(xnb, lambda mt, ch: QT[:, mt * T + ch * 512: mt * T + ch * 512 + 512],
                        twq, "q")
        twkv = cast_w_sum(Wk, Wv, ND, tok_q)
        kvs = vtok[:].rearrange("p c -> p c")[:, 0:ND * T]
        tok_kv = project(xnb, lambda mt, ch: kvs[:, mt * T + ch * 512: mt * T + ch * 512 + 512],
                         twkv, "kv")
        twd = cast_w(Wdkv, ND, 256, tok_kv)
        tok_lat = project(kvs, lambda mt, ch: latT[:, mt * T + ch * 512: mt * T + ch * 512 + 512],
                          twd, "lat", nk=ND, nmt=2, width=256)
        twuk = cast_w(Wuk, 2, D, tok_lat)
        tok_kr = project(latT, lambda mt, ch: KrT[:, mt * T + ch * 512: mt * T + ch * 512 + 512],
                         twuk, "kr", nk=2, nmt=ND, width=D)
        twuv = cast_w(Wuv, 2, D, tok_kr)
        vview = vtok[:].rearrange("p (tt h c) -> p tt h c", tt=NT, h=NHD, c=65)
        vmm = p.ctr("vmm")
        vev = p.ctr("vev")
        for tt in range(NT):
            for ch in range(2):
                i = tt * 2 + ch
                ps = psP if i % 2 == 0 else psP2
                for ks in range(2):
                    w = [twuv, tok_kr] if (i == 0 and ks == 0) else []
                    if ks == 0 and i >= 2:
                        w.append((vev, i - 1))
                    p.on("tensor", lambda e, tt=tt, ch=ch, ks=ks, ps=ps,
                         st=(ks == 0), sp=(ks == 1):
                         e.matmul(ps[:, :],
                                  latT[:, ks * T + tt * 128:ks * T + (tt + 1) * 128],
                                  wbf[:, ks * D + ch * 512: ks * D + ch * 512 + 512],
                                  start=st, stop=sp), w, vmm)
                p.on("scalar", lambda e, tt=tt, ch=ch, ps=ps: e.activation(
                    vview[:, tt, ch * 8:(ch + 1) * 8, 0:64],
                    ps[:, :].rearrange("p (h c) -> p h c", c=64), AF.Copy),
                    [(vmm, (i + 1) * 2)], vev)
        tok_v = (vev, NT * 2)
        tok_ones = p.on("vector", lambda e: e.memset(
            vview[:, :, :, 64:65], 1.0), [tok_v], "vone")

        dld = p.ctr("dld")
        smm = p.ctr("smm")
        sdv = p.ctr("sdv")
        sev = p.ctr("sev")
        avm = p.ctr("avm")
        dcp = p.ctr("dcp")
        dln = p.ctr("dln")
        dbc = p.ctr("dbc")
        aev = p.ctr("aev")
        anm = p.ctr("anm")
        hq_done = {}
        for qc in range(4):
            wdec = [alldma] if qc == 0 else [hq_done[(qc - 1, NHD - 1)]]
            tdec = p.dma(decb[:].rearrange("p (kt c) -> p kt c", kt=NT),
                         decay[12 - 4 * qc:28 - 4 * qc], waits=wdec, inc=dld)
            for h in range(NHD):
                base = (h % 2) * 64
                prevhq = None
                if (qc, h) != (0, 0):
                    prevhq = hq_done[(qc, h - 1)] if h > 0 else hq_done[(qc - 1, NHD - 1)]
                se_toks = []
                for kt in range(NT):
                    ps = psS0 if kt % 2 == 0 else psS1
                    w = [tok_q, tok_kr, tdec]
                    if prevhq is not None:
                        w.append(prevhq)
                    if kt >= 2:
                        w = [(sev, sev.v - 1)]
                    tsm = p.on("tensor", lambda e, h=h, qc=qc, kt=kt, ps=ps, base=base: e.matmul(
                        ps[:, :],
                        KrT[:, (h // 2) * T + kt * 128:(h // 2) * T + (kt + 1) * 128][base:base + 64, :],
                        QT[:, (h // 2) * T + qc * 512:(h // 2) * T + qc * 512 + 512][base:base + 64, :],
                        start=True, stop=True, tile_position=(base, 0)), w, smm)
                    wv = [tsm]
                    if kt >= 2:
                        wv.append(se_toks[kt - 2])
                    tdv = p.on("vector", lambda e, kt=kt, ps=ps: e.tensor_tensor(
                        sdb[:, (kt % 2) * 512:(kt % 2) * 512 + 512], ps[:, :],
                        decb[:, kt * 512:(kt + 1) * 512], OP.mult),
                        wv, sdv)
                    p.drain("vector")
                    tse = p.on("scalar", lambda e, kt=kt: e.activation(
                        expt[:, kt * 512:(kt + 1) * 512],
                        sdb[:, (kt % 2) * 512:(kt % 2) * 512 + 512], AF.Exp,
                        scale=SSC), [tdv], sev)
                    se_toks.append(tse)
                for kt in range(NT):
                    w = [se_toks[kt], tok_ones]
                    if kt == 0 and prevhq is not None:
                        w.append((anm, anm.v))
                    p.on("tensor", lambda e, kt=kt, h=h, st=(kt == 0), sp=(kt == NT - 1):
                         e.matmul(psAV[0:65, :],
                                  vtok[:, kt * NHD * 65 + h * 65: kt * NHD * 65 + (h + 1) * 65],
                                  expt[:, kt * 512:(kt + 1) * 512],
                                  start=st, stop=sp), w, avm)
                t1 = p.on("vector", lambda e: e.tensor_copy(rden[0:1, 0:512], psAV[64:65, :]),
                          [(avm, avm.v), (tok_v if (qc, h) == (0, 0) else (avm, avm.v))], dcp)
                p.drain("vector")
                t2 = p.on("scalar", lambda e: e.activation(rden[0:1, 512:1024], rden[0:1, 0:512],
                                                           AF.Ln), [t1], dln)
                p.drain("scalar")
                t3 = p.on("scalar", lambda e: e.activation(rden[0:1, 0:512], rden[0:1, 512:1024],
                                                           AF.Exp, scale=-1.0), [t2], dln)
                t4 = p.on("tensor", lambda e: e.matmul(psM[0:64, :], onesr[0:1, 0:64],
                                                       rden[0:1, 0:512],
                                                       start=True, stop=True), [t3], dbc)
                t5 = p.on("scalar", lambda e: e.activation(aout[0:64, 0:512], psAV[0:64, :],
                                                           AF.Copy), [t3], aev)
                hq_done[(qc, h)] = p.on("vector", lambda e, h=h, qc=qc, base=base: e.tensor_tensor(
                    aoT[:, (h // 2) * T + qc * 512:(h // 2) * T + qc * 512 + 512][base:base + 64, :],
                    aout[0:64, 0:512], psM[0:64, :], OP.mult), [t4, t5], anm)

        twO = cast_w(Wo, ND, D, hq_done[(3, NHD - 1)])
        sto = [p.ctr("so0"), p.ctr("so1")]

        def st_fn(mt, ch, tok_e):
            i = mt * 4 + ch
            p.dma(aT.rearrange("(mt p) t -> mt p t", p=128)[mt][:, ch * 512:(ch + 1) * 512],
                  aout[:, (i % 2) * 512:(i % 2) * 512 + 512], waits=[tok_e], inc=sto[i % 2])

        def dstf(mt, ch):
            i = mt * 4 + ch
            return aout[:, (i % 2) * 512:(i % 2) * 512 + 512]

        tok_o = project(aoT, dstf, twO, "o", store_fn=st_fn,
                        egate_fn=lambda i: (sto[i % 2], ((i - 2) // 2 + 1) * 16) if i >= 2 else None)



def emit_attn1(nc, p, pfx, B, xnT, WinT, WrT, WkvT, WbT, bbt, WqT, WoT,
               blk4, sel4, ident, aT, knD, qD, r4D, uwD, smD):
    NS = 32
    CH = 32
    NCH = T // CH
    with ExitStack() as ctx:
        xnb = ctx.enter_context(nc.sbuf_tensor(pfx + "xnb", [128, ND * T], BF16))
        xpb = ctx.enter_context(nc.sbuf_tensor(pfx + "xpb", [128, ND * T], BF16))
        wbf = ctx.enter_context(nc.sbuf_tensor(pfx + "wbf", [128, ND * D], BF16))
        wst = ctx.enter_context(nc.sbuf_tensor(pfx + "wst", [128, 2 * D], F32))
        kvf = ctx.enter_context(nc.sbuf_tensor(pfx + "kvf", [128, 2 * T], F32))
        betab = ctx.enter_context(nc.sbuf_tensor(pfx + "betab", [128, T], F32))
        qf = ctx.enter_context(nc.sbuf_tensor(pfx + "qf", [32, T], F32))
        lgt = ctx.enter_context(nc.sbuf_tensor(pfx + "lgt", [4, T], F32))
        rn4 = ctx.enter_context(nc.sbuf_tensor(pfx + "rn4", [4, T], F32))
        rsc = ctx.enter_context(nc.sbuf_tensor(pfx + "rsc", [4, T], F32))
        u4 = ctx.enter_context(nc.sbuf_tensor(pfx + "u4", [4, T], F32))
        s4 = ctx.enter_context(nc.sbuf_tensor(pfx + "s4", [4, T], F32))
        rnb = ctx.enter_context(nc.sbuf_tensor(pfx + "rnb", [128, T], F32))
        sqs = ctx.enter_context(nc.sbuf_tensor(pfx + "sqs", [128, 512], F32))
        knr = ctx.enter_context(nc.sbuf_tensor(pfx + "knr", [128, 2 * CH * 33], F32))
        qr = ctx.enter_context(nc.sbuf_tensor(pfx + "qr", [128, 2 * CH * NS], F32))
        tok = ctx.enter_context(nc.sbuf_tensor(pfx + "tok", [128, 64], F32))
        stg = ctx.enter_context(nc.sbuf_tensor(pfx + "stg", [128, 320], F32))
        cellb = ctx.enter_context(nc.sbuf_tensor(pfx + "cellb", [32, T], BF16))
        aor = ctx.enter_context(nc.sbuf_tensor(pfx + "aor", [128, 2 * 512], F32))
        idt = ctx.enter_context(nc.sbuf_tensor(pfx + "idt", [128, 128], F32))
        blkb = ctx.enter_context(nc.sbuf_tensor(pfx + "blkb", [128, 4], F32))
        selb = ctx.enter_context(nc.sbuf_tensor(pfx + "selb", [128, 32], F32))
        bbs = ctx.enter_context(nc.sbuf_tensor(pfx + "bbs", [128, 1], F32))
        Sst = ctx.enter_context(nc.sbuf_tensor(pfx + "Sst", [128, 33], F32))
        scr = ctx.enter_context(nc.sbuf_tensor(pfx + "scr", [128, 33], F32))
        scr2 = ctx.enter_context(nc.sbuf_tensor(pfx + "scr2", [128, NS], F32))
        z0t = ctx.enter_context(nc.sbuf_tensor(pfx + "z0t", [128, NS], F32))
        spt = ctx.enter_context(nc.sbuf_tensor(pfx + "spt", [128, NS], F32))
        zbt = ctx.enter_context(nc.sbuf_tensor(pfx + "zbt", [128, NS], F32))
        Ut = ctx.enter_context(nc.sbuf_tensor(pfx + "Ut", [128, NS], F32))
        nsum = ctx.enter_context(nc.sbuf_tensor(pfx + "nsum", [128, 4], F32))
        psP = ctx.enter_context(nc.psum_tensor(pfx + "psP", [128, 512], F32))
        psP2 = ctx.enter_context(nc.psum_tensor(pfx + "psP2", [128, 512], F32))
        psZ = ctx.enter_context(nc.psum_tensor(pfx + "psZ", [128, 512], F32))
        psB = ctx.enter_context(nc.psum_tensor(pfx + "psB", [128, 512], F32))
        xnbf = xnb[:].bitcast(F32)
        uwb = lambda: xnbf[:, 0:T]
        smb = lambda: xnbf[:, T:2 * T]
        wmb = lambda: xnbf[:, 2 * T:3 * T]
        outsq = lambda: xnbf[:, 3 * T:4 * T]

        p.dma(idt[:], ident, waits=B)
        p.dma(blkb[:], blk4, waits=B)
        p.dma(selb[:], sel4, waits=B)
        p.dma(bbs[:], bbt, waits=B)
        alldma = (p.ctr("dma"), p.ctr("dma").v)
        cast_dn = p.ctr("cast")
        xst = [p.ctr("xst0"), p.ctr("xst1")]
        for g in range(4):
            w = list(B) if g == 0 else [(cast_dn, 2 * g)]
            lt = p.dma(kvf[:, 0:2 * T].rearrange("p (a t) -> p a t", a=2),
                       xnT.rearrange("(dt p) t -> p dt t", p=128)[:, 2 * g:2 * g + 2, :],
                       waits=w, inc=xst[g % 2])
            for j in range(2):
                p.on("vector", lambda e, g=g, j=j: e.tensor_copy(
                    xnb[:, (2 * g + j) * T:(2 * g + j + 1) * T],
                    kvf[:, j * T:(j + 1) * T]), [lt], cast_dn)
        tok_xnb = (cast_dn, ND)

        wld = [p.ctr("wl0"), p.ctr("wl1")]
        wcast = p.ctr("wcast")

        def cast_w(wsrc, nslab, width, first_w):
            toks = []
            for s_i in range(nslab):
                w = [first_w] if s_i < 2 else [(wcast, wcast.v - 1)]
                lt = p.dma(wst[:, (s_i % 2) * D:(s_i % 2) * D + width],
                           wsrc.rearrange("(s p) c -> s p c", p=128)[s_i],
                           waits=w, inc=wld[s_i % 2])
                toks.append(p.on("vector", lambda e, s_i=s_i, width=width: e.tensor_copy(
                    wbf[:, s_i * width:(s_i + 1) * width],
                    wst[:, (s_i % 2) * D:(s_i % 2) * D + width]), [lt], wcast))
            return toks[-1]

        def project(src_bf, dst_fn, wtok, tag, nk=ND, nmt=ND, width=D, act=None,
                    bias=None, mwidth=128):
            mmc = p.ctr(tag + "m")
            evc = p.ctr(tag + "e")
            for mt in range(nmt):
                for ch in range(4):
                    i = mt * 4 + ch
                    ps = psP if i % 2 == 0 else psP2
                    for dt in range(nk):
                        w = [wtok] if (i == 0 and dt == 0) else []
                        if dt == 0 and i >= 2:
                            w.append((evc, i - 1))
                        p.on("tensor", lambda e, mt=mt, ch=ch, dt=dt, ps=ps,
                             st=(dt == 0), sp=(dt == nk - 1):
                             e.matmul(ps[0:mwidth, :],
                                      wbf[:, dt * width + mt * mwidth:dt * width + (mt + 1) * mwidth],
                                      src_bf[:, dt * T + ch * 512: dt * T + ch * 512 + 512],
                                      start=st, stop=sp), w, mmc)
                    if bias is not None:
                        p.on("scalar", lambda e, mt=mt, ch=ch, ps=ps: e.activation(
                            dst_fn(mt, ch), ps[0:mwidth, :], act, bias=bias[:, 0:1]),
                            [(mmc, (i + 1) * nk), alldma], evc)
                    else:
                        p.on("scalar", lambda e, mt=mt, ch=ch, ps=ps: e.activation(
                            dst_fn(mt, ch), ps[0:mwidth, :],
                            act if act is not None else AF.Copy),
                            [(mmc, (i + 1) * nk)], evc)
            return (evc, nmt * 4)

        twin = cast_w(WinT, ND, D, tok_xnb)
        tok_xp = project(xnb, lambda mt, ch: xpb[:, mt * T + ch * 512: mt * T + ch * 512 + 512],
                         twin, "xp", act=AF.Silu)
        twb = cast_w(WbT, ND, 128, tok_xp)
        tok_b = project(xpb, lambda mt, ch: betab[:, ch * 512:(ch + 1) * 512],
                        twb, "bt", nmt=1, width=128, act=AF.Sigmoid, bias=bbs)
        negb = p.ctr("negb")
        for chn in range(4):
            p.on("vector", lambda e, chn=chn: e.tensor_scalar(
                betab[:, chn * 512:(chn + 1) * 512],
                betab[:, chn * 512:(chn + 1) * 512], -1.0, None, OP.mult),
                [tok_b], negb)
        tok_bn = (negb, 4)
        twkv = cast_w(WkvT, ND, 256, tok_b)
        tok_kv = project(xpb, lambda mt, ch: kvf[:, mt * T + ch * 512: mt * T + ch * 512 + 512],
                         twkv, "kv", nmt=2, width=256)
        twr = cast_w(WrT, ND, 4, tok_kv)
        tok_r = project(xpb, lambda mt, ch: lgt[0:4, ch * 512:(ch + 1) * 512],
                        twr, "rt", nmt=1, width=4, mwidth=4)
        twq = cast_w(WqT, ND, 32, tok_r)
        tok_qp = project(xpb, lambda mt, ch: qf[0:32, ch * 512:(ch + 1) * 512],
                         twq, "qp", nmt=1, width=32, mwidth=32)

        kq = p.ctr("kq")
        knm = p.ctr("knm")
        kne = p.ctr("kne")
        for ch in range(4):
            tsq = p.on("scalar", lambda e, ch=ch: e.activation(
                sqs[:, 0:512], kvf[:, ch * 512:(ch + 1) * 512], AF.Square),
                [tok_kv] if ch == 0 else [(knm, ch)], kq)
            tm = p.on("tensor", lambda e, ch=ch: e.matmul(
                psB[0:4, :], blkb[:], sqs[:, 0:512],
                start=True, stop=True), [tsq, alldma] + ([(kne, ch)] if ch else []), knm)
            p.on("vector", lambda e, ch=ch: e.tensor_copy(
                rn4[0:4, ch * 512:(ch + 1) * 512], psB[0:4, :]), [tm], kne)
        n1 = p.on("scalar", lambda e: e.activation(rsc[0:4, :], rn4[0:4, :], AF.Ln),
                  [(kne, 4)], "nrs")
        p.drain("scalar")
        n2 = p.on("scalar", lambda e: e.activation(rn4[0:4, :], rsc[0:4, :], AF.Exp, scale=0.5),
                  [n1], "nrs")
        n3 = p.on("vector", lambda e: e.tensor_scalar(rn4[0:4, :], rn4[0:4, :], 1e-6, None,
                                                      OP.add), [n2], "nrv")
        n4 = p.on("scalar", lambda e: e.activation(rsc[0:4, :], rn4[0:4, :], AF.Ln), [n3], "nrs")
        p.drain("scalar")
        n5 = p.on("scalar", lambda e: e.activation(rn4[0:4, :], rsc[0:4, :], AF.Exp, scale=-1.0),
                  [n4], "nrs")
        b1 = p.dma(r4D, rn4[0:4, :], waits=[n5], inc="b1a")
        b2toks = []
        for n in range(4):
            b2toks.append(p.dma(rnb[n * 32:(n + 1) * 32, :],
                                r4D[n:n + 1, :].to_broadcast([32, T]), waits=[b1], inc="b2r"))
        tok_rnb = (p.ctr("b2r"), p.ctr("b2r").v)
        knd = p.ctr("knd")
        for ch in range(4):
            p.on("vector", lambda e, ch=ch: e.tensor_tensor(
                kvf[:, ch * 512:(ch + 1) * 512], kvf[:, ch * 512:(ch + 1) * 512],
                rnb[:, ch * 512:(ch + 1) * 512], OP.mult), [tok_rnb, (kq, 4)], knd)
        tok_kn = (knd, 4)

        tmm = p.ctr("tmm")
        tev = p.ctr("tev")
        t2v = p.ctr("t2v")
        t2s = p.ctr("t2s")
        bmm = p.ctr("bmm")
        bev = p.ctr("bev")
        p.on("vector", lambda e: e.memset(tok[:, 8:16], -1e30), [alldma], "ms")
        for k in range(NT):
            w = [tok_r, alldma, (kne, 4)] if k == 0 else [(bev, 2 * k)]
            tk = p.on("tensor", lambda e, k=k: e.transpose(
                psB[0:128, 0:4], lgt[0:4, k * 128:(k + 1) * 128], idt[0:4, 0:4]), w, tmm)
            te = p.on("vector", lambda e: e.tensor_copy(tok[:, 0:4], psB[0:128, 0:4]),
                      [tk], tev)
            c1 = p.on("vector", lambda e: e.tensor_copy(tok[:, 8:12], tok[:, 0:4]), [te], t2v)
            p.drain("vector")
            c2 = p.on("vector", lambda e: e.max(out=tok[:, 16:24], in_=tok[:, 8:16]), [c1], t2v)
            p.drain("vector")
            c3 = p.on("vector", lambda e: e.tensor_scalar(tok[:, 24:28], tok[:, 0:4],
                                                          tok[:, 17:18], None, OP.is_ge),
                      [c2], t2v)
            c4 = p.on("scalar", lambda e: e.activation(tok[:, 28:32], tok[:, 0:4], AF.Exp),
                      [c2], t2s)
            c5 = p.on("vector", lambda e: e.reduce_sum(tok[:, 32:33], tok[:, 28:32], axis=AX),
                      [c4, c3], t2v)
            p.drain("vector")
            c6 = p.on("vector", lambda e: e.reciprocal(tok[:, 33:34], tok[:, 32:33]), [c5], t2v)
            p.drain("vector")
            c7 = p.on("vector", lambda e: e.tensor_scalar(tok[:, 34:38], tok[:, 28:32],
                                                          tok[:, 33:34], None, OP.mult),
                      [c6], t2v)
            p.drain("vector")
            c8 = p.on("vector", lambda e: e.tensor_tensor(tok[:, 38:42], tok[:, 24:28],
                                                          tok[:, 34:38], OP.mult), [c7], t2v)
            p.drain("vector")
            c9 = p.on("vector", lambda e: e.reduce_sum(tok[:, 42:43], tok[:, 38:42], axis=AX),
                      [c8], t2v)
            p.drain("vector")
            c10 = p.on("vector", lambda e: e.tensor_scalar(tok[:, 43:44], tok[:, 42:43], 1e-8,
                                                           None, OP.add), [c9], t2v)
            p.drain("vector")
            c11 = p.on("vector", lambda e: e.reciprocal(tok[:, 44:45], tok[:, 43:44]), [c10], t2v)
            p.drain("vector")
            c12 = p.on("vector", lambda e: e.tensor_scalar(tok[:, 45:49], tok[:, 38:42],
                                                           tok[:, 44:45], -1.0,
                                                           OP.mult, OP.mult),
                       [c11], t2v)
            p.drain("vector")
            tku = p.on("tensor", lambda e, k=k: e.transpose(psB[0:4, 128:256],
                                                             tok[:, 45:49], idt[:]),
                       [c12], bmm)
            p.on("vector", lambda e, k=k: e.tensor_copy(u4[0:4, k * 128:(k + 1) * 128],
                                                        psB[0:4, 128:256]),
                 [tku], bev)
            tks = p.on("tensor", lambda e, k=k: e.transpose(psB[0:4, 256:384],
                                                            tok[:, 34:38], idt[:]),
                       [c12], bmm)
            p.on("vector", lambda e, k=k: e.tensor_copy(s4[0:4, k * 128:(k + 1) * 128],
                                                        psB[0:4, 256:384]),
                 [tks], bev)
        bu = p.dma(uwD, u4[0:4, :], waits=[(bev, 2 * NT), tok_xp], inc="b1b")
        bs = p.dma(smD, s4[0:4, :], waits=[(bev, 2 * NT)], inc="b1c")
        for n in range(4):
            p.dma(uwb()[n * 32:(n + 1) * 32, :],
                  uwD[n:n + 1, :].to_broadcast([32, T]), waits=[bu], inc="b2u")
            p.dma(smb()[n * 32:(n + 1) * 32, :],
                  smD[n:n + 1, :].to_broadcast([32, T]), waits=[bs], inc="b2u")
        tok_uw = (p.ctr("b2u"), p.ctr("b2u").v)
        wmd = p.ctr("wmd")
        for ch in range(4):
            p.on("vector", lambda e, ch=ch: e.tensor_scalar(
                wmb()[:, ch * 512:(ch + 1) * 512], uwb()[:, ch * 512:(ch + 1) * 512],
                1.0, 1.0, OP.mult, OP.add), [tok_uw], wmd)
        tok_wm = (wmd, 4)

        kdm = p.ctr("kdm")
        kde = p.ctr("kde")
        kds = [p.ctr("kds0"), p.ctr("kds1")]
        for k in range(NT):
            w = [tok_kn, (bev, 2 * NT)] if k == 0 else []
            if k >= 2:
                w.append((kds[k % 2], (k // 2) * 16))
            tk = p.on("tensor", lambda e, k=k: e.transpose(
                psB[:, 0:128], kvf[:, k * 128:(k + 1) * 128], idt[:]),
                w + ([(kde, k)] if k >= 1 else []), kdm)
            te = p.on("vector", lambda e, k=k: e.tensor_scalar(
                stg[:, (k % 2) * 128:(k % 2) * 128 + 128], psB[:, 0:128],
                -1.0, None, OP.mult),
                [tk] + ([(kds[k % 2], (k // 2) * 16)] if k >= 2 else []), kde)
            p.dma(knD.rearrange("(k p) c -> k p c", p=128)[k],
                  stg[:, (k % 2) * 128:(k % 2) * 128 + 128],
                  waits=[te], inc=kds[k % 2])
        qdm = p.ctr("qdm")
        qde = p.ctr("qde")
        qds = [p.ctr("qds0"), p.ctr("qds1")]
        for k in range(NT):
            w = [tok_qp, (kde, NT)] if k == 0 else []
            tk = p.on("tensor", lambda e, k=k: e.transpose(
                psB[0:128, 128:160], qf[0:32, k * 128:(k + 1) * 128], idt[0:32, 0:32]),
                w + ([(qde, k)] if k >= 1 else []), qdm)
            te = p.on("vector", lambda e, k=k: e.tensor_copy(
                stg[:, 256 + (k % 2) * 32:256 + (k % 2) * 32 + 32], psB[:, 128:160]),
                [tk] + ([(qds[k % 2], (k // 2) * 16)] if k >= 2 else []), qde)
            p.dma(qD.rearrange("(k p) c -> k p c", p=128)[k],
                  stg[:, 256 + (k % 2) * 32:256 + (k % 2) * 32 + 32],
                  waits=[te], inc=qds[k % 2])
        tok_knD = (kds[0], kds[0].v)
        tok_knD1 = (kds[1], kds[1].v)
        tok_qD = (qds[0], qds[0].v)
        tok_qD1 = (qds[1], qds[1].v)

        knl = [p.ctr("knl0"), p.ctr("knl1")]
        ql = [p.ctr("ql0"), p.ctr("ql1")]
        vz = p.ctr("vz")
        va = p.ctr("va")
        th = p.ctr("th")
        svc = p.ctr("sv")
        iz0 = p.on("vector", lambda e: e.memset(Sst[:], 0.0),
                   [tok_wm, tok_uw, tok_rnb, tok_bn, (tev, NT)], "ms")
        iz1 = p.on("vector", lambda e: e.memset(z0t[:], 0.0), [], "ms")
        iz2 = p.on("vector", lambda e: e.memset(spt[:], 0.0), [], "ms")
        izk = p.on("vector", lambda e: e.memset(knr[:], 1.0), [], "ms")
        iz = p.on("vector", lambda e: e.tensor_copy(
            Sst[:, 32:33], kvf[:, T:T + 1]), [iz2, tok_kv], "ms")
        for c in range(NCH):
            wdk = [tok_knD, tok_knD1, tok_qD, tok_qD1, izk] if c < 2 else \
                [(va, (c - 1) * CH)]
            kt4 = []
            for n in range(4):
                kt4.append(p.dma(
                    knr[n * 32:(n + 1) * 32, (c % 2) * CH * 33:((c % 2) + 1) * CH * 33]
                    .rearrange("p (t j) -> p t j", j=33)[:, :, 0:NS],
                    knD.rearrange("t (n j) -> n t j", n=4)[n, c * CH:(c + 1) * CH, :]
                    .unsqueeze(0).to_broadcast([32, CH, NS]),
                    waits=wdk, inc=knl[c % 2]))
            qt1 = p.dma(qr[:, (c % 2) * CH * NS:((c % 2) + 1) * CH * NS]
                        .rearrange("p (t j) -> p t j", j=NS),
                        qD[c * CH:(c + 1) * CH, :].unsqueeze(0).to_broadcast([128, CH, NS]),
                        waits=wdk, inc=ql[c % 2])
            for s_ in range(CH):
                t = c * CH + s_
                knt = knr[:, (c % 2) * CH * 33 + s_ * 33:(c % 2) * CH * 33 + (s_ + 1) * 33]
                qt = qr[:, (c % 2) * CH * NS + s_ * NS:(c % 2) * CH * NS + (s_ + 1) * NS]
                w1 = [iz, iz1] if t == 0 else []
                if s_ == 0:
                    w1 += [kt4[-1], qt1]
                t_op1 = p.on("vector", lambda e, knt=knt: e.scalar_tensor_tensor(
                    scr[:], Sst[:], 1.0, knt, OP.mult, OP.mult, accum_out=nsum[:, 0:1]),
                    w1, svc)
                t_op5 = None
                if t > 0:
                    t_op5 = p.on("vector", lambda e, qtp=qt_prev, tp=t - 1:
                                 e.scalar_tensor_tensor(
                        scr2[:], Sst[:, 0:NS], 1.0, qtp, OP.mult, OP.mult,
                        accum_out=outsq()[:, tp:tp + 1]), [], va)
                t_z = p.on("vector", lambda e, knt=knt: e.scalar_tensor_tensor(
                    zbt[:], knt[:, 0:NS], nsum[:, 0:1], z0t[:], OP.mult, OP.add),
                    [t_op1], vz)
                if t + 1 < T:
                    p.on("vector", lambda e, t=t: e.tensor_copy(
                        Sst[:, 32:33], kvf[:, T + t + 1:T + t + 2]), [], "sv2")
                t_th = p.on("scalar", lambda e: e.activation(Ut[:], zbt[:], AF.Tanh),
                            [(vz, t + 1)], th)
                t_s4 = p.on("vector", lambda e, t=t: e.scalar_tensor_tensor(
                    Sst[:, 0:NS], Ut[:], uwb()[:, t:t + 1], spt[:], OP.mult, OP.add),
                    [(th, t + 1), t_op5], svc)
                if t + 1 < T:
                    p.on("vector", lambda e, t=t: e.tensor_scalar(
                        z0t[:], Sst[:, 0:NS], betab[:, t + 1:t + 2], None, OP.mult),
                        [t_s4], "sv2")
                    p.on("vector", lambda e, t=t: e.tensor_scalar(
                        spt[:], Sst[:, 0:NS], wmb()[:, t + 1:t + 2], None, OP.mult),
                        [], "sv2")
                qt_prev = qt
        p.on("vector", lambda e: e.scalar_tensor_tensor(
            scr2[:], Sst[:, 0:NS], 1.0, qt_prev, OP.mult, OP.mult,
            accum_out=outsq()[:, T - 1:T]), [t_s4], va)

        p.on("vector", lambda e: e.tensor_copy(nsum[:, 3:4], nsum[:, 0:1]), [(va, T)], va)
        pg = p.ctr("pg")
        sg = p.ctr("sg")
        for ch in range(4):
            sl = lambda b, ch=ch: b()[:, ch * 512:(ch + 1) * 512]
            g1 = p.on("scalar", lambda e, ch=ch: e.activation(
                betab[:, ch * 512:(ch + 1) * 512], outsq()[:, ch * 512:(ch + 1) * 512],
                AF.Sigmoid), [(va, T + 1)], sg)
            g2b = p.on("vector", lambda e, ch=ch: e.tensor_tensor(
                rnb[:, ch * 512:(ch + 1) * 512], outsq()[:, ch * 512:(ch + 1) * 512],
                outsq()[:, ch * 512:(ch + 1) * 512], OP.mult), [g1], pg)
            g3 = p.on("vector", lambda e, ch=ch: e.tensor_tensor(
                rnb[:, ch * 512:(ch + 1) * 512], rnb[:, ch * 512:(ch + 1) * 512],
                betab[:, ch * 512:(ch + 1) * 512], OP.mult), [g2b], pg)
            p.drain("vector")
            g4 = p.on("vector", lambda e, ch=ch: e.tensor_tensor(
                rnb[:, ch * 512:(ch + 1) * 512], rnb[:, ch * 512:(ch + 1) * 512],
                smb()[:, ch * 512:(ch + 1) * 512], OP.mult), [g3], pg)
        crm = p.ctr("crm")
        cre = p.ctr("cre")
        for ch in range(4):
            tm = p.on("tensor", lambda e, ch=ch: e.matmul(
                psB[0:32, 0:512], selb[:], rnb[:, ch * 512:(ch + 1) * 512],
                start=True, stop=True), [(pg, 12)] + ([(cre, ch)] if ch else []), crm)
            p.on("scalar", lambda e, ch=ch: e.activation(
                cellb[0:32, ch * 512:(ch + 1) * 512], psB[0:32, 0:512], AF.Copy),
                [tm], cre)
        lwo = p.dma(wst[0:32, 0:D], WoT, waits=[], inc="wl0")
        cwo = p.on("vector", lambda e: e.tensor_copy(wbf[0:32, 0:D], wst[0:32, 0:D]),
                   [lwo], wcast)
        fmm = p.ctr("fmm")
        fev = p.ctr("fev")
        fso = [p.ctr("fso0"), p.ctr("fso1")]
        for mt in range(ND):
            for ch in range(4):
                i = mt * 4 + ch
                ps = psP if i % 2 == 0 else psP2
                w = [cwo, (cre, 4)] if i == 0 else []
                if i >= 2:
                    w.append((fev, i - 1))
                p.on("tensor", lambda e, mt=mt, ch=ch, ps=ps: e.matmul(
                    ps[:, :], wbf[0:32, mt * 128:(mt + 1) * 128],
                    cellb[0:32, ch * 512:(ch + 1) * 512],
                    start=True, stop=True), w, fmm)
                ew = [(fmm, i + 1)]
                if i >= 2:
                    ew.append((fso[i % 2], ((i - 2) // 2 + 1) * 16))
                te = p.on("scalar", lambda e, i=i, ps=ps: e.activation(
                    aor[:, (i % 2) * 512:(i % 2) * 512 + 512], ps[:, :], AF.Copy), ew, fev)
                p.dma(aT.rearrange("(mt p) t -> mt p t", p=128)[mt][:, ch * 512:(ch + 1) * 512],
                      aor[:, (i % 2) * 512:(i % 2) * 512 + 512], waits=[te], inc=fso[i % 2])




def _consts():
    import ml_dtypes
    c = 1e-4
    dec = np.zeros((28, 128, 512), np.float64)
    pp = np.arange(128)[:, None]
    jj = np.arange(512)[None, :]
    for mi in range(28):
        m = mi - 12
        diff = (128 * m + pp) - jj
        dec[mi] = np.exp(-c * np.abs(diff))
    blk = np.zeros((128, 4), np.float32)
    for n in range(4):
        blk[n * 32:(n + 1) * 32, n] = 1.0
    sel = np.zeros((128, 32), np.float32)
    for n in range(4):
        for i in range(32):
            sel[n * 32 + i, i] = 1.0
    return {
        "ident": np.eye(128, dtype=np.float32),
        "onescol": np.ones((128, 1), np.float32),
        "ones1x": np.ones((1, 128), np.float32),
        "decay": dec.astype(ml_dtypes.bfloat16),
        "blk4": blk, "sel4": sel,
    }


def build_fused():
    nc = _nc()
    DI = lambda name, shape: nc.dram_tensor(name, shape, F32, kind="ExternalInput").ap()
    IN = lambda name, shape: nc.dram_tensor(name, shape, F32, kind="Internal").ap()
    x = DI("x", [T, D])
    ident = DI("ident", [128, 128])
    onescol = DI("onescol", [128, 1])
    ones1x = DI("ones1x", [1, 128])
    decay = nc.dram_tensor("decay", [28, 128, 512], BF16, kind="ExternalInput").ap()
    blk4 = DI("blk4", [128, 4])
    sel4 = DI("sel4", [128, 32])
    Wq = DI("Wq", [D, D]); Wk = DI("Wk", [D, D])
    Wv = DI("Wv", [D, D]); Wo = DI("Wo", [D, D])
    Wdkv = DI("Wdkv", [D, 256]); Wuk = DI("Wuk", [256, D]); Wuv = DI("Wuv", [256, D])
    wg = DI("wg", [2816, D]); wu = DI("wu", [2816, D]); wd = DI("wd", [D, 2816])
    WinT = DI("WinT", [D, D]); WrT = DI("WrT", [D, 4]); WkvT = DI("WkvT", [D, 256])
    WbT = DI("WbT", [D, 128]); bbt = DI("bbt", [128, 1])
    WqT = DI("WqT", [D, 32]); WoT = DI("WoT", [32, D])
    wupT = DI("wupT", [D, 4096]); wdnT = DI("wdnT", [4096, D])
    out = nc.dram_tensor("out", [T, D], F32, kind="ExternalOutput").ap()
    xnT = IN("xnT", [D, T]); a0T = IN("a0T", [D, T]); f0T = IN("f0T", [D, T])
    x1T = IN("x1T", [D, T]); xn1T = IN("xn1T", [D, T])
    a1T = IN("a1T", [D, T]); f1T = IN("f1T", [D, T])
    xq_c = nc.dram_tensor("xq_cache", [NT, 128, D], BF16, kind="Internal").ap()
    wup_c = nc.dram_tensor("wup_cache", [32, ND, 128, 128], BF16, kind="Internal").ap()
    wdn_c = nc.dram_tensor("wdn_cache", [ND, 32, 128, 128], BF16, kind="Internal").ap()
    knD = IN("knD", [T, 128]); qD = IN("qD", [T, 32])
    r4D = IN("r4D", [4, T]); uwD = IN("uwD", [4, T]); smD = IN("smD", [4, T])
    with ExitStack() as ctx:
        p = Prog(nc, ctx)

        def stage(fn, pfx, Bprev, *args):
            p.begin_stage()
            fn(nc, p, pfx, Bprev, *args)
            return p.end_stage()

        Bt = stage(emit_ln0, "a", [], x, ident, xnT)
        Bt = stage(emit_attn0, "b", Bt, xnT, Wq, Wk, Wv, Wo, Wdkv, Wuk, Wuv,
                   decay, ones1x, a0T)
        Bt = stage(emit_ffn0, "c", Bt, xnT, wg, wu, wd, ident, f0T, xq_c)
        Bt = stage(emit_res_ln, "d", Bt, x, a0T, f0T, ident, onescol, ones1x,
                   x1T, xn1T)
        Bt = stage(emit_attn1, "e", Bt, xn1T, WinT, WrT, WkvT, WbT, bbt, WqT,
                   WoT, blk4, sel4, ident, a1T, knD, qD, r4D, uwD, smD)
        Bt = stage(emit_ffn1, "f", Bt, xn1T, wupT, wdnT, onescol, ones1x, f1T,
                   wup_c, wdn_c)
        Bt = stage(emit_final, "g", Bt, x1T, a1T, f1T, ident, out)
        p.begin_stage()
        p.on("sync", lambda e: e.nop(), Bt, None)
        p.emit()
    return nc


def kernel(**inputs):
    C = _consts()
    x = np.ascontiguousarray(inputs["L0_attn_x"], np.float32)
    Wkv = inputs["L1_attn_W_kv"]
    perm = [n * 64 + j for n in range(4) for j in range(32)] + \
           [n * 64 + 32 + i for n in range(4) for i in range(32)]
    shared = {
        "ident": C["ident"], "onescol": C["onescol"], "ones1x": C["ones1x"],
        "decay": C["decay"], "blk4": C["blk4"], "sel4": C["sel4"],
        "Wq": inputs["L0_attn_Wq"], "Wk": inputs["L0_attn_Wk"],
        "Wv": inputs["L0_attn_Wv"], "Wo": inputs["L0_attn_Wo"],
        "Wdkv": inputs["L0_attn_W_down_kv"], "Wuk": inputs["L0_attn_W_up_k"],
        "Wuv": inputs["L0_attn_W_up_v"],
        "wg": inputs["L0_ffn_w_gate"], "wu": inputs["L0_ffn_w_up"],
        "wd": inputs["L0_ffn_w_down"],
        "WinT": np.ascontiguousarray(inputs["L1_attn_W_in"].T),
        "WrT": np.ascontiguousarray(inputs["L1_attn_W_router"].T),
        "WkvT": np.ascontiguousarray(np.asarray(Wkv)[perm].T),
        "WbT": np.ascontiguousarray(inputs["L1_attn_W_beta"].T),
        "bbt": np.ascontiguousarray(np.asarray(inputs["L1_attn_b_beta"],
                                               np.float32).reshape(128, 1)),
        "WqT": np.ascontiguousarray(inputs["L1_attn_W_q"].T),
        "WoT": np.ascontiguousarray(inputs["L1_attn_W_out"].T),
        "wupT": np.ascontiguousarray(inputs["L1_ffn_w_up"].T),
        "wdnT": np.ascontiguousarray(inputs["L1_ffn_w_down"].T),
    }
    nc = build_fused()
    in_maps = [dict(shared, x=x[b]) for b in range(B)]
    res = run_bass_kernel_spmd(nc, in_maps, core_ids=list(range(B)))
    return np.stack([np.asarray(res.results[b]["out"]) for b in range(B)], 0)



# version 09
# speedup vs baseline: 1.0325x; 1.0079x over previous
import numpy as np
from contextlib import ExitStack
import concourse.bass as bass
import concourse.mybir as mybir
from concourse.bass_utils import run_bass_kernel_spmd

F32 = mybir.dt.float32
BF16 = mybir.dt.bfloat16
U8 = mybir.dt.uint8
AF = mybir.ActivationFunctionType
OP = mybir.AluOpType
AX = mybir.AxisListType.X

B, T, D = 8, 2048, 1024
NT = T // 128
ND = D // 128
MAGIC = 1.5 * 2.0 ** 23
MAGIC2 = 1.5 * 2.0 ** 22
LDEC = 0.01 * 0.01


class Phys:
    def __init__(self, sem):
        self.sem = sem
        self.total = 0


class Ctr:

    def __init__(self, phys):
        self.phys = phys
        self.base = phys.total
        self.v = 0

    @property
    def sem(self):
        return self.phys.sem

    def inc(self, n):
        self.v += n
        self.phys.total = self.base + self.v
        return self.v


class Prog:
    ENGINES = ("sync", "vector", "scalar", "tensor", "gpsimd")

    def __init__(self, nc, ctx, npool=72):
        self.nc = nc
        self.ctx = ctx
        self.steps = {e: [] for e in self.ENGINES}
        self.pool = [Phys(ctx.enter_context(nc.semaphore("sp%d" % i)))
                     for i in range(npool)]
        self.pool_i = 0
        self._ctrs = {}
        self._stage_phys = set()
        self._dma_ctrs = set()
        self._marks = {e: 0 for e in self.ENGINES}

    def begin_stage(self):
        self._ctrs = {}
        self._stage_phys = set()
        self._dma_ctrs = set()
        self._marks = {e: len(self.steps[e]) for e in self.ENGINES}

    def ctr(self, name):
        if name not in self._ctrs:
            for _ in range(len(self.pool)):
                ph = self.pool[self.pool_i]
                self.pool_i = (self.pool_i + 1) % len(self.pool)
                if id(ph) not in self._stage_phys:
                    break
            else:
                raise RuntimeError("semaphore pool exhausted")
            self._stage_phys.add(id(ph))
            self._ctrs[name] = Ctr(ph)
        return self._ctrs[name]

    def on(self, eng, fn, waits=(), inc=None, n=1):
        c = self.ctr(inc) if isinstance(inc, str) else inc
        tv = c.inc(n) if c is not None else None
        self.steps[eng].append((fn, [w for w in waits if w is not None], c, n, tv))
        if c is not None:
            return (c, tv)
        return None

    def drain(self, eng):
        self.steps[eng].append((lambda e: e.drain(), [], None, 0, None))

    def dma(self, out, in_, waits=(), inc="dma", eng="sync"):
        c = self.ctr(inc) if isinstance(inc, str) else inc
        self._dma_ctrs.add(c)
        return self.on(eng, lambda e: e.dma_start(out=out, in_=in_), waits, c, 16)

    def end_stage(self):
        best = {}
        for e in self.ENGINES:
            for fn, waits, c, n, tv in reversed(self.steps[e][self._marks[e]:]):
                if c is not None:
                    if best.get(id(c), (None, -1))[1] < tv:
                        best[id(c)] = (c, tv)
                    break
        for c in self._dma_ctrs:
            if best.get(id(c), (None, -1))[1] < c.v:
                best[id(c)] = (c, c.v)
        return list(best.values())

    def emit(self):
        blk = self.ctx.enter_context(self.nc.Block())

        def mk(eng_name):
            def body(e):
                for fn, waits, c, n, tv in self.steps[eng_name]:
                    for (wc, wv) in waits:
                        e.wait_ge(wc.sem, wc.base + wv)
                    ins = fn(e)
                    if c is not None:
                        assert ins is not None
                        ins.then_inc(c.sem, n)
            return body

        for e in self.ENGINES:
            if self.steps[e]:
                getattr(blk, e)(mk(e))


def _nc():
    return bass.Bass("TRN2", target_bir_lowering=False)


def emit_transpose_seq(p, pairs, idt, psA, psB, wait_src, name, first_extra=None):
    mt = p.ctr(name + "_mt")
    ev = p.ctr(name + "_ev")
    toks = []
    for i, (s, d) in enumerate(pairs):
        bank = psA if i % 2 == 0 else psB
        waits = []
        if i == 0 and first_extra is not None:
            waits.append(first_extra)
        w = wait_src(i)
        if w is not None:
            waits.append(w)
        if i >= 2:
            waits.append((ev, i - 1))
        toks.append(p.on("tensor", lambda e, b=bank, s=s: e.transpose(b[:, 0:128], s, idt[:]),
                         waits, mt))
    evtoks = []
    for i, (s, d) in enumerate(pairs):
        bank = psA if i % 2 == 0 else psB
        evtoks.append(p.on("vector", lambda e, b=bank, d=d: e.tensor_copy(d, b[:, 0:128]),
                           [toks[i]], ev))
    return ev


def emit_rowstats(p, tiles_ap, n_dt, onc, psS, rowout, wait_fn, name, evac_eng="vector"):
    mm = p.ctr(name + "_mm")
    evc = p.ctr(name + "_ev")
    tok = None
    for ch in range(T // 512):
        mts = []
        for dt in range(n_dt):
            waits = [wait_fn(dt)]
            if ch > 0 and dt == 0:
                waits.append((evc, ch))
            mts.append(p.on("tensor", lambda e, ch=ch, dt=dt, st=(dt == 0), sp=(dt == n_dt - 1):
                            e.matmul(psS[0:1, :], onc[:],
                                     tiles_ap(dt)[:, ch * 512:(ch + 1) * 512],
                                     start=st, stop=sp), waits, mm))
        tok = p.on(evac_eng, lambda e, ch=ch: e.tensor_copy(rowout[0:1, ch * 512:(ch + 1) * 512],
                                                            psS[0:1, :]),
                   [mts[-1]], evc)
    return tok



def emit_ln0(nc, p, pfx, B, x, ident, xnT):
    with ExitStack() as ctx:
        xt = ctx.enter_context(nc.sbuf_tensor(pfx + "xt", [128, NT * D], F32))
        xo = ctx.enter_context(nc.sbuf_tensor(pfx + "xo", [128, ND * T], F32))
        sq = ctx.enter_context(nc.sbuf_tensor(pfx + "sq", [128, 2 * D], F32))
        idt = ctx.enter_context(nc.sbuf_tensor(pfx + "idt", [128, 128], F32))
        stats = ctx.enter_context(nc.sbuf_tensor(pfx + "stats", [128, 6 * NT], F32))
        mcol = ctx.enter_context(nc.sbuf_tensor(pfx + "mcol", [128, NT], F32))
        icol = ctx.enter_context(nc.sbuf_tensor(pfx + "icol", [128, NT], F32))
        psA = ctx.enter_context(nc.psum_tensor(pfx + "psA", [128, 512], F32))
        psB = ctx.enter_context(nc.psum_tensor(pfx + "psB", [128, 512], F32))
        p.dma(xt[:].rearrange("p (k d) -> p k d", k=NT),
              x.rearrange("(k p) d -> p k d", p=128), waits=B)
        p.dma(idt[:], ident, inc="dma", waits=B)
        tok_ld = (p.ctr("dma"), p.ctr("dma").v)

        nrm = p.ctr("nrm")
        for k in range(NT):
            xk = xt[:, k * D:(k + 1) * D]
            sqk = sq[:, (k % 2) * D:(k % 2) * D + D]
            c = lambda j, k=k: stats[:, j * NT + k:j * NT + k + 1]
            sq_w = [tok_ld] if k < 2 else [tok_ld, (nrm, k - 1)]
            t_sq = p.on("scalar", lambda e, xk=xk, sqk=sqk: e.activation(sqk, xk, AF.Square),
                        sq_w, "sqd")
            t_s1 = p.on("vector", lambda e, xk=xk, o=c(0): e.reduce_sum(o, xk, axis=AX),
                        [tok_ld], "vst")
            t_s2 = p.on("vector", lambda e, sqk=sqk, o=c(1): e.reduce_sum(o, sqk, axis=AX),
                        [t_sq], "vst")
            t_m = p.on("vector", lambda e, i=c(0), o=mcol[:, k:k + 1]:
                       e.tensor_scalar(o, i, 1.0 / D, None, OP.mult), [t_s1], "vst")
            t_m2 = p.on("vector", lambda e, i=mcol[:, k:k + 1], o=c(5):
                        e.tensor_scalar(o, i, i, None, OP.mult), [t_m], "vst")
            t_e2 = p.on("vector", lambda e, i=c(1), o=c(2):
                        e.tensor_scalar(o, i, 1.0 / D, None, OP.mult), [t_s2], "vst")
            t_v = p.on("vector", lambda e, a=c(5), b=c(2), o=c(3):
                       e.scalar_tensor_tensor(o, a, -1.0, b, OP.mult, OP.add),
                       [t_m2, t_e2], "vst")
            p.drain("vector")
            t_ve = p.on("vector", lambda e, io=c(3), o=c(4):
                        e.tensor_scalar(o, io, 1e-5, None, OP.add), [t_v], "vst")
            t_ln = p.on("scalar", lambda e, i=c(4), o=c(3):
                        e.activation(o, i, AF.Ln), [t_ve], "sqd")
            p.drain("scalar")
            t_iv = p.on("scalar", lambda e, i=c(3), o=icol[:, k:k + 1]:
                        e.activation(o, i, AF.Exp, scale=-0.5), [t_ln], "sqd")
            p.on("vector", lambda e, xk=xk, m=mcol[:, k:k + 1], iv=icol[:, k:k + 1]:
                 e.tensor_scalar(xk, xk, m, iv, OP.subtract, OP.mult),
                 [t_iv], nrm)

        pairs = []
        for k in range(NT):
            for dt in range(ND):
                pairs.append((xt[:, k * D + dt * 128: k * D + (dt + 1) * 128],
                              xo[:, dt * T + k * 128: dt * T + (k + 1) * 128]))
        ev = emit_transpose_seq(p, pairs, idt, psA, psB,
                                lambda i: (nrm, i // ND + 1) if i % ND == 0 else None, "tr",
                                first_extra=tok_ld)
        sto = p.ctr("sto")
        for dt in range(ND):
            p.dma(xnT.rearrange("(dt p) t -> dt p t", p=128)[dt],
                  xo[:, dt * T:(dt + 1) * T],
                  waits=[(ev, (NT - 1) * ND + dt + 1)], inc=sto)



def emit_res_ln(nc, p, pfx, B, x, a, f, ident, onescol, ones1x, x1T, xn1T):
    with ExitStack() as ctx:
        xt = ctx.enter_context(nc.sbuf_tensor(pfx + "xt", [128, NT * D], F32))
        x1 = ctx.enter_context(nc.sbuf_tensor(pfx + "x1", [128, ND * T], F32))
        ar = xt[:, 0:2 * T]
        fb = xt[:, 2 * T:4 * T]
        sqb = xt[:, 4 * T:6 * T]
        idt = ctx.enter_context(nc.sbuf_tensor(pfx + "idt", [128, 128], F32))
        onc = ctx.enter_context(nc.sbuf_tensor(pfx + "onc", [128, 1], F32))
        on1 = ctx.enter_context(nc.sbuf_tensor(pfx + "on1", [1, 128], F32))
        row = ctx.enter_context(nc.sbuf_tensor(pfx + "row", [1, 6 * T], F32))
        psA = ctx.enter_context(nc.psum_tensor(pfx + "psA", [128, 512], F32))
        psB = ctx.enter_context(nc.psum_tensor(pfx + "psB", [128, 512], F32))
        psS = ctx.enter_context(nc.psum_tensor(pfx + "psS", [128, 512], F32))
        psI = ctx.enter_context(nc.psum_tensor(pfx + "psI", [128, 512], F32))
        psM = ctx.enter_context(nc.psum_tensor(pfx + "psM", [128, 512], F32))

        p.dma(xt[:].rearrange("p (k d) -> p k d", k=NT),
              x.rearrange("(k p) d -> p k d", p=128), waits=B)
        p.dma(idt[:], ident, waits=B)
        p.dma(onc[:], onescol, waits=B)
        p.dma(on1[:], ones1x, waits=B)
        alldma = (p.ctr("dma"), p.ctr("dma").v)

        pairs = []
        for k in range(NT):
            for dt in range(ND):
                pairs.append((xt[:, k * D + dt * 128: k * D + (dt + 1) * 128],
                              x1[:, dt * T + k * 128: dt * T + (k + 1) * 128]))
        ev = emit_transpose_seq(p, pairs, idt, psA, psB,
                                lambda i: None, "tr", first_extra=alldma)
        evall = (ev, NT * ND)

        add1 = p.ctr("add1")
        for dt in range(ND):
            wa = [evall] if dt < 2 else [(add1, dt - 1)]
            la = p.dma(ar[:, (dt % 2) * T:(dt % 2) * T + T],
                       a.rearrange("(dt p) t -> dt p t", p=128)[dt],
                       waits=wa, inc="dma%d" % (dt % 2))
            lf = p.dma(fb[:, (dt % 2) * T:(dt % 2) * T + T],
                       f.rearrange("(dt p) t -> dt p t", p=128)[dt],
                       waits=wa, inc="dmf%d" % (dt % 2))
            t1 = p.on("vector", lambda e, dt=dt: e.tensor_tensor(
                x1[:, dt * T:(dt + 1) * T], x1[:, dt * T:(dt + 1) * T],
                ar[:, (dt % 2) * T:(dt % 2) * T + T], OP.add), [la, lf], "vadd")
            p.on("vector", lambda e, dt=dt: e.tensor_tensor(
                x1[:, dt * T:(dt + 1) * T], x1[:, dt * T:(dt + 1) * T],
                fb[:, (dt % 2) * T:(dt % 2) * T + T], OP.add), [t1], add1)
        sto = p.ctr("sto")
        for dt in range(ND):
            p.dma(x1T.rearrange("(dt p) t -> dt p t", p=128)[dt],
                  x1[:, dt * T:(dt + 1) * T], waits=[(add1, dt + 1)], inc=sto)

        tok_s1 = emit_rowstats(p, lambda dt: x1[:, dt * T:(dt + 1) * T], ND,
                               onc, psS, row, lambda dt: (add1, ND), "s1")
        sqd = p.ctr("sqd")
        s2mm = p.ctr("s2mm")
        s2ev = p.ctr("s2ev")
        for ch in range(4):
            for dt in range(ND):
                i = ch * ND + dt
                sq_w = [(add1, ND)] if i < 2 else [(s2mm, i - 1)]
                t_q = p.on("scalar", lambda e, ch=ch, dt=dt: e.activation(
                    sqb[:, (dt % 2) * T: (dt % 2) * T + 512],
                    x1[:, dt * T + ch * 512: dt * T + ch * 512 + 512], AF.Square),
                    sq_w, sqd)
                waits = [t_q]
                if dt == 0 and ch > 0:
                    waits.append((s2ev, ch))
                p.on("tensor", lambda e, ch=ch, dt=dt, st=(dt == 0), sp=(dt == ND - 1):
                     e.matmul(psI[0:1, :], onc[:],
                              sqb[:, (dt % 2) * T: (dt % 2) * T + 512],
                              start=st, stop=sp), waits, s2mm)
            p.on("vector", lambda e, ch=ch: e.tensor_copy(
                row[0:1, T + ch * 512: T + ch * 512 + 512], psI[0:1, :]),
                [(s2mm, (ch + 1) * ND)], s2ev)

        r = lambda j: row[0:1, j * T:(j + 1) * T]
        t1 = p.on("vector", lambda e: e.tensor_scalar(r(2), r(0), 1.0 / D, None, OP.mult),
                  [tok_s1], "rs")
        p.drain("vector")
        t2 = p.on("vector", lambda e: e.tensor_tensor(r(3), r(2), r(2), OP.mult),
                  [t1], "rs")
        t3 = p.on("vector", lambda e: e.tensor_scalar(r(5), r(1), 1.0 / D, None, OP.mult),
                  [(s2ev, 4)], "rs")
        t4 = p.on("vector", lambda e: e.scalar_tensor_tensor(r(3), r(3), -1.0, r(5),
                                                             OP.mult, OP.add),
                  [t2, t3], "rs")
        p.drain("vector")
        t5 = p.on("vector", lambda e: e.tensor_scalar(r(3), r(3), 1e-5, None, OP.add),
                  [t4], "rs")
        t6 = p.on("scalar", lambda e: e.activation(r(1), r(3), AF.Ln), [t5], "ras")
        p.drain("scalar")
        t7 = p.on("scalar", lambda e: e.activation(r(4), r(1), AF.Exp, scale=-0.5), [t6], "ras")
        t8 = p.on("vector", lambda e: e.tensor_tensor(r(5), r(2), r(4), OP.mult), [t7], "rs")

        nsto = p.ctr("nsto")
        for ch in range(4):
            wb = [t8, (sto, 8 * 16)]
            if ch > 0:
                wb.append((nsto, ch * ND))
            bi = p.on("tensor", lambda e, ch=ch: e.matmul(
                psI[:, :], on1[:], r(4)[:, ch * 512:(ch + 1) * 512], start=True, stop=True),
                wb, "bmm")
            bm = p.on("tensor", lambda e, ch=ch: e.matmul(
                psM[:, :], on1[:], r(5)[:, ch * 512:(ch + 1) * 512], start=True, stop=True),
                [t8], "bmm")
            for dt in range(ND):
                tn = p.on("vector", lambda e, ch=ch, dt=dt: e.tensor_tensor(
                    x1[:, dt * T + ch * 512: dt * T + ch * 512 + 512],
                    x1[:, dt * T + ch * 512: dt * T + ch * 512 + 512], psI[:, :], OP.mult),
                    [bi, bm], "vn")
                p.on("vector", lambda e, ch=ch, dt=dt: e.tensor_tensor(
                    x1[:, dt * T + ch * 512: dt * T + ch * 512 + 512],
                    x1[:, dt * T + ch * 512: dt * T + ch * 512 + 512], psM[:, :], OP.subtract),
                    [tn], nsto)
        for dt in range(ND):
            p.dma(xn1T.rearrange("(dt p) t -> dt p t", p=128)[dt],
                  x1[:, dt * T:(dt + 1) * T], waits=[(nsto, 4 * ND)], inc=sto)



def emit_final(nc, p, pfx, B, x1, a, f, ident, out):
    with ExitStack() as ctx:
        xb = ctx.enter_context(nc.sbuf_tensor(pfx + "xb", [128, ND * T], F32))
        ab = ctx.enter_context(nc.sbuf_tensor(pfx + "ab", [128, ND * T], F32))
        ob = ctx.enter_context(nc.sbuf_tensor(pfx + "ob", [128, NT * D], F32))
        idt = ctx.enter_context(nc.sbuf_tensor(pfx + "idt", [128, 128], F32))
        psA = ctx.enter_context(nc.psum_tensor(pfx + "psA", [128, 512], F32))
        psB = ctx.enter_context(nc.psum_tensor(pfx + "psB", [128, 512], F32))
        t_x = p.dma(xb[:].rearrange("p (dt t) -> p dt t", dt=ND),
                    x1.rearrange("(dt p) t -> p dt t", p=128), waits=B)
        t_a = p.dma(ab[:].rearrange("p (dt t) -> p dt t", dt=ND),
                    a.rearrange("(dt p) t -> p dt t", p=128), waits=B)
        p.dma(idt[:], ident, waits=B)
        alldma = (p.ctr("dma"), p.ctr("dma").v)
        add1 = p.ctr("add1")
        fl = []
        for dt in range(ND):
            w = [alldma] if dt < 2 else [(add1, dt - 1)]
            fl.append(p.dma(ob[:, (dt % 2) * T:(dt % 2) * T + T],
                            f.rearrange("(dt p) t -> dt p t", p=128)[dt],
                            waits=w, inc="dmf%d" % (dt % 2)))
        for dt in range(ND):
            t1 = p.on("vector", lambda e, dt=dt: e.tensor_tensor(
                xb[:, dt * T:(dt + 1) * T], xb[:, dt * T:(dt + 1) * T],
                ab[:, dt * T:(dt + 1) * T], OP.add), [alldma, fl[dt]], "vadd")
            p.on("vector", lambda e, dt=dt: e.tensor_tensor(
                xb[:, dt * T:(dt + 1) * T], xb[:, dt * T:(dt + 1) * T],
                ob[:, (dt % 2) * T:(dt % 2) * T + T], OP.add), [t1], add1)
        pairs = []
        for k in range(NT):
            for dt in range(ND):
                pairs.append((xb[:, dt * T + k * 128: dt * T + (k + 1) * 128],
                              ob[:, k * D + dt * 128: k * D + (dt + 1) * 128]))
        ev = emit_transpose_seq(p, pairs, idt, psA, psB,
                                lambda i: (add1, ND), "tr")
        sto = p.ctr("sto")
        for k in range(NT):
            p.dma(out.rearrange("(k p) d -> k p d", p=128)[k],
                  ob[:, k * D:(k + 1) * D], waits=[(ev, (k + 1) * ND)], inc=sto)





def emit_ffn1(nc, p, pfx, B, xnT, wupT, wdnT, onescol, ones1x, fT, wup_c, wdn_c):
    DF = 4096
    NF = DF // 128
    with ExitStack() as ctx:
        xb = ctx.enter_context(nc.sbuf_tensor(pfx + "xb", [128, ND * T], F32))
        xnb = ctx.enter_context(nc.sbuf_tensor(pfx + "xnb", [128, ND * T], BF16))
        wring = ctx.enter_context(nc.sbuf_tensor(pfx + "wring", [128, 2 * ND * 128], BF16))
        wdb = ctx.enter_context(nc.sbuf_tensor(pfx + "wdb", [128, 2 * NF * 128], BF16))
        i2sb = ctx.enter_context(nc.sbuf_tensor(pfx + "i2sb", [128, 1024], F32))
        fsl = ctx.enter_context(nc.sbuf_tensor(pfx + "fsl", [128, 2 * 1024], F32))
        row = ctx.enter_context(nc.sbuf_tensor(pfx + "row", [1, 4 * T], F32))
        acc = ctx.enter_context(nc.sbuf_tensor(pfx + "acc", [128, 4], F32))
        iac = ctx.enter_context(nc.sbuf_tensor(pfx + "iac", [128, 2], F32))
        onc = ctx.enter_context(nc.sbuf_tensor(pfx + "onc", [128, 1], F32))
        on1 = ctx.enter_context(nc.sbuf_tensor(pfx + "on1", [1, 128], F32))
        psS = ctx.enter_context(nc.psum_tensor(pfx + "psS", [128, 512], F32))
        psI = ctx.enter_context(nc.psum_tensor(pfx + "psI", [128, 512], F32))
        psU0 = ctx.enter_context(nc.psum_tensor(pfx + "psU0", [128, 512], F32))
        psU1 = ctx.enter_context(nc.psum_tensor(pfx + "psU1", [128, 512], F32))
        hb = xb[:].bitcast(BF16)
        stg = xb

        p.dma(xb[:].rearrange("p (dt t) -> p dt t", dt=ND),
              xnT.rearrange("(dt p) t -> p dt t", p=128), waits=B)
        p.dma(onc[:], onescol, waits=B)
        p.dma(on1[:], ones1x, waits=B)
        alldma = (p.ctr("dma"), p.ctr("dma").v)

        sqd = p.ctr("sqd")
        s1mm = p.ctr("s1mm")
        s1ev = p.ctr("s1ev")
        for ch in range(4):
            for dt in range(ND):
                i = ch * ND + dt
                sq_w = [alldma] if i < 2 else [(s1mm, i - 1)]
                t_q = p.on("scalar", lambda e, ch=ch, dt=dt: e.activation(
                    fsl[:, (dt % 2) * 1024:(dt % 2) * 1024 + 512],
                    xb[:, dt * T + ch * 512: dt * T + ch * 512 + 512], AF.Square),
                    sq_w, sqd)
                waits = [t_q]
                if dt == 0 and ch > 0:
                    waits.append((s1ev, ch))
                p.on("tensor", lambda e, dt=dt, st=(dt == 0), sp=(dt == ND - 1):
                     e.matmul(psS[0:1, :], onc[:],
                              fsl[:, (dt % 2) * 1024:(dt % 2) * 1024 + 512],
                              start=st, stop=sp), waits, s1mm)
            p.on("vector", lambda e, ch=ch: e.tensor_copy(row[0:1, ch * 512:(ch + 1) * 512],
                                                          psS[0:1, :]),
                 [(s1mm, (ch + 1) * ND)], s1ev)
        r = lambda j: row[0:1, j * T:(j + 1) * T]
        t1 = p.on("scalar", lambda e: e.activation(r(1), r(0), AF.Ln), [(s1ev, 4)], "ras")
        p.drain("scalar")
        t2 = p.on("scalar", lambda e: e.activation(r(2), r(1), AF.Exp, scale=0.5), [t1], "ras")
        t3 = p.on("vector", lambda e: e.tensor_scalar(r(1), r(2), 1.0 / 32.0, 1e-8,
                                                      OP.mult, OP.add), [t2], "rs")
        t4 = p.on("scalar", lambda e: e.activation(r(2), r(1), AF.Ln), [t3], "ras")
        p.drain("scalar")
        t5 = p.on("scalar", lambda e: e.activation(r(3), r(2), AF.Exp, scale=-1.0), [t4], "ras")
        xnbd = p.ctr("xnbd")
        for ch in range(4):
            bi = p.on("tensor", lambda e, ch=ch: e.matmul(
                psI[:, :], on1[:], r(3)[:, ch * 512:(ch + 1) * 512], start=True, stop=True),
                [t5] + ([(xnbd, ch * ND)] if ch else []), "bmm")
            for dt in range(ND):
                p.on("vector", lambda e, ch=ch, dt=dt: e.tensor_tensor(
                    xnb[:, dt * T + ch * 512: dt * T + ch * 512 + 512],
                    xb[:, dt * T + ch * 512: dt * T + ch * 512 + 512], psI[:, :], OP.mult),
                    [bi], xnbd)
        tok_xnb = (xnbd, 4 * ND)

        def quant_weight(tag, wT, n_slab, slab_cols, count, cache_store, res_tok):
            al = p.ctr(tag + "_al")
            toks = []
            res_list = res_tok if isinstance(res_tok, list) else [res_tok]
            for s in range(n_slab):
                w = list(res_list) if s < 2 else [(al, s - 1)]
                toks.append(p.dma(stg[:, (s % 2) * slab_cols:(s % 2 + 1) * slab_cols],
                                  wT.rearrange("(s p) c -> s p c", p=128)[s],
                                  waits=w, inc=tag + "_ld%d" % (s % 2)))
            for s in range(n_slab):
                t_r = p.on("vector", lambda e, s=s: e.reduce_sum(
                    acc[:, (s % 2) + 2:(s % 2) + 3],
                    stg[:, (s % 2) * slab_cols:(s % 2 + 1) * slab_cols],
                    axis=AX, apply_absolute_value=True), [toks[s]], tag + "_r")
                p.drain("vector")
                if s == 0:
                    p.on("vector", lambda e: e.tensor_copy(acc[:, 0:1], acc[:, 2:3]),
                         [t_r], al)
                else:
                    p.on("vector", lambda e, s=s: e.tensor_tensor(
                        acc[:, 0:1], acc[:, 0:1], acc[:, (s % 2) + 2:(s % 2) + 3], OP.add),
                        [t_r], al)
                p.drain("vector")
            tm = p.on("tensor", lambda e: e.matmul(psS[0:1, 0:1], acc[:, 0:1], onc[:],
                                                   start=True, stop=True),
                      [(al, n_slab)], tag + "_tm")
            te = p.on("vector", lambda e: e.tensor_copy(iac[0:1, 0:1], psS[0:1, 0:1]),
                      [tm], tag + "_te")
            p.drain("vector")
            tr = p.on("vector", lambda e: e.reciprocal(iac[0:1, 1:2], iac[0:1, 0:1]),
                      [te], tag + "_te")
            p.drain("vector")
            ts_ = p.on("vector", lambda e: e.tensor_scalar(iac[0:1, 0:1], iac[0:1, 1:2],
                                                           float(count), None, OP.mult),
                       [tr], tag + "_te")
            tb = p.on("tensor", lambda e: e.matmul(psS[:, 0:1], on1[:], iac[0:1, 0:1],
                                                   start=True, stop=True), [ts_], tag + "_tm")
            tc = p.on("vector", lambda e: e.tensor_copy(iac[:, 1:2], psS[:, 0:1]), [tb], tag + "_te")
            qd = p.ctr(tag + "_qd")
            dns = [p.ctr(tag + "_dn0"), p.ctr(tag + "_dn1")]
            for s in range(n_slab):
                done = dns[s % 2]
                w = [tc] if s < 2 else [(done, ((s - 2) // 2 + 1) * 16)]
                tok2 = p.dma(stg[:, (s % 2) * slab_cols:(s % 2 + 1) * slab_cols],
                             wT.rearrange("(s p) c -> s p c", p=128)[s],
                             waits=w, inc=tag + "_qld%d" % (s % 2))
                toks2 = {s: tok2}
                sl = stg[:, (s % 2) * slab_cols:(s % 2 + 1) * slab_cols]
                gb = stg[:, (2 + s % 2) * slab_cols:(3 + s % 2) * slab_cols]
                gbb = gb.bitcast(BF16)[:, 0:slab_cols]
                ta = p.on("vector", lambda e, sl=sl: e.tensor_scalar(
                    sl, sl, iac[:, 1:2], None, OP.mult), [toks2[s]], qd)
                p.drain("vector")
                tg = p.on("vector", lambda e, sl=sl, gb=gb: e.tensor_scalar(
                    gb, sl, 0.5, None, OP.is_gt), [ta], qd)
                tl = p.on("vector", lambda e, sl=sl: e.tensor_scalar(
                    sl, sl, -0.5, None, OP.is_lt), [tg], qd)
                p.drain("vector")
                tb2 = p.on("vector", lambda e, sl=sl, gb=gb, gbb=gbb: e.tensor_tensor(
                    gbb, gb, sl, OP.subtract), [tl], qd)
                cache_store(s, gbb, [tb2], done)
            return [(d, d.v) for d in dns]

        def store_up(s, srcb, waits, done):
            return p.dma(wup_c[:, s, :, :].rearrange("a p c -> p a c"),
                         srcb.rearrange("p (a c) -> p a c", a=NF),
                         waits=waits, inc=done)

        def store_dn(s, srcb, waits, done):
            return p.dma(wdn_c[:, s, :, :].rearrange("a p c -> p a c"),
                         srcb.rearrange("p (a c) -> p a c", a=ND),
                         waits=waits, inc=done)

        tok_wup = quant_weight("wu", wupT, ND, DF, D * DF, store_up, tok_xnb)
        tok_wdn = quant_weight("wd", wdnT, NF, D, DF * D, store_dn, tok_wup)

        upmm = p.ctr("upmm")
        gel = p.ctr("gel")
        s2mm = p.ctr("s2mm")
        s2ev = p.ctr("s2ev")
        i2d = p.ctr("i2d")
        dnmm = p.ctr("dnmm")
        fev = p.ctr("fev")
        fstos = [p.ctr("fsto0"), p.ctr("fsto1")]
        gtok = {}
        dn_ev = {}
        for hv in range(2):
            hoff = hv * 1024
            for ft in range(NF):
                g = hv * NF + ft
                wl_w = list(tok_wup)
                if g >= 2:
                    wl_w.append((gel, (g - 2 + 1) * 2))
                wl = p.dma(wring[:, (ft % 2) * ND * 128:(ft % 2 + 1) * ND * 128],
                           wup_c[ft].rearrange("b p c -> p b c"),
                           waits=wl_w, inc="uld%d" % (ft % 2))
                for ch in range(2):
                    ps = psU0 if ch == 0 else psU1
                    for dt in range(ND):
                        w = []
                        if dt == 0:
                            w = [wl, tok_xnb]
                            if g >= 1:
                                w.append((gel, (g - 1) * 2 + ch + 1))
                            if hv == 1 and ft == 0:
                                w.append((fev, 2 * ND))
                        p.on("tensor", lambda e, ft=ft, ch=ch, dt=dt, ps=ps, hoff=hoff,
                             st=(dt == 0), sp=(dt == ND - 1):
                             e.matmul(ps[:, :],
                                      wring[:, (ft % 2) * ND * 128 + dt * 128:(ft % 2) * ND * 128 + (dt + 1) * 128],
                                      xnb[:, dt * T + hoff + ch * 512: dt * T + hoff + ch * 512 + 512],
                                      start=st, stop=sp), w, upmm)
                    gw = [(upmm, (g * 2 + ch + 1) * ND)]
                    if hv == 0 and ft == 0 and ch == 0:
                        gw.extend(tok_wdn)
                    if hv == 1 and ft == 0:
                        gw.append((fev, hv * 2 * ND))
                    gtok[(hv, ft, ch)] = p.on("scalar", lambda e, ft=ft, ch=ch, ps=ps: e.activation(
                        hb[:, ft * 1024 + ch * 512: ft * 1024 + ch * 512 + 512],
                        ps[:, :], AF.Gelu), gw, gel)
            for ch in range(2):
                for ft in range(NF):
                    i = hv * 2 * NF + ch * NF + ft
                    sq_w = [gtok[(hv, ft, ch)]]
                    if ch * NF + ft >= 2:
                        sq_w.append((s2mm, i - 1))
                    elif hv == 1:
                        sq_w.append((s2mm, hv * 2 * NF))
                        sq_w.append((fstos[0], 64))
                        sq_w.append((fstos[1], 64))
                    t_q = p.on("scalar", lambda e, ft=ft, ch=ch: e.activation(
                        fsl[:, (ft % 2) * 1024:(ft % 2) * 1024 + 512],
                        hb[:, ft * 1024 + ch * 512: ft * 1024 + ch * 512 + 512], AF.Square),
                        sq_w, sqd)
                    waits = [t_q]
                    if ft == 0 and (ch > 0 or hv > 0):
                        waits.append((s2ev, hv * 2 + ch))
                    p.on("tensor", lambda e, ft=ft, st=(ft == 0), sp=(ft == NF - 1):
                         e.matmul(psS[0:1, :], onc[:],
                                  fsl[:, (ft % 2) * 1024:(ft % 2) * 1024 + 512],
                                  start=st, stop=sp), waits, s2mm)
                p.on("vector", lambda e, ch=ch: e.tensor_copy(
                    row[0:1, ch * 512: ch * 512 + 512], psS[0:1, :]),
                     [(s2mm, (hv * 2 + ch + 1) * NF)], s2ev)
            rr = lambda j: row[0:1, j * T: j * T + 1024]
            u1 = p.on("scalar", lambda e, rr=rr: e.activation(rr(1), rr(0), AF.Ln),
                      [(s2ev, hv * 2 + 2)], "ras")
            p.drain("scalar")
            u2 = p.on("scalar", lambda e, rr=rr: e.activation(rr(2), rr(1), AF.Exp, scale=0.5),
                      [u1], "ras")
            u3 = p.on("vector", lambda e, rr=rr: e.tensor_scalar(rr(1), rr(2), 1.0 / 64.0, 1e-8,
                                                                 OP.mult, OP.add), [u2], "rs")
            u4 = p.on("scalar", lambda e, rr=rr: e.activation(rr(2), rr(1), AF.Ln), [u3], "ras")
            p.drain("scalar")
            u5 = p.on("scalar", lambda e, rr=rr: e.activation(rr(1), rr(2), AF.Exp, scale=-1.0),
                      [u4], "ras")
            for ch in range(2):
                bw = [u5]
                if hv > 0 or ch > 0:
                    bw.append((i2d, hv * 2 + ch))
                bi = p.on("tensor", lambda e, ch=ch, rr=rr: e.matmul(
                    psI[:, :], on1[:], rr(1)[:, ch * 512:(ch + 1) * 512],
                    start=True, stop=True), bw, "bmm")
                p.on("vector", lambda e, ch=ch: e.tensor_copy(
                    i2sb[:, ch * 512:(ch + 1) * 512], psI[:, :]), [bi], i2d)
            for do in range(ND):
                gdo = hv * ND + do
                wl_w = list(tok_wdn)
                if gdo >= 2:
                    wl_w.append(dn_ev[gdo - 2])
                wl = p.dma(wdb[:, (do % 2) * NF * 128:(do % 2 + 1) * NF * 128],
                           wdn_c[do].rearrange("b p c -> p b c"),
                           waits=wl_w, inc="dld%d" % (do % 2))
                for ch in range(2):
                    ps = psU0 if ch == 0 else psU1
                    for ft in range(NF):
                        w = []
                        if ft == 0:
                            w = [wl, gtok[(hv, NF - 1, 1)], (gel, (hv * NF + NF) * 2)]
                            if gdo >= 1:
                                w.append((fev, gdo * 2))
                        p.on("tensor", lambda e, do=do, ft=ft, ch=ch, ps=ps, hoff=hoff,
                             st=(ft == 0), sp=(ft == NF - 1):
                             e.matmul(ps[:, :],
                                      wdb[:, (do % 2) * NF * 128 + ft * 128:(do % 2) * NF * 128 + (ft + 1) * 128],
                                      hb[:, ft * 1024 + ch * 512: ft * 1024 + ch * 512 + 512],
                                      start=st, stop=sp), w, dnmm)
                    ew = [(dnmm, (gdo * 2 + ch + 1) * NF), (i2d, hv * 2 + 2)]
                    if gdo >= 2:
                        ew.append((fstos[do % 2], ((gdo - 2) // 2 + 1) * 16))
                    dn_ev[gdo] = p.on("vector", lambda e, do=do, ch=ch, ps=ps: e.tensor_tensor(
                        fsl[:, (do % 2) * 1024 + ch * 512:(do % 2) * 1024 + ch * 512 + 512],
                        ps[:, :], i2sb[:, ch * 512:(ch + 1) * 512], OP.mult), ew, fev)
                p.dma(fT.rearrange("(do p) t -> do p t", p=128)[do][:, hoff:hoff + 1024],
                      fsl[:, (do % 2) * 1024:(do % 2) * 1024 + 1024],
                      waits=[(fev, (gdo + 1) * 2)], inc=fstos[do % 2])



def emit_fp4_quant(p, src_ap, dst_bf_ap, cols, scr_a, scr_b, scr_c, scl, rcp,
                   wait_tok, tag):
    nb = cols // 16
    v = lambda fn, w, i="qv": p.on("vector", fn, w, i)
    s = lambda fn, w, i="qsc": p.on("scalar", fn, w, i)
    t1 = v(lambda e: e.tensor_reduce(scl, src_ap.rearrange("p (b j) -> p b j", j=16),
                                     axis=AX, op=OP.max, apply_absolute_value=True),
           [wait_tok])
    p.drain("vector")
    t2 = v(lambda e: e.tensor_scalar(scl, scl, 1.0 / 6.0, 1e-12, OP.mult, OP.max), [t1])
    p.drain("vector")
    t3 = v(lambda e: e.reciprocal(rcp, scl), [t2])
    p.drain("vector")
    t3b = v(lambda e: e.tensor_scalar(scl, scl, 0.5, None, OP.mult), [t3])
    t4 = s(lambda e: e.activation(scr_a, src_ap, AF.Abs), [wait_tok])
    rcp_rep = rcp.unsqueeze(2).to_broadcast([128, nb, 16])
    t5 = v(lambda e: e.tensor_tensor(scr_a.rearrange("p (b j) -> p b j", j=16),
                                     scr_a.rearrange("p (b j) -> p b j", j=16),
                                     rcp_rep, OP.mult), [t3, t4])
    p.drain("vector")
    steps = [
        lambda e: e.tensor_scalar(scr_b, scr_a, 2.5, None, OP.is_gt),
        lambda e: e.scalar_tensor_tensor(scr_c, scr_a, 3.5, scr_b, OP.is_gt, OP.add),
        lambda e: e.scalar_tensor_tensor(scr_b, scr_a, 5.0, scr_c, OP.is_gt, OP.add),
        lambda e: e.scalar_tensor_tensor(scr_c, scr_a, 5.0, scr_b, OP.is_gt, OP.add),
        lambda e: e.tensor_scalar(scr_b, scr_c, 2.0, None, OP.mult),
        lambda e: e.scalar_tensor_tensor(scr_c, scr_a, 0.25, scr_b, OP.is_gt, OP.add),
        lambda e: e.scalar_tensor_tensor(scr_b, scr_a, 0.75, scr_c, OP.is_gt, OP.add),
        lambda e: e.scalar_tensor_tensor(scr_c, scr_a, 1.25, scr_b, OP.is_gt, OP.add),
        lambda e: e.scalar_tensor_tensor(scr_b, scr_a, 1.75, scr_c, OP.is_gt, OP.add),
    ]
    tprev = t5
    for fn in steps:
        tprev = v(fn, [tprev])
        p.drain("vector")
    t13 = s(lambda e: e.activation(scr_c, src_ap, AF.Sign), [tprev])
    t14 = v(lambda e: e.tensor_tensor(dst_bf_ap, scr_b, scr_c, OP.mult), [t13, tprev])
    p.drain("vector")
    scl_rep = scl.unsqueeze(2).to_broadcast([128, nb, 16])
    t15 = v(lambda e: e.tensor_tensor(dst_bf_ap.rearrange("p (b j) -> p b j", j=16),
                                      dst_bf_ap.rearrange("p (b j) -> p b j", j=16),
                                      scl_rep, OP.mult), [t14, t3b])
    return t15


def emit_ffn0(nc, p, pfx, B, xnT, wg, wu, wd, ident, fT, xq_c):
    DN = 2816
    NH = DN // 128
    CHUNKS = [512, 512, 512, 512, 512, 256]
    debug = False
    wg_dbg = None
    with ExitStack() as ctx:
        big = ctx.enter_context(nc.sbuf_tensor(pfx + "big", [128, 13056], F32))
        wgT = ctx.enter_context(nc.sbuf_tensor(pfx + "wgT", [128, ND * DN], BF16))
        wuT = ctx.enter_context(nc.sbuf_tensor(pfx + "wuT", [128, ND * DN], BF16))
        wdT = ctx.enter_context(nc.sbuf_tensor(pfx + "wdT", [128, NH * D], BF16))
        hq = ctx.enter_context(nc.sbuf_tensor(pfx + "hq", [128, DN], BF16))
        hqT = ctx.enter_context(nc.sbuf_tensor(pfx + "hqT", [128, NH * 128], BF16))
        gs = ctx.enter_context(nc.sbuf_tensor(pfx + "gs", [128, DN], BF16))
        xqr = ctx.enter_context(nc.sbuf_tensor(pfx + "xqr", [128, 2 * D], BF16))
        fst = ctx.enter_context(nc.sbuf_tensor(pfx + "fst", [128, 2 * 128], F32))
        idt = ctx.enter_context(nc.sbuf_tensor(pfx + "idt", [128, 128], F32))
        idb = ctx.enter_context(nc.sbuf_tensor(pfx + "idb", [128, 128], BF16))
        psA = ctx.enter_context(nc.psum_tensor(pfx + "psA", [128, 1024], BF16))
        psB = ctx.enter_context(nc.psum_tensor(pfx + "psB", [128, 1024], BF16))
        psG = ctx.enter_context(nc.psum_tensor(pfx + "psG", [128, 1536], F32))
        psU = ctx.enter_context(nc.psum_tensor(pfx + "psU", [128, 1536], F32))
        psAb = psA[:, 0:128]
        psBb = psB[:, 0:128]
        h_f = big[:, 0:DN]
        h_a = big[:, DN:2 * DN]
        h_b = big[:, 2 * DN:3 * DN]
        h_i = big[:, 3 * DN:4 * DN]
        h_scl = big[:, 4 * DN:4 * DN + DN // 16]
        h_rcp = big[:, 4 * DN + DN // 16:4 * DN + 2 * (DN // 16)]

        p.dma(idt[:], ident, waits=B)
        alldma = (p.ctr("dma"), p.ctr("dma").v)
        tok_idb = p.on("vector", lambda e: e.tensor_copy(idb[:], idt[:]), [alldma], "misc")

        xa_mt = p.ctr("xa_mt")
        xa_ev = p.ctr("xa_ev")
        xq_st = [p.ctr("xq_st0"), p.ctr("xq_st1")]
        qprev = alldma
        qhist = {}
        for k in range(NT):
            qprev2 = qhist.get(k - 2)
            xtok = big[:, (k % 2) * D:(k % 2) * D + D]
            xqk = big[:, 2 * D + (k % 2) * D:2 * D + (k % 2) * D + D]
            xqkb = xqk.bitcast(BF16)[:, 0:D]
            sa = big[:, 4 * D:5 * D]
            sb = big[:, 5 * D:6 * D]
            si = big[:, 6 * D:7 * D]
            sc = big[:, 7 * D:7 * D + 64]
            rc = big[:, 7 * D + 64:7 * D + 128]
            ldt = []
            for dt in range(ND):
                i = k * ND + dt
                w = [(xa_mt, i - 3)] if i >= 4 else list(B)
                ldt.append(p.dma(big[:, 8 * D + ((i % 4) * 128):8 * D + ((i % 4) * 128) + 128],
                                 xnT.rearrange("(dt p) t -> dt p t", p=128)[dt][:, k * 128:(k + 1) * 128],
                                 waits=w, inc="xl%d" % (i % 4)))
            for dt in range(ND):
                i = k * ND + dt
                bank = psG if i % 2 == 0 else psU
                w2 = [ldt[dt]]
                if i >= 2:
                    w2.append((xa_ev, i - 1))
                tk = p.on("tensor", lambda e, b=bank, i=i:
                          e.transpose(b[:, 0:128], big[:, 8 * D + ((i % 4) * 128):8 * D + ((i % 4) * 128) + 128], idt[:]),
                          w2, xa_mt)
                ew = [tk]
                if dt == 0 and k >= 2:
                    ew.append(qprev2)
                p.on("vector", lambda e, b=bank, xtok=xtok, dt=dt: e.tensor_copy(
                    xtok[:, dt * 128:(dt + 1) * 128], b[:, 0:128]), ew, xa_ev)
            qt = emit_fp4_quant(p, xtok, xqkb, D, sa, sb, si, sc, rc,
                                (xa_ev, (k + 1) * ND), "x%d" % k)
            qprev = qt
            qhist[k] = qt
            for dt in range(ND):
                i = k * ND + dt
                bank = psAb if i % 2 == 0 else psBb
                w2 = [qt, tok_idb] if dt == 0 else []
                if k * ND + dt >= 2:
                    w2.append((p.ctr("xb_ev"), p.ctr("xb_ev").v - 1))
                tk = p.on("tensor", lambda e, b=bank, xqkb=xqkb, dt=dt:
                          e.transpose(b, xqkb[:, dt * 128:(dt + 1) * 128], idb[:]), w2, "xb_mt")
                p.on("vector", lambda e, b=bank, k=k, dt=dt: e.tensor_copy(
                    xqr[:, (k % 2) * D + dt * 128:(k % 2) * D + (dt + 1) * 128], b),
                    [tk] + ([(xq_st[k % 2], 16 * (k // 2))] if (dt == 0 and k >= 2) else []),
                    "xb_ev")
            p.dma(xq_c[k], xqr[:, (k % 2) * D:(k % 2) * D + D],
                  waits=[(p.ctr("xb_ev"), (k + 1) * ND)], inc=xq_st[k % 2])

        tok_xq_all = [(c, c.v) for c in xq_st]

        def quant_w(tag, wsrc, n_slab, slab_cols, dstfn, start_tok):
            dn = p.ctr(tag + "_dn")
            prev = start_tok
            for s_i in range(n_slab):
                sa = big[:, 0:slab_cols]
                sb = big[:, slab_cols:2 * slab_cols]
                si = big[:, 2 * slab_cols:3 * slab_cols]
                sf = big[:, 3 * slab_cols:4 * slab_cols]
                sc = big[:, 4 * slab_cols:4 * slab_cols + slab_cols // 16]
                rc = big[:, 4 * slab_cols + slab_cols // 16:4 * slab_cols + 2 * (slab_cols // 16)]
                qb = big[:, 4 * slab_cols + 256:4 * slab_cols + 256 + (slab_cols + 1) // 2]
                qbb = qb.bitcast(BF16)[:, 0:slab_cols]
                ldt = p.dma(sf, wsrc.rearrange("(s p) c -> s p c", p=128)[s_i],
                            waits=[prev], inc=tag + "_ld")
                qt = emit_fp4_quant(p, sf, qbb, slab_cols, sa, sb, si, sc, rc,
                                    ldt, tag + "%d" % s_i)
                ntile = slab_cols // 128
                for j in range(ntile):
                    bank = psAb if j % 2 == 0 else psBb
                    w2 = [qt] if j == 0 else []
                    if j >= 2:
                        w2.append((dn, dn.v - 1))
                    tk = p.on("tensor", lambda e, b=bank, s2=qbb[:, j * 128:(j + 1) * 128]:
                              e.transpose(b, s2, idb[:]), w2, tag + "_mt")
                    prev = p.on("vector", lambda e, b=bank, d2=dstfn(s_i, j):
                                e.tensor_copy(d2, b), [tk], dn)
            return prev

        tok_wg = quant_w("wg", wg, NH, D,
                         lambda s_i, j: wgT[:, j * DN + s_i * 128:j * DN + (s_i + 1) * 128],
                         qprev)
        tok_wu = quant_w("wu", wu, NH, D,
                         lambda s_i, j: wuT[:, j * DN + s_i * 128:j * DN + (s_i + 1) * 128],
                         tok_wg)
        tok_wd = quant_w("wd", wd, ND, DN,
                         lambda s_i, j: wdT[:, j * D + s_i * 128:j * D + (s_i + 1) * 128],
                         tok_wu)

        gmm = p.ctr("gmm")
        gev = p.ctr("gev")
        hmt = p.ctr("hmt")
        hev = p.ctr("hev")
        dmm = p.ctr("dmm")
        devq = p.ctr("devq")
        dtr = p.ctr("dtr")
        dev2 = p.ctr("dev2")
        fso = [p.ctr("fso0"), p.ctr("fso1")]
        xlr = [p.ctr("xlr0"), p.ctr("xlr1")]
        coff = [0, 512, 1024, 1536, 2048, 2560]
        dstage = gs[:].bitcast(F32)
        last_qh = tok_wd
        last_devac = None
        last_hevac = None
        xq_tok = {}
        for tt in range(NT):
            w = list(tok_xq_all) if tt < 2 else [xq_tok[tt - 2][1]]
            xq_tok[tt] = (None, None)
            lt = p.dma(xqr[:, (tt % 2) * D:(tt % 2) * D + D], xq_c[tt],
                       waits=w + ([last_qh] if tt < 2 else []), inc=xlr[tt % 2])
            ev_toks = []
            gmm_last = None
            for grp in range(2):
                chs = [0, 1, 2] if grp == 0 else [3, 4, 5]
                for dt in range(ND):
                    for ci, c in enumerate(chs):
                        w = []
                        if dt == 0 and ci == 0:
                            w = [lt]
                            if grp == 0:
                                w.append(tok_wd if tt == 0 else last_devac)
                            else:
                                w.extend(ev_toks)
                        p.on("tensor", lambda e, tt=tt, dt=dt, c=c,
                             st=(dt == 0), sp=(dt == ND - 1):
                             e.matmul(psG[:, (c % 3) * 512:(c % 3) * 512 + CHUNKS[c]],
                                      xqr[:, (tt % 2) * D + dt * 128:(tt % 2) * D + (dt + 1) * 128],
                                      wgT[:, dt * DN + coff[c]: dt * DN + coff[c] + CHUNKS[c]],
                                      start=st, stop=sp), w, gmm)
                        gmm_last = p.on("tensor", lambda e, tt=tt, dt=dt, c=c,
                                        st=(dt == 0), sp=(dt == ND - 1):
                                        e.matmul(psU[:, (c % 3) * 512:(c % 3) * 512 + CHUNKS[c]],
                                                 xqr[:, (tt % 2) * D + dt * 128:(tt % 2) * D + (dt + 1) * 128],
                                                 wuT[:, dt * DN + coff[c]: dt * DN + coff[c] + CHUNKS[c]],
                                                 start=st, stop=sp), [], gmm)
                base = gmm.v
                for c in chs:
                    off = (c % 3) * 512
                    sw = [(gmm, base)]
                    if c == chs[0] and tt > 0:
                        sw.append(last_qh)
                    tsil = p.on("scalar", lambda e, c=c, off=off: e.activation(
                        gs[:, coff[c]:coff[c] + CHUNKS[c]], psG[:, off:off + CHUNKS[c]],
                        AF.Silu), sw, "gsil")
                    ev_toks.append(p.on("vector", lambda e, c=c, off=off: e.tensor_tensor(
                        h_f[:, coff[c]:coff[c] + CHUNKS[c]], psU[:, off:off + CHUNKS[c]],
                        gs[:, coff[c]:coff[c] + CHUNKS[c]], OP.mult), [tsil], gev))
            qh = emit_fp4_quant(p, h_f, hq[:], DN, h_a, h_b, h_i, h_scl, h_rcp,
                                ev_toks[-1], "h%d" % tt)
            xq_tok[tt] = (None, qh)
            last_qh = qh
            htoks = []
            for j in range(NH):
                bank = psAb if j % 2 == 0 else psBb
                w2 = [qh] if j == 0 else []
                if j >= 2:
                    w2.append((hev, hev.v - 1))
                if tt > 0 and j < 2:
                    w2.append(last_hevac)
                tk = p.on("tensor", lambda e, b=bank, s2=hq[:, j * 128:(j + 1) * 128]:
                          e.transpose(b, s2, idb[:]), w2, hmt)
                htoks.append(p.on("vector", lambda e, b=bank, d2=hqT[:, j * 128:(j + 1) * 128]:
                                  e.tensor_copy(d2, b), [tk], hev))
            last_hevac = htoks[-1]
            psD = psG
            for j in range(NH):
                w = list(ev_toks) + [htoks[-1]] if j == 0 else []
                for half in range(2):
                    p.on("tensor", lambda e, j=j, half=half, st=(j == 0), sp=(j == NH - 1):
                         e.matmul(psD[:, half * 512: half * 512 + 512],
                                  hqT[:, j * 128:(j + 1) * 128],
                                  wdT[:, j * D + half * 512: j * D + half * 512 + 512],
                                  start=st, stop=sp), w if half == 0 else [], dmm)
            base_d = dmm.v
            de = p.on("vector", lambda e: e.tensor_copy(dstage[:, 0:1024], psD[:, 0:1024]),
                      [(dmm, base_d)] + list(ev_toks), devq)
            for do in range(ND):
                bank = psU[:, 0:128] if do % 2 == 0 else psU[:, 512:640]
                w2 = [de]
                if do >= 2:
                    w2.append((dev2, dev2.v - 1))
                tk = p.on("tensor", lambda e, b=bank, do=do:
                          e.transpose(b, dstage[:, do * 128:(do + 1) * 128], idt[:]),
                          w2, dtr)
                e2 = p.on("vector", lambda e, b=bank, do=do: e.tensor_copy(
                    fst[:, (do % 2) * 128:(do % 2) * 128 + 128], b),
                    [tk] + ([(fso[do % 2], (((tt * ND + do) - 2) // 2 + 1) * 16)]
                            if tt * ND + do >= 2 else []), dev2)
                p.dma(fT.rearrange("(do p) t -> do p t", p=128)[do][:, tt * 128:(tt + 1) * 128],
                      fst[:, (do % 2) * 128:(do % 2) * 128 + 128],
                      waits=[e2], inc=fso[do % 2])
                last_devac = e2



def emit_attn0(nc, p, pfx, B, xnT, Wq, Wk, Wv, Wo, Wdkv, Wuk, Wuv, decay,
               ones1x_d, aT):
    NHD = 16
    SSC = float(1.0 / (np.sqrt(np.float64(np.float32(1024.0))) + 1e-08))
    with ExitStack() as ctx:
        xnb = ctx.enter_context(nc.sbuf_tensor(pfx + "xnb", [128, ND * T], BF16))
        QT = ctx.enter_context(nc.sbuf_tensor(pfx + "QT", [128, ND * T], BF16))
        KrT = ctx.enter_context(nc.sbuf_tensor(pfx + "KrT", [128, ND * T], BF16))
        vtok = ctx.enter_context(nc.sbuf_tensor(pfx + "vtok", [128, NT * NHD * 65], BF16))
        latT = ctx.enter_context(nc.sbuf_tensor(pfx + "latT", [128, 2 * T], BF16))
        wbf = ctx.enter_context(nc.sbuf_tensor(pfx + "wbf", [128, ND * D], BF16))
        wst = ctx.enter_context(nc.sbuf_tensor(pfx + "wst", [128, 2 * D], F32))
        expt = ctx.enter_context(nc.sbuf_tensor(pfx + "expt", [128, NT * 512], BF16))
        decb = ctx.enter_context(nc.sbuf_tensor(pfx + "decb", [128, NT * 512], BF16))
        sdb = ctx.enter_context(nc.sbuf_tensor(pfx + "sdb", [128, 2 * 512], BF16))
        aout = ctx.enter_context(nc.sbuf_tensor(pfx + "aout", [128, 2 * 512], F32))
        onesr = ctx.enter_context(nc.sbuf_tensor(pfx + "onesr", [1, 128], F32))
        psS0 = ctx.enter_context(nc.psum_tensor(pfx + "psS0", [128, 512], F32))
        psS1 = ctx.enter_context(nc.psum_tensor(pfx + "psS1", [128, 512], F32))
        psAV = ctx.enter_context(nc.psum_tensor(pfx + "psAV", [128, 512], F32))
        psM = ctx.enter_context(nc.psum_tensor(pfx + "psM", [128, 512], F32))
        psP = ctx.enter_context(nc.psum_tensor(pfx + "psP", [128, 512], F32))
        psP2 = ctx.enter_context(nc.psum_tensor(pfx + "psP2", [128, 512], F32))
        aoT = xnb
        latf = latT[:].bitcast(F32)
        rden = latf[0:1, 0:1024]
        ones1x = None

        stage = QT[:].bitcast(F32)
        stage2 = KrT[:].bitcast(F32)
        p.dma(stage[:, 0:ND * T // 2].rearrange("p (dt t) -> p dt t", dt=ND // 2),
              xnT.rearrange("(dt p) t -> p dt t", p=128)[:, 0:ND // 2, :], waits=B)
        p.dma(stage2[:, 0:ND * T // 2].rearrange("p (dt t) -> p dt t", dt=ND // 2),
              xnT.rearrange("(dt p) t -> p dt t", p=128)[:, ND // 2:, :], waits=B)
        p.dma(onesr[:], ones1x_d, waits=B)
        alldma = (p.ctr("dma"), p.ctr("dma").v)
        cast_dn = p.ctr("cast")
        for dt in range(ND):
            st_ap = (stage if dt < 4 else stage2)[:, (dt % 4) * T:(dt % 4) * T + T]
            p.on("vector", lambda e, dt=dt, st_ap=st_ap: e.tensor_copy(
                xnb[:, dt * T:(dt + 1) * T], st_ap), [alldma], cast_dn)
        tok_xnb = (cast_dn, ND)

        wld = [p.ctr("wl0"), p.ctr("wl1")]
        wcast = p.ctr("wcast")

        def cast_w(wsrc, nslab, width, first_w):
            toks = []
            for s_i in range(nslab):
                w = [first_w] if s_i < 2 else [(wcast, wcast.v - 1)]
                lt = p.dma(wst[:, (s_i % 2) * D:(s_i % 2) * D + width],
                           wsrc.rearrange("(s p) c -> s p c", p=128)[s_i],
                           waits=w, inc=wld[s_i % 2])
                toks.append(p.on("vector", lambda e, s_i=s_i, width=width: e.tensor_copy(
                    wbf[:, s_i * width:(s_i + 1) * width],
                    wst[:, (s_i % 2) * D:(s_i % 2) * D + width]), [lt], wcast))
            return toks[-1]

        def cast_w_sum(wa, wb, nslab, first_w):
            toks = []
            for s_i in range(nslab):
                w = [first_w] if s_i == 0 else [toks[-1]]
                la = p.dma(wst[:, 0:D], wa.rearrange("(s p) c -> s p c", p=128)[s_i],
                           waits=w, inc=wld[0])
                lb = p.dma(wst[:, D:2 * D], wb.rearrange("(s p) c -> s p c", p=128)[s_i],
                           waits=w, inc=wld[1])
                toks.append(p.on("vector", lambda e, s_i=s_i: e.tensor_tensor(
                    wbf[:, s_i * D:(s_i + 1) * D], wst[:, 0:D], wst[:, D:2 * D], OP.add),
                    [la, lb], wcast))
            return toks[-1]

        def project(src_bf, dst_fn, wtok, tag, nk=ND, nmt=ND, width=D, store_fn=None,
                    egate_fn=None):
            mmc = p.ctr(tag + "m")
            evc = p.ctr(tag + "e")
            for mt in range(nmt):
                for ch in range(4):
                    i = mt * 4 + ch
                    ps = psP if i % 2 == 0 else psP2
                    for dt in range(nk):
                        w = [wtok] if (i == 0 and dt == 0) else []
                        if dt == 0 and i >= 2:
                            w.append((evc, i - 1))
                        p.on("tensor", lambda e, mt=mt, ch=ch, dt=dt, ps=ps,
                             st=(dt == 0), sp=(dt == nk - 1):
                             e.matmul(ps[:, :],
                                      wbf[:, dt * width + mt * 128:dt * width + (mt + 1) * 128],
                                      src_bf[:, dt * T + ch * 512: dt * T + ch * 512 + 512],
                                      start=st, stop=sp), w, mmc)
                    ew = [(mmc, (i + 1) * nk)]
                    if egate_fn is not None:
                        g = egate_fn(i)
                        if g is not None:
                            ew.append(g)
                    tok_e = p.on("scalar", lambda e, mt=mt, ch=ch, ps=ps: e.activation(
                        dst_fn(mt, ch), ps[:, :], AF.Copy),
                        ew, evc)
                    if store_fn is not None:
                        store_fn(mt, ch, tok_e)
            return (evc, nmt * 4)

        twq = cast_w(Wq, ND, D, tok_xnb)
        tok_q = project(xnb, lambda mt, ch: QT[:, mt * T + ch * 512: mt * T + ch * 512 + 512],
                        twq, "q")
        twkv = cast_w_sum(Wk, Wv, ND, tok_q)
        kvs = vtok[:].rearrange("p c -> p c")[:, 0:ND * T]
        tok_kv = project(xnb, lambda mt, ch: kvs[:, mt * T + ch * 512: mt * T + ch * 512 + 512],
                         twkv, "kv")
        twd = cast_w(Wdkv, ND, 256, tok_kv)
        tok_lat = project(kvs, lambda mt, ch: latT[:, mt * T + ch * 512: mt * T + ch * 512 + 512],
                          twd, "lat", nk=ND, nmt=2, width=256)
        twuk = cast_w(Wuk, 2, D, tok_lat)
        tok_kr = project(latT, lambda mt, ch: KrT[:, mt * T + ch * 512: mt * T + ch * 512 + 512],
                         twuk, "kr", nk=2, nmt=ND, width=D)
        twuv = cast_w(Wuv, 2, D, tok_kr)
        vview = vtok[:].rearrange("p (tt h c) -> p tt h c", tt=NT, h=NHD, c=65)
        vmm = p.ctr("vmm")
        vev = p.ctr("vev")
        for tt in range(NT):
            for ch in range(2):
                i = tt * 2 + ch
                ps = psP if i % 2 == 0 else psP2
                for ks in range(2):
                    w = [twuv, tok_kr] if (i == 0 and ks == 0) else []
                    if ks == 0 and i >= 2:
                        w.append((vev, i - 1))
                    p.on("tensor", lambda e, tt=tt, ch=ch, ks=ks, ps=ps,
                         st=(ks == 0), sp=(ks == 1):
                         e.matmul(ps[:, :],
                                  latT[:, ks * T + tt * 128:ks * T + (tt + 1) * 128],
                                  wbf[:, ks * D + ch * 512: ks * D + ch * 512 + 512],
                                  start=st, stop=sp), w, vmm)
                p.on("scalar", lambda e, tt=tt, ch=ch, ps=ps: e.activation(
                    vview[:, tt, ch * 8:(ch + 1) * 8, 0:64],
                    ps[:, :].rearrange("p (h c) -> p h c", c=64), AF.Copy),
                    [(vmm, (i + 1) * 2)], vev)
        tok_v = (vev, NT * 2)
        tok_ones = p.on("vector", lambda e: e.memset(
            vview[:, :, :, 64:65], 1.0), [tok_v], "vone")

        dld = p.ctr("dld")
        smm = p.ctr("smm")
        sdv = p.ctr("sdv")
        sev = p.ctr("sev")
        avm = p.ctr("avm")
        dcp = p.ctr("dcp")
        dln = p.ctr("dln")
        dbc = p.ctr("dbc")
        aev = p.ctr("aev")
        anm = p.ctr("anm")
        hq_done = {}
        for qc in range(4):
            wdec = [alldma] if qc == 0 else [hq_done[(qc - 1, NHD - 1)]]
            tdec = p.dma(decb[:].rearrange("p (kt c) -> p kt c", kt=NT),
                         decay[12 - 4 * qc:28 - 4 * qc], waits=wdec, inc=dld)
            for h in range(NHD):
                base = (h % 2) * 64
                prevhq = None
                if (qc, h) != (0, 0):
                    prevhq = hq_done[(qc, h - 1)] if h > 0 else hq_done[(qc - 1, NHD - 1)]
                se_toks = []
                for kt in range(NT):
                    ps = psS0 if kt % 2 == 0 else psS1
                    w = [tok_q, tok_kr, tdec]
                    if prevhq is not None:
                        w.append(prevhq)
                    if kt >= 2:
                        w = [(sev, sev.v - 1)]
                    tsm = p.on("tensor", lambda e, h=h, qc=qc, kt=kt, ps=ps, base=base: e.matmul(
                        ps[:, :],
                        KrT[:, (h // 2) * T + kt * 128:(h // 2) * T + (kt + 1) * 128][base:base + 64, :],
                        QT[:, (h // 2) * T + qc * 512:(h // 2) * T + qc * 512 + 512][base:base + 64, :],
                        start=True, stop=True, tile_position=(base, 0)), w, smm)
                    wv = [tsm]
                    if kt >= 2:
                        wv.append(se_toks[kt - 2])
                    tdv = p.on("vector", lambda e, kt=kt, ps=ps: e.tensor_tensor(
                        sdb[:, (kt % 2) * 512:(kt % 2) * 512 + 512], ps[:, :],
                        decb[:, kt * 512:(kt + 1) * 512], OP.mult),
                        wv, sdv)
                    tse = p.on("scalar", lambda e, kt=kt: e.activation(
                        expt[:, kt * 512:(kt + 1) * 512],
                        sdb[:, (kt % 2) * 512:(kt % 2) * 512 + 512], AF.Exp,
                        scale=SSC), [tdv], sev)
                    se_toks.append(tse)
                for kt in range(NT):
                    w = [se_toks[kt], tok_ones]
                    if kt == 0 and prevhq is not None:
                        w.append((anm, anm.v))
                    p.on("tensor", lambda e, kt=kt, h=h, st=(kt == 0), sp=(kt == NT - 1):
                         e.matmul(psAV[0:65, :],
                                  vtok[:, kt * NHD * 65 + h * 65: kt * NHD * 65 + (h + 1) * 65],
                                  expt[:, kt * 512:(kt + 1) * 512],
                                  start=st, stop=sp), w, avm)
                t1 = p.on("vector", lambda e: e.tensor_copy(rden[0:1, 0:512], psAV[64:65, :]),
                          [(avm, avm.v), (tok_v if (qc, h) == (0, 0) else (avm, avm.v))], dcp)
                p.drain("vector")
                t2 = p.on("scalar", lambda e: e.activation(rden[0:1, 512:1024], rden[0:1, 0:512],
                                                           AF.Ln), [t1], dln)
                p.drain("scalar")
                t3 = p.on("scalar", lambda e: e.activation(rden[0:1, 0:512], rden[0:1, 512:1024],
                                                           AF.Exp, scale=-1.0), [t2], dln)
                t4 = p.on("tensor", lambda e: e.matmul(psM[0:64, :], onesr[0:1, 0:64],
                                                       rden[0:1, 0:512],
                                                       start=True, stop=True), [t3], dbc)
                t5 = p.on("scalar", lambda e: e.activation(aout[0:64, 0:512], psAV[0:64, :],
                                                           AF.Copy), [t3], aev)
                hq_done[(qc, h)] = p.on("vector", lambda e, h=h, qc=qc, base=base: e.tensor_tensor(
                    aoT[:, (h // 2) * T + qc * 512:(h // 2) * T + qc * 512 + 512][base:base + 64, :],
                    aout[0:64, 0:512], psM[0:64, :], OP.mult), [t4, t5], anm)

        twO = cast_w(Wo, ND, D, hq_done[(3, NHD - 1)])
        sto = [p.ctr("so0"), p.ctr("so1")]

        def st_fn(mt, ch, tok_e):
            i = mt * 4 + ch
            p.dma(aT.rearrange("(mt p) t -> mt p t", p=128)[mt][:, ch * 512:(ch + 1) * 512],
                  aout[:, (i % 2) * 512:(i % 2) * 512 + 512], waits=[tok_e], inc=sto[i % 2])

        def dstf(mt, ch):
            i = mt * 4 + ch
            return aout[:, (i % 2) * 512:(i % 2) * 512 + 512]

        tok_o = project(aoT, dstf, twO, "o", store_fn=st_fn,
                        egate_fn=lambda i: (sto[i % 2], ((i - 2) // 2 + 1) * 16) if i >= 2 else None)



def emit_attn1(nc, p, pfx, B, xnT, WinT, WrT, WkvT, WbT, bbt, WqT, WoT,
               blk4, sel4, ident, aT, knD, qD, r4D, uwD, smD):
    NS = 32
    CH = 32
    NCH = T // CH
    with ExitStack() as ctx:
        xnb = ctx.enter_context(nc.sbuf_tensor(pfx + "xnb", [128, ND * T], BF16))
        xpb = ctx.enter_context(nc.sbuf_tensor(pfx + "xpb", [128, ND * T], BF16))
        wbf = ctx.enter_context(nc.sbuf_tensor(pfx + "wbf", [128, ND * D], BF16))
        wst = ctx.enter_context(nc.sbuf_tensor(pfx + "wst", [128, 2 * D], F32))
        kvf = ctx.enter_context(nc.sbuf_tensor(pfx + "kvf", [128, 2 * T], F32))
        betab = ctx.enter_context(nc.sbuf_tensor(pfx + "betab", [128, T], F32))
        qf = ctx.enter_context(nc.sbuf_tensor(pfx + "qf", [32, T], F32))
        lgt = ctx.enter_context(nc.sbuf_tensor(pfx + "lgt", [4, T], F32))
        rn4 = ctx.enter_context(nc.sbuf_tensor(pfx + "rn4", [4, T], F32))
        rsc = ctx.enter_context(nc.sbuf_tensor(pfx + "rsc", [4, T], F32))
        u4 = ctx.enter_context(nc.sbuf_tensor(pfx + "u4", [4, T], F32))
        s4 = ctx.enter_context(nc.sbuf_tensor(pfx + "s4", [4, T], F32))
        rnb = ctx.enter_context(nc.sbuf_tensor(pfx + "rnb", [128, T], F32))
        sqs = ctx.enter_context(nc.sbuf_tensor(pfx + "sqs", [128, 512], F32))
        knr = ctx.enter_context(nc.sbuf_tensor(pfx + "knr", [128, 2 * CH * 33], F32))
        qr = ctx.enter_context(nc.sbuf_tensor(pfx + "qr", [128, 2 * CH * NS], F32))
        tok = ctx.enter_context(nc.sbuf_tensor(pfx + "tok", [128, 64], F32))
        stg = ctx.enter_context(nc.sbuf_tensor(pfx + "stg", [128, 320], F32))
        cellb = ctx.enter_context(nc.sbuf_tensor(pfx + "cellb", [32, T], BF16))
        aor = ctx.enter_context(nc.sbuf_tensor(pfx + "aor", [128, 2 * 512], F32))
        idt = ctx.enter_context(nc.sbuf_tensor(pfx + "idt", [128, 128], F32))
        blkb = ctx.enter_context(nc.sbuf_tensor(pfx + "blkb", [128, 4], F32))
        selb = ctx.enter_context(nc.sbuf_tensor(pfx + "selb", [128, 32], F32))
        bbs = ctx.enter_context(nc.sbuf_tensor(pfx + "bbs", [128, 1], F32))
        Sst = ctx.enter_context(nc.sbuf_tensor(pfx + "Sst", [128, 33], F32))
        scr = ctx.enter_context(nc.sbuf_tensor(pfx + "scr", [128, 33], F32))
        scr2 = ctx.enter_context(nc.sbuf_tensor(pfx + "scr2", [128, NS], F32))
        z0t = ctx.enter_context(nc.sbuf_tensor(pfx + "z0t", [128, NS], F32))
        spt = ctx.enter_context(nc.sbuf_tensor(pfx + "spt", [128, NS], F32))
        zbt = ctx.enter_context(nc.sbuf_tensor(pfx + "zbt", [128, NS], F32))
        Ut = ctx.enter_context(nc.sbuf_tensor(pfx + "Ut", [128, NS], F32))
        nsum = ctx.enter_context(nc.sbuf_tensor(pfx + "nsum", [128, 4], F32))
        psP = ctx.enter_context(nc.psum_tensor(pfx + "psP", [128, 512], F32))
        psP2 = ctx.enter_context(nc.psum_tensor(pfx + "psP2", [128, 512], F32))
        psZ = ctx.enter_context(nc.psum_tensor(pfx + "psZ", [128, 512], F32))
        psB = ctx.enter_context(nc.psum_tensor(pfx + "psB", [128, 512], F32))
        xnbf = xnb[:].bitcast(F32)
        uwb = lambda: xnbf[:, 0:T]
        smb = lambda: xnbf[:, T:2 * T]
        wmb = lambda: xnbf[:, 2 * T:3 * T]
        outsq = lambda: xnbf[:, 3 * T:4 * T]

        p.dma(idt[:], ident, waits=B)
        p.dma(blkb[:], blk4, waits=B)
        p.dma(selb[:], sel4, waits=B)
        p.dma(bbs[:], bbt, waits=B)
        alldma = (p.ctr("dma"), p.ctr("dma").v)
        cast_dn = p.ctr("cast")
        xst = [p.ctr("xst0"), p.ctr("xst1")]
        for g in range(4):
            w = list(B) if g == 0 else [(cast_dn, 2 * g)]
            lt = p.dma(kvf[:, 0:2 * T].rearrange("p (a t) -> p a t", a=2),
                       xnT.rearrange("(dt p) t -> p dt t", p=128)[:, 2 * g:2 * g + 2, :],
                       waits=w, inc=xst[g % 2])
            for j in range(2):
                p.on("vector", lambda e, g=g, j=j: e.tensor_copy(
                    xnb[:, (2 * g + j) * T:(2 * g + j + 1) * T],
                    kvf[:, j * T:(j + 1) * T]), [lt], cast_dn)
        tok_xnb = (cast_dn, ND)

        wld = [p.ctr("wl0"), p.ctr("wl1")]
        wcast = p.ctr("wcast")

        def cast_w(wsrc, nslab, width, first_w):
            toks = []
            for s_i in range(nslab):
                w = [first_w] if s_i < 2 else [(wcast, wcast.v - 1)]
                lt = p.dma(wst[:, (s_i % 2) * D:(s_i % 2) * D + width],
                           wsrc.rearrange("(s p) c -> s p c", p=128)[s_i],
                           waits=w, inc=wld[s_i % 2])
                toks.append(p.on("vector", lambda e, s_i=s_i, width=width: e.tensor_copy(
                    wbf[:, s_i * width:(s_i + 1) * width],
                    wst[:, (s_i % 2) * D:(s_i % 2) * D + width]), [lt], wcast))
            return toks[-1]

        def project(src_bf, dst_fn, wtok, tag, nk=ND, nmt=ND, width=D, act=None,
                    bias=None, mwidth=128):
            mmc = p.ctr(tag + "m")
            evc = p.ctr(tag + "e")
            for mt in range(nmt):
                for ch in range(4):
                    i = mt * 4 + ch
                    ps = psP if i % 2 == 0 else psP2
                    for dt in range(nk):
                        w = [wtok] if (i == 0 and dt == 0) else []
                        if dt == 0 and i >= 2:
                            w.append((evc, i - 1))
                        p.on("tensor", lambda e, mt=mt, ch=ch, dt=dt, ps=ps,
                             st=(dt == 0), sp=(dt == nk - 1):
                             e.matmul(ps[0:mwidth, :],
                                      wbf[:, dt * width + mt * mwidth:dt * width + (mt + 1) * mwidth],
                                      src_bf[:, dt * T + ch * 512: dt * T + ch * 512 + 512],
                                      start=st, stop=sp), w, mmc)
                    if bias is not None:
                        p.on("scalar", lambda e, mt=mt, ch=ch, ps=ps: e.activation(
                            dst_fn(mt, ch), ps[0:mwidth, :], act, bias=bias[:, 0:1]),
                            [(mmc, (i + 1) * nk), alldma], evc)
                    else:
                        p.on("scalar", lambda e, mt=mt, ch=ch, ps=ps: e.activation(
                            dst_fn(mt, ch), ps[0:mwidth, :],
                            act if act is not None else AF.Copy),
                            [(mmc, (i + 1) * nk)], evc)
            return (evc, nmt * 4)

        twin = cast_w(WinT, ND, D, tok_xnb)
        tok_xp = project(xnb, lambda mt, ch: xpb[:, mt * T + ch * 512: mt * T + ch * 512 + 512],
                         twin, "xp", act=AF.Silu)
        twb = cast_w(WbT, ND, 128, tok_xp)
        tok_b = project(xpb, lambda mt, ch: betab[:, ch * 512:(ch + 1) * 512],
                        twb, "bt", nmt=1, width=128, act=AF.Sigmoid, bias=bbs)
        negb = p.ctr("negb")
        for chn in range(4):
            p.on("vector", lambda e, chn=chn: e.tensor_scalar(
                betab[:, chn * 512:(chn + 1) * 512],
                betab[:, chn * 512:(chn + 1) * 512], -1.0, None, OP.mult),
                [tok_b], negb)
        tok_bn = (negb, 4)
        twkv = cast_w(WkvT, ND, 256, tok_b)
        tok_kv = project(xpb, lambda mt, ch: kvf[:, mt * T + ch * 512: mt * T + ch * 512 + 512],
                         twkv, "kv", nmt=2, width=256)
        twr = cast_w(WrT, ND, 4, tok_kv)
        tok_r = project(xpb, lambda mt, ch: lgt[0:4, ch * 512:(ch + 1) * 512],
                        twr, "rt", nmt=1, width=4, mwidth=4)
        twq = cast_w(WqT, ND, 32, tok_r)
        tok_qp = project(xpb, lambda mt, ch: qf[0:32, ch * 512:(ch + 1) * 512],
                         twq, "qp", nmt=1, width=32, mwidth=32)

        kq = p.ctr("kq")
        knm = p.ctr("knm")
        kne = p.ctr("kne")
        for ch in range(4):
            tsq = p.on("scalar", lambda e, ch=ch: e.activation(
                sqs[:, 0:512], kvf[:, ch * 512:(ch + 1) * 512], AF.Square),
                [tok_kv] if ch == 0 else [(knm, ch)], kq)
            tm = p.on("tensor", lambda e, ch=ch: e.matmul(
                psB[0:4, :], blkb[:], sqs[:, 0:512],
                start=True, stop=True), [tsq, alldma] + ([(kne, ch)] if ch else []), knm)
            p.on("vector", lambda e, ch=ch: e.tensor_copy(
                rn4[0:4, ch * 512:(ch + 1) * 512], psB[0:4, :]), [tm], kne)
        n1 = p.on("scalar", lambda e: e.activation(rsc[0:4, :], rn4[0:4, :], AF.Ln),
                  [(kne, 4)], "nrs")
        p.drain("scalar")
        n2 = p.on("scalar", lambda e: e.activation(rn4[0:4, :], rsc[0:4, :], AF.Exp, scale=0.5),
                  [n1], "nrs")
        n3 = p.on("vector", lambda e: e.tensor_scalar(rn4[0:4, :], rn4[0:4, :], 1e-6, None,
                                                      OP.add), [n2], "nrv")
        n4 = p.on("scalar", lambda e: e.activation(rsc[0:4, :], rn4[0:4, :], AF.Ln), [n3], "nrs")
        p.drain("scalar")
        n5 = p.on("scalar", lambda e: e.activation(rn4[0:4, :], rsc[0:4, :], AF.Exp, scale=-1.0),
                  [n4], "nrs")
        b1 = p.dma(r4D, rn4[0:4, :], waits=[n5], inc="b1a")
        b2toks = []
        for n in range(4):
            b2toks.append(p.dma(rnb[n * 32:(n + 1) * 32, :],
                                r4D[n:n + 1, :].to_broadcast([32, T]), waits=[b1], inc="b2r"))
        tok_rnb = (p.ctr("b2r"), p.ctr("b2r").v)
        knd = p.ctr("knd")
        for ch in range(4):
            p.on("vector", lambda e, ch=ch: e.tensor_tensor(
                kvf[:, ch * 512:(ch + 1) * 512], kvf[:, ch * 512:(ch + 1) * 512],
                rnb[:, ch * 512:(ch + 1) * 512], OP.mult), [tok_rnb, (kq, 4)], knd)
        tok_kn = (knd, 4)

        tmm = p.ctr("tmm")
        tev = p.ctr("tev")
        t2v = p.ctr("t2v")
        t2s = p.ctr("t2s")
        bmm = p.ctr("bmm")
        bev = p.ctr("bev")
        p.on("vector", lambda e: e.memset(tok[:, 8:16], -1e30), [alldma], "ms")
        for k in range(NT):
            w = [tok_r, alldma, (kne, 4)] if k == 0 else [(bev, 2 * k)]
            tk = p.on("tensor", lambda e, k=k: e.transpose(
                psB[0:128, 0:4], lgt[0:4, k * 128:(k + 1) * 128], idt[0:4, 0:4]), w, tmm)
            te = p.on("vector", lambda e: e.tensor_copy(tok[:, 0:4], psB[0:128, 0:4]),
                      [tk], tev)
            c1 = p.on("vector", lambda e: e.tensor_copy(tok[:, 8:12], tok[:, 0:4]), [te], t2v)
            p.drain("vector")
            c2 = p.on("vector", lambda e: e.max(out=tok[:, 16:24], in_=tok[:, 8:16]), [c1], t2v)
            p.drain("vector")
            c3 = p.on("vector", lambda e: e.tensor_scalar(tok[:, 24:28], tok[:, 0:4],
                                                          tok[:, 17:18], None, OP.is_ge),
                      [c2], t2v)
            c4 = p.on("scalar", lambda e: e.activation(tok[:, 28:32], tok[:, 0:4], AF.Exp),
                      [c2], t2s)
            c5 = p.on("vector", lambda e: e.reduce_sum(tok[:, 32:33], tok[:, 28:32], axis=AX),
                      [c4, c3], t2v)
            p.drain("vector")
            c6 = p.on("vector", lambda e: e.reciprocal(tok[:, 33:34], tok[:, 32:33]), [c5], t2v)
            p.drain("vector")
            c7 = p.on("vector", lambda e: e.tensor_scalar(tok[:, 34:38], tok[:, 28:32],
                                                          tok[:, 33:34], None, OP.mult),
                      [c6], t2v)
            p.drain("vector")
            c8 = p.on("vector", lambda e: e.tensor_tensor(tok[:, 38:42], tok[:, 24:28],
                                                          tok[:, 34:38], OP.mult), [c7], t2v)
            p.drain("vector")
            c9 = p.on("vector", lambda e: e.reduce_sum(tok[:, 42:43], tok[:, 38:42], axis=AX),
                      [c8], t2v)
            p.drain("vector")
            c10 = p.on("vector", lambda e: e.tensor_scalar(tok[:, 43:44], tok[:, 42:43], 1e-8,
                                                           None, OP.add), [c9], t2v)
            p.drain("vector")
            c11 = p.on("vector", lambda e: e.reciprocal(tok[:, 44:45], tok[:, 43:44]), [c10], t2v)
            p.drain("vector")
            c12 = p.on("vector", lambda e: e.tensor_scalar(tok[:, 45:49], tok[:, 38:42],
                                                           tok[:, 44:45], -1.0,
                                                           OP.mult, OP.mult),
                       [c11], t2v)
            p.drain("vector")
            tku = p.on("tensor", lambda e, k=k: e.transpose(psB[0:4, 128:256],
                                                             tok[:, 45:49], idt[:]),
                       [c12], bmm)
            p.on("vector", lambda e, k=k: e.tensor_copy(u4[0:4, k * 128:(k + 1) * 128],
                                                        psB[0:4, 128:256]),
                 [tku], bev)
            tks = p.on("tensor", lambda e, k=k: e.transpose(psB[0:4, 256:384],
                                                            tok[:, 34:38], idt[:]),
                       [c12], bmm)
            p.on("vector", lambda e, k=k: e.tensor_copy(s4[0:4, k * 128:(k + 1) * 128],
                                                        psB[0:4, 256:384]),
                 [tks], bev)
        bu = p.dma(uwD, u4[0:4, :], waits=[(bev, 2 * NT), tok_xp], inc="b1b")
        bs = p.dma(smD, s4[0:4, :], waits=[(bev, 2 * NT)], inc="b1c")
        for n in range(4):
            p.dma(uwb()[n * 32:(n + 1) * 32, :],
                  uwD[n:n + 1, :].to_broadcast([32, T]), waits=[bu], inc="b2u")
            p.dma(smb()[n * 32:(n + 1) * 32, :],
                  smD[n:n + 1, :].to_broadcast([32, T]), waits=[bs], inc="b2u")
        tok_uw = (p.ctr("b2u"), p.ctr("b2u").v)
        wmd = p.ctr("wmd")
        for ch in range(4):
            p.on("vector", lambda e, ch=ch: e.tensor_scalar(
                wmb()[:, ch * 512:(ch + 1) * 512], uwb()[:, ch * 512:(ch + 1) * 512],
                1.0, 1.0, OP.mult, OP.add), [tok_uw], wmd)
        tok_wm = (wmd, 4)

        kdm = p.ctr("kdm")
        kde = p.ctr("kde")
        kds = [p.ctr("kds0"), p.ctr("kds1")]
        for k in range(NT):
            w = [tok_kn, (bev, 2 * NT)] if k == 0 else []
            if k >= 2:
                w.append((kds[k % 2], (k // 2) * 16))
            tk = p.on("tensor", lambda e, k=k: e.transpose(
                psB[:, 0:128], kvf[:, k * 128:(k + 1) * 128], idt[:]),
                w + ([(kde, k)] if k >= 1 else []), kdm)
            te = p.on("vector", lambda e, k=k: e.tensor_scalar(
                stg[:, (k % 2) * 128:(k % 2) * 128 + 128], psB[:, 0:128],
                -1.0, None, OP.mult),
                [tk] + ([(kds[k % 2], (k // 2) * 16)] if k >= 2 else []), kde)
            p.dma(knD.rearrange("(k p) c -> k p c", p=128)[k],
                  stg[:, (k % 2) * 128:(k % 2) * 128 + 128],
                  waits=[te], inc=kds[k % 2])
        qdm = p.ctr("qdm")
        qde = p.ctr("qde")
        qds = [p.ctr("qds0"), p.ctr("qds1")]
        for k in range(NT):
            w = [tok_qp, (kde, NT)] if k == 0 else []
            tk = p.on("tensor", lambda e, k=k: e.transpose(
                psB[0:128, 128:160], qf[0:32, k * 128:(k + 1) * 128], idt[0:32, 0:32]),
                w + ([(qde, k)] if k >= 1 else []), qdm)
            te = p.on("vector", lambda e, k=k: e.tensor_copy(
                stg[:, 256 + (k % 2) * 32:256 + (k % 2) * 32 + 32], psB[:, 128:160]),
                [tk] + ([(qds[k % 2], (k // 2) * 16)] if k >= 2 else []), qde)
            p.dma(qD.rearrange("(k p) c -> k p c", p=128)[k],
                  stg[:, 256 + (k % 2) * 32:256 + (k % 2) * 32 + 32],
                  waits=[te], inc=qds[k % 2])
        tok_knD = (kds[0], kds[0].v)
        tok_knD1 = (kds[1], kds[1].v)
        tok_qD = (qds[0], qds[0].v)
        tok_qD1 = (qds[1], qds[1].v)

        knl = [p.ctr("knl0"), p.ctr("knl1")]
        ql = [p.ctr("ql0"), p.ctr("ql1")]
        vz = p.ctr("vz")
        va = p.ctr("va")
        th = p.ctr("th")
        svc = p.ctr("sv")
        iz0 = p.on("vector", lambda e: e.memset(Sst[:], 0.0),
                   [tok_wm, tok_uw, tok_rnb, tok_bn, (tev, NT)], "ms")
        iz1 = p.on("vector", lambda e: e.memset(z0t[:], 0.0), [], "ms")
        iz2 = p.on("vector", lambda e: e.memset(spt[:], 0.0), [], "ms")
        izk = p.on("vector", lambda e: e.memset(knr[:], 1.0), [], "ms")
        iz = p.on("vector", lambda e: e.tensor_copy(
            Sst[:, 32:33], kvf[:, T:T + 1]), [iz2, tok_kv], "ms")
        for c in range(NCH):
            wdk = [tok_knD, tok_knD1, tok_qD, tok_qD1, izk] if c < 2 else \
                [(va, (c - 1) * CH)]
            kt4 = []
            for n in range(4):
                kt4.append(p.dma(
                    knr[n * 32:(n + 1) * 32, (c % 2) * CH * 33:((c % 2) + 1) * CH * 33]
                    .rearrange("p (t j) -> p t j", j=33)[:, :, 0:NS],
                    knD.rearrange("t (n j) -> n t j", n=4)[n, c * CH:(c + 1) * CH, :]
                    .unsqueeze(0).to_broadcast([32, CH, NS]),
                    waits=wdk, inc=knl[c % 2]))
            qt1 = p.dma(qr[:, (c % 2) * CH * NS:((c % 2) + 1) * CH * NS]
                        .rearrange("p (t j) -> p t j", j=NS),
                        qD[c * CH:(c + 1) * CH, :].unsqueeze(0).to_broadcast([128, CH, NS]),
                        waits=wdk, inc=ql[c % 2])
            for s_ in range(CH):
                t = c * CH + s_
                knt = knr[:, (c % 2) * CH * 33 + s_ * 33:(c % 2) * CH * 33 + (s_ + 1) * 33]
                qt = qr[:, (c % 2) * CH * NS + s_ * NS:(c % 2) * CH * NS + (s_ + 1) * NS]
                w1 = [iz, iz1] if t == 0 else []
                if s_ == 0:
                    w1 += [kt4[-1], qt1]
                t_op1 = p.on("vector", lambda e, knt=knt: e.scalar_tensor_tensor(
                    scr[:], Sst[:], 1.0, knt, OP.mult, OP.mult, accum_out=nsum[:, 0:1]),
                    w1, svc)
                t_op5 = None
                if t > 0:
                    t_op5 = p.on("vector", lambda e, qtp=qt_prev, tp=t - 1:
                                 e.scalar_tensor_tensor(
                        scr2[:], Sst[:, 0:NS], 1.0, qtp, OP.mult, OP.mult,
                        accum_out=outsq()[:, tp:tp + 1]), [], va)
                t_z = p.on("vector", lambda e, knt=knt: e.scalar_tensor_tensor(
                    zbt[:], knt[:, 0:NS], nsum[:, 0:1], z0t[:], OP.mult, OP.add),
                    [t_op1], vz)
                if t + 1 < T:
                    p.on("vector", lambda e, t=t: e.tensor_copy(
                        Sst[:, 32:33], kvf[:, T + t + 1:T + t + 2]), [], "sv2")
                t_th = p.on("scalar", lambda e: e.activation(Ut[:], zbt[:], AF.Tanh),
                            [(vz, t + 1)], th)
                t_s4 = p.on("vector", lambda e, t=t: e.scalar_tensor_tensor(
                    Sst[:, 0:NS], Ut[:], uwb()[:, t:t + 1], spt[:], OP.mult, OP.add),
                    [(th, t + 1), t_op5], svc)
                if t + 1 < T:
                    p.on("vector", lambda e, t=t: e.tensor_scalar(
                        z0t[:], Sst[:, 0:NS], betab[:, t + 1:t + 2], None, OP.mult),
                        [t_s4], "sv2")
                    p.on("vector", lambda e, t=t: e.tensor_scalar(
                        spt[:], Sst[:, 0:NS], wmb()[:, t + 1:t + 2], None, OP.mult),
                        [], "sv2")
                qt_prev = qt
        p.on("vector", lambda e: e.scalar_tensor_tensor(
            scr2[:], Sst[:, 0:NS], 1.0, qt_prev, OP.mult, OP.mult,
            accum_out=outsq()[:, T - 1:T]), [t_s4], va)

        p.on("vector", lambda e: e.tensor_copy(nsum[:, 3:4], nsum[:, 0:1]), [(va, T)], va)
        pg = p.ctr("pg")
        sg = p.ctr("sg")
        for ch in range(4):
            sl = lambda b, ch=ch: b()[:, ch * 512:(ch + 1) * 512]
            g1 = p.on("scalar", lambda e, ch=ch: e.activation(
                betab[:, ch * 512:(ch + 1) * 512], outsq()[:, ch * 512:(ch + 1) * 512],
                AF.Sigmoid), [(va, T + 1)], sg)
            g2b = p.on("vector", lambda e, ch=ch: e.tensor_tensor(
                rnb[:, ch * 512:(ch + 1) * 512], outsq()[:, ch * 512:(ch + 1) * 512],
                outsq()[:, ch * 512:(ch + 1) * 512], OP.mult), [g1], pg)
            g3 = p.on("vector", lambda e, ch=ch: e.tensor_tensor(
                rnb[:, ch * 512:(ch + 1) * 512], rnb[:, ch * 512:(ch + 1) * 512],
                betab[:, ch * 512:(ch + 1) * 512], OP.mult), [g2b], pg)
            p.drain("vector")
            g4 = p.on("vector", lambda e, ch=ch: e.tensor_tensor(
                rnb[:, ch * 512:(ch + 1) * 512], rnb[:, ch * 512:(ch + 1) * 512],
                smb()[:, ch * 512:(ch + 1) * 512], OP.mult), [g3], pg)
        crm = p.ctr("crm")
        cre = p.ctr("cre")
        for ch in range(4):
            tm = p.on("tensor", lambda e, ch=ch: e.matmul(
                psB[0:32, 0:512], selb[:], rnb[:, ch * 512:(ch + 1) * 512],
                start=True, stop=True), [(pg, 12)] + ([(cre, ch)] if ch else []), crm)
            p.on("scalar", lambda e, ch=ch: e.activation(
                cellb[0:32, ch * 512:(ch + 1) * 512], psB[0:32, 0:512], AF.Copy),
                [tm], cre)
        lwo = p.dma(wst[0:32, 0:D], WoT, waits=[], inc="wl0")
        cwo = p.on("vector", lambda e: e.tensor_copy(wbf[0:32, 0:D], wst[0:32, 0:D]),
                   [lwo], wcast)
        fmm = p.ctr("fmm")
        fev = p.ctr("fev")
        fso = [p.ctr("fso0"), p.ctr("fso1")]
        for mt in range(ND):
            for ch in range(4):
                i = mt * 4 + ch
                ps = psP if i % 2 == 0 else psP2
                w = [cwo, (cre, 4)] if i == 0 else []
                if i >= 2:
                    w.append((fev, i - 1))
                p.on("tensor", lambda e, mt=mt, ch=ch, ps=ps: e.matmul(
                    ps[:, :], wbf[0:32, mt * 128:(mt + 1) * 128],
                    cellb[0:32, ch * 512:(ch + 1) * 512],
                    start=True, stop=True), w, fmm)
                ew = [(fmm, i + 1)]
                if i >= 2:
                    ew.append((fso[i % 2], ((i - 2) // 2 + 1) * 16))
                te = p.on("scalar", lambda e, i=i, ps=ps: e.activation(
                    aor[:, (i % 2) * 512:(i % 2) * 512 + 512], ps[:, :], AF.Copy), ew, fev)
                p.dma(aT.rearrange("(mt p) t -> mt p t", p=128)[mt][:, ch * 512:(ch + 1) * 512],
                      aor[:, (i % 2) * 512:(i % 2) * 512 + 512], waits=[te], inc=fso[i % 2])




def _consts():
    import ml_dtypes
    c = 1e-4
    dec = np.zeros((28, 128, 512), np.float64)
    pp = np.arange(128)[:, None]
    jj = np.arange(512)[None, :]
    for mi in range(28):
        m = mi - 12
        diff = (128 * m + pp) - jj
        dec[mi] = np.exp(-c * np.abs(diff))
    blk = np.zeros((128, 4), np.float32)
    for n in range(4):
        blk[n * 32:(n + 1) * 32, n] = 1.0
    sel = np.zeros((128, 32), np.float32)
    for n in range(4):
        for i in range(32):
            sel[n * 32 + i, i] = 1.0
    return {
        "ident": np.eye(128, dtype=np.float32),
        "onescol": np.ones((128, 1), np.float32),
        "ones1x": np.ones((1, 128), np.float32),
        "decay": dec.astype(ml_dtypes.bfloat16),
        "blk4": blk, "sel4": sel,
    }


def build_fused():
    nc = _nc()
    DI = lambda name, shape: nc.dram_tensor(name, shape, F32, kind="ExternalInput").ap()
    IN = lambda name, shape: nc.dram_tensor(name, shape, F32, kind="Internal").ap()
    x = DI("x", [T, D])
    ident = DI("ident", [128, 128])
    onescol = DI("onescol", [128, 1])
    ones1x = DI("ones1x", [1, 128])
    decay = nc.dram_tensor("decay", [28, 128, 512], BF16, kind="ExternalInput").ap()
    blk4 = DI("blk4", [128, 4])
    sel4 = DI("sel4", [128, 32])
    Wq = DI("Wq", [D, D]); Wk = DI("Wk", [D, D])
    Wv = DI("Wv", [D, D]); Wo = DI("Wo", [D, D])
    Wdkv = DI("Wdkv", [D, 256]); Wuk = DI("Wuk", [256, D]); Wuv = DI("Wuv", [256, D])
    wg = DI("wg", [2816, D]); wu = DI("wu", [2816, D]); wd = DI("wd", [D, 2816])
    WinT = DI("WinT", [D, D]); WrT = DI("WrT", [D, 4]); WkvT = DI("WkvT", [D, 256])
    WbT = DI("WbT", [D, 128]); bbt = DI("bbt", [128, 1])
    WqT = DI("WqT", [D, 32]); WoT = DI("WoT", [32, D])
    wupT = DI("wupT", [D, 4096]); wdnT = DI("wdnT", [4096, D])
    out = nc.dram_tensor("out", [T, D], F32, kind="ExternalOutput").ap()
    xnT = IN("xnT", [D, T]); a0T = IN("a0T", [D, T]); f0T = IN("f0T", [D, T])
    x1T = IN("x1T", [D, T]); xn1T = IN("xn1T", [D, T])
    a1T = IN("a1T", [D, T]); f1T = IN("f1T", [D, T])
    xq_c = nc.dram_tensor("xq_cache", [NT, 128, D], BF16, kind="Internal").ap()
    wup_c = nc.dram_tensor("wup_cache", [32, ND, 128, 128], BF16, kind="Internal").ap()
    wdn_c = nc.dram_tensor("wdn_cache", [ND, 32, 128, 128], BF16, kind="Internal").ap()
    knD = IN("knD", [T, 128]); qD = IN("qD", [T, 32])
    r4D = IN("r4D", [4, T]); uwD = IN("uwD", [4, T]); smD = IN("smD", [4, T])
    with ExitStack() as ctx:
        p = Prog(nc, ctx)

        def stage(fn, pfx, Bprev, *args):
            p.begin_stage()
            fn(nc, p, pfx, Bprev, *args)
            return p.end_stage()

        Bt = stage(emit_ln0, "a", [], x, ident, xnT)
        Bt = stage(emit_attn0, "b", Bt, xnT, Wq, Wk, Wv, Wo, Wdkv, Wuk, Wuv,
                   decay, ones1x, a0T)
        Bt = stage(emit_ffn0, "c", Bt, xnT, wg, wu, wd, ident, f0T, xq_c)
        Bt = stage(emit_res_ln, "d", Bt, x, a0T, f0T, ident, onescol, ones1x,
                   x1T, xn1T)
        Bt = stage(emit_attn1, "e", Bt, xn1T, WinT, WrT, WkvT, WbT, bbt, WqT,
                   WoT, blk4, sel4, ident, a1T, knD, qD, r4D, uwD, smD)
        Bt = stage(emit_ffn1, "f", Bt, xn1T, wupT, wdnT, onescol, ones1x, f1T,
                   wup_c, wdn_c)
        Bt = stage(emit_final, "g", Bt, x1T, a1T, f1T, ident, out)
        p.begin_stage()
        p.on("sync", lambda e: e.nop(), Bt, None)
        p.emit()
    return nc


def kernel(**inputs):
    C = _consts()
    x = np.ascontiguousarray(inputs["L0_attn_x"], np.float32)
    Wkv = inputs["L1_attn_W_kv"]
    perm = [n * 64 + j for n in range(4) for j in range(32)] + \
           [n * 64 + 32 + i for n in range(4) for i in range(32)]
    shared = {
        "ident": C["ident"], "onescol": C["onescol"], "ones1x": C["ones1x"],
        "decay": C["decay"], "blk4": C["blk4"], "sel4": C["sel4"],
        "Wq": inputs["L0_attn_Wq"], "Wk": inputs["L0_attn_Wk"],
        "Wv": inputs["L0_attn_Wv"], "Wo": inputs["L0_attn_Wo"],
        "Wdkv": inputs["L0_attn_W_down_kv"], "Wuk": inputs["L0_attn_W_up_k"],
        "Wuv": inputs["L0_attn_W_up_v"],
        "wg": inputs["L0_ffn_w_gate"], "wu": inputs["L0_ffn_w_up"],
        "wd": inputs["L0_ffn_w_down"],
        "WinT": np.ascontiguousarray(inputs["L1_attn_W_in"].T),
        "WrT": np.ascontiguousarray(inputs["L1_attn_W_router"].T),
        "WkvT": np.ascontiguousarray(np.asarray(Wkv)[perm].T),
        "WbT": np.ascontiguousarray(inputs["L1_attn_W_beta"].T),
        "bbt": np.ascontiguousarray(np.asarray(inputs["L1_attn_b_beta"],
                                               np.float32).reshape(128, 1)),
        "WqT": np.ascontiguousarray(inputs["L1_attn_W_q"].T),
        "WoT": np.ascontiguousarray(inputs["L1_attn_W_out"].T),
        "wupT": np.ascontiguousarray(inputs["L1_ffn_w_up"].T),
        "wdnT": np.ascontiguousarray(inputs["L1_ffn_w_down"].T),
    }
    nc = build_fused()
    in_maps = [dict(shared, x=x[b]) for b in range(B)]
    res = run_bass_kernel_spmd(nc, in_maps, core_ids=list(range(B)))
    return np.stack([np.asarray(res.results[b]["out"]) for b in range(B)], 0)



# version 10 (final)
# speedup vs baseline: 1.0403x; 1.0076x over previous
import numpy as np
from contextlib import ExitStack
import concourse.bass as bass
import concourse.mybir as mybir
from concourse.bass_utils import run_bass_kernel_spmd

F32 = mybir.dt.float32
BF16 = mybir.dt.bfloat16
U8 = mybir.dt.uint8
AF = mybir.ActivationFunctionType
OP = mybir.AluOpType
AX = mybir.AxisListType.X

B, T, D = 8, 2048, 1024
NT = T // 128
ND = D // 128
MAGIC = 1.5 * 2.0 ** 23
MAGIC2 = 1.5 * 2.0 ** 22
LDEC = 0.01 * 0.01


class Phys:
    def __init__(self, sem):
        self.sem = sem
        self.total = 0


class Ctr:

    def __init__(self, phys):
        self.phys = phys
        self.base = phys.total
        self.v = 0

    @property
    def sem(self):
        return self.phys.sem

    def inc(self, n):
        self.v += n
        self.phys.total = self.base + self.v
        return self.v


class Prog:
    ENGINES = ("sync", "vector", "scalar", "tensor", "gpsimd")

    def __init__(self, nc, ctx, npool=72):
        self.nc = nc
        self.ctx = ctx
        self.steps = {e: [] for e in self.ENGINES}
        self.pool = [Phys(ctx.enter_context(nc.semaphore("sp%d" % i)))
                     for i in range(npool)]
        self.pool_i = 0
        self._ctrs = {}
        self._stage_phys = set()
        self._dma_ctrs = set()
        self._marks = {e: 0 for e in self.ENGINES}

    def begin_stage(self):
        self._ctrs = {}
        self._stage_phys = set()
        self._dma_ctrs = set()
        self._marks = {e: len(self.steps[e]) for e in self.ENGINES}

    def ctr(self, name):
        if name not in self._ctrs:
            for _ in range(len(self.pool)):
                ph = self.pool[self.pool_i]
                self.pool_i = (self.pool_i + 1) % len(self.pool)
                if id(ph) not in self._stage_phys:
                    break
            else:
                raise RuntimeError("semaphore pool exhausted")
            self._stage_phys.add(id(ph))
            self._ctrs[name] = Ctr(ph)
        return self._ctrs[name]

    def on(self, eng, fn, waits=(), inc=None, n=1):
        c = self.ctr(inc) if isinstance(inc, str) else inc
        tv = c.inc(n) if c is not None else None
        self.steps[eng].append((fn, [w for w in waits if w is not None], c, n, tv))
        if c is not None:
            return (c, tv)
        return None

    def drain(self, eng):
        self.steps[eng].append((lambda e: e.drain(), [], None, 0, None))

    def dma(self, out, in_, waits=(), inc="dma", eng="sync"):
        c = self.ctr(inc) if isinstance(inc, str) else inc
        self._dma_ctrs.add(c)
        return self.on(eng, lambda e: e.dma_start(out=out, in_=in_), waits, c, 16)

    def end_stage(self):
        best = {}
        for e in self.ENGINES:
            for fn, waits, c, n, tv in reversed(self.steps[e][self._marks[e]:]):
                if c is not None:
                    if best.get(id(c), (None, -1))[1] < tv:
                        best[id(c)] = (c, tv)
                    break
        for c in self._dma_ctrs:
            if best.get(id(c), (None, -1))[1] < c.v:
                best[id(c)] = (c, c.v)
        return list(best.values())

    def emit(self):
        blk = self.ctx.enter_context(self.nc.Block())

        def mk(eng_name):
            def body(e):
                for fn, waits, c, n, tv in self.steps[eng_name]:
                    for (wc, wv) in waits:
                        e.wait_ge(wc.sem, wc.base + wv)
                    ins = fn(e)
                    if c is not None:
                        assert ins is not None
                        ins.then_inc(c.sem, n)
            return body

        for e in self.ENGINES:
            if self.steps[e]:
                getattr(blk, e)(mk(e))


def _nc():
    return bass.Bass("TRN2", target_bir_lowering=False)


def emit_transpose_seq(p, pairs, idt, psA, psB, wait_src, name, first_extra=None):
    mt = p.ctr(name + "_mt")
    ev = p.ctr(name + "_ev")
    toks = []
    for i, (s, d) in enumerate(pairs):
        bank = psA if i % 2 == 0 else psB
        waits = []
        if i == 0 and first_extra is not None:
            waits.append(first_extra)
        w = wait_src(i)
        if w is not None:
            waits.append(w)
        if i >= 2:
            waits.append((ev, i - 1))
        toks.append(p.on("tensor", lambda e, b=bank, s=s: e.transpose(b[:, 0:128], s, idt[:]),
                         waits, mt))
    evtoks = []
    for i, (s, d) in enumerate(pairs):
        bank = psA if i % 2 == 0 else psB
        evtoks.append(p.on("vector", lambda e, b=bank, d=d: e.tensor_copy(d, b[:, 0:128]),
                           [toks[i]], ev))
    return ev


def emit_rowstats(p, tiles_ap, n_dt, onc, psS, rowout, wait_fn, name, evac_eng="vector"):
    mm = p.ctr(name + "_mm")
    evc = p.ctr(name + "_ev")
    tok = None
    for ch in range(T // 512):
        mts = []
        for dt in range(n_dt):
            waits = [wait_fn(dt)]
            if ch > 0 and dt == 0:
                waits.append((evc, ch))
            mts.append(p.on("tensor", lambda e, ch=ch, dt=dt, st=(dt == 0), sp=(dt == n_dt - 1):
                            e.matmul(psS[0:1, :], onc[:],
                                     tiles_ap(dt)[:, ch * 512:(ch + 1) * 512],
                                     start=st, stop=sp), waits, mm))
        tok = p.on(evac_eng, lambda e, ch=ch: e.tensor_copy(rowout[0:1, ch * 512:(ch + 1) * 512],
                                                            psS[0:1, :]),
                   [mts[-1]], evc)
    return tok



def emit_ln0(nc, p, pfx, B, x, ident, xnT):
    with ExitStack() as ctx:
        xt = ctx.enter_context(nc.sbuf_tensor(pfx + "xt", [128, NT * D], F32))
        xo = ctx.enter_context(nc.sbuf_tensor(pfx + "xo", [128, ND * T], F32))
        sq = ctx.enter_context(nc.sbuf_tensor(pfx + "sq", [128, 2 * D], F32))
        idt = ctx.enter_context(nc.sbuf_tensor(pfx + "idt", [128, 128], F32))
        stats = ctx.enter_context(nc.sbuf_tensor(pfx + "stats", [128, 6 * NT], F32))
        mcol = ctx.enter_context(nc.sbuf_tensor(pfx + "mcol", [128, NT], F32))
        icol = ctx.enter_context(nc.sbuf_tensor(pfx + "icol", [128, NT], F32))
        psA = ctx.enter_context(nc.psum_tensor(pfx + "psA", [128, 512], F32))
        psB = ctx.enter_context(nc.psum_tensor(pfx + "psB", [128, 512], F32))
        p.dma(xt[:].rearrange("p (k d) -> p k d", k=NT),
              x.rearrange("(k p) d -> p k d", p=128), waits=B)
        p.dma(idt[:], ident, inc="dma", waits=B)
        tok_ld = (p.ctr("dma"), p.ctr("dma").v)

        nrm = p.ctr("nrm")
        for k in range(NT):
            xk = xt[:, k * D:(k + 1) * D]
            sqk = sq[:, (k % 2) * D:(k % 2) * D + D]
            c = lambda j, k=k: stats[:, j * NT + k:j * NT + k + 1]
            sq_w = [tok_ld] if k < 2 else [tok_ld, (nrm, k - 1)]
            t_sq = p.on("scalar", lambda e, xk=xk, sqk=sqk: e.activation(sqk, xk, AF.Square),
                        sq_w, "sqd")
            t_s1 = p.on("vector", lambda e, xk=xk, o=c(0): e.reduce_sum(o, xk, axis=AX),
                        [tok_ld], "vst")
            t_s2 = p.on("vector", lambda e, sqk=sqk, o=c(1): e.reduce_sum(o, sqk, axis=AX),
                        [t_sq], "vst")
            t_m = p.on("vector", lambda e, i=c(0), o=mcol[:, k:k + 1]:
                       e.tensor_scalar(o, i, 1.0 / D, None, OP.mult), [t_s1], "vst")
            t_m2 = p.on("vector", lambda e, i=mcol[:, k:k + 1], o=c(5):
                        e.tensor_scalar(o, i, i, None, OP.mult), [t_m], "vst")
            t_e2 = p.on("vector", lambda e, i=c(1), o=c(2):
                        e.tensor_scalar(o, i, 1.0 / D, None, OP.mult), [t_s2], "vst")
            t_v = p.on("vector", lambda e, a=c(5), b=c(2), o=c(3):
                       e.scalar_tensor_tensor(o, a, -1.0, b, OP.mult, OP.add),
                       [t_m2, t_e2], "vst")
            t_ve = p.on("vector", lambda e, io=c(3), o=c(4):
                        e.tensor_scalar(o, io, 1e-5, None, OP.add), [t_v], "vst")
            t_ln = p.on("scalar", lambda e, i=c(4), o=c(3):
                        e.activation(o, i, AF.Ln), [t_ve], "sqd")
            t_iv = p.on("scalar", lambda e, i=c(3), o=icol[:, k:k + 1]:
                        e.activation(o, i, AF.Exp, scale=-0.5), [t_ln], "sqd")
            p.on("vector", lambda e, xk=xk, m=mcol[:, k:k + 1], iv=icol[:, k:k + 1]:
                 e.tensor_scalar(xk, xk, m, iv, OP.subtract, OP.mult),
                 [t_iv], nrm)

        pairs = []
        for k in range(NT):
            for dt in range(ND):
                pairs.append((xt[:, k * D + dt * 128: k * D + (dt + 1) * 128],
                              xo[:, dt * T + k * 128: dt * T + (k + 1) * 128]))
        ev = emit_transpose_seq(p, pairs, idt, psA, psB,
                                lambda i: (nrm, i // ND + 1) if i % ND == 0 else None, "tr",
                                first_extra=tok_ld)
        sto = p.ctr("sto")
        for dt in range(ND):
            p.dma(xnT.rearrange("(dt p) t -> dt p t", p=128)[dt],
                  xo[:, dt * T:(dt + 1) * T],
                  waits=[(ev, (NT - 1) * ND + dt + 1)], inc=sto)



def emit_res_ln(nc, p, pfx, B, x, a, f, ident, onescol, ones1x, x1T, xn1T):
    with ExitStack() as ctx:
        xt = ctx.enter_context(nc.sbuf_tensor(pfx + "xt", [128, NT * D], F32))
        x1 = ctx.enter_context(nc.sbuf_tensor(pfx + "x1", [128, ND * T], F32))
        ar = xt[:, 0:2 * T]
        fb = xt[:, 2 * T:4 * T]
        sqb = xt[:, 4 * T:6 * T]
        idt = ctx.enter_context(nc.sbuf_tensor(pfx + "idt", [128, 128], F32))
        onc = ctx.enter_context(nc.sbuf_tensor(pfx + "onc", [128, 1], F32))
        on1 = ctx.enter_context(nc.sbuf_tensor(pfx + "on1", [1, 128], F32))
        row = ctx.enter_context(nc.sbuf_tensor(pfx + "row", [1, 6 * T], F32))
        psA = ctx.enter_context(nc.psum_tensor(pfx + "psA", [128, 512], F32))
        psB = ctx.enter_context(nc.psum_tensor(pfx + "psB", [128, 512], F32))
        psS = ctx.enter_context(nc.psum_tensor(pfx + "psS", [128, 512], F32))
        psI = ctx.enter_context(nc.psum_tensor(pfx + "psI", [128, 512], F32))
        psM = ctx.enter_context(nc.psum_tensor(pfx + "psM", [128, 512], F32))

        p.dma(xt[:].rearrange("p (k d) -> p k d", k=NT),
              x.rearrange("(k p) d -> p k d", p=128), waits=B)
        p.dma(idt[:], ident, waits=B)
        p.dma(onc[:], onescol, waits=B)
        p.dma(on1[:], ones1x, waits=B)
        alldma = (p.ctr("dma"), p.ctr("dma").v)

        pairs = []
        for k in range(NT):
            for dt in range(ND):
                pairs.append((xt[:, k * D + dt * 128: k * D + (dt + 1) * 128],
                              x1[:, dt * T + k * 128: dt * T + (k + 1) * 128]))
        ev = emit_transpose_seq(p, pairs, idt, psA, psB,
                                lambda i: None, "tr", first_extra=alldma)
        evall = (ev, NT * ND)

        add1 = p.ctr("add1")
        for dt in range(ND):
            wa = [evall] if dt < 2 else [(add1, dt - 1)]
            la = p.dma(ar[:, (dt % 2) * T:(dt % 2) * T + T],
                       a.rearrange("(dt p) t -> dt p t", p=128)[dt],
                       waits=wa, inc="dma%d" % (dt % 2))
            lf = p.dma(fb[:, (dt % 2) * T:(dt % 2) * T + T],
                       f.rearrange("(dt p) t -> dt p t", p=128)[dt],
                       waits=wa, inc="dmf%d" % (dt % 2))
            t1 = p.on("vector", lambda e, dt=dt: e.tensor_tensor(
                x1[:, dt * T:(dt + 1) * T], x1[:, dt * T:(dt + 1) * T],
                ar[:, (dt % 2) * T:(dt % 2) * T + T], OP.add), [la, lf], "vadd")
            p.on("vector", lambda e, dt=dt: e.tensor_tensor(
                x1[:, dt * T:(dt + 1) * T], x1[:, dt * T:(dt + 1) * T],
                fb[:, (dt % 2) * T:(dt % 2) * T + T], OP.add), [t1], add1)
        sto = p.ctr("sto")
        for dt in range(ND):
            p.dma(x1T.rearrange("(dt p) t -> dt p t", p=128)[dt],
                  x1[:, dt * T:(dt + 1) * T], waits=[(add1, dt + 1)], inc=sto)

        tok_s1 = emit_rowstats(p, lambda dt: x1[:, dt * T:(dt + 1) * T], ND,
                               onc, psS, row, lambda dt: (add1, ND), "s1")
        sqd = p.ctr("sqd")
        s2mm = p.ctr("s2mm")
        s2ev = p.ctr("s2ev")
        for ch in range(4):
            for dt in range(ND):
                i = ch * ND + dt
                sq_w = [(add1, ND)] if i < 2 else [(s2mm, i - 1)]
                t_q = p.on("scalar", lambda e, ch=ch, dt=dt: e.activation(
                    sqb[:, (dt % 2) * T: (dt % 2) * T + 512],
                    x1[:, dt * T + ch * 512: dt * T + ch * 512 + 512], AF.Square),
                    sq_w, sqd)
                waits = [t_q]
                if dt == 0 and ch > 0:
                    waits.append((s2ev, ch))
                p.on("tensor", lambda e, ch=ch, dt=dt, st=(dt == 0), sp=(dt == ND - 1):
                     e.matmul(psI[0:1, :], onc[:],
                              sqb[:, (dt % 2) * T: (dt % 2) * T + 512],
                              start=st, stop=sp), waits, s2mm)
            p.on("vector", lambda e, ch=ch: e.tensor_copy(
                row[0:1, T + ch * 512: T + ch * 512 + 512], psI[0:1, :]),
                [(s2mm, (ch + 1) * ND)], s2ev)

        r = lambda j: row[0:1, j * T:(j + 1) * T]
        t1 = p.on("vector", lambda e: e.tensor_scalar(r(2), r(0), 1.0 / D, None, OP.mult),
                  [tok_s1], "rs")
        t2 = p.on("vector", lambda e: e.tensor_tensor(r(3), r(2), r(2), OP.mult),
                  [t1], "rs")
        t3 = p.on("vector", lambda e: e.tensor_scalar(r(5), r(1), 1.0 / D, None, OP.mult),
                  [(s2ev, 4)], "rs")
        t4 = p.on("vector", lambda e: e.scalar_tensor_tensor(r(3), r(3), -1.0, r(5),
                                                             OP.mult, OP.add),
                  [t2, t3], "rs")
        t5 = p.on("vector", lambda e: e.tensor_scalar(r(3), r(3), 1e-5, None, OP.add),
                  [t4], "rs")
        t6 = p.on("scalar", lambda e: e.activation(r(1), r(3), AF.Ln), [t5], "ras")
        t7 = p.on("scalar", lambda e: e.activation(r(4), r(1), AF.Exp, scale=-0.5), [t6], "ras")
        t8 = p.on("vector", lambda e: e.tensor_tensor(r(5), r(2), r(4), OP.mult), [t7], "rs")

        nsto = p.ctr("nsto")
        for ch in range(4):
            wb = [t8, (sto, 8 * 16)]
            if ch > 0:
                wb.append((nsto, ch * ND))
            bi = p.on("tensor", lambda e, ch=ch: e.matmul(
                psI[:, :], on1[:], r(4)[:, ch * 512:(ch + 1) * 512], start=True, stop=True),
                wb, "bmm")
            bm = p.on("tensor", lambda e, ch=ch: e.matmul(
                psM[:, :], on1[:], r(5)[:, ch * 512:(ch + 1) * 512], start=True, stop=True),
                [t8], "bmm")
            for dt in range(ND):
                tn = p.on("vector", lambda e, ch=ch, dt=dt: e.tensor_tensor(
                    x1[:, dt * T + ch * 512: dt * T + ch * 512 + 512],
                    x1[:, dt * T + ch * 512: dt * T + ch * 512 + 512], psI[:, :], OP.mult),
                    [bi, bm], "vn")
                p.on("vector", lambda e, ch=ch, dt=dt: e.tensor_tensor(
                    x1[:, dt * T + ch * 512: dt * T + ch * 512 + 512],
                    x1[:, dt * T + ch * 512: dt * T + ch * 512 + 512], psM[:, :], OP.subtract),
                    [tn], nsto)
        for dt in range(ND):
            p.dma(xn1T.rearrange("(dt p) t -> dt p t", p=128)[dt],
                  x1[:, dt * T:(dt + 1) * T], waits=[(nsto, 4 * ND)], inc=sto)



def emit_final(nc, p, pfx, B, x1, a, f, ident, out):
    with ExitStack() as ctx:
        xb = ctx.enter_context(nc.sbuf_tensor(pfx + "xb", [128, ND * T], F32))
        ab = ctx.enter_context(nc.sbuf_tensor(pfx + "ab", [128, ND * T], F32))
        ob = ctx.enter_context(nc.sbuf_tensor(pfx + "ob", [128, NT * D], F32))
        idt = ctx.enter_context(nc.sbuf_tensor(pfx + "idt", [128, 128], F32))
        psA = ctx.enter_context(nc.psum_tensor(pfx + "psA", [128, 512], F32))
        psB = ctx.enter_context(nc.psum_tensor(pfx + "psB", [128, 512], F32))
        t_x = p.dma(xb[:].rearrange("p (dt t) -> p dt t", dt=ND),
                    x1.rearrange("(dt p) t -> p dt t", p=128), waits=B)
        t_a = p.dma(ab[:].rearrange("p (dt t) -> p dt t", dt=ND),
                    a.rearrange("(dt p) t -> p dt t", p=128), waits=B)
        p.dma(idt[:], ident, waits=B)
        alldma = (p.ctr("dma"), p.ctr("dma").v)
        add1 = p.ctr("add1")
        fl = []
        for dt in range(ND):
            w = [alldma] if dt < 2 else [(add1, dt - 1)]
            fl.append(p.dma(ob[:, (dt % 2) * T:(dt % 2) * T + T],
                            f.rearrange("(dt p) t -> dt p t", p=128)[dt],
                            waits=w, inc="dmf%d" % (dt % 2)))
        for dt in range(ND):
            t1 = p.on("vector", lambda e, dt=dt: e.tensor_tensor(
                xb[:, dt * T:(dt + 1) * T], xb[:, dt * T:(dt + 1) * T],
                ab[:, dt * T:(dt + 1) * T], OP.add), [alldma, fl[dt]], "vadd")
            p.on("vector", lambda e, dt=dt: e.tensor_tensor(
                xb[:, dt * T:(dt + 1) * T], xb[:, dt * T:(dt + 1) * T],
                ob[:, (dt % 2) * T:(dt % 2) * T + T], OP.add), [t1], add1)
        pairs = []
        for k in range(NT):
            for dt in range(ND):
                pairs.append((xb[:, dt * T + k * 128: dt * T + (k + 1) * 128],
                              ob[:, k * D + dt * 128: k * D + (dt + 1) * 128]))
        ev = emit_transpose_seq(p, pairs, idt, psA, psB,
                                lambda i: (add1, ND), "tr")
        sto = p.ctr("sto")
        for k in range(NT):
            p.dma(out.rearrange("(k p) d -> k p d", p=128)[k],
                  ob[:, k * D:(k + 1) * D], waits=[(ev, (k + 1) * ND)], inc=sto)





def emit_ffn1(nc, p, pfx, B, xnT, wupT, wdnT, onescol, ones1x, fT, wup_c, wdn_c):
    DF = 4096
    NF = DF // 128
    with ExitStack() as ctx:
        xb = ctx.enter_context(nc.sbuf_tensor(pfx + "xb", [128, ND * T], F32))
        xnb = ctx.enter_context(nc.sbuf_tensor(pfx + "xnb", [128, ND * T], BF16))
        wring = ctx.enter_context(nc.sbuf_tensor(pfx + "wring", [128, 2 * ND * 128], BF16))
        wdb = ctx.enter_context(nc.sbuf_tensor(pfx + "wdb", [128, 2 * NF * 128], BF16))
        i2sb = ctx.enter_context(nc.sbuf_tensor(pfx + "i2sb", [128, 1024], F32))
        fsl = ctx.enter_context(nc.sbuf_tensor(pfx + "fsl", [128, 2 * 1024], F32))
        row = ctx.enter_context(nc.sbuf_tensor(pfx + "row", [1, 4 * T], F32))
        acc = ctx.enter_context(nc.sbuf_tensor(pfx + "acc", [128, 4], F32))
        iac = ctx.enter_context(nc.sbuf_tensor(pfx + "iac", [128, 2], F32))
        onc = ctx.enter_context(nc.sbuf_tensor(pfx + "onc", [128, 1], F32))
        on1 = ctx.enter_context(nc.sbuf_tensor(pfx + "on1", [1, 128], F32))
        psS = ctx.enter_context(nc.psum_tensor(pfx + "psS", [128, 512], F32))
        psI = ctx.enter_context(nc.psum_tensor(pfx + "psI", [128, 512], F32))
        psU0 = ctx.enter_context(nc.psum_tensor(pfx + "psU0", [128, 512], F32))
        psU1 = ctx.enter_context(nc.psum_tensor(pfx + "psU1", [128, 512], F32))
        hb = xb[:].bitcast(BF16)
        stg = xb

        p.dma(xb[:].rearrange("p (dt t) -> p dt t", dt=ND),
              xnT.rearrange("(dt p) t -> p dt t", p=128), waits=B)
        p.dma(onc[:], onescol, waits=B)
        p.dma(on1[:], ones1x, waits=B)
        alldma = (p.ctr("dma"), p.ctr("dma").v)

        sqd = p.ctr("sqd")
        s1mm = p.ctr("s1mm")
        s1ev = p.ctr("s1ev")
        for ch in range(4):
            for dt in range(ND):
                i = ch * ND + dt
                sq_w = [alldma] if i < 2 else [(s1mm, i - 1)]
                t_q = p.on("scalar", lambda e, ch=ch, dt=dt: e.activation(
                    fsl[:, (dt % 2) * 1024:(dt % 2) * 1024 + 512],
                    xb[:, dt * T + ch * 512: dt * T + ch * 512 + 512], AF.Square),
                    sq_w, sqd)
                waits = [t_q]
                if dt == 0 and ch > 0:
                    waits.append((s1ev, ch))
                p.on("tensor", lambda e, dt=dt, st=(dt == 0), sp=(dt == ND - 1):
                     e.matmul(psS[0:1, :], onc[:],
                              fsl[:, (dt % 2) * 1024:(dt % 2) * 1024 + 512],
                              start=st, stop=sp), waits, s1mm)
            p.on("vector", lambda e, ch=ch: e.tensor_copy(row[0:1, ch * 512:(ch + 1) * 512],
                                                          psS[0:1, :]),
                 [(s1mm, (ch + 1) * ND)], s1ev)
        r = lambda j: row[0:1, j * T:(j + 1) * T]
        t1 = p.on("scalar", lambda e: e.activation(r(1), r(0), AF.Ln), [(s1ev, 4)], "ras")
        t2 = p.on("scalar", lambda e: e.activation(r(2), r(1), AF.Exp, scale=0.5), [t1], "ras")
        t3 = p.on("vector", lambda e: e.tensor_scalar(r(1), r(2), 1.0 / 32.0, 1e-8,
                                                      OP.mult, OP.add), [t2], "rs")
        t4 = p.on("scalar", lambda e: e.activation(r(2), r(1), AF.Ln), [t3], "ras")
        t5 = p.on("scalar", lambda e: e.activation(r(3), r(2), AF.Exp, scale=-1.0), [t4], "ras")
        xnbd = p.ctr("xnbd")
        for ch in range(4):
            bi = p.on("tensor", lambda e, ch=ch: e.matmul(
                psI[:, :], on1[:], r(3)[:, ch * 512:(ch + 1) * 512], start=True, stop=True),
                [t5] + ([(xnbd, ch * ND)] if ch else []), "bmm")
            for dt in range(ND):
                p.on("vector", lambda e, ch=ch, dt=dt: e.tensor_tensor(
                    xnb[:, dt * T + ch * 512: dt * T + ch * 512 + 512],
                    xb[:, dt * T + ch * 512: dt * T + ch * 512 + 512], psI[:, :], OP.mult),
                    [bi], xnbd)
        tok_xnb = (xnbd, 4 * ND)

        def quant_weight(tag, wT, n_slab, slab_cols, count, cache_store, res_tok):
            al = p.ctr(tag + "_al")
            toks = []
            res_list = res_tok if isinstance(res_tok, list) else [res_tok]
            for s in range(n_slab):
                w = list(res_list) if s < 2 else [(al, s - 1)]
                toks.append(p.dma(stg[:, (s % 2) * slab_cols:(s % 2 + 1) * slab_cols],
                                  wT.rearrange("(s p) c -> s p c", p=128)[s],
                                  waits=w, inc=tag + "_ld%d" % (s % 2)))
            for s in range(n_slab):
                t_r = p.on("vector", lambda e, s=s: e.reduce_sum(
                    acc[:, (s % 2) + 2:(s % 2) + 3],
                    stg[:, (s % 2) * slab_cols:(s % 2 + 1) * slab_cols],
                    axis=AX, apply_absolute_value=True), [toks[s]], tag + "_r")
                if s == 0:
                    p.on("vector", lambda e: e.tensor_copy(acc[:, 0:1], acc[:, 2:3]),
                         [t_r], al)
                else:
                    p.on("vector", lambda e, s=s: e.tensor_tensor(
                        acc[:, 0:1], acc[:, 0:1], acc[:, (s % 2) + 2:(s % 2) + 3], OP.add),
                        [t_r], al)
            tm = p.on("tensor", lambda e: e.matmul(psS[0:1, 0:1], acc[:, 0:1], onc[:],
                                                   start=True, stop=True),
                      [(al, n_slab)], tag + "_tm")
            te = p.on("vector", lambda e: e.tensor_copy(iac[0:1, 0:1], psS[0:1, 0:1]),
                      [tm], tag + "_te")
            tr = p.on("vector", lambda e: e.reciprocal(iac[0:1, 1:2], iac[0:1, 0:1]),
                      [te], tag + "_te")
            ts_ = p.on("vector", lambda e: e.tensor_scalar(iac[0:1, 0:1], iac[0:1, 1:2],
                                                           float(count), None, OP.mult),
                       [tr], tag + "_te")
            tb = p.on("tensor", lambda e: e.matmul(psS[:, 0:1], on1[:], iac[0:1, 0:1],
                                                   start=True, stop=True), [ts_], tag + "_tm")
            tc = p.on("vector", lambda e: e.tensor_copy(iac[:, 1:2], psS[:, 0:1]), [tb], tag + "_te")
            qd = p.ctr(tag + "_qd")
            dns = [p.ctr(tag + "_dn0"), p.ctr(tag + "_dn1")]
            for s in range(n_slab):
                done = dns[s % 2]
                w = [tc] if s < 2 else [(done, ((s - 2) // 2 + 1) * 16)]
                tok2 = p.dma(stg[:, (s % 2) * slab_cols:(s % 2 + 1) * slab_cols],
                             wT.rearrange("(s p) c -> s p c", p=128)[s],
                             waits=w, inc=tag + "_qld%d" % (s % 2))
                toks2 = {s: tok2}
                sl = stg[:, (s % 2) * slab_cols:(s % 2 + 1) * slab_cols]
                gb = stg[:, (2 + s % 2) * slab_cols:(3 + s % 2) * slab_cols]
                gbb = gb.bitcast(BF16)[:, 0:slab_cols]
                ta = p.on("vector", lambda e, sl=sl: e.tensor_scalar(
                    sl, sl, iac[:, 1:2], None, OP.mult), [toks2[s]], qd)
                tg = p.on("vector", lambda e, sl=sl, gb=gb: e.tensor_scalar(
                    gb, sl, 0.5, None, OP.is_gt), [ta], qd)
                tl = p.on("vector", lambda e, sl=sl: e.tensor_scalar(
                    sl, sl, -0.5, None, OP.is_lt), [tg], qd)
                tb2 = p.on("vector", lambda e, sl=sl, gb=gb, gbb=gbb: e.tensor_tensor(
                    gbb, gb, sl, OP.subtract), [tl], qd)
                cache_store(s, gbb, [tb2], done)
            return [(d, d.v) for d in dns]

        def store_up(s, srcb, waits, done):
            return p.dma(wup_c[:, s, :, :].rearrange("a p c -> p a c"),
                         srcb.rearrange("p (a c) -> p a c", a=NF),
                         waits=waits, inc=done)

        def store_dn(s, srcb, waits, done):
            return p.dma(wdn_c[:, s, :, :].rearrange("a p c -> p a c"),
                         srcb.rearrange("p (a c) -> p a c", a=ND),
                         waits=waits, inc=done)

        tok_wup = quant_weight("wu", wupT, ND, DF, D * DF, store_up, tok_xnb)
        tok_wdn = quant_weight("wd", wdnT, NF, D, DF * D, store_dn, tok_wup)

        upmm = p.ctr("upmm")
        gel = p.ctr("gel")
        s2mm = p.ctr("s2mm")
        s2ev = p.ctr("s2ev")
        i2d = p.ctr("i2d")
        dnmm = p.ctr("dnmm")
        fev = p.ctr("fev")
        fstos = [p.ctr("fsto0"), p.ctr("fsto1")]
        gtok = {}
        dn_ev = {}
        for hv in range(2):
            hoff = hv * 1024
            for ft in range(NF):
                g = hv * NF + ft
                wl_w = list(tok_wup)
                if g >= 2:
                    wl_w.append((gel, (g - 2 + 1) * 2))
                wl = p.dma(wring[:, (ft % 2) * ND * 128:(ft % 2 + 1) * ND * 128],
                           wup_c[ft].rearrange("b p c -> p b c"),
                           waits=wl_w, inc="uld%d" % (ft % 2))
                for ch in range(2):
                    ps = psU0 if ch == 0 else psU1
                    for dt in range(ND):
                        w = []
                        if dt == 0:
                            w = [wl, tok_xnb]
                            if g >= 1:
                                w.append((gel, (g - 1) * 2 + ch + 1))
                            if hv == 1 and ft == 0:
                                w.append((fev, 2 * ND))
                        p.on("tensor", lambda e, ft=ft, ch=ch, dt=dt, ps=ps, hoff=hoff,
                             st=(dt == 0), sp=(dt == ND - 1):
                             e.matmul(ps[:, :],
                                      wring[:, (ft % 2) * ND * 128 + dt * 128:(ft % 2) * ND * 128 + (dt + 1) * 128],
                                      xnb[:, dt * T + hoff + ch * 512: dt * T + hoff + ch * 512 + 512],
                                      start=st, stop=sp), w, upmm)
                    gw = [(upmm, (g * 2 + ch + 1) * ND)]
                    if hv == 0 and ft == 0 and ch == 0:
                        gw.extend(tok_wdn)
                    if hv == 1 and ft == 0:
                        gw.append((fev, hv * 2 * ND))
                    gtok[(hv, ft, ch)] = p.on("scalar", lambda e, ft=ft, ch=ch, ps=ps: e.activation(
                        hb[:, ft * 1024 + ch * 512: ft * 1024 + ch * 512 + 512],
                        ps[:, :], AF.Gelu), gw, gel)
            for ch in range(2):
                for ft in range(NF):
                    i = hv * 2 * NF + ch * NF + ft
                    sq_w = [gtok[(hv, ft, ch)]]
                    if ch * NF + ft >= 2:
                        sq_w.append((s2mm, i - 1))
                    elif hv == 1:
                        sq_w.append((s2mm, hv * 2 * NF))
                        sq_w.append((fstos[0], 64))
                        sq_w.append((fstos[1], 64))
                    t_q = p.on("scalar", lambda e, ft=ft, ch=ch: e.activation(
                        fsl[:, (ft % 2) * 1024:(ft % 2) * 1024 + 512],
                        hb[:, ft * 1024 + ch * 512: ft * 1024 + ch * 512 + 512], AF.Square),
                        sq_w, sqd)
                    waits = [t_q]
                    if ft == 0 and (ch > 0 or hv > 0):
                        waits.append((s2ev, hv * 2 + ch))
                    p.on("tensor", lambda e, ft=ft, st=(ft == 0), sp=(ft == NF - 1):
                         e.matmul(psS[0:1, :], onc[:],
                                  fsl[:, (ft % 2) * 1024:(ft % 2) * 1024 + 512],
                                  start=st, stop=sp), waits, s2mm)
                p.on("vector", lambda e, ch=ch: e.tensor_copy(
                    row[0:1, ch * 512: ch * 512 + 512], psS[0:1, :]),
                     [(s2mm, (hv * 2 + ch + 1) * NF)], s2ev)
            rr = lambda j: row[0:1, j * T: j * T + 1024]
            u1 = p.on("scalar", lambda e, rr=rr: e.activation(rr(1), rr(0), AF.Ln),
                      [(s2ev, hv * 2 + 2)], "ras")
            u2 = p.on("scalar", lambda e, rr=rr: e.activation(rr(2), rr(1), AF.Exp, scale=0.5),
                      [u1], "ras")
            u3 = p.on("vector", lambda e, rr=rr: e.tensor_scalar(rr(1), rr(2), 1.0 / 64.0, 1e-8,
                                                                 OP.mult, OP.add), [u2], "rs")
            u4 = p.on("scalar", lambda e, rr=rr: e.activation(rr(2), rr(1), AF.Ln), [u3], "ras")
            u5 = p.on("scalar", lambda e, rr=rr: e.activation(rr(1), rr(2), AF.Exp, scale=-1.0),
                      [u4], "ras")
            for ch in range(2):
                bw = [u5]
                if hv > 0 or ch > 0:
                    bw.append((i2d, hv * 2 + ch))
                bi = p.on("tensor", lambda e, ch=ch, rr=rr: e.matmul(
                    psI[:, :], on1[:], rr(1)[:, ch * 512:(ch + 1) * 512],
                    start=True, stop=True), bw, "bmm")
                p.on("vector", lambda e, ch=ch: e.tensor_copy(
                    i2sb[:, ch * 512:(ch + 1) * 512], psI[:, :]), [bi], i2d)
            for do in range(ND):
                gdo = hv * ND + do
                wl_w = list(tok_wdn)
                if gdo >= 2:
                    wl_w.append(dn_ev[gdo - 2])
                wl = p.dma(wdb[:, (do % 2) * NF * 128:(do % 2 + 1) * NF * 128],
                           wdn_c[do].rearrange("b p c -> p b c"),
                           waits=wl_w, inc="dld%d" % (do % 2))
                for ch in range(2):
                    ps = psU0 if ch == 0 else psU1
                    for ft in range(NF):
                        w = []
                        if ft == 0:
                            w = [wl, gtok[(hv, NF - 1, 1)], (gel, (hv * NF + NF) * 2)]
                            if gdo >= 1:
                                w.append((fev, gdo * 2))
                        p.on("tensor", lambda e, do=do, ft=ft, ch=ch, ps=ps, hoff=hoff,
                             st=(ft == 0), sp=(ft == NF - 1):
                             e.matmul(ps[:, :],
                                      wdb[:, (do % 2) * NF * 128 + ft * 128:(do % 2) * NF * 128 + (ft + 1) * 128],
                                      hb[:, ft * 1024 + ch * 512: ft * 1024 + ch * 512 + 512],
                                      start=st, stop=sp), w, dnmm)
                    ew = [(dnmm, (gdo * 2 + ch + 1) * NF), (i2d, hv * 2 + 2)]
                    if gdo >= 2:
                        ew.append((fstos[do % 2], ((gdo - 2) // 2 + 1) * 16))
                    dn_ev[gdo] = p.on("vector", lambda e, do=do, ch=ch, ps=ps: e.tensor_tensor(
                        fsl[:, (do % 2) * 1024 + ch * 512:(do % 2) * 1024 + ch * 512 + 512],
                        ps[:, :], i2sb[:, ch * 512:(ch + 1) * 512], OP.mult), ew, fev)
                p.dma(fT.rearrange("(do p) t -> do p t", p=128)[do][:, hoff:hoff + 1024],
                      fsl[:, (do % 2) * 1024:(do % 2) * 1024 + 1024],
                      waits=[(fev, (gdo + 1) * 2)], inc=fstos[do % 2])



def emit_fp4_quant(p, src_ap, dst_bf_ap, cols, scr_a, scr_b, scr_c, scl, rcp,
                   wait_tok, tag):
    nb = cols // 16
    v = lambda fn, w, i="qv": p.on("vector", fn, w, i)
    s = lambda fn, w, i="qsc": p.on("scalar", fn, w, i)
    t1 = v(lambda e: e.tensor_reduce(scl, src_ap.rearrange("p (b j) -> p b j", j=16),
                                     axis=AX, op=OP.max, apply_absolute_value=True),
           [wait_tok])
    t2 = v(lambda e: e.tensor_scalar(scl, scl, 1.0 / 6.0, 1e-12, OP.mult, OP.max), [t1])
    t3 = v(lambda e: e.reciprocal(rcp, scl), [t2])
    t3b = v(lambda e: e.tensor_scalar(scl, scl, 0.5, None, OP.mult), [t3])
    t4 = s(lambda e: e.activation(scr_a, src_ap, AF.Abs), [wait_tok])
    rcp_rep = rcp.unsqueeze(2).to_broadcast([128, nb, 16])
    t5 = v(lambda e: e.tensor_tensor(scr_a.rearrange("p (b j) -> p b j", j=16),
                                     scr_a.rearrange("p (b j) -> p b j", j=16),
                                     rcp_rep, OP.mult), [t3, t4])
    steps = [
        lambda e: e.tensor_scalar(scr_b, scr_a, 2.5, None, OP.is_gt),
        lambda e: e.scalar_tensor_tensor(scr_c, scr_a, 3.5, scr_b, OP.is_gt, OP.add),
        lambda e: e.scalar_tensor_tensor(scr_b, scr_a, 5.0, scr_c, OP.is_gt, OP.add),
        lambda e: e.scalar_tensor_tensor(scr_c, scr_a, 5.0, scr_b, OP.is_gt, OP.add),
        lambda e: e.tensor_scalar(scr_b, scr_c, 2.0, None, OP.mult),
        lambda e: e.scalar_tensor_tensor(scr_c, scr_a, 0.25, scr_b, OP.is_gt, OP.add),
        lambda e: e.scalar_tensor_tensor(scr_b, scr_a, 0.75, scr_c, OP.is_gt, OP.add),
        lambda e: e.scalar_tensor_tensor(scr_c, scr_a, 1.25, scr_b, OP.is_gt, OP.add),
        lambda e: e.scalar_tensor_tensor(scr_b, scr_a, 1.75, scr_c, OP.is_gt, OP.add),
    ]
    tprev = t5
    for fn in steps:
        tprev = v(fn, [tprev])
    t13 = s(lambda e: e.activation(scr_c, src_ap, AF.Sign), [tprev])
    t14 = v(lambda e: e.tensor_tensor(dst_bf_ap, scr_b, scr_c, OP.mult), [t13, tprev])
    scl_rep = scl.unsqueeze(2).to_broadcast([128, nb, 16])
    t15 = v(lambda e: e.tensor_tensor(dst_bf_ap.rearrange("p (b j) -> p b j", j=16),
                                      dst_bf_ap.rearrange("p (b j) -> p b j", j=16),
                                      scl_rep, OP.mult), [t14, t3b])
    return t15


def emit_ffn0(nc, p, pfx, B, xnT, wg, wu, wd, ident, fT, xq_c):
    DN = 2816
    NH = DN // 128
    CHUNKS = [512, 512, 512, 512, 512, 256]
    debug = False
    wg_dbg = None
    with ExitStack() as ctx:
        big = ctx.enter_context(nc.sbuf_tensor(pfx + "big", [128, 13056], F32))
        wgT = ctx.enter_context(nc.sbuf_tensor(pfx + "wgT", [128, ND * DN], BF16))
        wuT = ctx.enter_context(nc.sbuf_tensor(pfx + "wuT", [128, ND * DN], BF16))
        wdT = ctx.enter_context(nc.sbuf_tensor(pfx + "wdT", [128, NH * D], BF16))
        hq = ctx.enter_context(nc.sbuf_tensor(pfx + "hq", [128, DN], BF16))
        hqT = ctx.enter_context(nc.sbuf_tensor(pfx + "hqT", [128, NH * 128], BF16))
        gs = ctx.enter_context(nc.sbuf_tensor(pfx + "gs", [128, DN], BF16))
        xqr = ctx.enter_context(nc.sbuf_tensor(pfx + "xqr", [128, 2 * D], BF16))
        fst = ctx.enter_context(nc.sbuf_tensor(pfx + "fst", [128, 2 * 128], F32))
        idt = ctx.enter_context(nc.sbuf_tensor(pfx + "idt", [128, 128], F32))
        idb = ctx.enter_context(nc.sbuf_tensor(pfx + "idb", [128, 128], BF16))
        psA = ctx.enter_context(nc.psum_tensor(pfx + "psA", [128, 1024], BF16))
        psB = ctx.enter_context(nc.psum_tensor(pfx + "psB", [128, 1024], BF16))
        psG = ctx.enter_context(nc.psum_tensor(pfx + "psG", [128, 1536], F32))
        psU = ctx.enter_context(nc.psum_tensor(pfx + "psU", [128, 1536], F32))
        psAb = psA[:, 0:128]
        psBb = psB[:, 0:128]
        h_f = big[:, 0:DN]
        h_a = big[:, DN:2 * DN]
        h_b = big[:, 2 * DN:3 * DN]
        h_i = big[:, 3 * DN:4 * DN]
        h_scl = big[:, 4 * DN:4 * DN + DN // 16]
        h_rcp = big[:, 4 * DN + DN // 16:4 * DN + 2 * (DN // 16)]

        p.dma(idt[:], ident, waits=B)
        alldma = (p.ctr("dma"), p.ctr("dma").v)
        tok_idb = p.on("vector", lambda e: e.tensor_copy(idb[:], idt[:]), [alldma], "misc")

        xa_mt = p.ctr("xa_mt")
        xa_ev = p.ctr("xa_ev")
        xq_st = [p.ctr("xq_st0"), p.ctr("xq_st1")]
        qprev = alldma
        qhist = {}
        for k in range(NT):
            qprev2 = qhist.get(k - 2)
            xtok = big[:, (k % 2) * D:(k % 2) * D + D]
            xqk = big[:, 2 * D + (k % 2) * D:2 * D + (k % 2) * D + D]
            xqkb = xqk.bitcast(BF16)[:, 0:D]
            sa = big[:, 4 * D:5 * D]
            sb = big[:, 5 * D:6 * D]
            si = big[:, 6 * D:7 * D]
            sc = big[:, 7 * D:7 * D + 64]
            rc = big[:, 7 * D + 64:7 * D + 128]
            ldt = []
            for dt in range(ND):
                i = k * ND + dt
                w = [(xa_mt, i - 3)] if i >= 4 else list(B)
                ldt.append(p.dma(big[:, 8 * D + ((i % 4) * 128):8 * D + ((i % 4) * 128) + 128],
                                 xnT.rearrange("(dt p) t -> dt p t", p=128)[dt][:, k * 128:(k + 1) * 128],
                                 waits=w, inc="xl%d" % (i % 4)))
            for dt in range(ND):
                i = k * ND + dt
                bank = psG if i % 2 == 0 else psU
                w2 = [ldt[dt]]
                if i >= 2:
                    w2.append((xa_ev, i - 1))
                tk = p.on("tensor", lambda e, b=bank, i=i:
                          e.transpose(b[:, 0:128], big[:, 8 * D + ((i % 4) * 128):8 * D + ((i % 4) * 128) + 128], idt[:]),
                          w2, xa_mt)
                ew = [tk]
                if dt == 0 and k >= 2:
                    ew.append(qprev2)
                p.on("vector", lambda e, b=bank, xtok=xtok, dt=dt: e.tensor_copy(
                    xtok[:, dt * 128:(dt + 1) * 128], b[:, 0:128]), ew, xa_ev)
            qt = emit_fp4_quant(p, xtok, xqkb, D, sa, sb, si, sc, rc,
                                (xa_ev, (k + 1) * ND), "x%d" % k)
            qprev = qt
            qhist[k] = qt
            for dt in range(ND):
                i = k * ND + dt
                bank = psAb if i % 2 == 0 else psBb
                w2 = [qt, tok_idb] if dt == 0 else []
                if k * ND + dt >= 2:
                    w2.append((p.ctr("xb_ev"), p.ctr("xb_ev").v - 1))
                tk = p.on("tensor", lambda e, b=bank, xqkb=xqkb, dt=dt:
                          e.transpose(b, xqkb[:, dt * 128:(dt + 1) * 128], idb[:]), w2, "xb_mt")
                p.on("vector", lambda e, b=bank, k=k, dt=dt: e.tensor_copy(
                    xqr[:, (k % 2) * D + dt * 128:(k % 2) * D + (dt + 1) * 128], b),
                    [tk] + ([(xq_st[k % 2], 16 * (k // 2))] if (dt == 0 and k >= 2) else []),
                    "xb_ev")
            p.dma(xq_c[k], xqr[:, (k % 2) * D:(k % 2) * D + D],
                  waits=[(p.ctr("xb_ev"), (k + 1) * ND)], inc=xq_st[k % 2])

        tok_xq_all = [(c, c.v) for c in xq_st]

        def quant_w(tag, wsrc, n_slab, slab_cols, dstfn, start_tok):
            dn = p.ctr(tag + "_dn")
            prev = start_tok
            for s_i in range(n_slab):
                sa = big[:, 0:slab_cols]
                sb = big[:, slab_cols:2 * slab_cols]
                si = big[:, 2 * slab_cols:3 * slab_cols]
                sf = big[:, 3 * slab_cols:4 * slab_cols]
                sc = big[:, 4 * slab_cols:4 * slab_cols + slab_cols // 16]
                rc = big[:, 4 * slab_cols + slab_cols // 16:4 * slab_cols + 2 * (slab_cols // 16)]
                qb = big[:, 4 * slab_cols + 256:4 * slab_cols + 256 + (slab_cols + 1) // 2]
                qbb = qb.bitcast(BF16)[:, 0:slab_cols]
                ldt = p.dma(sf, wsrc.rearrange("(s p) c -> s p c", p=128)[s_i],
                            waits=[prev], inc=tag + "_ld")
                qt = emit_fp4_quant(p, sf, qbb, slab_cols, sa, sb, si, sc, rc,
                                    ldt, tag + "%d" % s_i)
                ntile = slab_cols // 128
                for j in range(ntile):
                    bank = psAb if j % 2 == 0 else psBb
                    w2 = [qt] if j == 0 else []
                    if j >= 2:
                        w2.append((dn, dn.v - 1))
                    tk = p.on("tensor", lambda e, b=bank, s2=qbb[:, j * 128:(j + 1) * 128]:
                              e.transpose(b, s2, idb[:]), w2, tag + "_mt")
                    prev = p.on("vector", lambda e, b=bank, d2=dstfn(s_i, j):
                                e.tensor_copy(d2, b), [tk], dn)
            return prev

        tok_wg = quant_w("wg", wg, NH, D,
                         lambda s_i, j: wgT[:, j * DN + s_i * 128:j * DN + (s_i + 1) * 128],
                         qprev)
        tok_wu = quant_w("wu", wu, NH, D,
                         lambda s_i, j: wuT[:, j * DN + s_i * 128:j * DN + (s_i + 1) * 128],
                         tok_wg)
        tok_wd = quant_w("wd", wd, ND, DN,
                         lambda s_i, j: wdT[:, j * D + s_i * 128:j * D + (s_i + 1) * 128],
                         tok_wu)

        gmm = p.ctr("gmm")
        gev = p.ctr("gev")
        hmt = p.ctr("hmt")
        hev = p.ctr("hev")
        dmm = p.ctr("dmm")
        devq = p.ctr("devq")
        dtr = p.ctr("dtr")
        dev2 = p.ctr("dev2")
        fso = [p.ctr("fso0"), p.ctr("fso1")]
        xlr = [p.ctr("xlr0"), p.ctr("xlr1")]
        coff = [0, 512, 1024, 1536, 2048, 2560]
        dstage = gs[:].bitcast(F32)
        last_qh = tok_wd
        last_devac = None
        last_hevac = None
        xq_tok = {}
        for tt in range(NT):
            w = list(tok_xq_all) if tt < 2 else [xq_tok[tt - 2][1]]
            xq_tok[tt] = (None, None)
            lt = p.dma(xqr[:, (tt % 2) * D:(tt % 2) * D + D], xq_c[tt],
                       waits=w + ([last_qh] if tt < 2 else []), inc=xlr[tt % 2])
            ev_toks = []
            gmm_last = None
            for grp in range(2):
                chs = [0, 1, 2] if grp == 0 else [3, 4, 5]
                for dt in range(ND):
                    for ci, c in enumerate(chs):
                        w = []
                        if dt == 0 and ci == 0:
                            w = [lt]
                            if grp == 0:
                                w.append(tok_wd if tt == 0 else last_devac)
                            else:
                                w.extend(ev_toks)
                        p.on("tensor", lambda e, tt=tt, dt=dt, c=c,
                             st=(dt == 0), sp=(dt == ND - 1):
                             e.matmul(psG[:, (c % 3) * 512:(c % 3) * 512 + CHUNKS[c]],
                                      xqr[:, (tt % 2) * D + dt * 128:(tt % 2) * D + (dt + 1) * 128],
                                      wgT[:, dt * DN + coff[c]: dt * DN + coff[c] + CHUNKS[c]],
                                      start=st, stop=sp), w, gmm)
                        gmm_last = p.on("tensor", lambda e, tt=tt, dt=dt, c=c,
                                        st=(dt == 0), sp=(dt == ND - 1):
                                        e.matmul(psU[:, (c % 3) * 512:(c % 3) * 512 + CHUNKS[c]],
                                                 xqr[:, (tt % 2) * D + dt * 128:(tt % 2) * D + (dt + 1) * 128],
                                                 wuT[:, dt * DN + coff[c]: dt * DN + coff[c] + CHUNKS[c]],
                                                 start=st, stop=sp), [], gmm)
                base = gmm.v
                for c in chs:
                    off = (c % 3) * 512
                    sw = [(gmm, base)]
                    if c == chs[0] and tt > 0:
                        sw.append(last_qh)
                    tsil = p.on("scalar", lambda e, c=c, off=off: e.activation(
                        gs[:, coff[c]:coff[c] + CHUNKS[c]], psG[:, off:off + CHUNKS[c]],
                        AF.Silu), sw, "gsil")
                    ev_toks.append(p.on("vector", lambda e, c=c, off=off: e.tensor_tensor(
                        h_f[:, coff[c]:coff[c] + CHUNKS[c]], psU[:, off:off + CHUNKS[c]],
                        gs[:, coff[c]:coff[c] + CHUNKS[c]], OP.mult), [tsil], gev))
            qh = emit_fp4_quant(p, h_f, hq[:], DN, h_a, h_b, h_i, h_scl, h_rcp,
                                ev_toks[-1], "h%d" % tt)
            xq_tok[tt] = (None, qh)
            last_qh = qh
            htoks = []
            for j in range(NH):
                bank = psAb if j % 2 == 0 else psBb
                w2 = [qh] if j == 0 else []
                if j >= 2:
                    w2.append((hev, hev.v - 1))
                if tt > 0 and j < 2:
                    w2.append(last_hevac)
                tk = p.on("tensor", lambda e, b=bank, s2=hq[:, j * 128:(j + 1) * 128]:
                          e.transpose(b, s2, idb[:]), w2, hmt)
                htoks.append(p.on("vector", lambda e, b=bank, d2=hqT[:, j * 128:(j + 1) * 128]:
                                  e.tensor_copy(d2, b), [tk], hev))
            last_hevac = htoks[-1]
            psD = psG
            for j in range(NH):
                w = list(ev_toks) + [htoks[-1]] if j == 0 else []
                for half in range(2):
                    p.on("tensor", lambda e, j=j, half=half, st=(j == 0), sp=(j == NH - 1):
                         e.matmul(psD[:, half * 512: half * 512 + 512],
                                  hqT[:, j * 128:(j + 1) * 128],
                                  wdT[:, j * D + half * 512: j * D + half * 512 + 512],
                                  start=st, stop=sp), w if half == 0 else [], dmm)
            base_d = dmm.v
            de = p.on("vector", lambda e: e.tensor_copy(dstage[:, 0:1024], psD[:, 0:1024]),
                      [(dmm, base_d)] + list(ev_toks), devq)
            for do in range(ND):
                bank = psU[:, 0:128] if do % 2 == 0 else psU[:, 512:640]
                w2 = [de]
                if do >= 2:
                    w2.append((dev2, dev2.v - 1))
                tk = p.on("tensor", lambda e, b=bank, do=do:
                          e.transpose(b, dstage[:, do * 128:(do + 1) * 128], idt[:]),
                          w2, dtr)
                e2 = p.on("vector", lambda e, b=bank, do=do: e.tensor_copy(
                    fst[:, (do % 2) * 128:(do % 2) * 128 + 128], b),
                    [tk] + ([(fso[do % 2], (((tt * ND + do) - 2) // 2 + 1) * 16)]
                            if tt * ND + do >= 2 else []), dev2)
                p.dma(fT.rearrange("(do p) t -> do p t", p=128)[do][:, tt * 128:(tt + 1) * 128],
                      fst[:, (do % 2) * 128:(do % 2) * 128 + 128],
                      waits=[e2], inc=fso[do % 2])
                last_devac = e2



def emit_attn0(nc, p, pfx, B, xnT, Wq, Wk, Wv, Wo, Wdkv, Wuk, Wuv, decay,
               ones1x_d, aT):
    NHD = 16
    SSC = float(1.0 / (np.sqrt(np.float64(np.float32(1024.0))) + 1e-08))
    with ExitStack() as ctx:
        xnb = ctx.enter_context(nc.sbuf_tensor(pfx + "xnb", [128, ND * T], BF16))
        QT = ctx.enter_context(nc.sbuf_tensor(pfx + "QT", [128, ND * T], BF16))
        KrT = ctx.enter_context(nc.sbuf_tensor(pfx + "KrT", [128, ND * T], BF16))
        vtok = ctx.enter_context(nc.sbuf_tensor(pfx + "vtok", [128, NT * NHD * 65], BF16))
        latT = ctx.enter_context(nc.sbuf_tensor(pfx + "latT", [128, 2 * T], BF16))
        wbf = ctx.enter_context(nc.sbuf_tensor(pfx + "wbf", [128, ND * D], BF16))
        wst = ctx.enter_context(nc.sbuf_tensor(pfx + "wst", [128, 2 * D], F32))
        expt = ctx.enter_context(nc.sbuf_tensor(pfx + "expt", [128, NT * 512], BF16))
        decb = ctx.enter_context(nc.sbuf_tensor(pfx + "decb", [128, NT * 512], BF16))
        sdb = ctx.enter_context(nc.sbuf_tensor(pfx + "sdb", [128, 2 * 512], BF16))
        aout = ctx.enter_context(nc.sbuf_tensor(pfx + "aout", [128, 2 * 512], F32))
        onesr = ctx.enter_context(nc.sbuf_tensor(pfx + "onesr", [1, 128], F32))
        psS0 = ctx.enter_context(nc.psum_tensor(pfx + "psS0", [128, 512], F32))
        psS1 = ctx.enter_context(nc.psum_tensor(pfx + "psS1", [128, 512], F32))
        psAV = ctx.enter_context(nc.psum_tensor(pfx + "psAV", [128, 512], F32))
        psM = ctx.enter_context(nc.psum_tensor(pfx + "psM", [128, 512], F32))
        psP = ctx.enter_context(nc.psum_tensor(pfx + "psP", [128, 512], F32))
        psP2 = ctx.enter_context(nc.psum_tensor(pfx + "psP2", [128, 512], F32))
        aoT = xnb
        latf = latT[:].bitcast(F32)
        rden = latf[0:1, 0:1024]
        ones1x = None

        stage = QT[:].bitcast(F32)
        stage2 = KrT[:].bitcast(F32)
        p.dma(stage[:, 0:ND * T // 2].rearrange("p (dt t) -> p dt t", dt=ND // 2),
              xnT.rearrange("(dt p) t -> p dt t", p=128)[:, 0:ND // 2, :], waits=B)
        p.dma(stage2[:, 0:ND * T // 2].rearrange("p (dt t) -> p dt t", dt=ND // 2),
              xnT.rearrange("(dt p) t -> p dt t", p=128)[:, ND // 2:, :], waits=B)
        p.dma(onesr[:], ones1x_d, waits=B)
        alldma = (p.ctr("dma"), p.ctr("dma").v)
        cast_dn = p.ctr("cast")
        for dt in range(ND):
            st_ap = (stage if dt < 4 else stage2)[:, (dt % 4) * T:(dt % 4) * T + T]
            p.on("vector", lambda e, dt=dt, st_ap=st_ap: e.tensor_copy(
                xnb[:, dt * T:(dt + 1) * T], st_ap), [alldma], cast_dn)
        tok_xnb = (cast_dn, ND)

        wld = [p.ctr("wl0"), p.ctr("wl1")]
        wcast = p.ctr("wcast")

        def cast_w(wsrc, nslab, width, first_w):
            toks = []
            for s_i in range(nslab):
                w = [first_w] if s_i < 2 else [(wcast, wcast.v - 1)]
                lt = p.dma(wst[:, (s_i % 2) * D:(s_i % 2) * D + width],
                           wsrc.rearrange("(s p) c -> s p c", p=128)[s_i],
                           waits=w, inc=wld[s_i % 2])
                toks.append(p.on("vector", lambda e, s_i=s_i, width=width: e.tensor_copy(
                    wbf[:, s_i * width:(s_i + 1) * width],
                    wst[:, (s_i % 2) * D:(s_i % 2) * D + width]), [lt], wcast))
            return toks[-1]

        def cast_w_sum(wa, wb, nslab, first_w):
            toks = []
            for s_i in range(nslab):
                w = [first_w] if s_i == 0 else [toks[-1]]
                la = p.dma(wst[:, 0:D], wa.rearrange("(s p) c -> s p c", p=128)[s_i],
                           waits=w, inc=wld[0])
                lb = p.dma(wst[:, D:2 * D], wb.rearrange("(s p) c -> s p c", p=128)[s_i],
                           waits=w, inc=wld[1])
                toks.append(p.on("vector", lambda e, s_i=s_i: e.tensor_tensor(
                    wbf[:, s_i * D:(s_i + 1) * D], wst[:, 0:D], wst[:, D:2 * D], OP.add),
                    [la, lb], wcast))
            return toks[-1]

        def project(src_bf, dst_fn, wtok, tag, nk=ND, nmt=ND, width=D, store_fn=None,
                    egate_fn=None):
            mmc = p.ctr(tag + "m")
            evc = p.ctr(tag + "e")
            for mt in range(nmt):
                for ch in range(4):
                    i = mt * 4 + ch
                    ps = psP if i % 2 == 0 else psP2
                    for dt in range(nk):
                        w = [wtok] if (i == 0 and dt == 0) else []
                        if dt == 0 and i >= 2:
                            w.append((evc, i - 1))
                        p.on("tensor", lambda e, mt=mt, ch=ch, dt=dt, ps=ps,
                             st=(dt == 0), sp=(dt == nk - 1):
                             e.matmul(ps[:, :],
                                      wbf[:, dt * width + mt * 128:dt * width + (mt + 1) * 128],
                                      src_bf[:, dt * T + ch * 512: dt * T + ch * 512 + 512],
                                      start=st, stop=sp), w, mmc)
                    ew = [(mmc, (i + 1) * nk)]
                    if egate_fn is not None:
                        g = egate_fn(i)
                        if g is not None:
                            ew.append(g)
                    tok_e = p.on("scalar", lambda e, mt=mt, ch=ch, ps=ps: e.activation(
                        dst_fn(mt, ch), ps[:, :], AF.Copy),
                        ew, evc)
                    if store_fn is not None:
                        store_fn(mt, ch, tok_e)
            return (evc, nmt * 4)

        twq = cast_w(Wq, ND, D, tok_xnb)
        tok_q = project(xnb, lambda mt, ch: QT[:, mt * T + ch * 512: mt * T + ch * 512 + 512],
                        twq, "q")
        twkv = cast_w_sum(Wk, Wv, ND, tok_q)
        kvs = vtok[:].rearrange("p c -> p c")[:, 0:ND * T]
        tok_kv = project(xnb, lambda mt, ch: kvs[:, mt * T + ch * 512: mt * T + ch * 512 + 512],
                         twkv, "kv")
        twd = cast_w(Wdkv, ND, 256, tok_kv)
        tok_lat = project(kvs, lambda mt, ch: latT[:, mt * T + ch * 512: mt * T + ch * 512 + 512],
                          twd, "lat", nk=ND, nmt=2, width=256)
        twuk = cast_w(Wuk, 2, D, tok_lat)
        tok_kr = project(latT, lambda mt, ch: KrT[:, mt * T + ch * 512: mt * T + ch * 512 + 512],
                         twuk, "kr", nk=2, nmt=ND, width=D)
        twuv = cast_w(Wuv, 2, D, tok_kr)
        vview = vtok[:].rearrange("p (tt h c) -> p tt h c", tt=NT, h=NHD, c=65)
        vmm = p.ctr("vmm")
        vev = p.ctr("vev")
        for tt in range(NT):
            for ch in range(2):
                i = tt * 2 + ch
                ps = psP if i % 2 == 0 else psP2
                for ks in range(2):
                    w = [twuv, tok_kr] if (i == 0 and ks == 0) else []
                    if ks == 0 and i >= 2:
                        w.append((vev, i - 1))
                    p.on("tensor", lambda e, tt=tt, ch=ch, ks=ks, ps=ps,
                         st=(ks == 0), sp=(ks == 1):
                         e.matmul(ps[:, :],
                                  latT[:, ks * T + tt * 128:ks * T + (tt + 1) * 128],
                                  wbf[:, ks * D + ch * 512: ks * D + ch * 512 + 512],
                                  start=st, stop=sp), w, vmm)
                p.on("scalar", lambda e, tt=tt, ch=ch, ps=ps: e.activation(
                    vview[:, tt, ch * 8:(ch + 1) * 8, 0:64],
                    ps[:, :].rearrange("p (h c) -> p h c", c=64), AF.Copy),
                    [(vmm, (i + 1) * 2)], vev)
        tok_v = (vev, NT * 2)
        tok_ones = p.on("vector", lambda e: e.memset(
            vview[:, :, :, 64:65], 1.0), [tok_v], "vone")

        dld = p.ctr("dld")
        smm = p.ctr("smm")
        sdv = p.ctr("sdv")
        sev = p.ctr("sev")
        avm = p.ctr("avm")
        dcp = p.ctr("dcp")
        dln = p.ctr("dln")
        dbc = p.ctr("dbc")
        aev = p.ctr("aev")
        anm = p.ctr("anm")
        hq_done = {}
        for qc in range(4):
            wdec = [alldma] if qc == 0 else [hq_done[(qc - 1, NHD - 1)]]
            tdec = p.dma(decb[:].rearrange("p (kt c) -> p kt c", kt=NT),
                         decay[12 - 4 * qc:28 - 4 * qc], waits=wdec, inc=dld)
            for h in range(NHD):
                base = (h % 2) * 64
                prevhq = None
                if (qc, h) != (0, 0):
                    prevhq = hq_done[(qc, h - 1)] if h > 0 else hq_done[(qc - 1, NHD - 1)]
                se_toks = []
                for kt in range(NT):
                    ps = psS0 if kt % 2 == 0 else psS1
                    w = [tok_q, tok_kr, tdec]
                    if prevhq is not None:
                        w.append(prevhq)
                    if kt >= 2:
                        w = [(sev, sev.v - 1)]
                    tsm = p.on("tensor", lambda e, h=h, qc=qc, kt=kt, ps=ps, base=base: e.matmul(
                        ps[:, :],
                        KrT[:, (h // 2) * T + kt * 128:(h // 2) * T + (kt + 1) * 128][base:base + 64, :],
                        QT[:, (h // 2) * T + qc * 512:(h // 2) * T + qc * 512 + 512][base:base + 64, :],
                        start=True, stop=True, tile_position=(base, 0)), w, smm)
                    wv = [tsm]
                    if kt >= 2:
                        wv.append(se_toks[kt - 2])
                    tdv = p.on("vector", lambda e, kt=kt, ps=ps: e.tensor_tensor(
                        sdb[:, (kt % 2) * 512:(kt % 2) * 512 + 512], ps[:, :],
                        decb[:, kt * 512:(kt + 1) * 512], OP.mult),
                        wv, sdv)
                    tse = p.on("scalar", lambda e, kt=kt: e.activation(
                        expt[:, kt * 512:(kt + 1) * 512],
                        sdb[:, (kt % 2) * 512:(kt % 2) * 512 + 512], AF.Exp,
                        scale=SSC), [tdv], sev)
                    se_toks.append(tse)
                for kt in range(NT):
                    w = [se_toks[kt], tok_ones]
                    if kt == 0 and prevhq is not None:
                        w.append((anm, anm.v))
                    p.on("tensor", lambda e, kt=kt, h=h, st=(kt == 0), sp=(kt == NT - 1):
                         e.matmul(psAV[0:65, :],
                                  vtok[:, kt * NHD * 65 + h * 65: kt * NHD * 65 + (h + 1) * 65],
                                  expt[:, kt * 512:(kt + 1) * 512],
                                  start=st, stop=sp), w, avm)
                t1 = p.on("vector", lambda e: e.tensor_copy(rden[0:1, 0:512], psAV[64:65, :]),
                          [(avm, avm.v), (tok_v if (qc, h) == (0, 0) else (avm, avm.v))], dcp)
                t2 = p.on("scalar", lambda e: e.activation(rden[0:1, 512:1024], rden[0:1, 0:512],
                                                           AF.Ln), [t1], dln)
                t3 = p.on("scalar", lambda e: e.activation(rden[0:1, 0:512], rden[0:1, 512:1024],
                                                           AF.Exp, scale=-1.0), [t2], dln)
                t4 = p.on("tensor", lambda e: e.matmul(psM[0:64, :], onesr[0:1, 0:64],
                                                       rden[0:1, 0:512],
                                                       start=True, stop=True), [t3], dbc)
                t5 = p.on("scalar", lambda e: e.activation(aout[0:64, 0:512], psAV[0:64, :],
                                                           AF.Copy), [t3], aev)
                hq_done[(qc, h)] = p.on("vector", lambda e, h=h, qc=qc, base=base: e.tensor_tensor(
                    aoT[:, (h // 2) * T + qc * 512:(h // 2) * T + qc * 512 + 512][base:base + 64, :],
                    aout[0:64, 0:512], psM[0:64, :], OP.mult), [t4, t5], anm)

        twO = cast_w(Wo, ND, D, hq_done[(3, NHD - 1)])
        sto = [p.ctr("so0"), p.ctr("so1")]

        def st_fn(mt, ch, tok_e):
            i = mt * 4 + ch
            p.dma(aT.rearrange("(mt p) t -> mt p t", p=128)[mt][:, ch * 512:(ch + 1) * 512],
                  aout[:, (i % 2) * 512:(i % 2) * 512 + 512], waits=[tok_e], inc=sto[i % 2])

        def dstf(mt, ch):
            i = mt * 4 + ch
            return aout[:, (i % 2) * 512:(i % 2) * 512 + 512]

        tok_o = project(aoT, dstf, twO, "o", store_fn=st_fn,
                        egate_fn=lambda i: (sto[i % 2], ((i - 2) // 2 + 1) * 16) if i >= 2 else None)



def emit_attn1(nc, p, pfx, B, xnT, WinT, WrT, WkvT, WbT, bbt, WqT, WoT,
               blk4, sel4, ident, aT, knD, qD, r4D, uwD, smD):
    NS = 32
    CH = 32
    NCH = T // CH
    with ExitStack() as ctx:
        xnb = ctx.enter_context(nc.sbuf_tensor(pfx + "xnb", [128, ND * T], BF16))
        xpb = ctx.enter_context(nc.sbuf_tensor(pfx + "xpb", [128, ND * T], BF16))
        wbf = ctx.enter_context(nc.sbuf_tensor(pfx + "wbf", [128, ND * D], BF16))
        wst = ctx.enter_context(nc.sbuf_tensor(pfx + "wst", [128, 2 * D], F32))
        kvf = ctx.enter_context(nc.sbuf_tensor(pfx + "kvf", [128, 2 * T], F32))
        betab = ctx.enter_context(nc.sbuf_tensor(pfx + "betab", [128, T], F32))
        qf = ctx.enter_context(nc.sbuf_tensor(pfx + "qf", [32, T], F32))
        lgt = ctx.enter_context(nc.sbuf_tensor(pfx + "lgt", [4, T], F32))
        rn4 = ctx.enter_context(nc.sbuf_tensor(pfx + "rn4", [4, T], F32))
        rsc = ctx.enter_context(nc.sbuf_tensor(pfx + "rsc", [4, T], F32))
        u4 = ctx.enter_context(nc.sbuf_tensor(pfx + "u4", [4, T], F32))
        s4 = ctx.enter_context(nc.sbuf_tensor(pfx + "s4", [4, T], F32))
        rnb = ctx.enter_context(nc.sbuf_tensor(pfx + "rnb", [128, T], F32))
        sqs = ctx.enter_context(nc.sbuf_tensor(pfx + "sqs", [128, 512], F32))
        knr = ctx.enter_context(nc.sbuf_tensor(pfx + "knr", [128, 2 * CH * 33], F32))
        qr = ctx.enter_context(nc.sbuf_tensor(pfx + "qr", [128, 2 * CH * NS], F32))
        tok = ctx.enter_context(nc.sbuf_tensor(pfx + "tok", [128, 64], F32))
        stg = ctx.enter_context(nc.sbuf_tensor(pfx + "stg", [128, 320], F32))
        cellb = ctx.enter_context(nc.sbuf_tensor(pfx + "cellb", [32, T], BF16))
        aor = ctx.enter_context(nc.sbuf_tensor(pfx + "aor", [128, 2 * 512], F32))
        idt = ctx.enter_context(nc.sbuf_tensor(pfx + "idt", [128, 128], F32))
        blkb = ctx.enter_context(nc.sbuf_tensor(pfx + "blkb", [128, 4], F32))
        selb = ctx.enter_context(nc.sbuf_tensor(pfx + "selb", [128, 32], F32))
        bbs = ctx.enter_context(nc.sbuf_tensor(pfx + "bbs", [128, 1], F32))
        Sst = ctx.enter_context(nc.sbuf_tensor(pfx + "Sst", [128, 33], F32))
        scr = ctx.enter_context(nc.sbuf_tensor(pfx + "scr", [128, 33], F32))
        scr2 = ctx.enter_context(nc.sbuf_tensor(pfx + "scr2", [128, NS], F32))
        z0t = ctx.enter_context(nc.sbuf_tensor(pfx + "z0t", [128, NS], F32))
        spt = ctx.enter_context(nc.sbuf_tensor(pfx + "spt", [128, NS], F32))
        zbt = ctx.enter_context(nc.sbuf_tensor(pfx + "zbt", [128, NS], F32))
        Ut = ctx.enter_context(nc.sbuf_tensor(pfx + "Ut", [128, NS], F32))
        nsum = ctx.enter_context(nc.sbuf_tensor(pfx + "nsum", [128, 4], F32))
        psP = ctx.enter_context(nc.psum_tensor(pfx + "psP", [128, 512], F32))
        psP2 = ctx.enter_context(nc.psum_tensor(pfx + "psP2", [128, 512], F32))
        psZ = ctx.enter_context(nc.psum_tensor(pfx + "psZ", [128, 512], F32))
        psB = ctx.enter_context(nc.psum_tensor(pfx + "psB", [128, 512], F32))
        xnbf = xnb[:].bitcast(F32)
        uwb = lambda: xnbf[:, 0:T]
        smb = lambda: xnbf[:, T:2 * T]
        wmb = lambda: xnbf[:, 2 * T:3 * T]
        outsq = lambda: xnbf[:, 3 * T:4 * T]

        p.dma(idt[:], ident, waits=B)
        p.dma(blkb[:], blk4, waits=B)
        p.dma(selb[:], sel4, waits=B)
        p.dma(bbs[:], bbt, waits=B)
        alldma = (p.ctr("dma"), p.ctr("dma").v)
        cast_dn = p.ctr("cast")
        xst = [p.ctr("xst0"), p.ctr("xst1")]
        for g in range(4):
            w = list(B) if g == 0 else [(cast_dn, 2 * g)]
            lt = p.dma(kvf[:, 0:2 * T].rearrange("p (a t) -> p a t", a=2),
                       xnT.rearrange("(dt p) t -> p dt t", p=128)[:, 2 * g:2 * g + 2, :],
                       waits=w, inc=xst[g % 2])
            for j in range(2):
                p.on("vector", lambda e, g=g, j=j: e.tensor_copy(
                    xnb[:, (2 * g + j) * T:(2 * g + j + 1) * T],
                    kvf[:, j * T:(j + 1) * T]), [lt], cast_dn)
        tok_xnb = (cast_dn, ND)

        wld = [p.ctr("wl0"), p.ctr("wl1")]
        wcast = p.ctr("wcast")

        def cast_w(wsrc, nslab, width, first_w):
            toks = []
            for s_i in range(nslab):
                w = [first_w] if s_i < 2 else [(wcast, wcast.v - 1)]
                lt = p.dma(wst[:, (s_i % 2) * D:(s_i % 2) * D + width],
                           wsrc.rearrange("(s p) c -> s p c", p=128)[s_i],
                           waits=w, inc=wld[s_i % 2])
                toks.append(p.on("vector", lambda e, s_i=s_i, width=width: e.tensor_copy(
                    wbf[:, s_i * width:(s_i + 1) * width],
                    wst[:, (s_i % 2) * D:(s_i % 2) * D + width]), [lt], wcast))
            return toks[-1]

        def project(src_bf, dst_fn, wtok, tag, nk=ND, nmt=ND, width=D, act=None,
                    bias=None, mwidth=128):
            mmc = p.ctr(tag + "m")
            evc = p.ctr(tag + "e")
            for mt in range(nmt):
                for ch in range(4):
                    i = mt * 4 + ch
                    ps = psP if i % 2 == 0 else psP2
                    for dt in range(nk):
                        w = [wtok] if (i == 0 and dt == 0) else []
                        if dt == 0 and i >= 2:
                            w.append((evc, i - 1))
                        p.on("tensor", lambda e, mt=mt, ch=ch, dt=dt, ps=ps,
                             st=(dt == 0), sp=(dt == nk - 1):
                             e.matmul(ps[0:mwidth, :],
                                      wbf[:, dt * width + mt * mwidth:dt * width + (mt + 1) * mwidth],
                                      src_bf[:, dt * T + ch * 512: dt * T + ch * 512 + 512],
                                      start=st, stop=sp), w, mmc)
                    if bias is not None:
                        p.on("scalar", lambda e, mt=mt, ch=ch, ps=ps: e.activation(
                            dst_fn(mt, ch), ps[0:mwidth, :], act, bias=bias[:, 0:1]),
                            [(mmc, (i + 1) * nk), alldma], evc)
                    else:
                        p.on("scalar", lambda e, mt=mt, ch=ch, ps=ps: e.activation(
                            dst_fn(mt, ch), ps[0:mwidth, :],
                            act if act is not None else AF.Copy),
                            [(mmc, (i + 1) * nk)], evc)
            return (evc, nmt * 4)

        twin = cast_w(WinT, ND, D, tok_xnb)
        tok_xp = project(xnb, lambda mt, ch: xpb[:, mt * T + ch * 512: mt * T + ch * 512 + 512],
                         twin, "xp", act=AF.Silu)
        twb = cast_w(WbT, ND, 128, tok_xp)
        tok_b = project(xpb, lambda mt, ch: betab[:, ch * 512:(ch + 1) * 512],
                        twb, "bt", nmt=1, width=128, act=AF.Sigmoid, bias=bbs)
        negb = p.ctr("negb")
        for chn in range(4):
            p.on("vector", lambda e, chn=chn: e.tensor_scalar(
                betab[:, chn * 512:(chn + 1) * 512],
                betab[:, chn * 512:(chn + 1) * 512], -1.0, None, OP.mult),
                [tok_b], negb)
        tok_bn = (negb, 4)
        twkv = cast_w(WkvT, ND, 256, tok_b)
        tok_kv = project(xpb, lambda mt, ch: kvf[:, mt * T + ch * 512: mt * T + ch * 512 + 512],
                         twkv, "kv", nmt=2, width=256)
        twr = cast_w(WrT, ND, 4, tok_kv)
        tok_r = project(xpb, lambda mt, ch: lgt[0:4, ch * 512:(ch + 1) * 512],
                        twr, "rt", nmt=1, width=4, mwidth=4)
        twq = cast_w(WqT, ND, 32, tok_r)
        tok_qp = project(xpb, lambda mt, ch: qf[0:32, ch * 512:(ch + 1) * 512],
                         twq, "qp", nmt=1, width=32, mwidth=32)

        kq = p.ctr("kq")
        knm = p.ctr("knm")
        kne = p.ctr("kne")
        for ch in range(4):
            tsq = p.on("scalar", lambda e, ch=ch: e.activation(
                sqs[:, 0:512], kvf[:, ch * 512:(ch + 1) * 512], AF.Square),
                [tok_kv] if ch == 0 else [(knm, ch)], kq)
            tm = p.on("tensor", lambda e, ch=ch: e.matmul(
                psB[0:4, :], blkb[:], sqs[:, 0:512],
                start=True, stop=True), [tsq, alldma] + ([(kne, ch)] if ch else []), knm)
            p.on("vector", lambda e, ch=ch: e.tensor_copy(
                rn4[0:4, ch * 512:(ch + 1) * 512], psB[0:4, :]), [tm], kne)
        n1 = p.on("scalar", lambda e: e.activation(rsc[0:4, :], rn4[0:4, :], AF.Ln),
                  [(kne, 4)], "nrs")
        n2 = p.on("scalar", lambda e: e.activation(rn4[0:4, :], rsc[0:4, :], AF.Exp, scale=0.5),
                  [n1], "nrs")
        n3 = p.on("vector", lambda e: e.tensor_scalar(rn4[0:4, :], rn4[0:4, :], 1e-6, None,
                                                      OP.add), [n2], "nrv")
        n4 = p.on("scalar", lambda e: e.activation(rsc[0:4, :], rn4[0:4, :], AF.Ln), [n3], "nrs")
        n5 = p.on("scalar", lambda e: e.activation(rn4[0:4, :], rsc[0:4, :], AF.Exp, scale=-1.0),
                  [n4], "nrs")
        b1 = p.dma(r4D, rn4[0:4, :], waits=[n5], inc="b1a")
        b2toks = []
        for n in range(4):
            b2toks.append(p.dma(rnb[n * 32:(n + 1) * 32, :],
                                r4D[n:n + 1, :].to_broadcast([32, T]), waits=[b1], inc="b2r"))
        tok_rnb = (p.ctr("b2r"), p.ctr("b2r").v)
        knd = p.ctr("knd")
        for ch in range(4):
            p.on("vector", lambda e, ch=ch: e.tensor_tensor(
                kvf[:, ch * 512:(ch + 1) * 512], kvf[:, ch * 512:(ch + 1) * 512],
                rnb[:, ch * 512:(ch + 1) * 512], OP.mult), [tok_rnb, (kq, 4)], knd)
        tok_kn = (knd, 4)

        tmm = p.ctr("tmm")
        tev = p.ctr("tev")
        t2v = p.ctr("t2v")
        t2s = p.ctr("t2s")
        bmm = p.ctr("bmm")
        bev = p.ctr("bev")
        p.on("vector", lambda e: e.memset(tok[:, 8:16], -1e30), [alldma], "ms")
        for k in range(NT):
            w = [tok_r, alldma, (kne, 4)] if k == 0 else [(bev, 2 * k)]
            tk = p.on("tensor", lambda e, k=k: e.transpose(
                psB[0:128, 0:4], lgt[0:4, k * 128:(k + 1) * 128], idt[0:4, 0:4]), w, tmm)
            te = p.on("vector", lambda e: e.tensor_copy(tok[:, 0:4], psB[0:128, 0:4]),
                      [tk], tev)
            c1 = p.on("vector", lambda e: e.tensor_copy(tok[:, 8:12], tok[:, 0:4]), [te], t2v)
            c2 = p.on("vector", lambda e: e.max(out=tok[:, 16:24], in_=tok[:, 8:16]), [c1], t2v)
            c3 = p.on("vector", lambda e: e.tensor_scalar(tok[:, 24:28], tok[:, 0:4],
                                                          tok[:, 17:18], None, OP.is_ge),
                      [c2], t2v)
            c4 = p.on("scalar", lambda e: e.activation(tok[:, 28:32], tok[:, 0:4], AF.Exp),
                      [c2], t2s)
            c5 = p.on("vector", lambda e: e.reduce_sum(tok[:, 32:33], tok[:, 28:32], axis=AX),
                      [c4, c3], t2v)
            c6 = p.on("vector", lambda e: e.reciprocal(tok[:, 33:34], tok[:, 32:33]), [c5], t2v)
            c7 = p.on("vector", lambda e: e.tensor_scalar(tok[:, 34:38], tok[:, 28:32],
                                                          tok[:, 33:34], None, OP.mult),
                      [c6], t2v)
            c8 = p.on("vector", lambda e: e.tensor_tensor(tok[:, 38:42], tok[:, 24:28],
                                                          tok[:, 34:38], OP.mult), [c7], t2v)
            c9 = p.on("vector", lambda e: e.reduce_sum(tok[:, 42:43], tok[:, 38:42], axis=AX),
                      [c8], t2v)
            c10 = p.on("vector", lambda e: e.tensor_scalar(tok[:, 43:44], tok[:, 42:43], 1e-8,
                                                           None, OP.add), [c9], t2v)
            c11 = p.on("vector", lambda e: e.reciprocal(tok[:, 44:45], tok[:, 43:44]), [c10], t2v)
            c12 = p.on("vector", lambda e: e.tensor_scalar(tok[:, 45:49], tok[:, 38:42],
                                                           tok[:, 44:45], -1.0,
                                                           OP.mult, OP.mult),
                       [c11], t2v)
            tku = p.on("tensor", lambda e, k=k: e.transpose(psB[0:4, 128:256],
                                                             tok[:, 45:49], idt[:]),
                       [c12], bmm)
            p.on("vector", lambda e, k=k: e.tensor_copy(u4[0:4, k * 128:(k + 1) * 128],
                                                        psB[0:4, 128:256]),
                 [tku], bev)
            tks = p.on("tensor", lambda e, k=k: e.transpose(psB[0:4, 256:384],
                                                            tok[:, 34:38], idt[:]),
                       [c12], bmm)
            p.on("vector", lambda e, k=k: e.tensor_copy(s4[0:4, k * 128:(k + 1) * 128],
                                                        psB[0:4, 256:384]),
                 [tks], bev)
        bu = p.dma(uwD, u4[0:4, :], waits=[(bev, 2 * NT), tok_xp], inc="b1b")
        bs = p.dma(smD, s4[0:4, :], waits=[(bev, 2 * NT)], inc="b1c")
        for n in range(4):
            p.dma(uwb()[n * 32:(n + 1) * 32, :],
                  uwD[n:n + 1, :].to_broadcast([32, T]), waits=[bu], inc="b2u")
            p.dma(smb()[n * 32:(n + 1) * 32, :],
                  smD[n:n + 1, :].to_broadcast([32, T]), waits=[bs], inc="b2u")
        tok_uw = (p.ctr("b2u"), p.ctr("b2u").v)
        wmd = p.ctr("wmd")
        for ch in range(4):
            p.on("vector", lambda e, ch=ch: e.tensor_scalar(
                wmb()[:, ch * 512:(ch + 1) * 512], uwb()[:, ch * 512:(ch + 1) * 512],
                1.0, 1.0, OP.mult, OP.add), [tok_uw], wmd)
        tok_wm = (wmd, 4)

        kdm = p.ctr("kdm")
        kde = p.ctr("kde")
        kds = [p.ctr("kds0"), p.ctr("kds1")]
        for k in range(NT):
            w = [tok_kn, (bev, 2 * NT)] if k == 0 else []
            if k >= 2:
                w.append((kds[k % 2], (k // 2) * 16))
            tk = p.on("tensor", lambda e, k=k: e.transpose(
                psB[:, 0:128], kvf[:, k * 128:(k + 1) * 128], idt[:]),
                w + ([(kde, k)] if k >= 1 else []), kdm)
            te = p.on("vector", lambda e, k=k: e.tensor_scalar(
                stg[:, (k % 2) * 128:(k % 2) * 128 + 128], psB[:, 0:128],
                -1.0, None, OP.mult),
                [tk] + ([(kds[k % 2], (k // 2) * 16)] if k >= 2 else []), kde)
            p.dma(knD.rearrange("(k p) c -> k p c", p=128)[k],
                  stg[:, (k % 2) * 128:(k % 2) * 128 + 128],
                  waits=[te], inc=kds[k % 2])
        qdm = p.ctr("qdm")
        qde = p.ctr("qde")
        qds = [p.ctr("qds0"), p.ctr("qds1")]
        for k in range(NT):
            w = [tok_qp, (kde, NT)] if k == 0 else []
            tk = p.on("tensor", lambda e, k=k: e.transpose(
                psB[0:128, 128:160], qf[0:32, k * 128:(k + 1) * 128], idt[0:32, 0:32]),
                w + ([(qde, k)] if k >= 1 else []), qdm)
            te = p.on("vector", lambda e, k=k: e.tensor_copy(
                stg[:, 256 + (k % 2) * 32:256 + (k % 2) * 32 + 32], psB[:, 128:160]),
                [tk] + ([(qds[k % 2], (k // 2) * 16)] if k >= 2 else []), qde)
            p.dma(qD.rearrange("(k p) c -> k p c", p=128)[k],
                  stg[:, 256 + (k % 2) * 32:256 + (k % 2) * 32 + 32],
                  waits=[te], inc=qds[k % 2])
        tok_knD = (kds[0], kds[0].v)
        tok_knD1 = (kds[1], kds[1].v)
        tok_qD = (qds[0], qds[0].v)
        tok_qD1 = (qds[1], qds[1].v)

        knl = [p.ctr("knl0"), p.ctr("knl1")]
        ql = [p.ctr("ql0"), p.ctr("ql1")]
        vz = p.ctr("vz")
        va = p.ctr("va")
        th = p.ctr("th")
        svc = p.ctr("sv")
        iz0 = p.on("vector", lambda e: e.memset(Sst[:], 0.0),
                   [tok_wm, tok_uw, tok_rnb, tok_bn, (tev, NT)], "ms")
        iz1 = p.on("vector", lambda e: e.memset(z0t[:], 0.0), [], "ms")
        iz2 = p.on("vector", lambda e: e.memset(spt[:], 0.0), [], "ms")
        izk = p.on("vector", lambda e: e.memset(knr[:], 1.0), [], "ms")
        iz = p.on("vector", lambda e: e.tensor_copy(
            Sst[:, 32:33], kvf[:, T:T + 1]), [iz2, tok_kv], "ms")
        for c in range(NCH):
            wdk = [tok_knD, tok_knD1, tok_qD, tok_qD1, izk] if c < 2 else \
                [(va, (c - 1) * CH)]
            kt4 = []
            for n in range(4):
                kt4.append(p.dma(
                    knr[n * 32:(n + 1) * 32, (c % 2) * CH * 33:((c % 2) + 1) * CH * 33]
                    .rearrange("p (t j) -> p t j", j=33)[:, :, 0:NS],
                    knD.rearrange("t (n j) -> n t j", n=4)[n, c * CH:(c + 1) * CH, :]
                    .unsqueeze(0).to_broadcast([32, CH, NS]),
                    waits=wdk, inc=knl[c % 2]))
            qt1 = p.dma(qr[:, (c % 2) * CH * NS:((c % 2) + 1) * CH * NS]
                        .rearrange("p (t j) -> p t j", j=NS),
                        qD[c * CH:(c + 1) * CH, :].unsqueeze(0).to_broadcast([128, CH, NS]),
                        waits=wdk, inc=ql[c % 2])
            for s_ in range(CH):
                t = c * CH + s_
                knt = knr[:, (c % 2) * CH * 33 + s_ * 33:(c % 2) * CH * 33 + (s_ + 1) * 33]
                qt = qr[:, (c % 2) * CH * NS + s_ * NS:(c % 2) * CH * NS + (s_ + 1) * NS]
                w1 = [iz, iz1] if t == 0 else []
                if s_ == 0:
                    w1 += [kt4[-1], qt1]
                t_op1 = p.on("vector", lambda e, knt=knt: e.scalar_tensor_tensor(
                    scr[:], Sst[:], 1.0, knt, OP.mult, OP.mult, accum_out=nsum[:, 0:1]),
                    w1, svc)
                t_op5 = None
                if t > 0:
                    t_op5 = p.on("vector", lambda e, qtp=qt_prev, tp=t - 1:
                                 e.scalar_tensor_tensor(
                        scr2[:], Sst[:, 0:NS], 1.0, qtp, OP.mult, OP.mult,
                        accum_out=outsq()[:, tp:tp + 1]), [], va)
                t_z = p.on("vector", lambda e, knt=knt: e.scalar_tensor_tensor(
                    zbt[:], knt[:, 0:NS], nsum[:, 0:1], z0t[:], OP.mult, OP.add),
                    [t_op1], vz)
                if t + 1 < T:
                    p.on("vector", lambda e, t=t: e.tensor_copy(
                        Sst[:, 32:33], kvf[:, T + t + 1:T + t + 2]), [], "sv2")
                t_th = p.on("scalar", lambda e: e.activation(Ut[:], zbt[:], AF.Tanh),
                            [(vz, t + 1)], th)
                t_s4 = p.on("vector", lambda e, t=t: e.scalar_tensor_tensor(
                    Sst[:, 0:NS], Ut[:], uwb()[:, t:t + 1], spt[:], OP.mult, OP.add),
                    [(th, t + 1), t_op5], svc)
                if t + 1 < T:
                    p.on("vector", lambda e, t=t: e.tensor_scalar(
                        z0t[:], Sst[:, 0:NS], betab[:, t + 1:t + 2], None, OP.mult),
                        [t_s4], "sv2")
                    p.on("vector", lambda e, t=t: e.tensor_scalar(
                        spt[:], Sst[:, 0:NS], wmb()[:, t + 1:t + 2], None, OP.mult),
                        [], "sv2")
                qt_prev = qt
        p.on("vector", lambda e: e.scalar_tensor_tensor(
            scr2[:], Sst[:, 0:NS], 1.0, qt_prev, OP.mult, OP.mult,
            accum_out=outsq()[:, T - 1:T]), [t_s4], va)

        p.on("vector", lambda e: e.tensor_copy(nsum[:, 3:4], nsum[:, 0:1]), [(va, T)], va)
        pg = p.ctr("pg")
        sg = p.ctr("sg")
        for ch in range(4):
            sl = lambda b, ch=ch: b()[:, ch * 512:(ch + 1) * 512]
            g1 = p.on("scalar", lambda e, ch=ch: e.activation(
                betab[:, ch * 512:(ch + 1) * 512], outsq()[:, ch * 512:(ch + 1) * 512],
                AF.Sigmoid), [(va, T + 1)], sg)
            g2b = p.on("vector", lambda e, ch=ch: e.tensor_tensor(
                rnb[:, ch * 512:(ch + 1) * 512], outsq()[:, ch * 512:(ch + 1) * 512],
                outsq()[:, ch * 512:(ch + 1) * 512], OP.mult), [g1], pg)
            g3 = p.on("vector", lambda e, ch=ch: e.tensor_tensor(
                rnb[:, ch * 512:(ch + 1) * 512], rnb[:, ch * 512:(ch + 1) * 512],
                betab[:, ch * 512:(ch + 1) * 512], OP.mult), [g2b], pg)
            g4 = p.on("vector", lambda e, ch=ch: e.tensor_tensor(
                rnb[:, ch * 512:(ch + 1) * 512], rnb[:, ch * 512:(ch + 1) * 512],
                smb()[:, ch * 512:(ch + 1) * 512], OP.mult), [g3], pg)
        crm = p.ctr("crm")
        cre = p.ctr("cre")
        for ch in range(4):
            tm = p.on("tensor", lambda e, ch=ch: e.matmul(
                psB[0:32, 0:512], selb[:], rnb[:, ch * 512:(ch + 1) * 512],
                start=True, stop=True), [(pg, 12)] + ([(cre, ch)] if ch else []), crm)
            p.on("scalar", lambda e, ch=ch: e.activation(
                cellb[0:32, ch * 512:(ch + 1) * 512], psB[0:32, 0:512], AF.Copy),
                [tm], cre)
        lwo = p.dma(wst[0:32, 0:D], WoT, waits=[], inc="wl0")
        cwo = p.on("vector", lambda e: e.tensor_copy(wbf[0:32, 0:D], wst[0:32, 0:D]),
                   [lwo], wcast)
        fmm = p.ctr("fmm")
        fev = p.ctr("fev")
        fso = [p.ctr("fso0"), p.ctr("fso1")]
        for mt in range(ND):
            for ch in range(4):
                i = mt * 4 + ch
                ps = psP if i % 2 == 0 else psP2
                w = [cwo, (cre, 4)] if i == 0 else []
                if i >= 2:
                    w.append((fev, i - 1))
                p.on("tensor", lambda e, mt=mt, ch=ch, ps=ps: e.matmul(
                    ps[:, :], wbf[0:32, mt * 128:(mt + 1) * 128],
                    cellb[0:32, ch * 512:(ch + 1) * 512],
                    start=True, stop=True), w, fmm)
                ew = [(fmm, i + 1)]
                if i >= 2:
                    ew.append((fso[i % 2], ((i - 2) // 2 + 1) * 16))
                te = p.on("scalar", lambda e, i=i, ps=ps: e.activation(
                    aor[:, (i % 2) * 512:(i % 2) * 512 + 512], ps[:, :], AF.Copy), ew, fev)
                p.dma(aT.rearrange("(mt p) t -> mt p t", p=128)[mt][:, ch * 512:(ch + 1) * 512],
                      aor[:, (i % 2) * 512:(i % 2) * 512 + 512], waits=[te], inc=fso[i % 2])




def _consts():
    import ml_dtypes
    c = 1e-4
    dec = np.zeros((28, 128, 512), np.float64)
    pp = np.arange(128)[:, None]
    jj = np.arange(512)[None, :]
    for mi in range(28):
        m = mi - 12
        diff = (128 * m + pp) - jj
        dec[mi] = np.exp(-c * np.abs(diff))
    blk = np.zeros((128, 4), np.float32)
    for n in range(4):
        blk[n * 32:(n + 1) * 32, n] = 1.0
    sel = np.zeros((128, 32), np.float32)
    for n in range(4):
        for i in range(32):
            sel[n * 32 + i, i] = 1.0
    return {
        "ident": np.eye(128, dtype=np.float32),
        "onescol": np.ones((128, 1), np.float32),
        "ones1x": np.ones((1, 128), np.float32),
        "decay": dec.astype(ml_dtypes.bfloat16),
        "blk4": blk, "sel4": sel,
    }


def build_fused():
    nc = _nc()
    DI = lambda name, shape: nc.dram_tensor(name, shape, F32, kind="ExternalInput").ap()
    IN = lambda name, shape: nc.dram_tensor(name, shape, F32, kind="Internal").ap()
    x = DI("x", [T, D])
    ident = DI("ident", [128, 128])
    onescol = DI("onescol", [128, 1])
    ones1x = DI("ones1x", [1, 128])
    decay = nc.dram_tensor("decay", [28, 128, 512], BF16, kind="ExternalInput").ap()
    blk4 = DI("blk4", [128, 4])
    sel4 = DI("sel4", [128, 32])
    Wq = DI("Wq", [D, D]); Wk = DI("Wk", [D, D])
    Wv = DI("Wv", [D, D]); Wo = DI("Wo", [D, D])
    Wdkv = DI("Wdkv", [D, 256]); Wuk = DI("Wuk", [256, D]); Wuv = DI("Wuv", [256, D])
    wg = DI("wg", [2816, D]); wu = DI("wu", [2816, D]); wd = DI("wd", [D, 2816])
    WinT = DI("WinT", [D, D]); WrT = DI("WrT", [D, 4]); WkvT = DI("WkvT", [D, 256])
    WbT = DI("WbT", [D, 128]); bbt = DI("bbt", [128, 1])
    WqT = DI("WqT", [D, 32]); WoT = DI("WoT", [32, D])
    wupT = DI("wupT", [D, 4096]); wdnT = DI("wdnT", [4096, D])
    out = nc.dram_tensor("out", [T, D], F32, kind="ExternalOutput").ap()
    xnT = IN("xnT", [D, T]); a0T = IN("a0T", [D, T]); f0T = IN("f0T", [D, T])
    x1T = IN("x1T", [D, T]); xn1T = IN("xn1T", [D, T])
    a1T = IN("a1T", [D, T]); f1T = IN("f1T", [D, T])
    xq_c = nc.dram_tensor("xq_cache", [NT, 128, D], BF16, kind="Internal").ap()
    wup_c = nc.dram_tensor("wup_cache", [32, ND, 128, 128], BF16, kind="Internal").ap()
    wdn_c = nc.dram_tensor("wdn_cache", [ND, 32, 128, 128], BF16, kind="Internal").ap()
    knD = IN("knD", [T, 128]); qD = IN("qD", [T, 32])
    r4D = IN("r4D", [4, T]); uwD = IN("uwD", [4, T]); smD = IN("smD", [4, T])
    with ExitStack() as ctx:
        p = Prog(nc, ctx)

        def stage(fn, pfx, Bprev, *args):
            p.begin_stage()
            fn(nc, p, pfx, Bprev, *args)
            return p.end_stage()

        Bt = stage(emit_ln0, "a", [], x, ident, xnT)
        Bt = stage(emit_attn0, "b", Bt, xnT, Wq, Wk, Wv, Wo, Wdkv, Wuk, Wuv,
                   decay, ones1x, a0T)
        Bt = stage(emit_ffn0, "c", Bt, xnT, wg, wu, wd, ident, f0T, xq_c)
        Bt = stage(emit_res_ln, "d", Bt, x, a0T, f0T, ident, onescol, ones1x,
                   x1T, xn1T)
        Bt = stage(emit_attn1, "e", Bt, xn1T, WinT, WrT, WkvT, WbT, bbt, WqT,
                   WoT, blk4, sel4, ident, a1T, knD, qD, r4D, uwD, smD)
        Bt = stage(emit_ffn1, "f", Bt, xn1T, wupT, wdnT, onescol, ones1x, f1T,
                   wup_c, wdn_c)
        Bt = stage(emit_final, "g", Bt, x1T, a1T, f1T, ident, out)
        p.begin_stage()
        p.on("sync", lambda e: e.nop(), Bt, None)
        p.emit()
    return nc


def kernel(**inputs):
    C = _consts()
    x = np.ascontiguousarray(inputs["L0_attn_x"], np.float32)
    Wkv = inputs["L1_attn_W_kv"]
    perm = [n * 64 + j for n in range(4) for j in range(32)] + \
           [n * 64 + 32 + i for n in range(4) for i in range(32)]
    shared = {
        "ident": C["ident"], "onescol": C["onescol"], "ones1x": C["ones1x"],
        "decay": C["decay"], "blk4": C["blk4"], "sel4": C["sel4"],
        "Wq": inputs["L0_attn_Wq"], "Wk": inputs["L0_attn_Wk"],
        "Wv": inputs["L0_attn_Wv"], "Wo": inputs["L0_attn_Wo"],
        "Wdkv": inputs["L0_attn_W_down_kv"], "Wuk": inputs["L0_attn_W_up_k"],
        "Wuv": inputs["L0_attn_W_up_v"],
        "wg": inputs["L0_ffn_w_gate"], "wu": inputs["L0_ffn_w_up"],
        "wd": inputs["L0_ffn_w_down"],
        "WinT": np.ascontiguousarray(inputs["L1_attn_W_in"].T),
        "WrT": np.ascontiguousarray(inputs["L1_attn_W_router"].T),
        "WkvT": np.ascontiguousarray(np.asarray(Wkv)[perm].T),
        "WbT": np.ascontiguousarray(inputs["L1_attn_W_beta"].T),
        "bbt": np.ascontiguousarray(np.asarray(inputs["L1_attn_b_beta"],
                                               np.float32).reshape(128, 1)),
        "WqT": np.ascontiguousarray(inputs["L1_attn_W_q"].T),
        "WoT": np.ascontiguousarray(inputs["L1_attn_W_out"].T),
        "wupT": np.ascontiguousarray(inputs["L1_ffn_w_up"].T),
        "wdnT": np.ascontiguousarray(inputs["L1_ffn_w_down"].T),
    }
    nc = build_fused()
    in_maps = [dict(shared, x=x[b]) for b in range(B)]
    res = run_bass_kernel_spmd(nc, in_maps, core_ids=list(range(B)))
    return np.stack([np.asarray(res.results[b]["out"]) for b in range(B)], 0)

